# version 07
# baseline (speedup 1.0000x reference)
import numpy as np
import ml_dtypes
import concourse.bass as bass
import concourse.mybir as mybir
from concourse.bass_utils import run_bass_kernel_spmd

F32 = mybir.dt.float32
BF16 = mybir.dt.bfloat16
I32 = mybir.dt.int32
AF = mybir.ActivationFunctionType
OP = mybir.AluOpType
AX = mybir.AxisListType
BF = ml_dtypes.bfloat16

T = 4096
NT = 32
NSB = 8
E = 1040
NH = 16
IF = 2816
NIC = 22

DEBUG_TAPS = []
STAGE = 99
GROUPS8 = [[0, 1, 2, 3, 4, 5, 6, 7]]
GROUPS2 = [[0, 1], [2, 3], [4, 5], [6, 7]]

ENGINES = ("PE", "ACT", "DVE", "POOL", "SP")
QUEUES = {"SP": "dmaSP", "ACT": "dmaACT", "POOL": "dmaPOOL"}


class Plan:
    LAGGED = ()
    RSLOT = 8

    def __init__(self):
        self.eng_ops = {e: [] for e in ENGINES}
        self.seq = {e: 0 for e in ENGINES}
        self.dma_cnt = {q: 0 for q in QUEUES.values()}
        self.slot_use = {}
        self.res = {}
        self.know = {e: {} for e in ENGINES}
        self.snap = {}
        self.echo = {}

    def _resource(self, r):
        return self.res.setdefault(r, {"w": None, "r": []})

    def _need(self, eng, ev):
        sp, val = ev
        return self.know[eng].get(sp, 0) < val

    def _absorb(self, eng, ev):
        k = self.know[eng]
        sp, val = ev
        if k.get(sp, 0) < val:
            k[sp] = val
            for sp2, v2 in self.snap.get(ev, {}).items():
                if k.get(sp2, 0) < v2:
                    k[sp2] = v2

    def op(self, eng, emit, reads=(), writes=(), aka=(), dma=False, cc_sem=None, tag="", stamp=True):
        waits = []
        for r in reads:
            st = self._resource(r)
            if st["w"] is not None and self._need(eng, st["w"]):
                waits.append(st["w"])
        for w in tuple(writes) + tuple(aka):
            st = self._resource(w)
            if st["w"] is not None and self._need(eng, st["w"]):
                waits.append(st["w"])
            for ev in st["r"]:
                if self._need(eng, ev):
                    waits.append(ev)
        wd = {}
        for sp, val in waits:
            wd[sp] = max(wd.get(sp, 0), val)
        sem_name = None
        if dma and cc_sem is None:
            q = QUEUES[eng]
            slot = self.dma_cnt[q] % self.RSLOT
            self.dma_cnt[q] += 1
            sem_name = f"{q}s{slot}"
            prior = self.slot_use.get(sem_name, 0)
            if prior > 0 and self.know[eng].get(sem_name, 0) < 16 * prior:
                wd[sem_name] = max(wd.get(sem_name, 0), 16 * prior)
            self.slot_use[sem_name] = prior + 1
        for sp in list(wd):
            if sp in self.LAGGED and sp != eng:
                val = wd[sp]
                if val == self.seq[sp]:
                    self.op(sp, self.echo[sp], tag="echo")
                wd[sp] = val + 1
        for spval in wd.items():
            self._absorb(eng, spval)
        if cc_sem is not None:
            ev = (cc_sem, 1)
        elif dma:
            ev = (sem_name, 16 * self.slot_use[sem_name])
        elif stamp:
            self.seq[eng] += 1
            ev = (eng, self.seq[eng])
        else:
            ev = (eng, self.seq[eng])
        self.snap.setdefault(ev, dict(self.know[eng]))
        for r in reads:
            self._resource(r)["r"].append(ev)
        for w in writes:
            st = self._resource(w)
            st["w"] = ev
            st["r"] = []
        self.eng_ops[eng].append({"emit": emit, "waits": sorted(wd.items()),
                                  "dma": dma, "cc": cc_sem, "tag": tag,
                                  "stamp": stamp, "sem": sem_name})
        return ev

    def barrier_all_on(self, eng):
        waits = {}
        for e2 in ENGINES:
            if e2 != eng and self.seq[e2] > 0:
                waits[e2] = self.seq[e2]
        for snm, n in self.slot_use.items():
            if n > 0:
                waits[snm] = 16 * n
        final = []
        for sp, val in sorted(waits.items()):
            if self.know[eng].get(sp, 0) < val:
                final.append((sp, val))
                self._absorb(eng, (sp, val))
        self.eng_ops[eng].append({"emit": None, "waits": final, "dma": False,
                                  "cc": None, "tag": "finalwait"})


class Alloc:
    def __init__(self, nc, limit):
        self.nc, self.limit, self.cur = nc, limit, 0
        self.table = {}

    def at(self, name, rows, cols, dtype, offset=None):
        nbytes = cols * {F32: 4, BF16: 2, I32: 4}[dtype]
        if offset is None:
            offset = self.cur
            self.cur += (nbytes + 31) & ~31
            assert self.cur <= self.limit, f"SBUF over budget at {name}: {self.cur}"
        else:
            assert offset + nbytes <= self.limit
        self.table[name] = (offset, nbytes)
        return self.nc.alloc_sbuf_tensor_at(name, [rows, cols], dtype, offset=offset)

    def off(self, name):
        return self.table[name][0]


def build_program(taps=()):
    nc = bass.Bass("TRN2", target_bir_lowering=False)
    p = Plan()
    taps = set(taps)
    base = nc.SBUF_PARTITION_SIZE_BYTES - nc.sbuf_bytes_remaining
    base = (base + 31) & ~31
    al = Alloc(nc, nc.SBUF_PARTITION_SIZE_BYTES - 512)
    al.cur = base

    d_x = nc.dram_tensor("x4096", [T, 1024], F32, kind="ExternalInput").ap()
    d_xh3 = nc.dram_tensor("xhalo", [3, 1024], F32, kind="ExternalInput").ap()
    d_mask3 = nc.dram_tensor("mask3", [3, 1], F32, kind="ExternalInput").ap()
    d_crow = nc.dram_tensor("crow", [1, 1024], F32, kind="ExternalInput").ap()
    d_wmod = nc.dram_tensor("wmod", [1024, 6144], F32, kind="ExternalInput").ap()
    d_winT = nc.dram_tensor("winT", [1024, E], F32, kind="ExternalInput").ap()
    d_convw = nc.dram_tensor("convw", [128, 32], F32, kind="ExternalInput").ap()
    d_convb = nc.dram_tensor("convb", [128, 8], F32, kind="ExternalInput").ap()
    d_wbase = nc.dram_tensor("wbase", [16, 1], F32, kind="ExternalInput").ap()
    d_woT = nc.dram_tensor("woT", [1024, 1024], F32, kind="ExternalInput").ap()
    d_wg = nc.dram_tensor("wg", [IF, 1024], F32, kind="ExternalInput").ap()
    d_wu = nc.dram_tensor("wu", [IF, 1024], F32, kind="ExternalInput").ap()
    d_wd = nc.dram_tensor("wd", [1024, IF], F32, kind="ExternalInput").ap()
    d_clipx = nc.dram_tensor("clipx", [1, 1], F32, kind="ExternalInput").ap()
    d_clipd = nc.dram_tensor("clipd", [1, 1], F32, kind="ExternalInput").ap()
    d_hsel = nc.dram_tensor("halfsel", [128, 2], F32, kind="ExternalInput").ap()
    d_wgT = nc.dram_tensor("wgTb", [NIC * 1024, 128], F32, kind="ExternalInput").ap()
    d_wuT = nc.dram_tensor("wuTb", [NIC * 1024, 128], F32, kind="ExternalInput").ap()
    d_wuT2 = d_wuT
    d_wdT = nc.dram_tensor("wdT", [IF, 1024], F32, kind="ExternalInput").ap()
    d_out = nc.dram_tensor("out", [T, 1024], F32, kind="ExternalOutput").ap()

    d_x2 = nc.dram_tensor("x2scr", [T, 1024], F32, kind="Internal").ap()
    d_h2 = nc.dram_tensor("h2scr", [T, 1024], F32, kind="Internal").ap()
    d_h3 = nc.dram_tensor("h3scr", [NIC * 128, T], F32, kind="Internal").ap()
    d_gin = nc.dram_tensor("ccgin", [64, 1056], F32, kind="Internal").ap()
    d_gout = nc.dram_tensor("ccgout", [128, 1056], F32, kind="Internal").ap()
    d_rin = nc.dram_tensor("ccrin", [1, 2], F32, kind="Internal").ap()
    d_rout = nc.dram_tensor("ccrout", [1, 2], F32, kind="Internal").ap()
    d_rin2 = nc.dram_tensor("ccrin2", [1, 2], F32, kind="Internal").ap()
    d_rout2 = nc.dram_tensor("ccrout2", [1, 2], F32, kind="Internal").ap()

    idn_f = nc.inline_tensor(np.eye(128, dtype=np.float32), name="c_idf").ap()
    idn_b = nc.inline_tensor(np.eye(128, dtype=BF), name="c_idb").ap()
    ut = np.zeros((128, 128), dtype=BF)
    for b_ in range(2):
        for kk in range(64):
            ut[b_ * 64 + kk, b_ * 64 + kk:(b_ + 1) * 64] = 1
    c_utbd = nc.inline_tensor(ut, name="c_utbd").ap()
    bcl = np.zeros((64, T), dtype=BF)
    for z in range(64):
        bcl[z, z * 64:(z + 1) * 64] = 1
    c_bcl = nc.inline_tensor(bcl, name="c_bcl").ap()
    c_lts = nc.inline_tensor(
        np.ascontiguousarray(np.tri(128, 128, -1).T.astype(np.float32)),
        name="c_lts").ap()
    c_onesb = nc.inline_tensor(np.ones((1, 128), dtype=BF), name="c_onesb").ap()
    c_onesf = nc.inline_tensor(np.ones((1, 128), dtype=np.float32), name="c_onesf").ap()

    s_big = al.at("s_big", 128, NT * 1024, BF16)
    s_afm = al.at("s_afm", 128, 4096, F32)
    s_afm2 = al.at("s_afm2", 128, 4096, BF16)
    s_tok16 = al.at("s_tok16", 128, 4 * NT * 16, BF16)
    s_id_f = al.at("s_id_f", 128, 128, F32)
    s_id_b = al.at("s_id_b", 128, 128, BF16)
    s_utbd = al.at("s_utbd", 128, 128, BF16)
    s_bcl = al.at("s_bcl", 64, T, BF16)
    s_lts = al.at("s_lts", 128, 128, F32)
    s_onesb = al.at("s_onesb", 1, 128, BF16)
    s_onesf = al.at("s_onesf", 1, 128, F32)
    s_cw = al.at("s_cw", 128, 32, F32)
    s_cb = al.at("s_cb", 128, 8, F32)
    s_wb = al.at("s_wb", 16, 1, F32)
    s_modc = al.at("s_modc", 128, 48, F32)
    s_mcol = al.at("s_mcol", 128, 10, F32)
    s_stats = al.at("s_stats", 128, 16, F32)
    s_ln2 = al.at("s_ln2", 128, 2 * NT, F32)
    s_carry = al.at("s_carry", 128, 24, F32)
    s_gmax = al.at("s_gmax", 128, 4, F32)
    s_qcol = al.at("s_qcol", 128, 8, F32)
    s_capl = al.at("s_capl", 128, 24, F32)
    s_uvw = al.at("s_uvw", 128, 16, BF16)
    s_uv = al.at("s_uv", 2, E, BF16)
    s_k8 = al.at("s_k8", 2, 1024, BF16)
    s_k8h = al.at("s_k8h", 2, 4, BF16)
    s_xTh = al.at("s_xTh", 128, 24, BF16)
    s_hb3 = al.at("s_hb3", 3, 1024, BF16)
    s_atot = al.at("s_atot", 128, 128, F32)
    s_row = al.at("s_row", 1, 1024, F32)
    s_scr = al.at("s_scr", 128, 1056, F32)
    s_scr2 = al.at("s_scr2", 128, 1056, F32)
    s_agg = al.at("s_agg", 64, 1056, F32)
    s_aggsl = al.at("s_aggsl", 2, 1056, F32, offset=al.off("s_scr2"))
    s_st2l = al.at("s_st2l", 64, 1056, BF16)
    s_sgu = al.at("s_sgu", 128, 160, F32)
    s_i2T = al.at("s_i2T", 2, 128, BF16)
    s_wb1 = al.at("s_wb1", 128, 2048, BF16, offset=al.off("s_scr"))
    s_psc = al.at("s_psc", 128, 16, F32)
    s_den = al.at("s_den", 128, 64, F32)
    s_hsel = al.at("s_hsel", 128, 2, F32)
    s_ind = al.at("s_ind", 128, 2, BF16)
    s_xstage = al.at("s_xstage", 128, 2048, F32)
    s_xT = al.at("s_xT", 128, 2 * 8 * 512, BF16)
    s_hc = al.at("s_hc", 128, 2 * 4096, BF16)
    s_win = al.at("s_win", 128, 8 * E, BF16)
    s_wmodc = al.at("s_wmodc", 128, 2048, F32)
    s_statF = al.at("s_statF", 128, 1056, F32, offset=al.off("s_wmodc") + 0)
    s_xiext = al.at("s_xiext", 128, 2 * 515 + 26, F32)
    s_convscr = al.at("s_convscr", 128, 2 * 512, F32)
    s_st2 = al.at("s_st2", 128, 1056, F32, offset=al.off("s_convscr"))
    s_ytile = al.at("s_ytile", 128, 1024, F32, offset=al.off("s_convscr"))
    s_dump = al.at("s_dump", 128, 1024, F32)
    s_i32 = al.at("s_i32", 128, 1024, I32, offset=al.off("s_dump"))
    s_psfm = al.at("s_psfm", 16, 768, F32, offset=al.off("s_dump"))
    s_dch = al.at("s_dch", 128, 128, F32, offset=al.off("s_dump") + 3072)
    s_rb = al.at("s_rb", 128, 128, F32, offset=al.off("s_dump") + 3584)
    s_x2t = al.at("s_x2t", 128, 2 * 1024, F32, offset=al.off("s_hc"))
    s_h2t = al.at("s_h2t", 128, 2 * 1024, F32, offset=al.off("s_hc") + 8192)
    s_cbexp = al.at("s_cbexp", 128, 2 * 1024, F32, offset=al.off("s_afm"))
    s_gaexp = al.at("s_gaexp", 128, 1024, F32, offset=al.off("s_afm") + 8192)
    s_gdq = al.at("s_gdq", 128, 1024, F32, offset=al.off("s_afm") + 12288)
    s_wqg = al.at("s_wqg", 128, 2048, BF16, offset=al.off("s_xT"))
    s_wqu = al.at("s_wqu", 128, 2048, BF16, offset=al.off("s_xT") + 4096)
    s_fsc = al.at("s_fsc", 128, 2048, F32, offset=al.off("s_xT") + 8192)
    s_wnat = al.at("s_wnat", 128, 2048, F32, offset=al.off("s_wmodc") + 4224)
    s_h3q = al.at("s_h3q", 128, NIC * 256, BF16, offset=al.off("s_hc"))
    s_gbexp = al.at("s_gbexp", 128, 1024, F32, offset=al.off("s_afm") + 8192)
    s_dexp = al.at("s_dexp", 128, 1024, F32, offset=al.off("s_afm"))
    s_x4 = al.at("s_x4", 128, 4096, F32, offset=al.off("s_hc"))

    SB_USED = al.cur
    print(f"[k] sbuf {SB_USED} bytes/partition")
    p.echo["ACT"] = lambda e: e.copy(s_qcol[0:1, 7:8], s_qcol[0:1, 6:7])

    def warmup(func):
        p.op("ACT", lambda e, f=func: e.activation(s_qcol[0:1, 7:8],
                                                   s_qcol[0:1, 6:7], f),
             reads=("fence0",), writes=("fence1",), tag=f"warm{func}")

    psum = [nc.alloc_psum_tensor(f"psA{i}", [128, 512], F32) for i in range(5)]
    ps_trA = nc.alloc_psum_tensor("ps_trA", [128, 512], F32)
    ps_trB = nc.alloc_psum_tensor("ps_trB", [128, 512], F32)
    ps_x = nc.alloc_psum_tensor("ps_x", [128, 512], F32)
    ps_tr = ps_trA

    def dma(eng, out, in_, reads, writes, tag="", transpose=False, aka=()):
        return p.op(eng, (lambda e, o=out, i=in_, tr=transpose:
                          e.dma_start(out=o, in_=i, transpose=tr)),
                    reads=reads, writes=writes, aka=aka, dma=True, tag=tag)

    def trans_f32(dst_ap, src_ap, rd, wr, tag, waka=()):
        r, c = src_ap.shape
        pa = ps_x[0:c, 0:r]
        p.op("PE", lambda e: e.transpose(pa, src_ap, s_id_f[0:r, 0:r]),
             reads=tuple(rd) + ("c:id_f",), writes=("ps_x",), tag="T" + tag)
        p.op("DVE", lambda e: e.tensor_copy(dst_ap, pa),
             reads=("ps_x",), writes=tuple(wr), aka=waka, tag="Tc" + tag)

    def trans_b16(dst_ap, src_ap, rd, wr, tag, waka=()):
        r, c = src_ap.shape
        pa = ps_trA.ap().bitcast(BF16)[0:c, 0:r]
        p.op("PE", lambda e: e.transpose(pa, src_ap, s_id_b[0:r, 0:r]),
             reads=tuple(rd) + ("c:id_b",), writes=("ps_tr0",), tag="T" + tag)
        p.op("DVE", lambda e: e.tensor_copy(dst_ap, pa),
             reads=("ps_tr0",), writes=tuple(wr), aka=waka, tag="Tc" + tag)

    tp_flip = [0]

    def pe_t128(dst_ap, src_ap, rd, wr, tag, waka=(), eveng="DVE"):
        h_ = tp_flip[0]; tp_flip[0] ^= 1
        bank = ps_trA if h_ == 0 else ps_trB
        pa = bank.ap().bitcast(BF16)[0:128, 0:128]
        res = f"ps_tr{h_}"
        p.op("PE", lambda e: e.transpose(pa, src_ap, s_id_b[:, :]),
             reads=tuple(rd) + ("c:id_b",), writes=(res,), tag="T" + tag)
        if eveng == "ACT":
            p.op("ACT", lambda e: e.activation(dst_ap, pa, AF.Copy),
                 reads=(res, "fence1"), writes=tuple(wr), aka=waka,
                 tag="Tc" + tag)
        else:
            p.op("DVE", lambda e: e.tensor_copy(dst_ap, pa),
                 reads=(res,), writes=tuple(wr), aka=waka, tag="Tc" + tag)

    tap_d = {}

    def tap(name, src_ap, shape, dtype=F32, reads=()):
        if name not in taps:
            return
        dt_ap = nc.dram_tensor("tap_" + name, list(shape), dtype, kind="ExternalOutput").ap()
        tap_d[name] = True
        dma("SP", dt_ap, src_ap, reads, (f"tap_{name}",), "tap" + name)

    for dst, src, res in [
        (s_id_f[:, :], idn_f, "c:id_f"), (s_id_b[:, :], idn_b, "c:id_b"),
        (s_utbd[:, :], c_utbd, "c:utbd"), (s_bcl[:, :], c_bcl, "c:bcl"),
        (s_lts[:, :], c_lts, "c:lts"), (s_onesb[:, :], c_onesb, "c:onesb"),
        (s_onesf[:, :], c_onesf, "c:onesf"), (s_cw[:, :], d_convw, "c:cw"),
        (s_cb[:, :], d_convb, "c:cb"), (s_wb[:, :], d_wbase, "c:wb"),
        (s_row[0:1, :], d_crow, "crow"),
        (s_mcol[0:1, 0:1], d_clipx, "c:clipx"), (s_mcol[0:1, 1:2], d_clipd, "c:clipd"),
    ]:
        dma("SP", dst, src, (), (res,), "ld")
    dma("POOL", s_win.ap().rearrange("p (c e) -> p c e", c=8),
        d_winT.rearrange("(c p) e -> p c e", p=128), (), ("win",), "ldwin")
    winv = s_win.ap().rearrange("p (c e) -> p c e", c=8)
    i2t = np.zeros((2, 128), dtype=BF)
    i2t[0, :64] = 1; i2t[1, 64:] = 1
    c_i2t = nc.inline_tensor(i2t, name="c_i2t").ap()
    dma("SP", s_i2T[:, :], c_i2t, (), ("c:i2t",), "ldi2t")
    ind2a = np.zeros((128, 2), dtype=BF)
    ind2a[:64, 0] = 1; ind2a[64:, 1] = 1
    c_ind = nc.inline_tensor(ind2a, name="c_ind").ap()
    dma("SP", s_ind[:, :], c_ind, (), ("c:ind",), "ldind")
    dma("SP", s_hsel[:, :], d_hsel, (), ("c:hsel",), "ldhsel")
    p.op("DVE", lambda e: e.memset(s_gmax[:, 0:2], 0.0), writes=("gm0",), tag="zgm")

    p.op("DVE", lambda e: e.memset(s_qcol[0:1, 6:7], 1.0), writes=("fence0",), tag="fz")
    p.op("DVE", lambda e: e.memset(s_afm[:, :], 0.0), writes=("afmA",), tag="zafm")
    warmup(AF.Silu)
    p.op("ACT", lambda e: e.activation(s_row[0:1, :], s_row[0:1, :], AF.Silu),
         reads=("crow",), writes=("crow",), tag="siluc")
    for k in range(8):
        trans_f32(s_capl[:, 16 + k:17 + k], s_row[0:1, k * 128:(k + 1) * 128],
                  ("crow",), ("silc",), f"sc{k}")
    wmv = s_wmodc.ap().rearrange("p (s e) -> p s e", s=4)

    def mod_tr(ob):
        for q4 in range(4):
            col = ob * 4 + q4
            trans_f32(s_modc[:, col:col + 1],
                      s_scr2[0:1, (ob % 2) * 512 + q4 * 128:
                             (ob % 2) * 512 + (q4 + 1) * 128],
                      (f"modev{ob % 2}",), ("modc",), f"mt{ob}_{q4}")

    for ob in range(12):
        bres = f"psum{2 + ob % 2}"
        for k in range(8):
            slot = (ob * 8 + k) % 4
            dma("POOL", wmv[:, slot, :],
                d_wmod[k * 128:(k + 1) * 128, ob * 512:(ob + 1) * 512],
                (), (f"wm{slot}",), f"ldwm{ob}_{k}")
            p.op("PE", lambda e, k=k, slot=slot, ob=ob: e.matmul(
                psum[2 + ob % 2][0:1, 0:512], s_capl[:, 16 + k:17 + k],
                wmv[:, slot, :], start=(k == 0), stop=(k == 7)),
                 reads=(f"wm{slot}", "silc"), writes=(bres,), tag=f"mm{ob}_{k}")
        p.op("DVE", lambda e, ob=ob: e.tensor_copy(
            s_scr2[0:1, (ob % 2) * 512:(ob % 2) * 512 + 512],
            psum[2 + ob % 2][0:1, 0:512]),
             reads=(bres,), writes=(f"modev{ob % 2}",), tag=f"me{ob}")
        if ob >= 1:
            mod_tr(ob - 1)
    mod_tr(11)

    def g_xform(c0, tagp):
        p.op("DVE", lambda e: e.tensor_scalar(s_scr[:, 0:8], s_modc[:, c0:c0 + 8],
                                              0.0, None, OP.is_ge),
             reads=("modc",), writes=("gsgn",), tag="gs" + tagp)
        p.op("DVE", lambda e: e.tensor_scalar(s_scr[:, 0:8], s_scr[:, 0:8], 2.0, -1.0,
                                              OP.mult, OP.add),
             reads=("gsgn",), writes=("gsgn",), tag="g2" + tagp)
        p.op("ACT", lambda e: e.activation(s_scr[:, 8:16], s_modc[:, c0:c0 + 8], AF.Abs),
             reads=("modc",), writes=("gabs",), tag="gb" + tagp)
        p.op("DVE", lambda e: e.tensor_scalar(s_scr[:, 8:16], s_scr[:, 8:16], 0.25,
                                              None, OP.max),
             reads=("gabs",), writes=("gabs",), tag="gm" + tagp)
        p.op("DVE", lambda e: e.tensor_tensor(s_modc[:, c0:c0 + 8], s_scr[:, 0:8],
                                              s_scr[:, 8:16], OP.mult),
             reads=("gsgn", "gabs"), writes=("modc",), tag="gf" + tagp)

    if STAGE >= 2:
        g_xform(16, "a")
        g_xform(40, "b")

    if STAGE < 2:
        p.op("DVE", lambda e: e.memset(s_capl[:, 0:16], 0.01), writes=("cacol", "sacol"), tag="capz")
    p.op("DVE", lambda e: e.tensor_scalar(s_capl[:, 0:8], s_modc[:, 8:16], 1.0, None, OP.add),
         reads=("modc",), writes=("cacol",), tag="cap1") if STAGE >= 2 else None
    p.op("DVE", lambda e: e.tensor_copy(s_capl[:, 8:16], s_modc[:, 0:8]),
         reads=("modc",), writes=("sacol",), tag="sacp") if STAGE >= 2 else None
    p.op("DVE", lambda e: e.tensor_copy(bass.AP(s_uvw, 0, [[16, 128], [2, 8]]),
                                        s_capl[:, 0:8]),
         reads=("cacol",), writes=("uvw",), tag="uvc")
    p.op("DVE", lambda e: e.tensor_copy(bass.AP(s_uvw, 1, [[16, 128], [2, 8]]),
                                        s_capl[:, 8:16]),
         reads=("sacol",), writes=("uvw",), tag="uvs")

    for (n0, n1) in ((0, 512), (512, 1024), (1024, E)):
        def mm_uv(e, n0=n0, n1=n1):
            inst = None
            for dc in range(8):
                inst = e.matmul(ps_x[0:2, 0:n1 - n0], s_uvw[:, dc * 2:(dc + 1) * 2],
                                winv[:, dc, n0:n1], start=(dc == 0), stop=(dc == 7))
            return inst
        p.op("PE", mm_uv, reads=("win", "uvw"), writes=("ps_x",), tag=f"uv{n0}")
        p.op("DVE", lambda e, n0=n0, n1=n1: e.tensor_copy(s_uv[0:2, n0:n1],
                                                          ps_x[0:2, 0:n1 - n0]),
             reads=("ps_x",), writes=("uv",), tag=f"uvc{n0}")

    for dc in range(8):
        p.op("DVE", lambda e, dc=dc: e.tensor_scalar(
            winv[:, dc, :], winv[:, dc, :], s_capl[:, dc:dc + 1], None, OP.mult),
             reads=("win", "cacol"), writes=("win",), tag=f"fold{dc}")

    xstage = s_xstage.ap().rearrange("p (r d) -> p r d", r=2)
    xhat = s_hc.ap().rearrange("p (r d) -> p r d", r=2)
    xcbv = s_hc.ap().rearrange("p (r tt c f) -> p r tt c f", r=2, tt=4, c=8)
    xTv = s_xT.ap().rearrange("p (r tt c f) -> p r tt c f", r=2, tt=4, c=8)
    hcfl = s_hc.ap().rearrange("p (r n) -> p r n", r=2)
    xTfl = s_xT.ap().rearrange("p (r n) -> p r n", r=2)
    xhv = s_big.ap().rearrange("p (t d) -> p t d", t=NT)

    def ln_tile(src_ap, nrows, rstd_ap, mh_ap, rdres, stat_wr, tag,
                out_ap=None, owr=(), oaka=(), scr_ap=None, scr_wr=None,
                scr_aka=()):
        if scr_ap is None:
            scr_ap, scr_wr, scr_aka = s_dump.ap(), ("dump",), ()
        p.op("ACT", lambda e, scr_ap=scr_ap: e.activation(
            scr_ap[0:nrows, :], src_ap, AF.Copy,
            accum_out=s_mcol[0:nrows, 3:4]),
             reads=rdres + ("fence1",), writes=tuple(scr_wr) + ("mcol3",),
             aka=scr_aka, tag="s1" + tag)
        p.op("ACT", lambda e, scr_ap=scr_ap: e.activation(
            scr_ap[0:nrows, :], src_ap, AF.Square,
            accum_out=s_mcol[0:nrows, 4:5]),
             reads=rdres, writes=tuple(scr_wr) + ("mcol4",), aka=scr_aka,
             tag="s2" + tag)
        p.op("DVE", lambda e: e.tensor_scalar(s_mcol[0:nrows, 3:4], s_mcol[0:nrows, 3:4],
                                              1.0 / 1024, None, OP.mult),
             reads=("mcol3",), writes=("mcol3",), tag="m" + tag)
        p.op("DVE", lambda e: e.scalar_tensor_tensor(
            s_mcol[0:nrows, 5:6], s_mcol[0:nrows, 3:4], 1.0, s_mcol[0:nrows, 3:4],
            OP.bypass, OP.mult),
             reads=("mcol3",), writes=("mcol5",), tag="m2" + tag)
        p.op("DVE", lambda e: e.tensor_scalar(s_mcol[0:nrows, 4:5], s_mcol[0:nrows, 4:5],
                                              1.0 / 1024, None, OP.mult),
             reads=("mcol4",), writes=("mcol4",), tag="e2" + tag)
        p.op("DVE", lambda e: e.scalar_tensor_tensor(
            s_mcol[0:nrows, 4:5], s_mcol[0:nrows, 4:5], 1e-5,
            s_mcol[0:nrows, 5:6], OP.add, OP.subtract),
             reads=("mcol4", "mcol5"), writes=("mcol4",), tag="v" + tag)
        p.op("ACT", lambda e: e.activation(s_mcol[0:nrows, 5:6], s_mcol[0:nrows, 4:5],
                                           AF.Sqrt),
             reads=("mcol4",), writes=("mcol5",), tag="l" + tag)
        p.op("DVE", lambda e: e.reciprocal(rstd_ap, s_mcol[0:nrows, 5:6]),
             reads=("mcol5",), writes=stat_wr, tag="r" + tag)
        p.op("DVE", lambda e: e.scalar_tensor_tensor(mh_ap, s_mcol[0:nrows, 3:4], -1.0,
                                                     rstd_ap, OP.mult, OP.mult),
             reads=("mcol3",) + stat_wr, writes=stat_wr, tag="mh" + tag)
        if out_ap is not None:
            p.op("DVE", lambda e: e.tensor_scalar(out_ap, src_ap, rstd_ap, None,
                                                  OP.mult),
                 reads=rdres + stat_wr, writes=owr, aka=oaka, tag="xo" + tag)

    if STAGE < 3:
        run_main = False
    else:
        run_main = True
    warmup(AF.Sqrt) if run_main else None
    if not run_main:
        pass
    run_main and dma("POOL", s_scr[0:3, 16:1040], d_xh3, (), ("xhalo",), "ldxh")
    dma("SP", s_mcol[0:3, 2:3], d_mask3, (), ("mask3",), "ldm3")
    ln_tile(s_scr[0:3, 16:1040], 3, s_mcol[0:3, 6:7], s_mcol[0:3, 7:8],
            ("xhalo",), ("hstat",), "H", out_ap=s_hb3[0:3, :], owr=("hb3",))
    p.op("DVE", lambda e: e.tensor_scalar(s_hb3[0:3, :], s_hb3[0:3, :],
                                          s_mcol[0:3, 2:3], None, OP.mult),
         reads=("hb3", "mask3"), writes=("hb3",), tag="mskh")
    p.op("DVE", lambda e: e.scalar_tensor_tensor(s_mcol[0:3, 8:9], s_mcol[0:3, 7:8], 1.0,
                                                 s_mcol[0:3, 2:3], OP.bypass, OP.mult),
         reads=("hstat", "mask3"), writes=("mcol8",), tag="mhm")
    for dcx in range(8):
        trans_b16(bass.AP(s_xTh, dcx * 3, [[24, 128], [1, 3]]),
                  s_hb3[0:3, dcx * 128:(dcx + 1) * 128], ("hb3",), ("xTh",), f"th{dcx}")
    p.op("DVE", lambda e: e.tensor_copy(s_mcol[0:3, 9:10], s_mcol[0:3, 2:3]),
         reads=("mask3",), writes=("mcol9",), tag="k8hm")
    p.op("PE", lambda e: e.transpose(ps_x[0:2, 0:3], s_mcol[0:3, 8:10], s_id_f[0:3, 0:3]),
         reads=("mcol8", "mcol9", "c:id_f"), writes=("ps_x",), tag="Tk8h")
    p.op("DVE", lambda e: e.tensor_copy(s_k8h[0:2, 0:3], ps_x[0:2, 0:3]),
         reads=("ps_x",), writes=("k8h",), tag="k8hc")
    for ec in range(8):
        def hmm(e, ec=ec):
            inst = None
            for dc in range(8):
                inst = e.matmul(ps_x[0:128, 0:3], winv[:, dc, ec * 128:(ec + 1) * 128],
                                bass.AP(s_xTh, dc * 3, [[24, 128], [1, 3]]),
                                start=(dc == 0), stop=False)
            return e.matmul(ps_x[0:128, 0:3], s_uv[0:2, ec * 128:(ec + 1) * 128],
                            s_k8h[0:2, 0:3], start=False, stop=True)
        p.op("PE", hmm, reads=("win", "xTh", "uv", "k8h"), writes=("ps_x",), tag=f"hmm{ec}")
        p.op("DVE", lambda e, ec=ec: e.tensor_copy(s_carry[:, ec * 3:(ec + 1) * 3],
                                                   ps_x[0:128, 0:3]),
             reads=("ps_x",), writes=(f"carry{ec}",), tag=f"hc{ec}")

    p.op("DVE", lambda e: e.memset(s_mcol[:, 9:10], 1.0), writes=("mcol9",), tag="onecol")

    for sbk in range(NSB):
        sl = sbk % 2
        for tt in range(4):
            gt = sbk * 4 + tt
            xsl = gt % 2
            dma("POOL", xstage[:, xsl, :],
                d_x[gt * 128:(gt + 1) * 128, :], (), (f"xs{xsl}",), f"ldx{gt}")
            src = xstage[:, xsl, :]
            rs = s_stats[:, 2 * (gt % 8):2 * (gt % 8) + 1]
            mh = s_stats[:, 2 * (gt % 8) + 1:2 * (gt % 8) + 2]
            ln_tile(src, 128, rs, mh, (f"xs{xsl}",), (f"st{gt % 8}",), f"t{gt}",
                    out_ap=xhat[:, sl, tt * 1024:(tt + 1) * 1024],
                    owr=(f"xhat{sl}",), oaka=(f"xcb{sl}",))
            p.op("DVE", lambda e, mh=mh: e.tensor_copy(s_mcol[:, 8:9], mh),
                 reads=(f"st{gt % 8}",), writes=("mcol8",), tag=f"mc{gt}")
            p.op("PE", lambda e: e.transpose(ps_x[0:2, 0:128], s_mcol[:, 8:10],
                                             s_id_f[:, :]),
                 reads=("mcol8", "mcol9", "c:id_f"), writes=("ps_x",), tag=f"Tk8{gt}")
            p.op("DVE", lambda e, sl=sl, tt=tt: e.tensor_copy(
                s_k8[0:2, sl * 512 + tt * 128:sl * 512 + (tt + 1) * 128],
                ps_x[0:2, 0:128]),
                 reads=("ps_x",), writes=("k8",), tag=f"mTc{gt}")
        for tt in range(4):
            for dc in range(8):
                pe_t128(xTv[:, sl, tt, dc, :],
                        xhat[:, sl, tt * 1024 + dc * 128:tt * 1024 + (dc + 1) * 128],
                        (f"xhat{sl}",), (f"xT{sl}",), f"xT{sbk}_{tt}_{dc}")
        for ec in range(9):
            bank = ec % 2
            M = 128 if ec < 8 else 16
            def mm_inp(e, sl=sl, ec=ec, bank=bank, M=M, sbk=sbk):
                for dc in range(8):
                    e.matmul(psum[bank][0:M, 0:512], winv[:, dc, ec * 128:ec * 128 + M],
                             xTv[:, sl, :, dc, :], start=(dc == 0), stop=False)
                return e.matmul(psum[bank][0:M, 0:512],
                                s_uv[0:2, ec * 128:ec * 128 + M],
                                s_k8[0:2, (sbk % 2) * 512:(sbk % 2 + 1) * 512],
                                start=False, stop=True)
            p.op("PE", mm_inp, reads=(f"xT{sl}", "win", "uv", "k8"),
                 writes=(f"psum{bank}",), tag=f"inp{sbk}_{ec}")
            if ec < 8:
                xi = ec % 2
                p.op("ACT", lambda e, xi=xi, bank=bank: e.copy(
                    s_xiext[:, xi * 515 + 3:(xi + 1) * 515], psum[bank][0:128, 0:512]),
                     reads=(f"psum{bank}",), writes=(f"xi{xi}",), tag=f"ev{sbk}_{ec}")
                p.op("DVE", lambda e, xi=xi, ec=ec: e.tensor_copy(
                    s_xiext[:, xi * 515:xi * 515 + 3], s_carry[:, ec * 3:(ec + 1) * 3]),
                     reads=(f"carry{ec}",), writes=(f"xi{xi}",), tag=f"ci{sbk}_{ec}")
                p.op("DVE", lambda e, xi=xi, ec=ec: e.tensor_scalar(
                    s_convscr[:, xi * 512:(xi + 1) * 512],
                    s_xiext[:, xi * 515:xi * 515 + 512],
                    s_cw[:, ec * 4:ec * 4 + 1], s_cb[:, ec:ec + 1],
                    OP.mult, OP.add),
                     reads=(f"xi{xi}", "c:cw", "c:cb"), writes=(f"cs{xi}",),
                     tag=f"c0_{sbk}_{ec}")
                for kk in range(1, 3):
                    p.op("DVE", lambda e, xi=xi, ec=ec, kk=kk: e.scalar_tensor_tensor(
                        s_convscr[:, xi * 512:(xi + 1) * 512],
                        s_xiext[:, xi * 515 + kk:xi * 515 + 512 + kk],
                        s_cw[:, ec * 4 + kk:ec * 4 + kk + 1],
                        s_convscr[:, xi * 512:(xi + 1) * 512], OP.mult, OP.add),
                         reads=(f"xi{xi}", "c:cw", f"cs{xi}"), writes=(f"cs{xi}",),
                         tag=f"c{kk}_{sbk}_{ec}")
                p.op("DVE", lambda e, xi=xi, sl=sl, ec=ec: e.scalar_tensor_tensor(
                    xcbv[:, sl, :, ec, :],
                    s_xiext[:, xi * 515 + 3:xi * 515 + 515],
                    s_cw[:, ec * 4 + 3:ec * 4 + 4],
                    s_convscr[:, xi * 512:(xi + 1) * 512], OP.mult, OP.add),
                     reads=(f"xi{xi}", "c:cw", f"cs{xi}"), writes=(f"xcb{sl}",),
                     aka=(f"xhat{sl}",), tag=f"cb_{sbk}_{ec}")
                p.op("DVE", lambda e, xi=xi, ec=ec: e.tensor_copy(
                    s_carry[:, ec * 3:(ec + 1) * 3],
                    s_xiext[:, xi * 515 + 512:xi * 515 + 515]),
                     reads=(f"xi{xi}",), writes=(f"carry{ec}",), tag=f"cu{sbk}_{ec}")
            else:
                p.op("DVE", lambda e, bank=bank, sbk=sbk: e.tensor_scalar(
                    s_afm[0:16, sbk * 512:(sbk + 1) * 512], psum[bank][0:16, 0:512],
                    s_wb[0:16, 0:1], None, OP.mult),
                     reads=(f"psum{bank}", "c:wb"), writes=("afmA",), tag=f"Aev{sbk}")
        for tt in range(4):
            gt = sbk * 4 + tt
            for dc in range(8):
                pe_t128(xhv[:, gt, dc * 128:(dc + 1) * 128],
                        xcbv[:, sl, tt, dc, :],
                        (f"xcb{sl}",), (f"xh{gt}",), f"xh{gt}_{dc}",
                        eveng="ACT")


    def pe_tr(dst_ap, src_ap, rd, wr, tag, waka=(), f32=False):
        r, c = src_ap.shape
        b0_ = src_ap.base_partition()
        if f32:
            pa = ps_x[0:c, 0:r]
            res = "ps_x"
            ident = s_id_f[b0_:b0_ + r, b0_:b0_ + r]
            idr = "c:id_f"
        else:
            h_ = tp_flip[0]; tp_flip[0] ^= 1
            bank = ps_trA if h_ == 0 else ps_trB
            pa = bank.ap().bitcast(BF16)[0:c, 0:r]
            res = f"ps_tr{h_}"
            ident = s_id_b[b0_:b0_ + r, b0_:b0_ + r]
            idr = "c:id_b"
        p.op("PE", lambda e: e.transpose(pa, src_ap, ident),
             reads=tuple(rd) + (idr,), writes=(res,), tag="T" + tag)
        p.op("DVE", lambda e: e.tensor_copy(dst_ap, pa),
             reads=(res,), writes=tuple(wr), aka=waka, tag="Tc" + tag)

    afm3 = s_afm.ap().rearrange("p (c l) -> p c l", c=64)
    A_v = lambda p0: s_afm.ap().rearrange("p (c l) -> p c l", c=64)

    s_rB = [s_dump.ap()[:, 0:512], s_dump.ap()[:, 512:1024],
            s_convscr.ap()[:, 0:512], s_convscr.ap()[:, 512:1024]]
    rB3 = [b.rearrange("p (c l) -> p c l", c=8) for b in s_rB]
    RAKA = {0: ("psfm", "dump"), 1: ("dump", "dch", "rbm"),
            2: ("cs0", "cs1", "ytile"), 3: ("cs0", "cs1", "ytile")}

    def rsh_in(e):
        return e.dma_start(out=s_rB[0], in_=s_afm[0:16, :])
    p.op("SP", rsh_in, reads=("afmA",), writes=("rsh0",), aka=RAKA[0],
         dma=True, tag="rshA")

    def hs_scan2(srcb, dstb, pongb, op, res_in, res_out, tagp):
        cur = srcb
        order = [pongb, dstb, pongb, dstb, pongb, dstb]
        for i, s in enumerate((1, 2, 4, 8, 16, 32)):
            nxt = order[i]
            rin = res_in if i == 0 else f"{tagp}s{i-1}"
            rout = f"{tagp}s{i}" if i < 5 else res_out
            raka = RAKA[nxt] if i < 2 else ()
            p.op("DVE", lambda e, cur=cur, nxt=nxt, s=s: e.tensor_copy(
                rB3[nxt][:, :, 0:s], rB3[cur][:, :, 0:s]),
                 reads=(rin,), writes=(rout,), aka=raka, tag=f"{tagp}c{i}")
            p.op("DVE", lambda e, cur=cur, nxt=nxt, s=s: e.tensor_tensor(
                rB3[nxt][:, :, s:64], rB3[cur][:, :, s:64],
                rB3[cur][:, :, 0:64 - s], op),
                 reads=(rin,), writes=(rout,), tag=f"{tagp}a{i}")
            cur = nxt

    hs_scan2(0, 1, 2, OP.add, "rsh0", "afmCS", "cs")
    hs_scan2(1, 0, 2, OP.min, "afmCS", "afmrm", "rm")

    p.op("DVE", lambda e: e.tensor_copy(s_atot[:, 96:104],
                                        rB3[1][:, :, 63:64].squeeze(2)),
         reads=("afmCS",), writes=("atotR",), tag="exA")
    p.op("DVE", lambda e: e.tensor_copy(s_atot[:, 104:112],
                                        rB3[0][:, :, 63:64].squeeze(2)),
         reads=("afmrm",), writes=("fm",), tag="exF")
    p.op("DVE", lambda e: e.reduce_max(s_atot[:, 112:120], rB3[1][:, :, :],
                                       AX.X),
         reads=("afmCS",), writes=("Mc",), tag="exM")
    def rsh_at(e):
        return e.dma_start(out=s_atot[0:16, 0:64], in_=s_atot[:, 96:104])
    p.op("SP", rsh_at, reads=("atotR",), writes=("atot",), dma=True,
         tag="rshAt")

    warmup(AF.Exp)
    t16 = s_tok16.ap().rearrange("p (f t c) -> p f t c", f=4, t=NT)
    p.op("DVE", lambda e: e.tensor_tensor(
        rB3[2][:, :, :],
        s_atot[:, 104:112].unsqueeze(2).broadcast_to([128, 8, 64]),
        rB3[1][:, :, :], OP.subtract),
         reads=("afmCS", "fm"), writes=("dstmp",), aka=RAKA[2], tag="dsub")
    p.op("DVE", lambda e: e.tensor_tensor(
        rB3[3][:, :, :], rB3[1][:, :, :],
        s_atot[:, 112:120].unsqueeze(2).broadcast_to([128, 8, 64]),
        OP.subtract),
         reads=("afmCS", "Mc"), writes=("ostmp",), aka=RAKA[3], tag="osub")

    def field_to_t16(fi, bsl, emit_exp, rd, waka=()):
        p.op("ACT", emit_exp, reads=rd + ("fence1",), writes=(f"fexp{bsl}",),
             aka=waka, tag=f"fx{fi}")

        def back(e, bsl=bsl):
            return e.dma_start(out=s_afm2[0:16, :], in_=s_rB[bsl])
        p.op("POOL", back, reads=(f"fexp{bsl}",), writes=("fmcur",),
             dma=True, tag=f"fb{fi}")
        for tl in range(NT):
            pe_tr(t16[:, fi, tl, :], s_afm2[0:16, tl * 128:(tl + 1) * 128],
                  ("fmcur",), (f"t16_{fi}",), f"tf{fi}_{tl}")

    field_to_t16(2, 2, lambda e: e.activation(s_rB[2], s_rB[2], AF.Exp),
                 ("dstmp",))
    field_to_t16(3, 3, lambda e: e.activation(s_rB[3], s_rB[3], AF.Exp),
                 ("ostmp",))
    field_to_t16(0, 1, lambda e: e.activation(s_rB[1], s_rB[1], AF.Exp,
                                              scale=-1.0), ("afmCS",),
                 waka=("afmCS",))
    field_to_t16(1, 0, lambda e: e.activation(s_rB[0], s_rB[0], AF.Exp),
                 ("afmrm",), waka=("afmrm",))

    wb1 = s_wb1.ap()
    for tl in range(NT):
        xt3 = xhv[:, tl, :].rearrange("p (h n) -> p h n", h=16)
        p.op("DVE", lambda e, tl=tl, xt3=xt3: e.tensor_tensor(
            wb1[:, 0:1024].rearrange("p (h n) -> p h n", h=16),
            t16[:, 2, tl, :].unsqueeze(2).broadcast_to([128, 16, 64]),
            xt3, OP.mult),
             reads=("t16_2", f"xh{tl}"), writes=("wb1",),
             aka=("gsgn", "gabs", "xhalo"), tag=f"w1_{tl}")
        p.op("DVE", lambda e, tl=tl: e.tensor_copy(wb1[:, 1024:1040],
                                                   t16[:, 2, tl, :]),
             reads=("t16_2",), writes=("wb1",), tag=f"w1c_{tl}")
        for (n0, n1, bank) in ((0, 512, 2), (512, 1024, 3), (1024, 1040, 4)):
            p.op("PE", lambda e, n0=n0, n1=n1, bank=bank: e.matmul(
                psum[bank][0:2, 0:n1 - n0], s_ind[:, :], wb1[:, n0:n1],
                start=True, stop=True),
                 reads=("wb1", "c:ind"), writes=(f"psum{bank}",), tag=f"ag{tl}_{n0}")
            p.op("DVE", lambda e, n0=n0, n1=n1, bank=bank: e.tensor_copy(
                s_aggsl[0:2, n0:n1], psum[bank][0:2, 0:n1 - n0]),
                 reads=(f"psum{bank}",), writes=("aggsl",),
                 aka=("modev0", "modev1"), tag=f"agc{tl}_{n0}")
        dma("SP", s_agg[0:0 + 2, 0:1056] if False else
            bass.AP(s_agg, 0, [[1056, 2], [1, 1056]]) if False else s_agg[0:2, :],
            s_aggsl[0:2, :], ("aggsl",), ("agg",), f"agd{tl}") if False else None
        def agd(e, tl=tl):
            return e.dma_start(out=s_agg[2 * tl:2 * tl + 2, 0:1040],
                               in_=s_aggsl[0:2, 0:1040])
        p.op("SP", agd, reads=("aggsl",), writes=("agg",), dma=True, tag=f"agd{tl}")

    pe_tr(s_agg[0:64, 1040:1056], s_atot[0:16, 0:64], ("atot",), ("agg",),
          "atT", f32=True)

    dma("POOL", d_gin, s_agg[0:64, :], ("agg",), ("d_gin",), "gi")
    p.op("POOL", lambda e: e.collective_compute(
        "AllGather", OP.bypass, replica_groups=GROUPS2,
        ins=[d_gin.opt()], outs=[d_gout.opt()]),
         reads=("d_gin",), writes=("d_gout",), cc_sem="cc0", tag="ccag")
    dma("POOL", s_statF[:, :], d_gout, ("d_gout",), ("statF",), "go",
        aka=("wmodc", "wm0", "wm1", "wm2"))

    MAGIC = 12582912.0
    sgu = s_sgu.ap()
    WNAT_AKA = ("wmodc", "wm2", "wm3", "xi0", "xi1")
    WB1_AKA = ("wb1", "rmrow2", "gsgn", "gabs", "xhalo")

    nslot = [0]

    def natload(src_ap, tag):
        slot = nslot[0] % 2
        nslot[0] += 1
        dma("SP", s_wnat[:, slot * 1024:slot * 1024 + src_ap.shape[1]], src_ap,
            (), (f"wnat{slot}",), tag, aka=WNAT_AKA)
        return slot

    for w_i, (dsrc, cb0) in enumerate(((d_wg, 0), (d_wu, 22))):
        for ic in range(NIC):
            slot = natload(dsrc[ic * 128:(ic + 1) * 128, :], f"nl{w_i}_{ic}")
            p.op("DVE", lambda e, slot=slot, c=cb0 + ic: e.reduce_max(
                sgu[:, c:c + 1], s_wnat[:, slot * 1024:(slot + 1) * 1024],
                AX.X, apply_absolute_value=True),
                 reads=(f"wnat{slot}",), writes=("sgGU",), tag=f"rmw{w_i}_{ic}")
    for hb in range(8):
        for c3, w in ((0, 1024), (1024, 1024), (2048, 768)):
            slot = natload(d_wd[hb * 128:(hb + 1) * 128, c3:c3 + w], f"nd{hb}_{c3}")
            if c3 == 0:
                p.op("DVE", lambda e, slot=slot, hb=hb, w=w: e.reduce_max(
                    sgu[:, 44 + hb:45 + hb], s_wnat[:, slot * 1024:slot * 1024 + w],
                    AX.X, apply_absolute_value=True),
                     reads=(f"wnat{slot}",), writes=("sgD",), tag=f"rmd{hb}_{c3}")
            else:
                p.op("DVE", lambda e, slot=slot, w=w: e.reduce_max(
                    sgu[:, 156:157], s_wnat[:, slot * 1024:slot * 1024 + w],
                    AX.X, apply_absolute_value=True),
                     reads=(f"wnat{slot}",), writes=("sgScr",), tag=f"rmd{hb}_{c3}")
                p.op("DVE", lambda e, hb=hb: e.tensor_tensor(
                    sgu[:, 44 + hb:45 + hb], sgu[:, 44 + hb:45 + hb],
                    sgu[:, 156:157], OP.max),
                     reads=("sgD", "sgScr"), writes=("sgD",), tag=f"rmx{hb}_{c3}")
    p.op("DVE", lambda e: e.reciprocal(sgu[:, 60:82], sgu[:, 0:22]),
         reads=("sgGU",), writes=("sgInv",), tag="ivg")
    p.op("DVE", lambda e: e.tensor_scalar(sgu[:, 60:82], sgu[:, 60:82], 127.0,
                                          None, OP.mult),
         reads=("sgInv",), writes=("sgInv",), tag="ivg2")
    p.op("DVE", lambda e: e.reciprocal(sgu[:, 82:104], sgu[:, 22:44]),
         reads=("sgGU", "sgInv"), writes=("sgInv",), tag="ivu")
    p.op("DVE", lambda e: e.tensor_scalar(sgu[:, 82:104], sgu[:, 82:104], 127.0,
                                          None, OP.mult),
         reads=("sgInv",), writes=("sgInv",), tag="ivu2")
    p.op("DVE", lambda e: e.reciprocal(sgu[:, 52:60], sgu[:, 44:52]),
         reads=("sgD", "sgInv"), writes=("sgInv",), tag="ivd")
    p.op("DVE", lambda e: e.tensor_scalar(sgu[:, 52:60], sgu[:, 52:60], 127.0,
                                          None, OP.mult),
         reads=("sgInv",), writes=("sgInv",), tag="ivd2")
    p.op("DVE", lambda e: e.tensor_scalar(sgu[:, 148:156], sgu[:, 44:52],
                                          1.0 / 127.0, None, OP.mult),
         reads=("sgD",), writes=("sgDdq",), tag="ddq")


    pe_tr(s_psfm[0:16, 0:128], s_statF[:, 1040:1056], ("statF",), ("psf0",),
          "atf", f32=True)
    psf = s_psfm.ap().rearrange("p (k c) -> p k c", k=6)

    def hs_line(csrc, cdst, cpong, op, rin, rout, tagp, n=7):
        cur = csrc
        order = []
        slots = [cdst, cpong]
        for i in range(n):
            order.append(slots[(n - 1 - i) % 2])
        for i in range(n):
            s = 1 << i
            nxt = order[i]
            ri = rin if i == 0 else f"{tagp}{i-1}"
            ro = rout if i == n - 1 else f"{tagp}{i}"
            p.op("DVE", lambda e, cur=cur, nxt=nxt, s=s: e.tensor_copy(
                psf[0:16, nxt, 0:s], psf[0:16, cur, 0:s]),
                 reads=(ri,), writes=(ro,), tag=f"{tagp}c{i}")
            p.op("DVE", lambda e, cur=cur, nxt=nxt, s=s: e.tensor_tensor(
                psf[0:16, nxt, s:128], psf[0:16, cur, s:128],
                psf[0:16, cur, 0:128 - s], op),
                 reads=(ri,), writes=(ro,), tag=f"{tagp}a{i}")
            cur = nxt
        assert cur == cdst, (tagp, cur)

    hs_line(0, 1, 4, OP.add, "psf0", "psend", "ps")
    hs_line(1, 2, 4, OP.min, "psend", "rmp", "rm2")
    p.op("DVE", lambda e: e.memset(psf[0:16, 3, 0:1], 0.0), writes=("rmps",), tag="rz")
    p.op("DVE", lambda e: e.tensor_copy(psf[0:16, 3, 1:128], psf[0:16, 2, 0:127]),
         reads=("rmp",), writes=("rmps",), tag="rsh")
    p.op("DVE", lambda e: e.tensor_scalar(psf[0:16, 3, :], psf[0:16, 3, :], 0.0,
                                          None, OP.min),
         reads=("rmps",), writes=("rmps",), tag="rmn")
    pe_tr(s_psc[:, 0:16], psf[0:16, 1, :], ("psend",), ("psendT",), "pst", f32=True)
    def rrow(e):
        return e.dma_start(out=s_scr2[0:1, 0:1024], in_=psf[0:8, 3, :])
    def rrow2(e):
        return e.dma_start(out=s_scr[0:1, 0:1024].bitcast(F32) if False
                           else s_scr[0:1, 0:1024], in_=psf[8:16, 3, :])
    p.op("SP", rrow, reads=("rmps",), writes=("rmrow",), aka=("scr2l", "aggsl"),
         dma=True, tag="rr1")
    p.op("SP", rrow2, reads=("rmps",), writes=("rmrow2",),
         aka=("wb1", "gsgn", "gabs", "xhalo"), dma=True, tag="rr2")

    for h in range(16):
        rsrc = s_scr2[0:1, (h % 8) * 128:(h % 8 + 1) * 128] if h < 8 else                s_scr[0:1, (h - 8) * 128:(h - 8 + 1) * 128]
        rres = "rmrow" if h < 8 else "rmrow2"
        p.op("PE", lambda e, rsrc=rsrc: e.matmul(
            ps_x[0:128, 0:128], s_onesf[0:1, :], rsrc, start=True, stop=True),
             reads=(rres, "c:onesf"), writes=("ps_x",), tag=f"rb{h}")
        p.op("DVE", lambda e, h=h: e.tensor_scalar(
            s_dch[:, :], ps_x[0:128, 0:128], s_psc[:, h:h + 1], None, OP.subtract),
             reads=("ps_x", "psendT"), writes=("dch",), tag=f"de{h}")
        p.op("DVE", lambda e: e.scalar_tensor_tensor(
            s_rb[:, :], s_dch[:, :], 1e4, s_lts[:, :], OP.add, OP.mult),
             reads=("dch", "c:lts"), writes=("rbm",), tag=f"dm{h}")
        p.op("DVE", lambda e: e.tensor_scalar(s_rb[:, :], s_rb[:, :], -1e4,
                                              None, OP.add),
             reads=("rbm",), writes=("rbm",), tag=f"dm2{h}")
        p.op("ACT", lambda e: e.activation(s_dch[:, :], s_rb[:, :], AF.Exp),
             reads=("rbm",), writes=("dch",), tag=f"dx{h}")
        p.op("PE", lambda e, h=h: e.matmul(
            psum[2][0:128, 0:64], s_dch[:, :], s_statF[:, h * 64:(h + 1) * 64],
            start=True, stop=True),
             reads=("dch", "statF"), writes=("psum2",), tag=f"s2m{h}")
        p.op("PE", lambda e, h=h: e.matmul(
            psum[3][0:128, 0:1], s_dch[:, :], s_statF[:, 1024 + h:1025 + h],
            start=True, stop=True),
             reads=("dch", "statF"), writes=("psum3",), tag=f"n2m{h}")
        p.op("DVE", lambda e, h=h: e.tensor_copy(
            s_st2[:, h * 64:(h + 1) * 64], psum[2][0:128, 0:64]),
             reads=("psum2",), writes=("st2",),
             aka=("cs0", "cs1", "dstmp", "ostmp", "fexp2", "fexp3",
                  "afmrm", "fexp0", "rsh0", "ytile"), tag=f"s2c{h}")
        p.op("DVE", lambda e, h=h: e.tensor_copy(
            s_st2[:, 1024 + h:1025 + h], psum[3][0:128, 0:1]),
             reads=("psum3",), writes=("st2",), tag=f"n2c{h}")
    p.op("DVE", lambda e: e.tensor_scalar(
        s_scr2[0:64, 0:1040], s_st2[0:64, 0:1040], s_hsel[0:64, 0:1], None, OP.mult),
         reads=("st2", "c:hsel"), writes=("scr2l",), aka=("rmrow",), tag="hs0")
    p.op("DVE", lambda e: e.tensor_copy(
        s_scr[0:64, 0:1040], s_st2[64:128, 0:1040]),
         reads=("st2",), writes=("wb1",),
         aka=("rmrow2", "gsgn", "gabs", "xhalo"), tag="hscp")
    p.op("DVE", lambda e: e.scalar_tensor_tensor(
        s_st2l[0:64, 0:1040], s_scr[0:64, 0:1040], s_hsel[0:64, 1:2],
        s_scr2[0:64, 0:1040], OP.mult, OP.add),
         reads=("wb1", "c:hsel", "scr2l"), writes=("st2l",), tag="hs1")

    def expand_row(dst_ap, col_ap, rd, wr, tagp, plus1=False, waka=()):
        p.op("PE", lambda e: e.transpose(ps_x[0:1, 0:128], col_ap, s_id_f[:, :]),
             reads=tuple(rd) + ("c:id_f",), writes=("ps_x",), tag="xr" + tagp)
        p.op("DVE", lambda e: e.tensor_scalar(
            s_row[0:1, 0:128], ps_x[0:1, 0:128], 1.0 if plus1 else 0.0, None, OP.add),
             reads=("ps_x",), writes=("mhr",), aka=("rmrow", "crow", "modev"),
             tag="xc" + tagp)
        p.op("PE", lambda e: e.matmul(ps_x[0:128, 0:128], s_onesf[0:1, :],
                                      s_row[0:1, 0:128], start=True, stop=True),
             reads=("mhr", "c:onesf"), writes=("ps_x",), tag="xm" + tagp)
        p.op("DVE", lambda e: e.tensor_copy(dst_ap, ps_x[0:128, 0:128]),
             reads=("ps_x",), writes=tuple(wr), aka=tuple(waka),
             tag="xv" + tagp)

    for j in range(8):
        expand_row(s_gaexp[:, j * 128:(j + 1) * 128], s_modc[:, 16 + j:17 + j],
                   ("modc",), ("gaexp",), f"ga{j}",)
        expand_row(s_cbexp[:, j * 128:(j + 1) * 128], s_modc[:, 32 + j:33 + j],
                   ("modc",), ("cbexp",), f"cb{j}", plus1=True)
        expand_row(s_cbexp[:, 1024 + j * 128:1024 + (j + 1) * 128],
                   s_modc[:, 24 + j:25 + j], ("modc",), ("sbexp",), f"sb{j}")

    dma("POOL", s_win.ap().rearrange("p (c e) -> p c e", c=8)[:, :, 0:1024],
        d_woT.rearrange("(c p) e -> p c e", p=128), ("win",), ("woT",), "ldwoT")

    h2slots = [s_h2t.ap()[:, 0:1024], s_h2t.ap()[:, 1024:2048],
               s_dump.ap()[:, 0:1024], s_statF.ap()[:, 0:1024]]
    h2aka = [("xcb0", "xcb1", "xhat0", "xhat1"),
             ("xcb0", "xcb1", "xhat0", "xhat1"),
             ("dump", "psfm", "rsh0", "afmrm", "fexp0", "fexp1", "dch", "rbm"),
             ("statF",)]
    x2slots = [s_x2t.ap()[:, 0:1024], s_x2t.ap()[:, 1024:2048], s_gdq.ap()]
    x2aka = [("xcb0", "xcb1", "xhat0", "xhat1"),
             ("xcb0", "xcb1", "xhat0", "xhat1"), ()]
    warmup(AF.Sqrt)
    yTv = xTv
    for tl in range(NT):
        sbk2 = tl // 4
        sl = sbk2 % 2
        tt = tl % 4
        xt3 = xhv[:, tl, :].rearrange("p (h n) -> p h n", h=16)
        p.op("DVE", lambda e, tl=tl, xt3=xt3: e.tensor_tensor(
            wb1[:, 0:1024].rearrange("p (h n) -> p h n", h=16),
            t16[:, 0, tl, :].unsqueeze(2).broadcast_to([128, 16, 64]),
            xt3, OP.mult),
             reads=("t16_0", f"xh{tl}"), writes=("wb1",), aka=("rmrow2",),
             tag=f"P{tl}")
        p.op("DVE", lambda e, tl=tl: e.tensor_copy(wb1[:, 1024:1040],
                                                   t16[:, 0, tl, :]),
             reads=("t16_0",), writes=("wb1",), tag=f"Pc{tl}")
        for (n0, n1, bank) in ((0, 512, 2), (512, 1024, 3), (1024, 1040, 4)):
            p.op("PE", lambda e, n0=n0, n1=n1, bank=bank: e.matmul(
                psum[bank][0:128, 0:n1 - n0], s_utbd[:, :], wb1[:, n0:n1],
                start=True, stop=True),
                 reads=("wb1", "c:utbd"), writes=(f"psum{bank}",), tag=f"cp{tl}_{n0}")
            p.op("DVE", lambda e, n0=n0, n1=n1, bank=bank: e.tensor_copy(
                s_scr2[:, n0:n1], psum[bank][0:128, 0:n1 - n0]),
                 reads=(f"psum{bank}",), writes=("scr2l",), tag=f"cpe{tl}_{n0}")
        for (n0, n1, bank) in ((0, 512, 2), (512, 1024, 3), (1024, 1040, 4)):
            p.op("PE", lambda e, tl=tl, n0=n0, n1=n1, bank=bank: e.matmul(
                psum[bank][0:128, 0:n1 - n0],
                s_bcl[0:64, tl * 128:(tl + 1) * 128], s_st2l[0:64, n0:n1],
                start=True, stop=True),
                 reads=("st2l", "c:bcl"), writes=(f"psum{bank}",), tag=f"sb{tl}_{n0}")
            p.op("DVE", lambda e, n0=n0, n1=n1, bank=bank: e.tensor_copy(
                wb1[:, n0:n1], psum[bank][0:128, 0:n1 - n0]),
                 reads=(f"psum{bank}",), writes=("wb1",), tag=f"sbe{tl}_{n0}")
        p.op("DVE", lambda e, tl=tl: e.tensor_tensor(
            s_den[:, 0:16], t16[:, 1, tl, :], s_scr2[:, 1024:1040], OP.mult),
             reads=("t16_1", "scr2l"), writes=("den",), tag=f"d1{tl}")
        p.op("DVE", lambda e, tl=tl: e.tensor_tensor(
            s_den[:, 16:32], t16[:, 3, tl, :], wb1[:, 1024:1040], OP.mult),
             reads=("t16_3", "wb1"), writes=("den",), tag=f"d2{tl}")
        p.op("DVE", lambda e: e.tensor_tensor(s_den[:, 0:16], s_den[:, 0:16],
                                              s_den[:, 16:32], OP.add),
             reads=("den",), writes=("den",), tag=f"d3{tl}")
        p.op("DVE", lambda e: e.reciprocal(s_den[:, 16:32], s_den[:, 0:16]),
             reads=("den",), writes=("den",), tag=f"dr{tl}")
        p.op("DVE", lambda e, tl=tl: e.tensor_tensor(
            s_den[:, 32:48], t16[:, 1, tl, :], s_den[:, 16:32], OP.mult),
             reads=("t16_1", "den"), writes=("zf",), tag=f"zf{tl}")
        p.op("DVE", lambda e, tl=tl: e.tensor_tensor(
            s_den[:, 48:64], t16[:, 3, tl, :], s_den[:, 16:32], OP.mult),
             reads=("t16_3", "den"), writes=("srf",), tag=f"sr{tl}")
        p.op("DVE", lambda e: e.tensor_tensor(
            s_ytile[:, :].rearrange("p (h n) -> p h n", h=16),
            s_den[:, 32:48].unsqueeze(2).broadcast_to([128, 16, 64]),
            s_scr2[:, 0:1024].rearrange("p (h n) -> p h n", h=16), OP.mult),
             reads=("zf", "scr2l"), writes=("ytile",), aka=("cs0", "cs1"),
             tag=f"y1{tl}")
        p.op("DVE", lambda e: e.tensor_tensor(
            s_scr2[:, 0:1024].rearrange("p (h n) -> p h n", h=16),
            s_den[:, 48:64].unsqueeze(2).broadcast_to([128, 16, 64]),
            wb1[:, 0:1024].rearrange("p (h n) -> p h n", h=16), OP.mult),
             reads=("srf", "wb1"), writes=("scr2l",), tag=f"y2{tl}")
        p.op("DVE", lambda e, tl=tl: e.tensor_tensor(
            xhv[:, tl, :], s_ytile[:, :], s_scr2[:, 0:1024], OP.add),
             reads=("ytile", "scr2l"), writes=(f"y{tl}",), aka=(f"xh{tl}",),
             tag=f"y3{tl}")
        for dc in range(8):
            pe_t128(yTv[:, sl, tt, dc, :],
                    xhv[:, tl, dc * 128:(dc + 1) * 128],
                    (f"y{tl}",), (f"yT{sl}",), f"yT{tl}_{dc}", waka=(f"xT{sl}",))
        if tt == 3:
            for t2 in range(4):
                gt = sbk2 * 4 + t2
                xsl = gt % 3
                hsl = gt % 4
                x2ap = x2slots[xsl]
                h2ap = h2slots[hsl]
                dma("POOL", xstage[:, gt % 2, :], d_x[gt * 128:(gt + 1) * 128, :],
                    (), (f"xs{gt % 2}",), f"ldx2_{gt}")
                for half in range(2):
                    bank = half

                    def mmo2(e, sl=sl, t2=t2, half=half, bank=bank):
                        inst = None
                        for dc in range(8):
                            inst = e.matmul(psum[bank][0:128, 0:512],
                                            yTv[:, sl, t2, dc, :],
                                            s_win.ap().rearrange("p (c e) -> p c e", c=8)
                                            [:, dc, half * 512:(half + 1) * 512],
                                            start=(dc == 0), stop=(dc == 7))
                        return inst
                    p.op("PE", mmo2, reads=(f"yT{sl}", "woT"),
                         writes=(f"psum{bank}",), tag=f"op{gt}_{half}")
                    p.op("DVE", lambda e, half=half, bank=bank, x2ap=x2ap:
                         e.tensor_tensor(
                             x2ap[:, half * 512:(half + 1) * 512],
                             psum[bank][0:128, 0:512],
                             s_gaexp[:, half * 512:(half + 1) * 512], OP.mult),
                         reads=(f"psum{bank}", "gaexp"), writes=(f"x2t{xsl}",),
                         aka=x2aka[xsl], tag=f"ev2{gt}_{half}")
                p.op("DVE", lambda e, x2ap=x2ap, gt=gt: e.tensor_tensor(
                    x2ap[:, 0:1024], x2ap[:, 0:1024],
                    xstage[:, gt % 2, :], OP.add),
                     reads=(f"x2t{xsl}", f"xs{gt % 2}"), writes=(f"x2t{xsl}",),
                     tag=f"res{gt}")
                dma("POOL", d_x2[gt * 128:(gt + 1) * 128, :], x2ap[:, 0:1024],
                    (f"x2t{xsl}",), (f"d_x2_{gt}",), f"stx2_{gt}")
                ln_tile(x2ap[:, 0:1024], 128,
                        s_ln2[:, 2 * gt:2 * gt + 1], s_ln2[:, 2 * gt + 1:2 * gt + 2],
                        (f"x2t{xsl}",), (f"ln2_{gt}",), f"L2{gt}",
                        scr_ap=h2ap, scr_wr=(f"h2t{hsl}",), scr_aka=h2aka[hsl])
                p.op("DVE", lambda e, x2ap=x2ap, h2ap=h2ap, gt=gt: e.tensor_scalar(
                    h2ap[:, 0:1024], x2ap[:, 0:1024],
                    s_ln2[:, 2 * gt:2 * gt + 1], s_ln2[:, 2 * gt + 1:2 * gt + 2],
                    OP.mult, OP.add),
                     reads=(f"x2t{xsl}", f"ln2_{gt}"), writes=(f"h2t{hsl}",),
                     aka=h2aka[hsl], tag=f"h2a{gt}")
                p.op("DVE", lambda e, h2ap=h2ap: e.scalar_tensor_tensor(
                    h2ap[:, 0:1024], h2ap[:, 0:1024], 1.0,
                    s_cbexp[:, 0:1024], OP.bypass, OP.mult),
                     reads=(f"h2t{hsl}", "cbexp"), writes=(f"h2t{hsl}",),
                     tag=f"h2b{gt}")
                p.op("DVE", lambda e, h2ap=h2ap: e.tensor_tensor(
                    h2ap[:, 0:1024], h2ap[:, 0:1024],
                    s_cbexp[:, 1024:2048], OP.add),
                     reads=(f"h2t{hsl}", "sbexp"), writes=(f"h2t{hsl}",),
                     tag=f"h2c{gt}")
                dma("POOL", d_h2[gt * 128:(gt + 1) * 128, :], h2ap[:, 0:1024],
                    (f"h2t{hsl}",), (f"d_h2_{gt}",), f"sth2_{gt}")
                p.op("DVE", lambda e, h2ap=h2ap: e.reduce_max(
                    s_gmax[:, 1:2], h2ap[:, 0:1024], AX.X,
                    apply_absolute_value=True),
                     reads=(f"h2t{hsl}",), writes=("gm1",), tag=f"am{gt}")
                p.op("DVE", lambda e: e.tensor_tensor(
                    s_gmax[:, 0:1], s_gmax[:, 0:1], s_gmax[:, 1:2], OP.max),
                     reads=("gm1", "gm0"), writes=("gm0",), tag=f"amx{gt}")

    tap("y0", xhv[:, 0, :], (128, 1024), BF16, reads=("y0",))
    tap("y17", xhv[:, 17, :], (128, 1024), BF16, reads=("y17",))
    tap("st2", s_st2[:, 0:1040], (128, 1040), reads=("st2",))
    tap("agg", s_agg[0:64, :], (64, 1056), reads=("agg",))
    tap("statF", s_statF[:, :], (128, 1056), reads=("statF",))
    tap("psf", s_psfm[0:16, :], (16, 768), reads=("rmps", "psend"))
    tap("x2a", s_x2t[:, :], (128, 2048), reads=("x2t0", "x2t1"))
    tap("gmax", s_gmax[:, 0:1], (128, 1), reads=("gm0",))

    tap("modc", s_modc[:, :], (128, 48), reads=("modc",))
    tap("capl", s_capl[:, :], (128, 24), reads=("silc", "cacol", "sacol"))
    tap("afm", s_afm[:, :], (128, 4096), reads=("afmA",))
    tap("xh0", xhv[:, 0, :], (128, 1024), BF16, reads=("xh0",))
    tap("xh5", xhv[:, 5, :], (128, 1024), BF16, reads=("xh5",))
    tap("xh31", xhv[:, 31, :], (128, 1024), BF16, reads=("xh31",))
    tap("uv", s_uv[0:2, :], (2, E), BF16, reads=("uv",))
    tap("k8", s_k8[0:2, :], (2, 1024), BF16, reads=("k8",))
    tap("xT", s_xT[:, :], (128, 8192), BF16, reads=("xT0", "xT1"))
    tap("carry", s_carry[:, :], (128, 24),
        reads=tuple(f"carry{ec}" for ec in range(8)))

    p.op("PE", lambda e: e.transpose(ps_x[0:1, 0:128], s_gmax[:, 0:1],
                                     s_id_f[:, :]),
         reads=("gm0", "c:id_f"), writes=("ps_x",), tag="Tgm")
    p.op("DVE", lambda e: e.tensor_copy(s_scr2[0:1, 0:128], ps_x[0:1, 0:128]),
         reads=("ps_x",), writes=("gmrow",), aka=("scr2l", "rmrow"), tag="gmc")
    p.op("DVE", lambda e: e.reduce_max(s_mcol[0:1, 8:9], s_scr2[0:1, 0:128],
                                       AX.X),
         reads=("gmrow",), writes=("gmcell",), aka=("mcol8",), tag="gmr")
    p.op("DVE", lambda e: e.memset(s_mcol[0:1, 9:10], 0.0),
         writes=("gmz",), aka=("mcol9",), tag="gmz")
    dma("POOL", d_rin, s_mcol[0:1, 8:10], ("gmcell", "gmz"), ("d_rin",), "gri")
    p.op("POOL", lambda e: e.collective_compute(
        "AllReduce", OP.max, replica_groups=GROUPS8,
        ins=[d_rin.opt()], outs=[d_rout.opt()]),
         reads=("d_rin",), writes=("d_rout",), cc_sem="cc1", tag="ccar1")
    dma("POOL", s_qcol[0:1, 0:1], d_rout[0:1, 0:1], ("d_rout",), ("gmaxg",), "gro")

    warmup(AF.Sigmoid)
    p.op("ACT", lambda e: e.activation(s_qcol[0:1, 4:6], s_mcol[0:1, 0:2],
                                       AF.Sigmoid),
         reads=("c:clipx", "c:clipd", "fence1"), writes=("sigs",), tag="sig")
    p.op("DVE", lambda e: e.tensor_tensor(s_qcol[0:1, 2:3], s_qcol[0:1, 4:5],
                                          s_qcol[0:1, 0:1], OP.mult),
         reads=("sigs", "gmaxg"), writes=("s1c",), tag="s1a")
    p.op("DVE", lambda e: e.tensor_scalar(s_qcol[0:1, 2:3], s_qcol[0:1, 2:3],
                                          1.0 / 127.0, None, OP.mult),
         reads=("s1c",), writes=("s1c",), tag="s1b")
    p.op("DVE", lambda e: e.reciprocal(s_qcol[0:1, 3:4], s_qcol[0:1, 2:3]),
         reads=("s1c",), writes=("inv1",), tag="s1r")
    p.op("DVE", lambda e: e.tensor_scalar(s_qcol[0:1, 6:7] if False else
                                          s_mcol[0:1, 8:9], s_qcol[0:1, 2:3],
                                          1.0 / 127.0, None, OP.mult),
         reads=("s1c",), writes=("s1d",), aka=("gmcell",), tag="s1d")

    def expand_cell(dst_col, src_cell, rd, wr, tagp):
        p.op("PE", lambda e: e.matmul(ps_x[0:128, 0:1], s_onesf[0:1, :],
                                      src_cell, start=True, stop=True),
             reads=tuple(rd) + ("c:onesf",), writes=("ps_x",), tag="xm" + tagp)
        p.op("DVE", lambda e: e.tensor_copy(dst_col, ps_x[0:128, 0:1]),
             reads=("ps_x",), writes=tuple(wr), tag="xc" + tagp)

    expand_cell(sgu[:, 159:160], s_qcol[0:1, 3:4], ("inv1",), ("inv1col",), "i1")
    expand_cell(sgu[:, 156:157], s_mcol[0:1, 8:9], ("s1d",), ("s1dcol",), "s1")
    p.op("DVE", lambda e: e.tensor_tensor(
        sgu[:, 104:126], sgu[:, 0:22],
        sgu[:, 156:157].broadcast_to([128, 22]), OP.mult),
         reads=("sgGU", "s1dcol"), writes=("cgcu",), tag="cg")
    p.op("DVE", lambda e: e.tensor_tensor(
        sgu[:, 126:148], sgu[:, 22:44],
        sgu[:, 156:157].broadcast_to([128, 22]), OP.mult),
         reads=("sgGU", "s1dcol", "cgcu"), writes=("cgcu",), tag="cu")

    warmup(AF.Copy)
    xqv = s_big.ap().rearrange("p (c t) -> p c t", c=8)
    fscv = s_fsc.ap().rearrange("p (r d) -> p r d", r=2)
    wb1v = s_wb1.ap().rearrange("p (r d) -> p r d", r=2)
    for gt in range(NT):
        xsl = gt % 2
        dma("POOL", fscv[:, xsl, :], d_h2[gt * 128:(gt + 1) * 128, :],
            (f"d_h2_{gt}",), (f"fh{xsl}",), f"ldh2_{gt}",
            aka=("xT1", "yT1"))
        p.op("ACT", lambda e, xsl=xsl: e.activation(
            fscv[:, xsl, :], fscv[:, xsl, :], AF.Copy, scale=sgu[:, 159:160]),
             reads=(f"fh{xsl}", "inv1col", "fence1"), writes=(f"fh{xsl}",),
             tag=f"qs{gt}")
        p.op("DVE", lambda e, xsl=xsl: e.tensor_scalar(
            fscv[:, xsl, :], fscv[:, xsl, :], MAGIC, MAGIC, OP.add, OP.subtract),
             reads=(f"fh{xsl}",), writes=(f"fh{xsl}",), tag=f"qr{gt}")
        p.op("DVE", lambda e, xsl=xsl: e.tensor_scalar(
            wb1v[:, xsl, 0:1024], fscv[:, xsl, :], 127.0, -128.0,
            OP.min, OP.max),
             reads=(f"fh{xsl}",), writes=(f"xq{xsl}",), aka=WB1_AKA,
             tag=f"qc{gt}")
        for dc in range(8):
            pe_t128(xqv[:, dc, gt * 128:(gt + 1) * 128],
                    wb1v[:, xsl, dc * 128:(dc + 1) * 128],
                    (f"xq{xsl}",), (f"xqT{dc}_{gt >> 2}",), f"xqT{gt}_{dc}",
                    waka=(f"y{dc * 4 + (gt >> 3)}", f"xh{dc * 4 + (gt >> 3)}"))

    warmup(AF.Silu)
    p.op("DVE", lambda e: e.memset(s_gmax[:, 2:4], 0.0), writes=("gm2",),
         tag="zg2")
    wqgv = s_wqg.ap().rearrange("p (r c e) -> p r c e", r=2, c=8)
    wquv = s_wqu.ap().rearrange("p (r c e) -> p r c e", r=2, c=8)
    rawslots = [s_wnat.ap()[:, 0:1024], s_wnat.ap()[:, 1024:2048],
                s_x4.ap()[:, 0:1024], s_x4.ap()[:, 1024:2048]]
    raw3 = [a.rearrange("p (c e) -> p c e", c=8) for a in rawslots]
    RAW_AKA = [WNAT_AKA, WNAT_AKA,
               ("x2t0", "x2t1", "h2t0", "h2t1"),
               ("x2t0", "x2t1", "h2t0", "h2t1")]
    dmpv = s_dump.ap().rearrange("p (c e) -> p c e", c=8)

    def wq_load(ic):
        for w_i, dsrc in enumerate((d_wgT, d_wuT2)):
            slot = 2 * (ic % 2) + w_i
            dma("POOL", raw3[slot][:, :, :],
                dsrc[ic * 1024:(ic + 1) * 1024, :]
                .rearrange("(c p) i -> p c i", p=128),
                (), (f"wnat{slot}",), f"lq{w_i}_{ic}", aka=RAW_AKA[slot])

    def wq_q(ic):
        wsl = ic % 2
        for w_i, (ivc, dstv, wres) in enumerate((
                (sgu[:, 60 + ic:61 + ic],
                 s_wqg.ap()[:, wsl * 1024:(wsl + 1) * 1024], f"wqg{wsl}"),
                (sgu[:, 82 + ic:83 + ic],
                 s_wqu.ap()[:, wsl * 1024:(wsl + 1) * 1024], f"wqu{wsl}"))):
            slot = 2 * (ic % 2) + w_i
            eslot = 2 * (ic % 2) + w_i
            expand_row(s_gdq[:, eslot * 128:(eslot + 1) * 128], ivc,
                       ("sgInv",), (f"ivex{eslot}",), f"ie{w_i}_{ic}",
                       waka=("x2t2",))
            p.op("DVE", lambda e, slot=slot, eslot=eslot: e.tensor_tensor(
                dmpv[:, :, :], raw3[slot][:, :, :],
                s_gdq[:, eslot * 128:(eslot + 1) * 128].unsqueeze(1)
                .broadcast_to([128, 8, 128]), OP.mult),
                 reads=(f"wnat{slot}", f"ivex{eslot}"), writes=("dump",),
                 aka=("h2t2",), tag=f"wm{w_i}_{ic}")
            p.op("DVE", lambda e, dstv=dstv: e.tensor_scalar(
                dstv, s_dump.ap(), MAGIC, MAGIC, OP.add, OP.subtract),
                 reads=("dump",), writes=(wres,), tag=f"wr{w_i}_{ic}")

    def mm_block(ic):
        wsl = ic % 2
        for tb in range(8):
            k = ic * 8 + tb
            bG = 0 if k % 2 == 0 else 2
            bU = 1 if k % 2 == 0 else 3
            h3sl = k % 2

            def mmg(e, wsl=wsl, tb=tb, bG=bG):
                inst = None
                for hc in range(8):
                    inst = e.matmul(psum[bG][0:128, 0:512], wqgv[:, wsl, hc, :],
                                    xqv[:, hc, tb * 512:(tb + 1) * 512],
                                    start=(hc == 0), stop=(hc == 7))
                return inst

            def mmu(e, wsl=wsl, tb=tb, bU=bU):
                inst = None
                for hc in range(8):
                    inst = e.matmul(psum[bU][0:128, 0:512], wquv[:, wsl, hc, :],
                                    xqv[:, hc, tb * 512:(tb + 1) * 512],
                                    start=(hc == 0), stop=(hc == 7))
                return inst

            xqres = tuple(f"xqT{hc}_{tb}" for hc in range(8))
            p.op("PE", mmg, reads=(f"wqg{wsl}",) + xqres, writes=(f"psum{bG}",),
                 tag=f"gm{ic}_{tb}")
            p.op("PE", mmu, reads=(f"wqu{wsl}",) + xqres, writes=(f"psum{bU}",),
                 tag=f"um{ic}_{tb}")
            p.op("ACT", lambda e, ic=ic, bG=bG: e.activation(
                s_fsc[:, 0:512], psum[bG][0:128, 0:512], AF.Silu,
                scale=sgu[:, 104 + ic:105 + ic]),
                 reads=(f"psum{bG}", "cgcu", "fence1"), writes=("fsil",),
                 aka=("fh0",), tag=f"sil{ic}_{tb}")
            p.op("ACT", lambda e, ic=ic, bU=bU: e.activation(
                s_fsc[:, 512:1024], psum[bU][0:128, 0:512], AF.Copy,
                scale=sgu[:, 126 + ic:127 + ic]),
                 reads=(f"psum{bU}", "cgcu", "fence1"), writes=("ftup",),
                 aka=("fh0",), tag=f"tup{ic}_{tb}")
            p.op("DVE", lambda e, h3sl=h3sl: e.tensor_tensor(
                s_fsc[:, 1024 + h3sl * 512:1024 + (h3sl + 1) * 512],
                s_fsc[:, 0:512], s_fsc[:, 512:1024], OP.mult),
                 reads=("fsil", "ftup"), writes=(f"fh3{h3sl}",),
                 aka=("fh1",), tag=f"h3{ic}_{tb}")
            p.op("DVE", lambda e, h3sl=h3sl: e.reduce_max(
                s_gmax[:, 3:4],
                s_fsc[:, 1024 + h3sl * 512:1024 + (h3sl + 1) * 512], AX.X,
                apply_absolute_value=True),
                 reads=(f"fh3{h3sl}",), writes=("gm3",), tag=f"hr{ic}_{tb}")
            p.op("DVE", lambda e: e.tensor_tensor(
                s_gmax[:, 2:3], s_gmax[:, 2:3], s_gmax[:, 3:4], OP.max),
                 reads=("gm3", "gm2"), writes=("gm2",), tag=f"hx{ic}_{tb}")
            dma("POOL", d_h3[ic * 128:(ic + 1) * 128, tb * 512:(tb + 1) * 512],
                s_fsc[:, 1024 + h3sl * 512:1024 + (h3sl + 1) * 512],
                (f"fh3{h3sl}",), (f"d_h3_{ic}_{tb}",), f"sh3_{ic}_{tb}")

    for ic in range(NIC + 2):
        if ic < NIC:
            wq_load(ic)
        if 1 <= ic < NIC + 1:
            wq_q(ic - 1)
        if ic >= 2:
            mm_block(ic - 2)

    p.op("PE", lambda e: e.transpose(ps_x[0:1, 0:128], s_gmax[:, 2:3],
                                     s_id_f[:, :]),
         reads=("gm2", "c:id_f"), writes=("ps_x",), tag="Tg2")
    p.op("DVE", lambda e: e.tensor_copy(s_scr2[0:1, 0:128], ps_x[0:1, 0:128]),
         reads=("ps_x",), writes=("gmrow",), tag="g2c")
    p.op("DVE", lambda e: e.reduce_max(s_mcol[0:1, 6:7], s_scr2[0:1, 0:128],
                                       AX.X),
         reads=("gmrow",), writes=("gm2cell",), tag="g2r")
    p.op("DVE", lambda e: e.memset(s_mcol[0:1, 7:8], 0.0),
         writes=("gm2z",), tag="g2z")
    dma("POOL", d_rin2, s_mcol[0:1, 6:8], ("gm2cell", "gm2z"), ("d_rin2",),
        "g2i")
    p.op("POOL", lambda e: e.collective_compute(
        "AllReduce", OP.max, replica_groups=GROUPS8,
        ins=[d_rin2.opt()], outs=[d_rout2.opt()]),
         reads=("d_rin2",), writes=("d_rout2",), cc_sem="cc2", tag="ccar2")
    dma("POOL", s_qcol[0:1, 1:2], d_rout2[0:1, 0:1], ("d_rout2",), ("gmax2g",),
        "g2o")

    for j in range(8):
        expand_row(s_dexp[:, j * 128:(j + 1) * 128], sgu[:, 52 + j:53 + j],
                   ("sgInv",), ("dexp",), f"dx{j}", waka=("cbexp", "sbexp"))
        expand_row(s_gbexp[:, j * 128:(j + 1) * 128], s_modc[:, 40 + j:41 + j],
                   ("modc",), ("gbexp",), f"gb{j}", waka=("gaexp",))
    wdqv = s_big.ap()[:, 0:NIC * 1024].rearrange("p (c e) -> p c e", c=NIC)
    for ic in range(NIC):
        slot = nslot[0] % 2
        nslot[0] += 1
        dma("POOL", s_wnat[:, slot * 1024:(slot + 1) * 1024],
            d_wdT[ic * 128:(ic + 1) * 128, :], (), (f"wnat{slot}",),
            f"lwd{ic}", aka=WNAT_AKA)
        p.op("DVE", lambda e, slot=slot: e.tensor_tensor(
            s_dump.ap(), s_wnat[:, slot * 1024:(slot + 1) * 1024],
            s_dexp.ap(), OP.mult),
             reads=(f"wnat{slot}", "dexp"), writes=("dump",),
             aka=("h2t2",), tag=f"wdm{ic}")
        p.op("DVE", lambda e, ic=ic: e.tensor_scalar(
            wdqv[:, ic, :], s_dump.ap(), MAGIC, MAGIC, OP.add, OP.subtract),
             reads=("dump",), writes=(f"wdq{ic}",),
             aka=tuple(f"xqT{ic // 4}_{sb}" for sb in range(8)),
             tag=f"wdr{ic}")

    p.op("DVE", lambda e: e.tensor_tensor(s_qcol[0:1, 2:3], s_qcol[0:1, 5:6],
                                          s_qcol[0:1, 1:2], OP.mult),
         reads=("sigs", "gmax2g", "s1c", "inv1"), writes=("s2c",), tag="s2a")
    p.op("DVE", lambda e: e.tensor_scalar(s_qcol[0:1, 2:3], s_qcol[0:1, 2:3],
                                          1.0 / 127.0, None, OP.mult),
         reads=("s2c",), writes=("s2c",), tag="s2b")
    p.op("DVE", lambda e: e.reciprocal(s_qcol[0:1, 3:4], s_qcol[0:1, 2:3]),
         reads=("s2c", "inv1col"), writes=("inv2",), tag="s2r")
    expand_cell(sgu[:, 158:159], s_qcol[0:1, 3:4], ("inv2",), ("inv2col",), "i2")
    expand_cell(sgu[:, 157:158], s_qcol[0:1, 2:3], ("s2c",), ("s2col",), "s2")
    for j in range(8):
        expand_row(s_gdq[:, j * 128:(j + 1) * 128], sgu[:, 148 + j:149 + j],
                   ("sgDdq",), ("gdq",), f"gq{j}",
                   waka=("ivex0", "ivex1", "ivex2", "ivex3"))
    p.op("DVE", lambda e: e.tensor_tensor(s_gdq.ap(), s_gdq.ap(),
                                          s_gbexp.ap(), OP.mult),
         reads=("gdq", "gbexp", "ivex0", "ivex1", "ivex2", "ivex3"),
         writes=("gdq",), tag="gdm")
    p.op("DVE", lambda e: e.tensor_scalar(s_gdq.ap(), s_gdq.ap(),
                                          sgu[:, 157:158], None, OP.mult),
         reads=("gdq", "s2col"), writes=("gdq",), tag="gds")

    fscv2 = s_fsc.ap().rearrange("p (s e) -> p s e", s=4)
    h3q2 = s_wqg.ap().rearrange("p (s e) -> p s e", s=4)
    x4v = s_x4.ap().rearrange("p (s e) -> p s e", s=4)
    banks = [psum[0], psum[1], psum[2], psum[3], psum[4], ps_x, ps_trA, ps_trB]
    bres = ["psum0", "psum1", "psum2", "psum3", "psum4", "ps_x", "ps_tr0",
            "ps_tr1"]
    otbuf = [s_ytile.ap(), s_dump.ap()[:, 0:1024]]
    otaka = [("cs0", "cs1", "ytile", "dstmp", "st2"),
             ("dump", "psfm", "ostmp", "st2")]
    for g in range(8):
        for t4 in range(4):
            t = g * 4 + t4
            dma("POOL", x4v[:, t4, :], d_x2[t * 128:(t + 1) * 128, :],
                (f"d_x2_{t}",), (f"xs{t4}",), f"lx2_{t}",
                aka=("x2t0", "x2t1", "h2t0", "h2t1", "wnat2", "wnat3"))
        for ic in range(NIC):
            fsl = ic % 4
            qsl = ic % 4
            fres = f"f6_{fsl}"
            qres = f"h3w{qsl}"
            dma("SP", fscv2[:, fsl, :],
                d_h3[ic * 128:(ic + 1) * 128, g * 512:(g + 1) * 512],
                (f"d_h3_{ic}_{g}",), (fres,),
                f"lh3_{g}_{ic}",
                aka=("fsil", "ftup", "fh30", "fh31", "fh0", "fh1"))
            p.op("ACT", lambda e, fsl=fsl: e.activation(
                fscv2[:, fsl, :], fscv2[:, fsl, :], AF.Copy,
                scale=sgu[:, 158:159]),
                 reads=(fres, "inv2col", "fence1"), writes=(fres,),
                 tag=f"q2s{g}_{ic}")
            p.op("DVE", lambda e, fsl=fsl: e.tensor_scalar(
                fscv2[:, fsl, :], fscv2[:, fsl, :], 127.0, -128.0,
                OP.min, OP.max),
                 reads=(fres,), writes=(fres,), tag=f"q2c{g}_{ic}")
            p.op("DVE", lambda e, fsl=fsl, qsl=qsl: e.tensor_scalar(
                h3q2[:, qsl, :], fscv2[:, fsl, :], MAGIC, MAGIC,
                OP.add, OP.subtract),
                 reads=(fres,), writes=(qres,),
                 aka=("wqg0", "wqg1"), tag=f"q2r{g}_{ic}")
            for t4 in range(4):
                for half in range(2):
                    bi = t4 * 2 + half
                    p.op("PE", lambda e, qsl=qsl, ic=ic, t4=t4, half=half,
                         bi=bi: e.matmul(
                             banks[bi][0:128, 0:512],
                             h3q2[:, qsl, t4 * 128:(t4 + 1) * 128],
                             wdqv[:, ic, half * 512:(half + 1) * 512],
                             start=(ic == 0), stop=(ic == NIC - 1)),
                         reads=(qres, f"wdq{ic}"), writes=(bres[bi],),
                         tag=f"dm{g}_{ic}_{bi}")
        for t4 in range(4):
            t = g * 4 + t4
            osl = t4 % 2
            for half in range(2):
                bi = t4 * 2 + half
                p.op("DVE", lambda e, bi=bi, half=half, osl=osl:
                     e.scalar_tensor_tensor(
                         otbuf[osl][:, half * 512:(half + 1) * 512],
                         banks[bi][0:128, 0:512], 1.0,
                         s_gdq[:, half * 512:(half + 1) * 512],
                         OP.bypass, OP.mult),
                     reads=(bres[bi], "gdq"), writes=(f"ot{osl}",),
                     aka=otaka[osl], tag=f"oe{t}_{half}")
                p.op("DVE", lambda e, half=half, t4=t4, osl=osl:
                     e.tensor_tensor(
                         otbuf[osl][:, half * 512:(half + 1) * 512],
                         otbuf[osl][:, half * 512:(half + 1) * 512],
                         x4v[:, t4, half * 512:(half + 1) * 512], OP.add),
                     reads=(f"ot{osl}", f"xs{t4}"), writes=(f"ot{osl}",),
                     tag=f"or{t}_{half}")
            dma("POOL", d_out[t * 128:(t + 1) * 128, :], otbuf[osl][:, 0:1024],
                (f"ot{osl}",), (f"ow{t % 4}",), f"sto{t}")

    tap("sgu", sgu[:, :], (128, 160), reads=("sgGU", "sgD", "sgInv", "sgDdq",
                                             "cgcu", "inv1col"))
    tap("qc", s_qcol[0:1, 0:6], (1, 6), reads=("s1c", "inv1", "sigs", "gmaxg",
                                               "s2c", "inv2", "gmax2g"))
    tap("xqT6", xqv[:, 6, 0:1024], (128, 1024), BF16,
        reads=("xqT6_0", "xqT6_1"))
    tap("gm2t", s_gmax[:, 2:3], (128, 1), reads=("gm2",))

    p.op("ACT", p.echo["ACT"], tag="echoF")
    p.barrier_all_on("POOL")
    p.barrier_all_on("SP")

    import contextlib
    stack = contextlib.ExitStack()
    sem_handles = {}
    sems = ["PE", "ACT", "DVE", "POOL", "cc0", "cc1", "cc2"]
    for q in QUEUES.values():
        sems += [f"{q}s{i}" for i in range(Plan.RSLOT)]
    for s in sems:
        sem_handles[s] = stack.enter_context(nc.semaphore("sem_" + s))
    blk = stack.enter_context(nc.Block())

    def run_engine(name):
        def body(h):
            for o in p.eng_ops[name]:
                for sp, val in o["waits"]:
                    h.wait_ge(sem_handles[sp], val)
                if o["emit"] is None:
                    continue
                inst = o["emit"](h)
                if o["cc"] is not None:
                    inst.then_inc(sem_handles[o["cc"]], 1)
                elif o["dma"]:
                    inst.then_inc(sem_handles[o["sem"]], 16)
                elif o.get("stamp", True):
                    inst.then_inc(sem_handles[name], 1)
        return body

    blk.tensor(run_engine("PE"))
    blk.scalar(run_engine("ACT"))
    blk.vector(run_engine("DVE"))
    blk.gpsimd(run_engine("POOL"))
    blk.sync(run_engine("SP"))
    stack.close()
    n_ops = sum(len(v) for v in p.eng_ops.values())
    return nc, tap_d, n_ops


def host_inputs(inputs):
    X = np.ascontiguousarray(inputs["L0_attn_x"], dtype=np.float32)
    winT = np.ascontiguousarray(np.asarray(inputs["L0_attn_in_proj_w"]).T, dtype=np.float32)
    convw = np.ascontiguousarray(
        np.asarray(inputs["L0_attn_conv_w"])[:, 0, :].reshape(8, 128, 4)
        .transpose(1, 0, 2).reshape(128, 32), dtype=np.float32)
    convb = np.ascontiguousarray(np.asarray(inputs["L0_attn_conv_b"]).reshape(8, 128).T,
                                 dtype=np.float32)
    wbase = np.ascontiguousarray(np.asarray(inputs["L0_attn_w_base"]).reshape(16, 1),
                                 dtype=np.float32)
    woT = np.ascontiguousarray(np.asarray(inputs["L0_attn_out_proj_w"]).T, dtype=np.float32)
    args = dict(
        wmod=np.ascontiguousarray(inputs["L0_wmod"], dtype=np.float32),
        winT=winT, convw=convw, convb=convb, wbase=wbase, woT=woT,
        wg=np.ascontiguousarray(inputs["L0_ffn_w_gate"], dtype=np.float32),
        wu=np.ascontiguousarray(inputs["L0_ffn_w_up"], dtype=np.float32),
        wd=np.ascontiguousarray(inputs["L0_ffn_w_down"], dtype=np.float32),
        wgTb=np.ascontiguousarray(
            np.asarray(inputs["L0_ffn_w_gate"], dtype=np.float32).T
            .reshape(1024, 22, 128).transpose(1, 0, 2).reshape(22 * 1024, 128)),
        wuTb=np.ascontiguousarray(
            np.asarray(inputs["L0_ffn_w_up"], dtype=np.float32).T
            .reshape(1024, 22, 128).transpose(1, 0, 2).reshape(22 * 1024, 128)),
        wdT=np.ascontiguousarray(np.asarray(inputs["L0_ffn_w_down"],
                                            dtype=np.float32).T),
        clipx=np.ascontiguousarray(np.asarray(inputs["L0_ffn_clip_x"]).reshape(1, 1),
                                   dtype=np.float32),
        clipd=np.ascontiguousarray(np.asarray(inputs["L0_ffn_clip_down"]).reshape(1, 1),
                                   dtype=np.float32),
    )
    Lc = np.ascontiguousarray(inputs["L0_c"], dtype=np.float32)
    zeros3 = np.zeros((3, 1024), dtype=np.float32)
    in_maps = []
    for c in range(8):
        b0, half = c >> 1, c & 1
        in_maps.append(dict(
            x4096=np.ascontiguousarray(X[b0, half * T:(half + 1) * T]),
            xhalo=(np.ascontiguousarray(X[b0, half * T - 3:half * T]) if half else zeros3),
            mask3=np.full((3, 1), 1.0 if half else 0.0, dtype=np.float32),
            crow=np.ascontiguousarray(Lc[b0:b0 + 1]),
            halfsel=np.ascontiguousarray(
                np.stack([np.full(128, 1.0 - half, np.float32),
                          np.full(128, float(half), np.float32)], axis=1)),
            **args))
    return in_maps


def kernel(**inputs):
    nc, _, _ = build_program(taps=DEBUG_TAPS)
    in_maps = host_inputs(inputs)
    res = run_bass_kernel_spmd(nc, in_maps, core_ids=list(range(8)))
    out = np.empty((4, 8192, 1024), dtype=np.float32)
    for c in range(8):
        b0, half = c >> 1, c & 1
        out[b0, half * T:(half + 1) * T] = np.asarray(res.results[c]["out"])
    return out


# version 09
# speedup vs baseline: 1.0283x; 1.0283x over previous
import numpy as np
import ml_dtypes
import concourse.bass as bass
import concourse.mybir as mybir
from concourse.bass_utils import run_bass_kernel_spmd

F32 = mybir.dt.float32
BF16 = mybir.dt.bfloat16
I32 = mybir.dt.int32
AF = mybir.ActivationFunctionType
OP = mybir.AluOpType
AX = mybir.AxisListType
BF = ml_dtypes.bfloat16

T = 4096
NT = 32
NSB = 8
E = 1040
NH = 16
IF = 2816
NIC = 22

DEBUG_TAPS = []
STAGE = 99
GROUPS8 = [[0, 1, 2, 3, 4, 5, 6, 7]]
GROUPS2 = [[0, 1], [2, 3], [4, 5], [6, 7]]

ENGINES = ("PE", "ACT", "DVE", "POOL", "SP")
QUEUES = {"SP": "dmaSP", "ACT": "dmaACT", "POOL": "dmaPOOL"}


class Plan:
    LAGGED = ()
    RSLOT = 8

    def __init__(self):
        self.eng_ops = {e: [] for e in ENGINES}
        self.seq = {e: 0 for e in ENGINES}
        self.dma_cnt = {q: 0 for q in QUEUES.values()}
        self.slot_use = {}
        self.res = {}
        self.know = {e: {} for e in ENGINES}
        self.snap = {}
        self.echo = {}

    def _resource(self, r):
        return self.res.setdefault(r, {"w": None, "r": []})

    def _need(self, eng, ev):
        sp, val = ev
        return self.know[eng].get(sp, 0) < val

    def _absorb(self, eng, ev):
        k = self.know[eng]
        sp, val = ev
        if k.get(sp, 0) < val:
            k[sp] = val
            for sp2, v2 in self.snap.get(ev, {}).items():
                if k.get(sp2, 0) < v2:
                    k[sp2] = v2

    def op(self, eng, emit, reads=(), writes=(), aka=(), dma=False, cc_sem=None, tag="", stamp=True):
        waits = []
        for r in reads:
            st = self._resource(r)
            if st["w"] is not None and self._need(eng, st["w"]):
                waits.append(st["w"])
        for w in tuple(writes) + tuple(aka):
            st = self._resource(w)
            if st["w"] is not None and self._need(eng, st["w"]):
                waits.append(st["w"])
            for ev in st["r"]:
                if self._need(eng, ev):
                    waits.append(ev)
        wd = {}
        for sp, val in waits:
            wd[sp] = max(wd.get(sp, 0), val)
        sem_name = None
        if dma and cc_sem is None:
            q = QUEUES[eng]
            slot = self.dma_cnt[q] % self.RSLOT
            self.dma_cnt[q] += 1
            sem_name = f"{q}s{slot}"
            prior = self.slot_use.get(sem_name, 0)
            if prior > 0 and self.know[eng].get(sem_name, 0) < 16 * prior:
                wd[sem_name] = max(wd.get(sem_name, 0), 16 * prior)
            self.slot_use[sem_name] = prior + 1
        for sp in list(wd):
            if sp in self.LAGGED and sp != eng:
                val = wd[sp]
                if val == self.seq[sp]:
                    self.op(sp, self.echo[sp], tag="echo")
                wd[sp] = val + 1
        for spval in wd.items():
            self._absorb(eng, spval)
        if cc_sem is not None:
            ev = (cc_sem, 1)
        elif dma:
            ev = (sem_name, 16 * self.slot_use[sem_name])
        elif stamp:
            self.seq[eng] += 1
            ev = (eng, self.seq[eng])
        else:
            ev = (eng, self.seq[eng])
        self.snap.setdefault(ev, dict(self.know[eng]))
        for r in reads:
            self._resource(r)["r"].append(ev)
        for w in writes:
            st = self._resource(w)
            st["w"] = ev
            st["r"] = []
        self.eng_ops[eng].append({"emit": emit, "waits": sorted(wd.items()),
                                  "dma": dma, "cc": cc_sem, "tag": tag,
                                  "stamp": stamp, "sem": sem_name})
        return ev

    def barrier_all_on(self, eng):
        waits = {}
        for e2 in ENGINES:
            if e2 != eng and self.seq[e2] > 0:
                waits[e2] = self.seq[e2]
        for snm, n in self.slot_use.items():
            if n > 0:
                waits[snm] = 16 * n
        final = []
        for sp, val in sorted(waits.items()):
            if self.know[eng].get(sp, 0) < val:
                final.append((sp, val))
                self._absorb(eng, (sp, val))
        self.eng_ops[eng].append({"emit": None, "waits": final, "dma": False,
                                  "cc": None, "tag": "finalwait"})


class Alloc:
    def __init__(self, nc, limit):
        self.nc, self.limit, self.cur = nc, limit, 0
        self.table = {}

    def at(self, name, rows, cols, dtype, offset=None):
        nbytes = cols * {F32: 4, BF16: 2, I32: 4}[dtype]
        if offset is None:
            offset = self.cur
            self.cur += (nbytes + 31) & ~31
            assert self.cur <= self.limit, f"SBUF over budget at {name}: {self.cur}"
        else:
            assert offset + nbytes <= self.limit
        self.table[name] = (offset, nbytes)
        return self.nc.alloc_sbuf_tensor_at(name, [rows, cols], dtype, offset=offset)

    def off(self, name):
        return self.table[name][0]


def build_program(taps=()):
    nc = bass.Bass("TRN2", target_bir_lowering=False)
    p = Plan()
    taps = set(taps)
    base = nc.SBUF_PARTITION_SIZE_BYTES - nc.sbuf_bytes_remaining
    base = (base + 31) & ~31
    al = Alloc(nc, nc.SBUF_PARTITION_SIZE_BYTES - 512)
    al.cur = base

    d_x = nc.dram_tensor("x4096", [T, 1024], F32, kind="ExternalInput").ap()
    d_xh3 = nc.dram_tensor("xhalo", [3, 1024], F32, kind="ExternalInput").ap()
    d_mask3 = nc.dram_tensor("mask3", [3, 1], F32, kind="ExternalInput").ap()
    d_crow = nc.dram_tensor("crow", [1, 1024], F32, kind="ExternalInput").ap()
    d_wmod = nc.dram_tensor("wmod", [1024, 6144], F32, kind="ExternalInput").ap()
    d_winT = nc.dram_tensor("winT", [1024, E], F32, kind="ExternalInput").ap()
    d_convw = nc.dram_tensor("convw", [128, 32], F32, kind="ExternalInput").ap()
    d_convb = nc.dram_tensor("convb", [128, 8], F32, kind="ExternalInput").ap()
    d_wbase = nc.dram_tensor("wbase", [16, 1], F32, kind="ExternalInput").ap()
    d_woT = nc.dram_tensor("woT", [1024, 1024], F32, kind="ExternalInput").ap()
    d_wg = nc.dram_tensor("wg", [IF, 1024], F32, kind="ExternalInput").ap()
    d_wu = nc.dram_tensor("wu", [IF, 1024], F32, kind="ExternalInput").ap()
    d_wd = nc.dram_tensor("wd", [1024, IF], F32, kind="ExternalInput").ap()
    d_clipx = nc.dram_tensor("clipx", [1, 1], F32, kind="ExternalInput").ap()
    d_clipd = nc.dram_tensor("clipd", [1, 1], F32, kind="ExternalInput").ap()
    d_hsel = nc.dram_tensor("halfsel", [128, 2], F32, kind="ExternalInput").ap()
    d_wgT = nc.dram_tensor("wgTb", [NIC * 1024, 128], F32, kind="ExternalInput").ap()
    d_wuT = nc.dram_tensor("wuTb", [NIC * 1024, 128], F32, kind="ExternalInput").ap()
    d_wuT2 = d_wuT
    d_wdT = nc.dram_tensor("wdT", [IF, 1024], F32, kind="ExternalInput").ap()
    d_out = nc.dram_tensor("out", [T, 1024], F32, kind="ExternalOutput").ap()

    d_x2 = nc.dram_tensor("x2scr", [T, 1024], F32, kind="Internal").ap()
    d_h2 = nc.dram_tensor("h2scr", [T, 1024], F32, kind="Internal").ap()
    d_h3 = nc.dram_tensor("h3scr", [NIC * 128, T], F32, kind="Internal").ap()
    d_gin = nc.dram_tensor("ccgin", [64, 1056], F32, kind="Internal").ap()
    d_gout = nc.dram_tensor("ccgout", [128, 1056], F32, kind="Internal").ap()
    d_rin = nc.dram_tensor("ccrin", [1, 2], F32, kind="Internal").ap()
    d_rout = nc.dram_tensor("ccrout", [1, 2], F32, kind="Internal").ap()
    d_rin2 = nc.dram_tensor("ccrin2", [1, 2], F32, kind="Internal").ap()
    d_rout2 = nc.dram_tensor("ccrout2", [1, 2], F32, kind="Internal").ap()

    idn_f = nc.inline_tensor(np.eye(128, dtype=np.float32), name="c_idf").ap()
    idn_b = nc.inline_tensor(np.eye(128, dtype=BF), name="c_idb").ap()
    ut = np.zeros((128, 128), dtype=BF)
    for b_ in range(2):
        for kk in range(64):
            ut[b_ * 64 + kk, b_ * 64 + kk:(b_ + 1) * 64] = 1
    c_utbd = nc.inline_tensor(ut, name="c_utbd").ap()
    bcl = np.zeros((64, T), dtype=BF)
    for z in range(64):
        bcl[z, z * 64:(z + 1) * 64] = 1
    c_bcl = nc.inline_tensor(bcl, name="c_bcl").ap()
    c_lts = nc.inline_tensor(
        np.ascontiguousarray(np.tri(128, 128, -1).T.astype(np.float32)),
        name="c_lts").ap()
    c_onesb = nc.inline_tensor(np.ones((1, 128), dtype=BF), name="c_onesb").ap()
    c_onesf = nc.inline_tensor(np.ones((1, 128), dtype=np.float32), name="c_onesf").ap()

    s_big = al.at("s_big", 128, NT * 1024, BF16)
    s_afm = al.at("s_afm", 128, 4096, F32)
    s_afm2 = al.at("s_afm2", 128, 4096, BF16)
    s_tok16 = al.at("s_tok16", 128, 4 * NT * 16, BF16)
    s_id_f = al.at("s_id_f", 128, 128, F32)
    s_id_b = al.at("s_id_b", 128, 128, BF16)
    s_utbd = al.at("s_utbd", 128, 128, BF16)
    s_bcl = al.at("s_bcl", 64, T, BF16)
    s_lts = al.at("s_lts", 128, 128, F32)
    s_onesb = al.at("s_onesb", 1, 128, BF16)
    s_onesf = al.at("s_onesf", 1, 128, F32)
    s_cw = al.at("s_cw", 128, 32, F32)
    s_cb = al.at("s_cb", 128, 8, F32)
    s_wb = al.at("s_wb", 16, 1, F32)
    s_modc = al.at("s_modc", 128, 48, F32)
    s_mcol = al.at("s_mcol", 128, 10, F32)
    s_stats = al.at("s_stats", 128, 16, F32)
    s_ln2 = al.at("s_ln2", 128, 2 * NT, F32)
    s_carry = al.at("s_carry", 128, 24, F32)
    s_gmax = al.at("s_gmax", 128, 4, F32)
    s_qcol = al.at("s_qcol", 128, 8, F32)
    s_capl = al.at("s_capl", 128, 24, F32)
    s_uvw = al.at("s_uvw", 128, 16, BF16)
    s_uv = al.at("s_uv", 2, E, BF16)
    s_k8 = al.at("s_k8", 2, 1024, BF16)
    s_k8h = al.at("s_k8h", 2, 4, BF16)
    s_xTh = al.at("s_xTh", 128, 24, BF16)
    s_hb3 = al.at("s_hb3", 3, 1024, BF16)
    s_atot = al.at("s_atot", 128, 128, F32)
    s_row = al.at("s_row", 1, 1024, F32)
    s_scr = al.at("s_scr", 128, 1056, F32)
    s_scr2 = al.at("s_scr2", 128, 1056, F32)
    s_agg = al.at("s_agg", 64, 1056, F32)
    s_aggsl = al.at("s_aggsl", 2, 1056, F32, offset=al.off("s_scr2"))
    s_st2l = al.at("s_st2l", 64, 1056, BF16)
    s_sgu = al.at("s_sgu", 128, 160, F32)
    s_i2T = al.at("s_i2T", 2, 128, BF16)
    s_wb1 = al.at("s_wb1", 128, 2048, BF16, offset=al.off("s_scr"))
    s_psc = al.at("s_psc", 128, 16, F32)
    s_den = al.at("s_den", 128, 64, F32)
    s_hsel = al.at("s_hsel", 128, 2, F32)
    s_ind = al.at("s_ind", 128, 2, BF16)
    s_xstage = al.at("s_xstage", 128, 2048, F32)
    s_xT = al.at("s_xT", 128, 2 * 8 * 512, BF16)
    s_hc = al.at("s_hc", 128, 2 * 4096, BF16)
    s_win = al.at("s_win", 128, 8 * E, BF16)
    s_wmodc = al.at("s_wmodc", 128, 2048, F32)
    s_statF = al.at("s_statF", 128, 1056, F32, offset=al.off("s_wmodc") + 0)
    s_xiext = al.at("s_xiext", 128, 2 * 515 + 26, F32)
    s_convscr = al.at("s_convscr", 128, 2 * 512, F32)
    s_st2 = al.at("s_st2", 128, 1056, F32, offset=al.off("s_convscr"))
    s_ytile = al.at("s_ytile", 128, 1024, F32, offset=al.off("s_convscr"))
    s_dump = al.at("s_dump", 128, 1024, F32)
    s_i32 = al.at("s_i32", 128, 1024, I32, offset=al.off("s_dump"))
    s_psfm = al.at("s_psfm", 16, 768, F32, offset=al.off("s_dump"))
    s_dch = al.at("s_dch", 128, 128, F32, offset=al.off("s_dump") + 3072)
    s_rb = al.at("s_rb", 128, 128, F32, offset=al.off("s_dump") + 3584)
    s_x2t = al.at("s_x2t", 128, 2 * 1024, F32, offset=al.off("s_hc"))
    s_h2t = al.at("s_h2t", 128, 2 * 1024, F32, offset=al.off("s_hc") + 8192)
    s_cbexp = al.at("s_cbexp", 128, 2 * 1024, F32, offset=al.off("s_afm"))
    s_gaexp = al.at("s_gaexp", 128, 1024, F32, offset=al.off("s_afm") + 8192)
    s_gdq = al.at("s_gdq", 128, 1024, F32, offset=al.off("s_afm") + 12288)
    s_wqg = al.at("s_wqg", 128, 2048, BF16, offset=al.off("s_xT"))
    s_wqu = al.at("s_wqu", 128, 2048, BF16, offset=al.off("s_xT") + 4096)
    s_fsc = al.at("s_fsc", 128, 2048, F32, offset=al.off("s_xT") + 8192)
    s_wnat = al.at("s_wnat", 128, 2048, F32, offset=al.off("s_wmodc") + 4224)
    s_h3q = al.at("s_h3q", 128, NIC * 256, BF16, offset=al.off("s_hc"))
    s_gbexp = al.at("s_gbexp", 128, 1024, F32, offset=al.off("s_afm") + 8192)
    s_dexp = al.at("s_dexp", 128, 1024, F32, offset=al.off("s_afm"))
    s_x4 = al.at("s_x4", 128, 4096, F32, offset=al.off("s_hc"))

    SB_USED = al.cur
    print(f"[k] sbuf {SB_USED} bytes/partition")
    p.echo["ACT"] = lambda e: e.copy(s_qcol[0:1, 7:8], s_qcol[0:1, 6:7])

    def warmup(func):
        p.op("ACT", lambda e, f=func: e.activation(s_qcol[0:1, 7:8],
                                                   s_qcol[0:1, 6:7], f),
             reads=("fence0",), writes=("fence1",), tag=f"warm{func}")

    psum = [nc.alloc_psum_tensor(f"psA{i}", [128, 512], F32) for i in range(5)]
    ps_trA = nc.alloc_psum_tensor("ps_trA", [128, 512], F32)
    ps_trB = nc.alloc_psum_tensor("ps_trB", [128, 512], F32)
    ps_x = nc.alloc_psum_tensor("ps_x", [128, 512], F32)
    ps_tr = ps_trA

    def dma(eng, out, in_, reads, writes, tag="", transpose=False, aka=()):
        return p.op(eng, (lambda e, o=out, i=in_, tr=transpose:
                          e.dma_start(out=o, in_=i, transpose=tr)),
                    reads=reads, writes=writes, aka=aka, dma=True, tag=tag)

    def trans_f32(dst_ap, src_ap, rd, wr, tag, waka=()):
        r, c = src_ap.shape
        pa = ps_x[0:c, 0:r]
        p.op("PE", lambda e: e.transpose(pa, src_ap, s_id_f[0:r, 0:r]),
             reads=tuple(rd) + ("c:id_f",), writes=("ps_x",), tag="T" + tag)
        p.op("DVE", lambda e: e.tensor_copy(dst_ap, pa),
             reads=("ps_x",), writes=tuple(wr), aka=waka, tag="Tc" + tag)

    def trans_b16(dst_ap, src_ap, rd, wr, tag, waka=()):
        r, c = src_ap.shape
        pa = ps_trA.ap().bitcast(BF16)[0:c, 0:r]
        p.op("PE", lambda e: e.transpose(pa, src_ap, s_id_b[0:r, 0:r]),
             reads=tuple(rd) + ("c:id_b",), writes=("ps_tr0",), tag="T" + tag)
        p.op("DVE", lambda e: e.tensor_copy(dst_ap, pa),
             reads=("ps_tr0",), writes=tuple(wr), aka=waka, tag="Tc" + tag)

    tp_flip = [0]

    TP2 = [0, ((ps_trA, "ps_tr0"), (ps_trB, "ps_tr1"))]

    def pe_t128(dst_ap, src_ap, rd, wr, tag, waka=(), eveng="DVE", rot=None):
        if rot is None:
            rot = TP2
        bank, res = rot[1][rot[0] % len(rot[1])]
        rot[0] += 1
        pa = bank.ap().bitcast(BF16)[0:128, 0:128]
        p.op("PE", lambda e: e.transpose(pa, src_ap, s_id_b[:, :]),
             reads=tuple(rd) + ("c:id_b",), writes=(res,), tag="T" + tag)
        if eveng == "ACT":
            p.op("ACT", lambda e: e.activation(dst_ap, pa, AF.Copy),
                 reads=(res, "fence1"), writes=tuple(wr), aka=waka,
                 tag="Tc" + tag)
        else:
            p.op("DVE", lambda e: e.tensor_copy(dst_ap, pa),
                 reads=(res,), writes=tuple(wr), aka=waka, tag="Tc" + tag)

    tap_d = {}

    def tap(name, src_ap, shape, dtype=F32, reads=()):
        if name not in taps:
            return
        dt_ap = nc.dram_tensor("tap_" + name, list(shape), dtype, kind="ExternalOutput").ap()
        tap_d[name] = True
        dma("SP", dt_ap, src_ap, reads, (f"tap_{name}",), "tap" + name)

    for dst, src, res in [
        (s_id_f[:, :], idn_f, "c:id_f"), (s_id_b[:, :], idn_b, "c:id_b"),
        (s_utbd[:, :], c_utbd, "c:utbd"), (s_bcl[:, :], c_bcl, "c:bcl"),
        (s_lts[:, :], c_lts, "c:lts"), (s_onesb[:, :], c_onesb, "c:onesb"),
        (s_onesf[:, :], c_onesf, "c:onesf"), (s_cw[:, :], d_convw, "c:cw"),
        (s_cb[:, :], d_convb, "c:cb"), (s_wb[:, :], d_wbase, "c:wb"),
        (s_row[0:1, :], d_crow, "crow"),
        (s_mcol[0:1, 0:1], d_clipx, "c:clipx"), (s_mcol[0:1, 1:2], d_clipd, "c:clipd"),
    ]:
        dma("SP", dst, src, (), (res,), "ld")
    dma("POOL", s_win.ap().rearrange("p (c e) -> p c e", c=8),
        d_winT.rearrange("(c p) e -> p c e", p=128), (), ("win",), "ldwin")
    winv = s_win.ap().rearrange("p (c e) -> p c e", c=8)
    i2t = np.zeros((2, 128), dtype=BF)
    i2t[0, :64] = 1; i2t[1, 64:] = 1
    c_i2t = nc.inline_tensor(i2t, name="c_i2t").ap()
    dma("SP", s_i2T[:, :], c_i2t, (), ("c:i2t",), "ldi2t")
    ind2a = np.zeros((128, 2), dtype=BF)
    ind2a[:64, 0] = 1; ind2a[64:, 1] = 1
    c_ind = nc.inline_tensor(ind2a, name="c_ind").ap()
    dma("SP", s_ind[:, :], c_ind, (), ("c:ind",), "ldind")
    dma("SP", s_hsel[:, :], d_hsel, (), ("c:hsel",), "ldhsel")
    p.op("DVE", lambda e: e.memset(s_gmax[:, 0:2], 0.0), writes=("gm0",), tag="zgm")

    p.op("DVE", lambda e: e.memset(s_qcol[0:1, 6:7], 1.0), writes=("fence0",), tag="fz")
    p.op("DVE", lambda e: e.memset(s_afm[:, :], 0.0), writes=("afmA",), tag="zafm")
    warmup(AF.Silu)
    p.op("ACT", lambda e: e.activation(s_row[0:1, :], s_row[0:1, :], AF.Silu),
         reads=("crow",), writes=("crow",), tag="siluc")
    for k in range(8):
        trans_f32(s_capl[:, 16 + k:17 + k], s_row[0:1, k * 128:(k + 1) * 128],
                  ("crow",), ("silc",), f"sc{k}")
    wmv = s_wmodc.ap().rearrange("p (s e) -> p s e", s=4)

    def mod_tr(ob):
        for q4 in range(4):
            col = ob * 4 + q4
            trans_f32(s_modc[:, col:col + 1],
                      s_scr2[0:1, (ob % 2) * 512 + q4 * 128:
                             (ob % 2) * 512 + (q4 + 1) * 128],
                      (f"modev{ob % 2}",), ("modc",), f"mt{ob}_{q4}")

    for ob in range(12):
        bres = f"psum{2 + ob % 2}"
        for k in range(8):
            slot = (ob * 8 + k) % 4
            dma("POOL", wmv[:, slot, :],
                d_wmod[k * 128:(k + 1) * 128, ob * 512:(ob + 1) * 512],
                (), (f"wm{slot}",), f"ldwm{ob}_{k}")
            p.op("PE", lambda e, k=k, slot=slot, ob=ob: e.matmul(
                psum[2 + ob % 2][0:1, 0:512], s_capl[:, 16 + k:17 + k],
                wmv[:, slot, :], start=(k == 0), stop=(k == 7)),
                 reads=(f"wm{slot}", "silc"), writes=(bres,), tag=f"mm{ob}_{k}")
        p.op("DVE", lambda e, ob=ob: e.tensor_copy(
            s_scr2[0:1, (ob % 2) * 512:(ob % 2) * 512 + 512],
            psum[2 + ob % 2][0:1, 0:512]),
             reads=(bres,), writes=(f"modev{ob % 2}",), tag=f"me{ob}")
        if ob >= 1:
            mod_tr(ob - 1)
    mod_tr(11)

    def g_xform(c0, tagp):
        p.op("DVE", lambda e: e.tensor_scalar(s_scr[:, 0:8], s_modc[:, c0:c0 + 8],
                                              0.0, None, OP.is_ge),
             reads=("modc",), writes=("gsgn",), tag="gs" + tagp)
        p.op("DVE", lambda e: e.tensor_scalar(s_scr[:, 0:8], s_scr[:, 0:8], 2.0, -1.0,
                                              OP.mult, OP.add),
             reads=("gsgn",), writes=("gsgn",), tag="g2" + tagp)
        p.op("ACT", lambda e: e.activation(s_scr[:, 8:16], s_modc[:, c0:c0 + 8], AF.Abs),
             reads=("modc",), writes=("gabs",), tag="gb" + tagp)
        p.op("DVE", lambda e: e.tensor_scalar(s_scr[:, 8:16], s_scr[:, 8:16], 0.25,
                                              None, OP.max),
             reads=("gabs",), writes=("gabs",), tag="gm" + tagp)
        p.op("DVE", lambda e: e.tensor_tensor(s_modc[:, c0:c0 + 8], s_scr[:, 0:8],
                                              s_scr[:, 8:16], OP.mult),
             reads=("gsgn", "gabs"), writes=("modc",), tag="gf" + tagp)

    if STAGE >= 2:
        g_xform(16, "a")
        g_xform(40, "b")

    if STAGE < 2:
        p.op("DVE", lambda e: e.memset(s_capl[:, 0:16], 0.01), writes=("cacol", "sacol"), tag="capz")
    p.op("DVE", lambda e: e.tensor_scalar(s_capl[:, 0:8], s_modc[:, 8:16], 1.0, None, OP.add),
         reads=("modc",), writes=("cacol",), tag="cap1") if STAGE >= 2 else None
    p.op("DVE", lambda e: e.tensor_copy(s_capl[:, 8:16], s_modc[:, 0:8]),
         reads=("modc",), writes=("sacol",), tag="sacp") if STAGE >= 2 else None
    p.op("DVE", lambda e: e.tensor_copy(bass.AP(s_uvw, 0, [[16, 128], [2, 8]]),
                                        s_capl[:, 0:8]),
         reads=("cacol",), writes=("uvw",), tag="uvc")
    p.op("DVE", lambda e: e.tensor_copy(bass.AP(s_uvw, 1, [[16, 128], [2, 8]]),
                                        s_capl[:, 8:16]),
         reads=("sacol",), writes=("uvw",), tag="uvs")

    for (n0, n1) in ((0, 512), (512, 1024), (1024, E)):
        def mm_uv(e, n0=n0, n1=n1):
            inst = None
            for dc in range(8):
                inst = e.matmul(ps_x[0:2, 0:n1 - n0], s_uvw[:, dc * 2:(dc + 1) * 2],
                                winv[:, dc, n0:n1], start=(dc == 0), stop=(dc == 7))
            return inst
        p.op("PE", mm_uv, reads=("win", "uvw"), writes=("ps_x",), tag=f"uv{n0}")
        p.op("DVE", lambda e, n0=n0, n1=n1: e.tensor_copy(s_uv[0:2, n0:n1],
                                                          ps_x[0:2, 0:n1 - n0]),
             reads=("ps_x",), writes=("uv",), tag=f"uvc{n0}")

    for dc in range(8):
        p.op("DVE", lambda e, dc=dc: e.tensor_scalar(
            winv[:, dc, :], winv[:, dc, :], s_capl[:, dc:dc + 1], None, OP.mult),
             reads=("win", "cacol"), writes=("win",), tag=f"fold{dc}")

    xstage = s_xstage.ap().rearrange("p (r d) -> p r d", r=2)
    xhat = s_hc.ap().rearrange("p (r d) -> p r d", r=2)
    xcbv = s_hc.ap().rearrange("p (r tt c f) -> p r tt c f", r=2, tt=4, c=8)
    xTv = s_xT.ap().rearrange("p (r tt c f) -> p r tt c f", r=2, tt=4, c=8)
    hcfl = s_hc.ap().rearrange("p (r n) -> p r n", r=2)
    xTfl = s_xT.ap().rearrange("p (r n) -> p r n", r=2)
    xhv = s_big.ap().rearrange("p (t d) -> p t d", t=NT)
    TP4 = [0, ((ps_trA, "ps_tr0"), (ps_trB, "ps_tr1"),
               (psum[2], "psum2"), (psum[3], "psum3"))]

    def ln_tile(src_ap, nrows, rstd_ap, mh_ap, rdres, stat_wr, tag,
                out_ap=None, owr=(), oaka=(), scr_ap=None, scr_wr=None,
                scr_aka=()):
        if scr_ap is None:
            scr_ap, scr_wr, scr_aka = s_dump.ap(), ("dump",), ()
        p.op("ACT", lambda e, scr_ap=scr_ap: e.activation(
            scr_ap[0:nrows, :], src_ap, AF.Copy,
            accum_out=s_mcol[0:nrows, 3:4]),
             reads=rdres + ("fence1",), writes=tuple(scr_wr) + ("mcol3",),
             aka=scr_aka, tag="s1" + tag)
        p.op("ACT", lambda e, scr_ap=scr_ap: e.activation(
            scr_ap[0:nrows, :], src_ap, AF.Square,
            accum_out=s_mcol[0:nrows, 4:5]),
             reads=rdres, writes=tuple(scr_wr) + ("mcol4",), aka=scr_aka,
             tag="s2" + tag)
        p.op("DVE", lambda e: e.tensor_scalar(s_mcol[0:nrows, 3:4], s_mcol[0:nrows, 3:4],
                                              1.0 / 1024, None, OP.mult),
             reads=("mcol3",), writes=("mcol3",), tag="m" + tag)
        p.op("DVE", lambda e: e.scalar_tensor_tensor(
            s_mcol[0:nrows, 5:6], s_mcol[0:nrows, 3:4], 1.0, s_mcol[0:nrows, 3:4],
            OP.bypass, OP.mult),
             reads=("mcol3",), writes=("mcol5",), tag="m2" + tag)
        p.op("DVE", lambda e: e.tensor_scalar(s_mcol[0:nrows, 4:5], s_mcol[0:nrows, 4:5],
                                              1.0 / 1024, None, OP.mult),
             reads=("mcol4",), writes=("mcol4",), tag="e2" + tag)
        p.op("DVE", lambda e: e.scalar_tensor_tensor(
            s_mcol[0:nrows, 4:5], s_mcol[0:nrows, 4:5], 1e-5,
            s_mcol[0:nrows, 5:6], OP.add, OP.subtract),
             reads=("mcol4", "mcol5"), writes=("mcol4",), tag="v" + tag)
        p.op("ACT", lambda e: e.activation(s_mcol[0:nrows, 5:6], s_mcol[0:nrows, 4:5],
                                           AF.Sqrt),
             reads=("mcol4",), writes=("mcol5",), tag="l" + tag)
        p.op("DVE", lambda e: e.reciprocal(rstd_ap, s_mcol[0:nrows, 5:6]),
             reads=("mcol5",), writes=stat_wr, tag="r" + tag)
        p.op("DVE", lambda e: e.scalar_tensor_tensor(mh_ap, s_mcol[0:nrows, 3:4], -1.0,
                                                     rstd_ap, OP.mult, OP.mult),
             reads=("mcol3",) + stat_wr, writes=stat_wr, tag="mh" + tag)
        if out_ap is not None:
            p.op("DVE", lambda e: e.tensor_scalar(out_ap, src_ap, rstd_ap, None,
                                                  OP.mult),
                 reads=rdres + stat_wr, writes=owr, aka=oaka, tag="xo" + tag)

    if STAGE < 3:
        run_main = False
    else:
        run_main = True
    warmup(AF.Sqrt) if run_main else None
    if not run_main:
        pass
    run_main and dma("POOL", s_scr[0:3, 16:1040], d_xh3, (), ("xhalo",), "ldxh")
    dma("SP", s_mcol[0:3, 2:3], d_mask3, (), ("mask3",), "ldm3")
    ln_tile(s_scr[0:3, 16:1040], 3, s_mcol[0:3, 6:7], s_mcol[0:3, 7:8],
            ("xhalo",), ("hstat",), "H", out_ap=s_hb3[0:3, :], owr=("hb3",))
    p.op("DVE", lambda e: e.tensor_scalar(s_hb3[0:3, :], s_hb3[0:3, :],
                                          s_mcol[0:3, 2:3], None, OP.mult),
         reads=("hb3", "mask3"), writes=("hb3",), tag="mskh")
    p.op("DVE", lambda e: e.scalar_tensor_tensor(s_mcol[0:3, 8:9], s_mcol[0:3, 7:8], 1.0,
                                                 s_mcol[0:3, 2:3], OP.bypass, OP.mult),
         reads=("hstat", "mask3"), writes=("mcol8",), tag="mhm")
    for dcx in range(8):
        trans_b16(bass.AP(s_xTh, dcx * 3, [[24, 128], [1, 3]]),
                  s_hb3[0:3, dcx * 128:(dcx + 1) * 128], ("hb3",), ("xTh",), f"th{dcx}")
    p.op("DVE", lambda e: e.tensor_copy(s_mcol[0:3, 9:10], s_mcol[0:3, 2:3]),
         reads=("mask3",), writes=("mcol9",), tag="k8hm")
    p.op("PE", lambda e: e.transpose(ps_x[0:2, 0:3], s_mcol[0:3, 8:10], s_id_f[0:3, 0:3]),
         reads=("mcol8", "mcol9", "c:id_f"), writes=("ps_x",), tag="Tk8h")
    p.op("DVE", lambda e: e.tensor_copy(s_k8h[0:2, 0:3], ps_x[0:2, 0:3]),
         reads=("ps_x",), writes=("k8h",), tag="k8hc")
    for ec in range(8):
        def hmm(e, ec=ec):
            inst = None
            for dc in range(8):
                inst = e.matmul(ps_x[0:128, 0:3], winv[:, dc, ec * 128:(ec + 1) * 128],
                                bass.AP(s_xTh, dc * 3, [[24, 128], [1, 3]]),
                                start=(dc == 0), stop=False)
            return e.matmul(ps_x[0:128, 0:3], s_uv[0:2, ec * 128:(ec + 1) * 128],
                            s_k8h[0:2, 0:3], start=False, stop=True)
        p.op("PE", hmm, reads=("win", "xTh", "uv", "k8h"), writes=("ps_x",), tag=f"hmm{ec}")
        p.op("DVE", lambda e, ec=ec: e.tensor_copy(s_carry[:, ec * 3:(ec + 1) * 3],
                                                   ps_x[0:128, 0:3]),
             reads=("ps_x",), writes=(f"carry{ec}",), tag=f"hc{ec}")

    p.op("DVE", lambda e: e.memset(s_mcol[:, 9:10], 1.0), writes=("mcol9",), tag="onecol")

    for sbk in range(NSB):
        sl = sbk % 2
        for tt in range(4):
            gt = sbk * 4 + tt
            xsl = gt % 2
            dma("POOL", xstage[:, xsl, :],
                d_x[gt * 128:(gt + 1) * 128, :], (), (f"xs{xsl}",), f"ldx{gt}")
            src = xstage[:, xsl, :]
            rs = s_stats[:, 2 * (gt % 8):2 * (gt % 8) + 1]
            mh = s_stats[:, 2 * (gt % 8) + 1:2 * (gt % 8) + 2]
            ln_tile(src, 128, rs, mh, (f"xs{xsl}",), (f"st{gt % 8}",), f"t{gt}",
                    out_ap=xhat[:, sl, tt * 1024:(tt + 1) * 1024],
                    owr=(f"xhat{sl}",), oaka=(f"xcb{sl}",))
            p.op("DVE", lambda e, mh=mh: e.tensor_copy(s_mcol[:, 8:9], mh),
                 reads=(f"st{gt % 8}",), writes=("mcol8",), tag=f"mc{gt}")
            p.op("PE", lambda e: e.transpose(ps_x[0:2, 0:128], s_mcol[:, 8:10],
                                             s_id_f[:, :]),
                 reads=("mcol8", "mcol9", "c:id_f"), writes=("ps_x",), tag=f"Tk8{gt}")
            p.op("DVE", lambda e, sl=sl, tt=tt: e.tensor_copy(
                s_k8[0:2, sl * 512 + tt * 128:sl * 512 + (tt + 1) * 128],
                ps_x[0:2, 0:128]),
                 reads=("ps_x",), writes=("k8",), tag=f"mTc{gt}")
        for tt in range(4):
            for dc in range(8):
                pe_t128(xTv[:, sl, tt, dc, :],
                        xhat[:, sl, tt * 1024 + dc * 128:tt * 1024 + (dc + 1) * 128],
                        (f"xhat{sl}",), (f"xT{sl}",), f"xT{sbk}_{tt}_{dc}",
                        rot=TP4)
        for ec in range(9):
            bank = (0, 1, 4)[ec % 3]
            M = 128 if ec < 8 else 16
            def mm_inp(e, sl=sl, ec=ec, bank=bank, M=M, sbk=sbk):
                for dc in range(8):
                    e.matmul(psum[bank][0:M, 0:512], winv[:, dc, ec * 128:ec * 128 + M],
                             xTv[:, sl, :, dc, :], start=(dc == 0), stop=False)
                return e.matmul(psum[bank][0:M, 0:512],
                                s_uv[0:2, ec * 128:ec * 128 + M],
                                s_k8[0:2, (sbk % 2) * 512:(sbk % 2 + 1) * 512],
                                start=False, stop=True)
            p.op("PE", mm_inp, reads=(f"xT{sl}", "win", "uv", "k8"),
                 writes=(f"psum{bank}",), tag=f"inp{sbk}_{ec}")
            if ec < 8:
                xi = ec % 2
                p.op("ACT", lambda e, xi=xi, bank=bank: e.copy(
                    s_xiext[:, xi * 515 + 3:(xi + 1) * 515], psum[bank][0:128, 0:512]),
                     reads=(f"psum{bank}",), writes=(f"xi{xi}",), tag=f"ev{sbk}_{ec}")
                p.op("DVE", lambda e, xi=xi, ec=ec: e.tensor_copy(
                    s_xiext[:, xi * 515:xi * 515 + 3], s_carry[:, ec * 3:(ec + 1) * 3]),
                     reads=(f"carry{ec}",), writes=(f"xi{xi}",), tag=f"ci{sbk}_{ec}")
                p.op("DVE", lambda e, xi=xi, ec=ec: e.tensor_scalar(
                    s_convscr[:, xi * 512:(xi + 1) * 512],
                    s_xiext[:, xi * 515:xi * 515 + 512],
                    s_cw[:, ec * 4:ec * 4 + 1], s_cb[:, ec:ec + 1],
                    OP.mult, OP.add),
                     reads=(f"xi{xi}", "c:cw", "c:cb"), writes=(f"cs{xi}",),
                     tag=f"c0_{sbk}_{ec}")
                for kk in range(1, 3):
                    p.op("DVE", lambda e, xi=xi, ec=ec, kk=kk: e.scalar_tensor_tensor(
                        s_convscr[:, xi * 512:(xi + 1) * 512],
                        s_xiext[:, xi * 515 + kk:xi * 515 + 512 + kk],
                        s_cw[:, ec * 4 + kk:ec * 4 + kk + 1],
                        s_convscr[:, xi * 512:(xi + 1) * 512], OP.mult, OP.add),
                         reads=(f"xi{xi}", "c:cw", f"cs{xi}"), writes=(f"cs{xi}",),
                         tag=f"c{kk}_{sbk}_{ec}")
                p.op("DVE", lambda e, xi=xi, sl=sl, ec=ec: e.scalar_tensor_tensor(
                    xcbv[:, sl, :, ec, :],
                    s_xiext[:, xi * 515 + 3:xi * 515 + 515],
                    s_cw[:, ec * 4 + 3:ec * 4 + 4],
                    s_convscr[:, xi * 512:(xi + 1) * 512], OP.mult, OP.add),
                     reads=(f"xi{xi}", "c:cw", f"cs{xi}"), writes=(f"xcb{sl}",),
                     aka=(f"xhat{sl}",), tag=f"cb_{sbk}_{ec}")
                p.op("DVE", lambda e, xi=xi, ec=ec: e.tensor_copy(
                    s_carry[:, ec * 3:(ec + 1) * 3],
                    s_xiext[:, xi * 515 + 512:xi * 515 + 515]),
                     reads=(f"xi{xi}",), writes=(f"carry{ec}",), tag=f"cu{sbk}_{ec}")
            else:
                p.op("DVE", lambda e, bank=bank, sbk=sbk: e.tensor_scalar(
                    s_afm[0:16, sbk * 512:(sbk + 1) * 512], psum[bank][0:16, 0:512],
                    s_wb[0:16, 0:1], None, OP.mult),
                     reads=(f"psum{bank}", "c:wb"), writes=("afmA",), tag=f"Aev{sbk}")
        for tt in range(4):
            gt = sbk * 4 + tt
            for dc in range(8):
                pe_t128(xhv[:, gt, dc * 128:(dc + 1) * 128],
                        xcbv[:, sl, tt, dc, :],
                        (f"xcb{sl}",), (f"xh{gt}",), f"xh{gt}_{dc}",
                        eveng="ACT", rot=TP4)


    def pe_tr(dst_ap, src_ap, rd, wr, tag, waka=(), f32=False):
        r, c = src_ap.shape
        b0_ = src_ap.base_partition()
        if f32:
            pa = ps_x[0:c, 0:r]
            res = "ps_x"
            ident = s_id_f[b0_:b0_ + r, b0_:b0_ + r]
            idr = "c:id_f"
        else:
            h_ = tp_flip[0]; tp_flip[0] ^= 1
            bank = ps_trA if h_ == 0 else ps_trB
            pa = bank.ap().bitcast(BF16)[0:c, 0:r]
            res = f"ps_tr{h_}"
            ident = s_id_b[b0_:b0_ + r, b0_:b0_ + r]
            idr = "c:id_b"
        p.op("PE", lambda e: e.transpose(pa, src_ap, ident),
             reads=tuple(rd) + (idr,), writes=(res,), tag="T" + tag)
        p.op("DVE", lambda e: e.tensor_copy(dst_ap, pa),
             reads=(res,), writes=tuple(wr), aka=waka, tag="Tc" + tag)

    afm3 = s_afm.ap().rearrange("p (c l) -> p c l", c=64)
    A_v = lambda p0: s_afm.ap().rearrange("p (c l) -> p c l", c=64)

    s_rB = [s_dump.ap()[:, 0:512], s_dump.ap()[:, 512:1024],
            s_convscr.ap()[:, 0:512], s_convscr.ap()[:, 512:1024]]
    rB3 = [b.rearrange("p (c l) -> p c l", c=8) for b in s_rB]
    RAKA = {0: ("psfm", "dump"), 1: ("dump", "dch", "rbm"),
            2: ("cs0", "cs1", "ytile"), 3: ("cs0", "cs1", "ytile")}

    def rsh_in(e):
        return e.dma_start(out=s_rB[0], in_=s_afm[0:16, :])
    p.op("SP", rsh_in, reads=("afmA",), writes=("rsh0",), aka=RAKA[0],
         dma=True, tag="rshA")

    def hs_scan2(srcb, dstb, pongb, op, res_in, res_out, tagp):
        cur = srcb
        order = [pongb, dstb, pongb, dstb, pongb, dstb]
        for i, s in enumerate((1, 2, 4, 8, 16, 32)):
            nxt = order[i]
            rin = res_in if i == 0 else f"{tagp}s{i-1}"
            rout = f"{tagp}s{i}" if i < 5 else res_out
            raka = RAKA[nxt] if i < 2 else ()
            p.op("DVE", lambda e, cur=cur, nxt=nxt, s=s: e.tensor_copy(
                rB3[nxt][:, :, 0:s], rB3[cur][:, :, 0:s]),
                 reads=(rin,), writes=(rout,), aka=raka, tag=f"{tagp}c{i}")
            p.op("DVE", lambda e, cur=cur, nxt=nxt, s=s: e.tensor_tensor(
                rB3[nxt][:, :, s:64], rB3[cur][:, :, s:64],
                rB3[cur][:, :, 0:64 - s], op),
                 reads=(rin,), writes=(rout,), tag=f"{tagp}a{i}")
            cur = nxt

    hs_scan2(0, 1, 2, OP.add, "rsh0", "afmCS", "cs")
    hs_scan2(1, 0, 2, OP.min, "afmCS", "afmrm", "rm")

    p.op("DVE", lambda e: e.tensor_copy(s_atot[:, 96:104],
                                        rB3[1][:, :, 63:64].squeeze(2)),
         reads=("afmCS",), writes=("atotR",), tag="exA")
    p.op("DVE", lambda e: e.tensor_copy(s_atot[:, 104:112],
                                        rB3[0][:, :, 63:64].squeeze(2)),
         reads=("afmrm",), writes=("fm",), tag="exF")
    p.op("DVE", lambda e: e.reduce_max(s_atot[:, 112:120], rB3[1][:, :, :],
                                       AX.X),
         reads=("afmCS",), writes=("Mc",), tag="exM")
    def rsh_at(e):
        return e.dma_start(out=s_atot[0:16, 0:64], in_=s_atot[:, 96:104])
    p.op("SP", rsh_at, reads=("atotR",), writes=("atot",), dma=True,
         tag="rshAt")

    warmup(AF.Exp)
    t16 = s_tok16.ap().rearrange("p (f t c) -> p f t c", f=4, t=NT)
    p.op("DVE", lambda e: e.tensor_tensor(
        rB3[2][:, :, :],
        s_atot[:, 104:112].unsqueeze(2).broadcast_to([128, 8, 64]),
        rB3[1][:, :, :], OP.subtract),
         reads=("afmCS", "fm"), writes=("dstmp",), aka=RAKA[2], tag="dsub")
    p.op("DVE", lambda e: e.tensor_tensor(
        rB3[3][:, :, :], rB3[1][:, :, :],
        s_atot[:, 112:120].unsqueeze(2).broadcast_to([128, 8, 64]),
        OP.subtract),
         reads=("afmCS", "Mc"), writes=("ostmp",), aka=RAKA[3], tag="osub")

    def field_to_t16(fi, bsl, emit_exp, rd, waka=()):
        p.op("ACT", emit_exp, reads=rd + ("fence1",), writes=(f"fexp{bsl}",),
             aka=waka, tag=f"fx{fi}")

        def back(e, bsl=bsl):
            return e.dma_start(out=s_afm2[0:16, :], in_=s_rB[bsl])
        p.op("POOL", back, reads=(f"fexp{bsl}",), writes=("fmcur",),
             dma=True, tag=f"fb{fi}")
        for tl in range(NT):
            pe_tr(t16[:, fi, tl, :], s_afm2[0:16, tl * 128:(tl + 1) * 128],
                  ("fmcur",), (f"t16_{fi}",), f"tf{fi}_{tl}")

    field_to_t16(2, 2, lambda e: e.activation(s_rB[2], s_rB[2], AF.Exp),
                 ("dstmp",))
    field_to_t16(3, 3, lambda e: e.activation(s_rB[3], s_rB[3], AF.Exp),
                 ("ostmp",))
    field_to_t16(0, 1, lambda e: e.activation(s_rB[1], s_rB[1], AF.Exp,
                                              scale=-1.0), ("afmCS",),
                 waka=("afmCS",))
    field_to_t16(1, 0, lambda e: e.activation(s_rB[0], s_rB[0], AF.Exp),
                 ("afmrm",), waka=("afmrm",))

    wb1 = s_wb1.ap()
    for tl in range(NT):
        xt3 = xhv[:, tl, :].rearrange("p (h n) -> p h n", h=16)
        p.op("DVE", lambda e, tl=tl, xt3=xt3: e.tensor_tensor(
            wb1[:, 0:1024].rearrange("p (h n) -> p h n", h=16),
            t16[:, 2, tl, :].unsqueeze(2).broadcast_to([128, 16, 64]),
            xt3, OP.mult),
             reads=("t16_2", f"xh{tl}"), writes=("wb1",),
             aka=("gsgn", "gabs", "xhalo"), tag=f"w1_{tl}")
        p.op("DVE", lambda e, tl=tl: e.tensor_copy(wb1[:, 1024:1040],
                                                   t16[:, 2, tl, :]),
             reads=("t16_2",), writes=("wb1",), tag=f"w1c_{tl}")
        for (n0, n1, bank) in ((0, 512, 2), (512, 1024, 3), (1024, 1040, 4)):
            p.op("PE", lambda e, n0=n0, n1=n1, bank=bank: e.matmul(
                psum[bank][0:2, 0:n1 - n0], s_ind[:, :], wb1[:, n0:n1],
                start=True, stop=True),
                 reads=("wb1", "c:ind"), writes=(f"psum{bank}",), tag=f"ag{tl}_{n0}")
            p.op("DVE", lambda e, n0=n0, n1=n1, bank=bank: e.tensor_copy(
                s_aggsl[0:2, n0:n1], psum[bank][0:2, 0:n1 - n0]),
                 reads=(f"psum{bank}",), writes=("aggsl",),
                 aka=("modev0", "modev1"), tag=f"agc{tl}_{n0}")
        dma("SP", s_agg[0:0 + 2, 0:1056] if False else
            bass.AP(s_agg, 0, [[1056, 2], [1, 1056]]) if False else s_agg[0:2, :],
            s_aggsl[0:2, :], ("aggsl",), ("agg",), f"agd{tl}") if False else None
        def agd(e, tl=tl):
            return e.dma_start(out=s_agg[2 * tl:2 * tl + 2, 0:1040],
                               in_=s_aggsl[0:2, 0:1040])
        p.op("SP", agd, reads=("aggsl",), writes=("agg",), dma=True, tag=f"agd{tl}")

    pe_tr(s_agg[0:64, 1040:1056], s_atot[0:16, 0:64], ("atot",), ("agg",),
          "atT", f32=True)

    dma("POOL", d_gin, s_agg[0:64, :], ("agg",), ("d_gin",), "gi")
    p.op("POOL", lambda e: e.collective_compute(
        "AllGather", OP.bypass, replica_groups=GROUPS2,
        ins=[d_gin.opt()], outs=[d_gout.opt()]),
         reads=("d_gin",), writes=("d_gout",), cc_sem="cc0", tag="ccag")
    dma("POOL", s_statF[:, :], d_gout, ("d_gout",), ("statF",), "go",
        aka=("wmodc", "wm0", "wm1", "wm2"))

    MAGIC = 12582912.0
    sgu = s_sgu.ap()
    WNAT_AKA = ("wmodc", "wm2", "wm3", "xi0", "xi1")
    WB1_AKA = ("wb1", "rmrow2", "gsgn", "gabs", "xhalo")

    nslot = [0]

    def natload(src_ap, tag):
        slot = nslot[0] % 2
        nslot[0] += 1
        dma("SP", s_wnat[:, slot * 1024:slot * 1024 + src_ap.shape[1]], src_ap,
            (), (f"wnat{slot}",), tag, aka=WNAT_AKA)
        return slot

    for w_i, (dsrc, cb0) in enumerate(((d_wg, 0), (d_wu, 22))):
        for ic in range(NIC):
            slot = natload(dsrc[ic * 128:(ic + 1) * 128, :], f"nl{w_i}_{ic}")
            p.op("DVE", lambda e, slot=slot, c=cb0 + ic: e.reduce_max(
                sgu[:, c:c + 1], s_wnat[:, slot * 1024:(slot + 1) * 1024],
                AX.X, apply_absolute_value=True),
                 reads=(f"wnat{slot}",), writes=("sgGU",), tag=f"rmw{w_i}_{ic}")
    for hb in range(8):
        for c3, w in ((0, 1024), (1024, 1024), (2048, 768)):
            slot = natload(d_wd[hb * 128:(hb + 1) * 128, c3:c3 + w], f"nd{hb}_{c3}")
            if c3 == 0:
                p.op("DVE", lambda e, slot=slot, hb=hb, w=w: e.reduce_max(
                    sgu[:, 44 + hb:45 + hb], s_wnat[:, slot * 1024:slot * 1024 + w],
                    AX.X, apply_absolute_value=True),
                     reads=(f"wnat{slot}",), writes=("sgD",), tag=f"rmd{hb}_{c3}")
            else:
                p.op("DVE", lambda e, slot=slot, w=w: e.reduce_max(
                    sgu[:, 156:157], s_wnat[:, slot * 1024:slot * 1024 + w],
                    AX.X, apply_absolute_value=True),
                     reads=(f"wnat{slot}",), writes=("sgScr",), tag=f"rmd{hb}_{c3}")
                p.op("DVE", lambda e, hb=hb: e.tensor_tensor(
                    sgu[:, 44 + hb:45 + hb], sgu[:, 44 + hb:45 + hb],
                    sgu[:, 156:157], OP.max),
                     reads=("sgD", "sgScr"), writes=("sgD",), tag=f"rmx{hb}_{c3}")
    p.op("DVE", lambda e: e.reciprocal(sgu[:, 60:82], sgu[:, 0:22]),
         reads=("sgGU",), writes=("sgInv",), tag="ivg")
    p.op("DVE", lambda e: e.tensor_scalar(sgu[:, 60:82], sgu[:, 60:82], 127.0,
                                          None, OP.mult),
         reads=("sgInv",), writes=("sgInv",), tag="ivg2")
    p.op("DVE", lambda e: e.reciprocal(sgu[:, 82:104], sgu[:, 22:44]),
         reads=("sgGU", "sgInv"), writes=("sgInv",), tag="ivu")
    p.op("DVE", lambda e: e.tensor_scalar(sgu[:, 82:104], sgu[:, 82:104], 127.0,
                                          None, OP.mult),
         reads=("sgInv",), writes=("sgInv",), tag="ivu2")
    p.op("DVE", lambda e: e.reciprocal(sgu[:, 52:60], sgu[:, 44:52]),
         reads=("sgD", "sgInv"), writes=("sgInv",), tag="ivd")
    p.op("DVE", lambda e: e.tensor_scalar(sgu[:, 52:60], sgu[:, 52:60], 127.0,
                                          None, OP.mult),
         reads=("sgInv",), writes=("sgInv",), tag="ivd2")
    p.op("DVE", lambda e: e.tensor_scalar(sgu[:, 148:156], sgu[:, 44:52],
                                          1.0 / 127.0, None, OP.mult),
         reads=("sgD",), writes=("sgDdq",), tag="ddq")


    pe_tr(s_psfm[0:16, 0:128], s_statF[:, 1040:1056], ("statF",), ("psf0",),
          "atf", f32=True)
    psf = s_psfm.ap().rearrange("p (k c) -> p k c", k=6)

    def hs_line(csrc, cdst, cpong, op, rin, rout, tagp, n=7):
        cur = csrc
        order = []
        slots = [cdst, cpong]
        for i in range(n):
            order.append(slots[(n - 1 - i) % 2])
        for i in range(n):
            s = 1 << i
            nxt = order[i]
            ri = rin if i == 0 else f"{tagp}{i-1}"
            ro = rout if i == n - 1 else f"{tagp}{i}"
            p.op("DVE", lambda e, cur=cur, nxt=nxt, s=s: e.tensor_copy(
                psf[0:16, nxt, 0:s], psf[0:16, cur, 0:s]),
                 reads=(ri,), writes=(ro,), tag=f"{tagp}c{i}")
            p.op("DVE", lambda e, cur=cur, nxt=nxt, s=s: e.tensor_tensor(
                psf[0:16, nxt, s:128], psf[0:16, cur, s:128],
                psf[0:16, cur, 0:128 - s], op),
                 reads=(ri,), writes=(ro,), tag=f"{tagp}a{i}")
            cur = nxt
        assert cur == cdst, (tagp, cur)

    hs_line(0, 1, 4, OP.add, "psf0", "psend", "ps")
    hs_line(1, 2, 4, OP.min, "psend", "rmp", "rm2")
    p.op("DVE", lambda e: e.memset(psf[0:16, 3, 0:1], 0.0), writes=("rmps",), tag="rz")
    p.op("DVE", lambda e: e.tensor_copy(psf[0:16, 3, 1:128], psf[0:16, 2, 0:127]),
         reads=("rmp",), writes=("rmps",), tag="rsh")
    p.op("DVE", lambda e: e.tensor_scalar(psf[0:16, 3, :], psf[0:16, 3, :], 0.0,
                                          None, OP.min),
         reads=("rmps",), writes=("rmps",), tag="rmn")
    pe_tr(s_psc[:, 0:16], psf[0:16, 1, :], ("psend",), ("psendT",), "pst", f32=True)
    def rrow(e):
        return e.dma_start(out=s_scr2[0:1, 0:1024], in_=psf[0:8, 3, :])
    def rrow2(e):
        return e.dma_start(out=s_scr[0:1, 0:1024].bitcast(F32) if False
                           else s_scr[0:1, 0:1024], in_=psf[8:16, 3, :])
    p.op("SP", rrow, reads=("rmps",), writes=("rmrow",), aka=("scr2l", "aggsl"),
         dma=True, tag="rr1")
    p.op("SP", rrow2, reads=("rmps",), writes=("rmrow2",),
         aka=("wb1", "gsgn", "gabs", "xhalo"), dma=True, tag="rr2")

    for h in range(16):
        rsrc = s_scr2[0:1, (h % 8) * 128:(h % 8 + 1) * 128] if h < 8 else                s_scr[0:1, (h - 8) * 128:(h - 8 + 1) * 128]
        rres = "rmrow" if h < 8 else "rmrow2"
        p.op("PE", lambda e, rsrc=rsrc: e.matmul(
            ps_x[0:128, 0:128], s_onesf[0:1, :], rsrc, start=True, stop=True),
             reads=(rres, "c:onesf"), writes=("ps_x",), tag=f"rb{h}")
        p.op("DVE", lambda e, h=h: e.tensor_scalar(
            s_dch[:, :], ps_x[0:128, 0:128], s_psc[:, h:h + 1], None, OP.subtract),
             reads=("ps_x", "psendT"), writes=("dch",), tag=f"de{h}")
        p.op("DVE", lambda e: e.scalar_tensor_tensor(
            s_rb[:, :], s_dch[:, :], 1e4, s_lts[:, :], OP.add, OP.mult),
             reads=("dch", "c:lts"), writes=("rbm",), tag=f"dm{h}")
        p.op("DVE", lambda e: e.tensor_scalar(s_rb[:, :], s_rb[:, :], -1e4,
                                              None, OP.add),
             reads=("rbm",), writes=("rbm",), tag=f"dm2{h}")
        p.op("ACT", lambda e: e.activation(s_dch[:, :], s_rb[:, :], AF.Exp),
             reads=("rbm",), writes=("dch",), tag=f"dx{h}")
        p.op("PE", lambda e, h=h: e.matmul(
            psum[2][0:128, 0:64], s_dch[:, :], s_statF[:, h * 64:(h + 1) * 64],
            start=True, stop=True),
             reads=("dch", "statF"), writes=("psum2",), tag=f"s2m{h}")
        p.op("PE", lambda e, h=h: e.matmul(
            psum[3][0:128, 0:1], s_dch[:, :], s_statF[:, 1024 + h:1025 + h],
            start=True, stop=True),
             reads=("dch", "statF"), writes=("psum3",), tag=f"n2m{h}")
        p.op("DVE", lambda e, h=h: e.tensor_copy(
            s_st2[:, h * 64:(h + 1) * 64], psum[2][0:128, 0:64]),
             reads=("psum2",), writes=("st2",),
             aka=("cs0", "cs1", "dstmp", "ostmp", "fexp2", "fexp3",
                  "afmrm", "fexp0", "rsh0", "ytile"), tag=f"s2c{h}")
        p.op("DVE", lambda e, h=h: e.tensor_copy(
            s_st2[:, 1024 + h:1025 + h], psum[3][0:128, 0:1]),
             reads=("psum3",), writes=("st2",), tag=f"n2c{h}")
    p.op("DVE", lambda e: e.tensor_scalar(
        s_scr2[0:64, 0:1040], s_st2[0:64, 0:1040], s_hsel[0:64, 0:1], None, OP.mult),
         reads=("st2", "c:hsel"), writes=("scr2l",), aka=("rmrow",), tag="hs0")
    p.op("DVE", lambda e: e.tensor_copy(
        s_scr[0:64, 0:1040], s_st2[64:128, 0:1040]),
         reads=("st2",), writes=("wb1",),
         aka=("rmrow2", "gsgn", "gabs", "xhalo"), tag="hscp")
    p.op("DVE", lambda e: e.scalar_tensor_tensor(
        s_st2l[0:64, 0:1040], s_scr[0:64, 0:1040], s_hsel[0:64, 1:2],
        s_scr2[0:64, 0:1040], OP.mult, OP.add),
         reads=("wb1", "c:hsel", "scr2l"), writes=("st2l",), tag="hs1")

    def expand_row(dst_ap, col_ap, rd, wr, tagp, plus1=False, waka=()):
        p.op("PE", lambda e: e.transpose(ps_x[0:1, 0:128], col_ap, s_id_f[:, :]),
             reads=tuple(rd) + ("c:id_f",), writes=("ps_x",), tag="xr" + tagp)
        p.op("DVE", lambda e: e.tensor_scalar(
            s_row[0:1, 0:128], ps_x[0:1, 0:128], 1.0 if plus1 else 0.0, None, OP.add),
             reads=("ps_x",), writes=("mhr",), aka=("rmrow", "crow", "modev"),
             tag="xc" + tagp)
        p.op("PE", lambda e: e.matmul(ps_x[0:128, 0:128], s_onesf[0:1, :],
                                      s_row[0:1, 0:128], start=True, stop=True),
             reads=("mhr", "c:onesf"), writes=("ps_x",), tag="xm" + tagp)
        p.op("DVE", lambda e: e.tensor_copy(dst_ap, ps_x[0:128, 0:128]),
             reads=("ps_x",), writes=tuple(wr), aka=tuple(waka),
             tag="xv" + tagp)

    for j in range(8):
        expand_row(s_gaexp[:, j * 128:(j + 1) * 128], s_modc[:, 16 + j:17 + j],
                   ("modc",), ("gaexp",), f"ga{j}",)
        expand_row(s_cbexp[:, j * 128:(j + 1) * 128], s_modc[:, 32 + j:33 + j],
                   ("modc",), ("cbexp",), f"cb{j}", plus1=True)
        expand_row(s_cbexp[:, 1024 + j * 128:1024 + (j + 1) * 128],
                   s_modc[:, 24 + j:25 + j], ("modc",), ("sbexp",), f"sb{j}")

    dma("POOL", s_win.ap().rearrange("p (c e) -> p c e", c=8)[:, :, 0:1024],
        d_woT.rearrange("(c p) e -> p c e", p=128), ("win",), ("woT",), "ldwoT")
    p.op("DVE", lambda e: e.tensor_tensor(
        s_win.ap().rearrange("p (c e) -> p c e", c=8)[:, :, 0:1024],
        s_win.ap().rearrange("p (c e) -> p c e", c=8)[:, :, 0:1024],
        s_gaexp.ap().unsqueeze(1).broadcast_to([128, 8, 1024]), OP.mult),
         reads=("woT", "gaexp"), writes=("woT",), tag="gafold")

    h2slots = [s_h2t.ap()[:, 0:1024], s_h2t.ap()[:, 1024:2048],
               s_dump.ap()[:, 0:1024], s_statF.ap()[:, 0:1024]]
    h2aka = [("xcb0", "xcb1", "xhat0", "xhat1"),
             ("xcb0", "xcb1", "xhat0", "xhat1"),
             ("dump", "psfm", "rsh0", "afmrm", "fexp0", "fexp1", "dch", "rbm"),
             ("statF",)]
    x2slots = [s_x2t.ap()[:, 0:1024], s_x2t.ap()[:, 1024:2048], s_gdq.ap()]
    x2aka = [("xcb0", "xcb1", "xhat0", "xhat1"),
             ("xcb0", "xcb1", "xhat0", "xhat1"), ()]
    warmup(AF.Sqrt)
    yTv = xTv
    for tl in range(NT):
        sbk2 = tl // 4
        sl = sbk2 % 2
        tt = tl % 4
        xt3 = xhv[:, tl, :].rearrange("p (h n) -> p h n", h=16)
        p.op("DVE", lambda e, tl=tl, xt3=xt3: e.tensor_tensor(
            wb1[:, 0:1024].rearrange("p (h n) -> p h n", h=16),
            t16[:, 0, tl, :].unsqueeze(2).broadcast_to([128, 16, 64]),
            xt3, OP.mult),
             reads=("t16_0", f"xh{tl}"), writes=("wb1",), aka=("rmrow2",),
             tag=f"P{tl}")
        p.op("DVE", lambda e, tl=tl: e.tensor_copy(wb1[:, 1024:1040],
                                                   t16[:, 0, tl, :]),
             reads=("t16_0",), writes=("wb1",), tag=f"Pc{tl}")
        for (n0, n1, bank) in ((0, 512, 2), (512, 1024, 3), (1024, 1040, 4)):
            p.op("PE", lambda e, n0=n0, n1=n1, bank=bank: e.matmul(
                psum[bank][0:128, 0:n1 - n0], s_utbd[:, :], wb1[:, n0:n1],
                start=True, stop=True),
                 reads=("wb1", "c:utbd"), writes=(f"psum{bank}",), tag=f"cp{tl}_{n0}")
        p.op("DVE", lambda e, tl=tl: e.tensor_tensor(
            s_den[:, 0:16], t16[:, 1, tl, :], psum[4][0:128, 0:16], OP.mult),
             reads=("t16_1", "psum4"), writes=("den",), tag=f"d1{tl}")
        for half, bank in ((0, 2), (1, 3)):
            p.op("DVE", lambda e, tl=tl, half=half, bank=bank: e.tensor_tensor(
                s_ytile[:, half * 512:(half + 1) * 512]
                .rearrange("p (h n) -> p h n", h=8),
                t16[:, 1, tl, half * 8:(half + 1) * 8].unsqueeze(2)
                .broadcast_to([128, 8, 64]),
                psum[bank][0:128, 0:512].rearrange("p (h n) -> p h n", h=8),
                OP.mult),
                 reads=("t16_1", f"psum{bank}"), writes=("ytile",),
                 aka=("cs0", "cs1", "st2", "dstmp", "ostmp"),
                 tag=f"y1{tl}_{half}")
        for (n0, n1, bank) in ((0, 512, 2), (512, 1024, 3), (1024, 1040, 4)):
            p.op("PE", lambda e, tl=tl, n0=n0, n1=n1, bank=bank: e.matmul(
                psum[bank][0:128, 0:n1 - n0],
                s_bcl[0:64, tl * 128:(tl + 1) * 128], s_st2l[0:64, n0:n1],
                start=True, stop=True),
                 reads=("st2l", "c:bcl"), writes=(f"psum{bank}",), tag=f"sb{tl}_{n0}")
        p.op("DVE", lambda e, tl=tl: e.tensor_tensor(
            s_den[:, 16:32], t16[:, 3, tl, :], psum[4][0:128, 0:16], OP.mult),
             reads=("t16_3", "psum4"), writes=("den",), tag=f"d2{tl}")
        p.op("DVE", lambda e: e.tensor_tensor(s_den[:, 0:16], s_den[:, 0:16],
                                              s_den[:, 16:32], OP.add),
             reads=("den",), writes=("den",), tag=f"d3{tl}")
        p.op("DVE", lambda e: e.reciprocal(s_den[:, 16:32], s_den[:, 0:16]),
             reads=("den",), writes=("den",), tag=f"dr{tl}")
        for half, bank in ((0, 2), (1, 3)):
            p.op("DVE", lambda e, tl=tl, half=half, bank=bank: e.tensor_tensor(
                s_scr2[:, half * 512:(half + 1) * 512]
                .rearrange("p (h n) -> p h n", h=8),
                t16[:, 3, tl, half * 8:(half + 1) * 8].unsqueeze(2)
                .broadcast_to([128, 8, 64]),
                psum[bank][0:128, 0:512].rearrange("p (h n) -> p h n", h=8),
                OP.mult),
                 reads=("t16_3", f"psum{bank}"), writes=("scr2l",),
                 tag=f"y2{tl}_{half}")
        p.op("DVE", lambda e, tl=tl: e.tensor_tensor(
            s_ytile[:, :], s_ytile[:, :], s_scr2[:, 0:1024], OP.add),
             reads=("ytile", "scr2l"), writes=("ytile",), tag=f"ya{tl}")
        p.op("DVE", lambda e, tl=tl: e.tensor_tensor(
            xhv[:, tl, :].rearrange("p (h n) -> p h n", h=16),
            s_den[:, 16:32].unsqueeze(2).broadcast_to([128, 16, 64]),
            s_ytile[:, :].rearrange("p (h n) -> p h n", h=16), OP.mult),
             reads=("den", "ytile"), writes=(f"y{tl}",), aka=(f"xh{tl}",),
             tag=f"y3{tl}")
        for dc in range(8):
            pe_t128(yTv[:, sl, tt, dc, :],
                    xhv[:, tl, dc * 128:(dc + 1) * 128],
                    (f"y{tl}",), (f"yT{sl}",), f"yT{tl}_{dc}", waka=(f"xT{sl}",))
        if tt == 3:
            for t2 in range(4):
                gt = sbk2 * 4 + t2
                xsl = gt % 3
                hsl = gt % 4
                x2ap = x2slots[xsl]
                h2ap = h2slots[hsl]
                dma("POOL", xstage[:, gt % 2, :], d_x[gt * 128:(gt + 1) * 128, :],
                    (), (f"xs{gt % 2}",), f"ldx2_{gt}")
                for half in range(2):
                    bank = half

                    def mmo2(e, sl=sl, t2=t2, half=half, bank=bank):
                        inst = None
                        for dc in range(8):
                            inst = e.matmul(psum[bank][0:128, 0:512],
                                            yTv[:, sl, t2, dc, :],
                                            s_win.ap().rearrange("p (c e) -> p c e", c=8)
                                            [:, dc, half * 512:(half + 1) * 512],
                                            start=(dc == 0), stop=(dc == 7))
                        return inst
                    p.op("PE", mmo2, reads=(f"yT{sl}", "woT"),
                         writes=(f"psum{bank}",), tag=f"op{gt}_{half}")
                    p.op("DVE", lambda e, half=half, bank=bank, x2ap=x2ap,
                         gt=gt: e.tensor_tensor(
                             x2ap[:, half * 512:(half + 1) * 512],
                             psum[bank][0:128, 0:512],
                             xstage[:, gt % 2, half * 512:(half + 1) * 512],
                             OP.add),
                         reads=(f"psum{bank}", f"xs{gt % 2}"),
                         writes=(f"x2t{xsl}",),
                         aka=x2aka[xsl], tag=f"ev2{gt}_{half}")
                dma("POOL", d_x2[gt * 128:(gt + 1) * 128, :], x2ap[:, 0:1024],
                    (f"x2t{xsl}",), (f"d_x2_{gt}",), f"stx2_{gt}")
                ln_tile(x2ap[:, 0:1024], 128,
                        s_ln2[:, 2 * gt:2 * gt + 1], s_ln2[:, 2 * gt + 1:2 * gt + 2],
                        (f"x2t{xsl}",), (f"ln2_{gt}",), f"L2{gt}",
                        scr_ap=h2ap, scr_wr=(f"h2t{hsl}",), scr_aka=h2aka[hsl])
                p.op("DVE", lambda e, x2ap=x2ap, h2ap=h2ap, gt=gt: e.tensor_scalar(
                    h2ap[:, 0:1024], x2ap[:, 0:1024],
                    s_ln2[:, 2 * gt:2 * gt + 1], s_ln2[:, 2 * gt + 1:2 * gt + 2],
                    OP.mult, OP.add),
                     reads=(f"x2t{xsl}", f"ln2_{gt}"), writes=(f"h2t{hsl}",),
                     aka=h2aka[hsl], tag=f"h2a{gt}")
                p.op("DVE", lambda e, h2ap=h2ap: e.scalar_tensor_tensor(
                    h2ap[:, 0:1024], h2ap[:, 0:1024], 1.0,
                    s_cbexp[:, 0:1024], OP.bypass, OP.mult),
                     reads=(f"h2t{hsl}", "cbexp"), writes=(f"h2t{hsl}",),
                     tag=f"h2b{gt}")
                p.op("DVE", lambda e, h2ap=h2ap: e.tensor_tensor(
                    h2ap[:, 0:1024], h2ap[:, 0:1024],
                    s_cbexp[:, 1024:2048], OP.add),
                     reads=(f"h2t{hsl}", "sbexp"), writes=(f"h2t{hsl}",),
                     tag=f"h2c{gt}")
                dma("POOL", d_h2[gt * 128:(gt + 1) * 128, :], h2ap[:, 0:1024],
                    (f"h2t{hsl}",), (f"d_h2_{gt}",), f"sth2_{gt}")
                p.op("DVE", lambda e, h2ap=h2ap: e.reduce_max(
                    s_gmax[:, 1:2], h2ap[:, 0:1024], AX.X,
                    apply_absolute_value=True),
                     reads=(f"h2t{hsl}",), writes=("gm1",), tag=f"am{gt}")
                p.op("DVE", lambda e: e.tensor_tensor(
                    s_gmax[:, 0:1], s_gmax[:, 0:1], s_gmax[:, 1:2], OP.max),
                     reads=("gm1", "gm0"), writes=("gm0",), tag=f"amx{gt}")

    tap("y0", xhv[:, 0, :], (128, 1024), BF16, reads=("y0",))
    tap("y17", xhv[:, 17, :], (128, 1024), BF16, reads=("y17",))
    tap("st2", s_st2[:, 0:1040], (128, 1040), reads=("st2",))
    tap("agg", s_agg[0:64, :], (64, 1056), reads=("agg",))
    tap("statF", s_statF[:, :], (128, 1056), reads=("statF",))
    tap("psf", s_psfm[0:16, :], (16, 768), reads=("rmps", "psend"))
    tap("x2a", s_x2t[:, :], (128, 2048), reads=("x2t0", "x2t1"))
    tap("gmax", s_gmax[:, 0:1], (128, 1), reads=("gm0",))

    tap("modc", s_modc[:, :], (128, 48), reads=("modc",))
    tap("capl", s_capl[:, :], (128, 24), reads=("silc", "cacol", "sacol"))
    tap("afm", s_afm[:, :], (128, 4096), reads=("afmA",))
    tap("xh0", xhv[:, 0, :], (128, 1024), BF16, reads=("xh0",))
    tap("xh5", xhv[:, 5, :], (128, 1024), BF16, reads=("xh5",))
    tap("xh31", xhv[:, 31, :], (128, 1024), BF16, reads=("xh31",))
    tap("uv", s_uv[0:2, :], (2, E), BF16, reads=("uv",))
    tap("k8", s_k8[0:2, :], (2, 1024), BF16, reads=("k8",))
    tap("xT", s_xT[:, :], (128, 8192), BF16, reads=("xT0", "xT1"))
    tap("carry", s_carry[:, :], (128, 24),
        reads=tuple(f"carry{ec}" for ec in range(8)))

    p.op("PE", lambda e: e.transpose(ps_x[0:1, 0:128], s_gmax[:, 0:1],
                                     s_id_f[:, :]),
         reads=("gm0", "c:id_f"), writes=("ps_x",), tag="Tgm")
    p.op("DVE", lambda e: e.tensor_copy(s_scr2[0:1, 0:128], ps_x[0:1, 0:128]),
         reads=("ps_x",), writes=("gmrow",), aka=("scr2l", "rmrow"), tag="gmc")
    p.op("DVE", lambda e: e.reduce_max(s_mcol[0:1, 8:9], s_scr2[0:1, 0:128],
                                       AX.X),
         reads=("gmrow",), writes=("gmcell",), aka=("mcol8",), tag="gmr")
    p.op("DVE", lambda e: e.memset(s_mcol[0:1, 9:10], 0.0),
         writes=("gmz",), aka=("mcol9",), tag="gmz")
    dma("POOL", d_rin, s_mcol[0:1, 8:10], ("gmcell", "gmz"), ("d_rin",), "gri")
    p.op("POOL", lambda e: e.collective_compute(
        "AllReduce", OP.max, replica_groups=GROUPS8,
        ins=[d_rin.opt()], outs=[d_rout.opt()]),
         reads=("d_rin",), writes=("d_rout",), cc_sem="cc1", tag="ccar1")
    dma("POOL", s_qcol[0:1, 0:1], d_rout[0:1, 0:1], ("d_rout",), ("gmaxg",), "gro")

    warmup(AF.Sigmoid)
    p.op("ACT", lambda e: e.activation(s_qcol[0:1, 4:6], s_mcol[0:1, 0:2],
                                       AF.Sigmoid),
         reads=("c:clipx", "c:clipd", "fence1"), writes=("sigs",), tag="sig")
    p.op("DVE", lambda e: e.tensor_tensor(s_qcol[0:1, 2:3], s_qcol[0:1, 4:5],
                                          s_qcol[0:1, 0:1], OP.mult),
         reads=("sigs", "gmaxg"), writes=("s1c",), tag="s1a")
    p.op("DVE", lambda e: e.tensor_scalar(s_qcol[0:1, 2:3], s_qcol[0:1, 2:3],
                                          1.0 / 127.0, None, OP.mult),
         reads=("s1c",), writes=("s1c",), tag="s1b")
    p.op("DVE", lambda e: e.reciprocal(s_qcol[0:1, 3:4], s_qcol[0:1, 2:3]),
         reads=("s1c",), writes=("inv1",), tag="s1r")
    p.op("DVE", lambda e: e.tensor_scalar(s_qcol[0:1, 6:7] if False else
                                          s_mcol[0:1, 8:9], s_qcol[0:1, 2:3],
                                          1.0 / 127.0, None, OP.mult),
         reads=("s1c",), writes=("s1d",), aka=("gmcell",), tag="s1d")

    def expand_cell(dst_col, src_cell, rd, wr, tagp):
        p.op("PE", lambda e: e.matmul(ps_x[0:128, 0:1], s_onesf[0:1, :],
                                      src_cell, start=True, stop=True),
             reads=tuple(rd) + ("c:onesf",), writes=("ps_x",), tag="xm" + tagp)
        p.op("DVE", lambda e: e.tensor_copy(dst_col, ps_x[0:128, 0:1]),
             reads=("ps_x",), writes=tuple(wr), tag="xc" + tagp)

    expand_cell(sgu[:, 159:160], s_qcol[0:1, 3:4], ("inv1",), ("inv1col",), "i1")
    expand_cell(sgu[:, 156:157], s_mcol[0:1, 8:9], ("s1d",), ("s1dcol",), "s1")
    p.op("DVE", lambda e: e.tensor_tensor(
        sgu[:, 104:126], sgu[:, 0:22],
        sgu[:, 156:157].broadcast_to([128, 22]), OP.mult),
         reads=("sgGU", "s1dcol"), writes=("cgcu",), tag="cg")
    p.op("DVE", lambda e: e.tensor_tensor(
        sgu[:, 126:148], sgu[:, 22:44],
        sgu[:, 156:157].broadcast_to([128, 22]), OP.mult),
         reads=("sgGU", "s1dcol", "cgcu"), writes=("cgcu",), tag="cu")

    warmup(AF.Copy)
    xqv = s_big.ap().rearrange("p (c t) -> p c t", c=8)
    fscv = s_fsc.ap().rearrange("p (r d) -> p r d", r=2)
    wb1v = s_wb1.ap().rearrange("p (r d) -> p r d", r=2)
    for gt in range(NT):
        xsl = gt % 2
        dma("POOL", fscv[:, xsl, :], d_h2[gt * 128:(gt + 1) * 128, :],
            (f"d_h2_{gt}",), (f"fh{xsl}",), f"ldh2_{gt}",
            aka=("xT1", "yT1"))
        p.op("ACT", lambda e, xsl=xsl: e.activation(
            fscv[:, xsl, :], fscv[:, xsl, :], AF.Copy, scale=sgu[:, 159:160]),
             reads=(f"fh{xsl}", "inv1col", "fence1"), writes=(f"fh{xsl}",),
             tag=f"qs{gt}")
        p.op("DVE", lambda e, xsl=xsl: e.tensor_scalar(
            fscv[:, xsl, :], fscv[:, xsl, :], MAGIC, MAGIC, OP.add, OP.subtract),
             reads=(f"fh{xsl}",), writes=(f"fh{xsl}",), tag=f"qr{gt}")
        p.op("DVE", lambda e, xsl=xsl: e.tensor_scalar(
            wb1v[:, xsl, 0:1024], fscv[:, xsl, :], 127.0, -128.0,
            OP.min, OP.max),
             reads=(f"fh{xsl}",), writes=(f"xq{xsl}",), aka=WB1_AKA,
             tag=f"qc{gt}")
        for dc in range(8):
            pe_t128(xqv[:, dc, gt * 128:(gt + 1) * 128],
                    wb1v[:, xsl, dc * 128:(dc + 1) * 128],
                    (f"xq{xsl}",), (f"xqT{dc}_{gt >> 2}",), f"xqT{gt}_{dc}",
                    waka=(f"y{dc * 4 + (gt >> 3)}", f"xh{dc * 4 + (gt >> 3)}"),
                    rot=TP4)

    warmup(AF.Silu)
    p.op("DVE", lambda e: e.memset(s_gmax[:, 2:4], 0.0), writes=("gm2",),
         tag="zg2")
    wqgv = s_wqg.ap().rearrange("p (r c e) -> p r c e", r=2, c=8)
    wquv = s_wqu.ap().rearrange("p (r c e) -> p r c e", r=2, c=8)
    rawslots = [s_wnat.ap()[:, 0:1024], s_wnat.ap()[:, 1024:2048],
                s_x4.ap()[:, 0:1024], s_x4.ap()[:, 1024:2048]]
    raw3 = [a.rearrange("p (c e) -> p c e", c=8) for a in rawslots]
    RAW_AKA = [WNAT_AKA, WNAT_AKA,
               ("x2t0", "x2t1", "h2t0", "h2t1"),
               ("x2t0", "x2t1", "h2t0", "h2t1")]
    dmpv = s_dump.ap().rearrange("p (c e) -> p c e", c=8)

    def wq_load(ic):
        for w_i, dsrc in enumerate((d_wgT, d_wuT2)):
            slot = 2 * (ic % 2) + w_i
            dma("POOL", raw3[slot][:, :, :],
                dsrc[ic * 1024:(ic + 1) * 1024, :]
                .rearrange("(c p) i -> p c i", p=128),
                (), (f"wnat{slot}",), f"lq{w_i}_{ic}", aka=RAW_AKA[slot])

    def wq_q(ic):
        wsl = ic % 2
        for w_i, (ivc, dstv, wres) in enumerate((
                (sgu[:, 60 + ic:61 + ic],
                 s_wqg.ap()[:, wsl * 1024:(wsl + 1) * 1024], f"wqg{wsl}"),
                (sgu[:, 82 + ic:83 + ic],
                 s_wqu.ap()[:, wsl * 1024:(wsl + 1) * 1024], f"wqu{wsl}"))):
            slot = 2 * (ic % 2) + w_i
            eslot = 2 * (ic % 2) + w_i
            expand_row(s_gdq[:, eslot * 128:(eslot + 1) * 128], ivc,
                       ("sgInv",), (f"ivex{eslot}",), f"ie{w_i}_{ic}",
                       waka=("x2t2",))
            p.op("DVE", lambda e, slot=slot, eslot=eslot: e.tensor_tensor(
                dmpv[:, :, :], raw3[slot][:, :, :],
                s_gdq[:, eslot * 128:(eslot + 1) * 128].unsqueeze(1)
                .broadcast_to([128, 8, 128]), OP.mult),
                 reads=(f"wnat{slot}", f"ivex{eslot}"), writes=("dump",),
                 aka=("h2t2",), tag=f"wm{w_i}_{ic}")
            p.op("DVE", lambda e, dstv=dstv: e.tensor_scalar(
                dstv, s_dump.ap(), MAGIC, MAGIC, OP.add, OP.subtract),
                 reads=("dump",), writes=(wres,), tag=f"wr{w_i}_{ic}")

    def mm_block(ic):
        wsl = ic % 2
        for tb in range(8):
            k = ic * 8 + tb
            bG = 0 if k % 2 == 0 else 2
            bU = 1 if k % 2 == 0 else 3
            h3sl = k % 2

            def mmg(e, wsl=wsl, tb=tb, bG=bG):
                inst = None
                for hc in range(8):
                    inst = e.matmul(psum[bG][0:128, 0:512], wqgv[:, wsl, hc, :],
                                    xqv[:, hc, tb * 512:(tb + 1) * 512],
                                    start=(hc == 0), stop=(hc == 7))
                return inst

            def mmu(e, wsl=wsl, tb=tb, bU=bU):
                inst = None
                for hc in range(8):
                    inst = e.matmul(psum[bU][0:128, 0:512], wquv[:, wsl, hc, :],
                                    xqv[:, hc, tb * 512:(tb + 1) * 512],
                                    start=(hc == 0), stop=(hc == 7))
                return inst

            xqres = tuple(f"xqT{hc}_{tb}" for hc in range(8))
            p.op("PE", mmg, reads=(f"wqg{wsl}",) + xqres, writes=(f"psum{bG}",),
                 tag=f"gm{ic}_{tb}")
            p.op("PE", mmu, reads=(f"wqu{wsl}",) + xqres, writes=(f"psum{bU}",),
                 tag=f"um{ic}_{tb}")
            p.op("ACT", lambda e, ic=ic, bG=bG: e.activation(
                s_fsc[:, 0:512], psum[bG][0:128, 0:512], AF.Silu,
                scale=sgu[:, 104 + ic:105 + ic]),
                 reads=(f"psum{bG}", "cgcu", "fence1"), writes=("fsil",),
                 aka=("fh0",), tag=f"sil{ic}_{tb}")
            p.op("ACT", lambda e, ic=ic, bU=bU: e.activation(
                s_fsc[:, 512:1024], psum[bU][0:128, 0:512], AF.Copy,
                scale=sgu[:, 126 + ic:127 + ic]),
                 reads=(f"psum{bU}", "cgcu", "fence1"), writes=("ftup",),
                 aka=("fh0",), tag=f"tup{ic}_{tb}")
            p.op("DVE", lambda e, h3sl=h3sl: e.tensor_tensor(
                s_fsc[:, 1024 + h3sl * 512:1024 + (h3sl + 1) * 512],
                s_fsc[:, 0:512], s_fsc[:, 512:1024], OP.mult),
                 reads=("fsil", "ftup"), writes=(f"fh3{h3sl}",),
                 aka=("fh1",), tag=f"h3{ic}_{tb}")
            p.op("DVE", lambda e, h3sl=h3sl: e.reduce_max(
                s_gmax[:, 3:4],
                s_fsc[:, 1024 + h3sl * 512:1024 + (h3sl + 1) * 512], AX.X,
                apply_absolute_value=True),
                 reads=(f"fh3{h3sl}",), writes=("gm3",), tag=f"hr{ic}_{tb}")
            p.op("DVE", lambda e: e.tensor_tensor(
                s_gmax[:, 2:3], s_gmax[:, 2:3], s_gmax[:, 3:4], OP.max),
                 reads=("gm3", "gm2"), writes=("gm2",), tag=f"hx{ic}_{tb}")
            dma("POOL", d_h3[ic * 128:(ic + 1) * 128, tb * 512:(tb + 1) * 512],
                s_fsc[:, 1024 + h3sl * 512:1024 + (h3sl + 1) * 512],
                (f"fh3{h3sl}",), (f"d_h3_{ic}_{tb}",), f"sh3_{ic}_{tb}")

    for ic in range(NIC + 2):
        if ic < NIC:
            wq_load(ic)
        if 1 <= ic < NIC + 1:
            wq_q(ic - 1)
        if ic >= 2:
            mm_block(ic - 2)

    p.op("PE", lambda e: e.transpose(ps_x[0:1, 0:128], s_gmax[:, 2:3],
                                     s_id_f[:, :]),
         reads=("gm2", "c:id_f"), writes=("ps_x",), tag="Tg2")
    p.op("DVE", lambda e: e.tensor_copy(s_scr2[0:1, 0:128], ps_x[0:1, 0:128]),
         reads=("ps_x",), writes=("gmrow",), tag="g2c")
    p.op("DVE", lambda e: e.reduce_max(s_mcol[0:1, 6:7], s_scr2[0:1, 0:128],
                                       AX.X),
         reads=("gmrow",), writes=("gm2cell",), tag="g2r")
    p.op("DVE", lambda e: e.memset(s_mcol[0:1, 7:8], 0.0),
         writes=("gm2z",), tag="g2z")
    dma("POOL", d_rin2, s_mcol[0:1, 6:8], ("gm2cell", "gm2z"), ("d_rin2",),
        "g2i")
    p.op("POOL", lambda e: e.collective_compute(
        "AllReduce", OP.max, replica_groups=GROUPS8,
        ins=[d_rin2.opt()], outs=[d_rout2.opt()]),
         reads=("d_rin2",), writes=("d_rout2",), cc_sem="cc2", tag="ccar2")
    dma("POOL", s_qcol[0:1, 1:2], d_rout2[0:1, 0:1], ("d_rout2",), ("gmax2g",),
        "g2o")

    for j in range(8):
        expand_row(s_dexp[:, j * 128:(j + 1) * 128], sgu[:, 52 + j:53 + j],
                   ("sgInv",), ("dexp",), f"dx{j}", waka=("cbexp", "sbexp"))
        expand_row(s_gbexp[:, j * 128:(j + 1) * 128], s_modc[:, 40 + j:41 + j],
                   ("modc",), ("gbexp",), f"gb{j}", waka=("gaexp",))
    wdqv = s_big.ap()[:, 0:NIC * 1024].rearrange("p (c e) -> p c e", c=NIC)
    for ic in range(NIC):
        slot = nslot[0] % 2
        nslot[0] += 1
        dma("POOL", s_wnat[:, slot * 1024:(slot + 1) * 1024],
            d_wdT[ic * 128:(ic + 1) * 128, :], (), (f"wnat{slot}",),
            f"lwd{ic}", aka=WNAT_AKA)
        p.op("DVE", lambda e, slot=slot: e.tensor_tensor(
            s_dump.ap(), s_wnat[:, slot * 1024:(slot + 1) * 1024],
            s_dexp.ap(), OP.mult),
             reads=(f"wnat{slot}", "dexp"), writes=("dump",),
             aka=("h2t2",), tag=f"wdm{ic}")
        p.op("DVE", lambda e, ic=ic: e.tensor_scalar(
            wdqv[:, ic, :], s_dump.ap(), MAGIC, MAGIC, OP.add, OP.subtract),
             reads=("dump",), writes=(f"wdq{ic}",),
             aka=tuple(f"xqT{ic // 4}_{sb}" for sb in range(8)),
             tag=f"wdr{ic}")

    p.op("DVE", lambda e: e.tensor_tensor(s_qcol[0:1, 2:3], s_qcol[0:1, 5:6],
                                          s_qcol[0:1, 1:2], OP.mult),
         reads=("sigs", "gmax2g", "s1c", "inv1"), writes=("s2c",), tag="s2a")
    p.op("DVE", lambda e: e.tensor_scalar(s_qcol[0:1, 2:3], s_qcol[0:1, 2:3],
                                          1.0 / 127.0, None, OP.mult),
         reads=("s2c",), writes=("s2c",), tag="s2b")
    p.op("DVE", lambda e: e.reciprocal(s_qcol[0:1, 3:4], s_qcol[0:1, 2:3]),
         reads=("s2c", "inv1col"), writes=("inv2",), tag="s2r")
    expand_cell(sgu[:, 158:159], s_qcol[0:1, 3:4], ("inv2",), ("inv2col",), "i2")
    expand_cell(sgu[:, 157:158], s_qcol[0:1, 2:3], ("s2c",), ("s2col",), "s2")
    for j in range(8):
        expand_row(s_gdq[:, j * 128:(j + 1) * 128], sgu[:, 148 + j:149 + j],
                   ("sgDdq",), ("gdq",), f"gq{j}",
                   waka=("ivex0", "ivex1", "ivex2", "ivex3"))
    p.op("DVE", lambda e: e.tensor_tensor(s_gdq.ap(), s_gdq.ap(),
                                          s_gbexp.ap(), OP.mult),
         reads=("gdq", "gbexp", "ivex0", "ivex1", "ivex2", "ivex3"),
         writes=("gdq",), tag="gdm")
    p.op("DVE", lambda e: e.tensor_scalar(s_gdq.ap(), s_gdq.ap(),
                                          sgu[:, 157:158], None, OP.mult),
         reads=("gdq", "s2col"), writes=("gdq",), tag="gds")

    fscv2 = s_fsc.ap().rearrange("p (s e) -> p s e", s=4)
    h3q2 = s_wqg.ap().rearrange("p (s e) -> p s e", s=4)
    x4v = s_x4.ap().rearrange("p (s e) -> p s e", s=4)
    banks = [psum[0], psum[1], psum[2], psum[3], psum[4], ps_x, ps_trA, ps_trB]
    bres = ["psum0", "psum1", "psum2", "psum3", "psum4", "ps_x", "ps_tr0",
            "ps_tr1"]
    otbuf = [s_ytile.ap(), s_dump.ap()[:, 0:1024]]
    otaka = [("cs0", "cs1", "ytile", "dstmp", "st2"),
             ("dump", "psfm", "ostmp", "st2")]
    for g in range(8):
        for t4 in range(4):
            t = g * 4 + t4
            dma("POOL", x4v[:, t4, :], d_x2[t * 128:(t + 1) * 128, :],
                (f"d_x2_{t}",), (f"xs{t4}",), f"lx2_{t}",
                aka=("x2t0", "x2t1", "h2t0", "h2t1", "wnat2", "wnat3"))
        for ic in range(NIC):
            fsl = ic % 4
            qsl = ic % 4
            fres = f"f6_{fsl}"
            qres = f"h3w{qsl}"
            dma("SP", fscv2[:, fsl, :],
                d_h3[ic * 128:(ic + 1) * 128, g * 512:(g + 1) * 512],
                (f"d_h3_{ic}_{g}",), (fres,),
                f"lh3_{g}_{ic}",
                aka=("fsil", "ftup", "fh30", "fh31", "fh0", "fh1"))
            p.op("ACT", lambda e, fsl=fsl: e.activation(
                fscv2[:, fsl, :], fscv2[:, fsl, :], AF.Copy,
                scale=sgu[:, 158:159]),
                 reads=(fres, "inv2col", "fence1"), writes=(fres,),
                 tag=f"q2s{g}_{ic}")
            p.op("DVE", lambda e, fsl=fsl: e.tensor_scalar(
                fscv2[:, fsl, :], fscv2[:, fsl, :], 127.0, -128.0,
                OP.min, OP.max),
                 reads=(fres,), writes=(fres,), tag=f"q2c{g}_{ic}")
            p.op("DVE", lambda e, fsl=fsl, qsl=qsl: e.tensor_scalar(
                h3q2[:, qsl, :], fscv2[:, fsl, :], MAGIC, MAGIC,
                OP.add, OP.subtract),
                 reads=(fres,), writes=(qres,),
                 aka=("wqg0", "wqg1"), tag=f"q2r{g}_{ic}")
            for t4 in range(4):
                for half in range(2):
                    bi = t4 * 2 + half
                    p.op("PE", lambda e, qsl=qsl, ic=ic, t4=t4, half=half,
                         bi=bi: e.matmul(
                             banks[bi][0:128, 0:512],
                             h3q2[:, qsl, t4 * 128:(t4 + 1) * 128],
                             wdqv[:, ic, half * 512:(half + 1) * 512],
                             start=(ic == 0), stop=(ic == NIC - 1)),
                         reads=(qres, f"wdq{ic}"), writes=(bres[bi],),
                         tag=f"dm{g}_{ic}_{bi}")
        for t4 in range(4):
            t = g * 4 + t4
            osl = t4 % 2
            for half in range(2):
                bi = t4 * 2 + half
                p.op("DVE", lambda e, bi=bi, half=half, osl=osl:
                     e.scalar_tensor_tensor(
                         otbuf[osl][:, half * 512:(half + 1) * 512],
                         banks[bi][0:128, 0:512], 1.0,
                         s_gdq[:, half * 512:(half + 1) * 512],
                         OP.bypass, OP.mult),
                     reads=(bres[bi], "gdq"), writes=(f"ot{osl}",),
                     aka=otaka[osl], tag=f"oe{t}_{half}")
                p.op("DVE", lambda e, half=half, t4=t4, osl=osl:
                     e.tensor_tensor(
                         otbuf[osl][:, half * 512:(half + 1) * 512],
                         otbuf[osl][:, half * 512:(half + 1) * 512],
                         x4v[:, t4, half * 512:(half + 1) * 512], OP.add),
                     reads=(f"ot{osl}", f"xs{t4}"), writes=(f"ot{osl}",),
                     tag=f"or{t}_{half}")
            dma("POOL", d_out[t * 128:(t + 1) * 128, :], otbuf[osl][:, 0:1024],
                (f"ot{osl}",), (f"ow{t % 4}",), f"sto{t}")

    tap("sgu", sgu[:, :], (128, 160), reads=("sgGU", "sgD", "sgInv", "sgDdq",
                                             "cgcu", "inv1col"))
    tap("qc", s_qcol[0:1, 0:6], (1, 6), reads=("s1c", "inv1", "sigs", "gmaxg",
                                               "s2c", "inv2", "gmax2g"))
    tap("xqT6", xqv[:, 6, 0:1024], (128, 1024), BF16,
        reads=("xqT6_0", "xqT6_1"))
    tap("gm2t", s_gmax[:, 2:3], (128, 1), reads=("gm2",))

    p.op("ACT", p.echo["ACT"], tag="echoF")
    p.barrier_all_on("POOL")
    p.barrier_all_on("SP")

    import contextlib
    stack = contextlib.ExitStack()
    sem_handles = {}
    sems = ["PE", "ACT", "DVE", "POOL", "cc0", "cc1", "cc2"]
    for q in QUEUES.values():
        sems += [f"{q}s{i}" for i in range(Plan.RSLOT)]
    for s in sems:
        sem_handles[s] = stack.enter_context(nc.semaphore("sem_" + s))
    blk = stack.enter_context(nc.Block())

    def run_engine(name):
        def body(h):
            for o in p.eng_ops[name]:
                for sp, val in o["waits"]:
                    h.wait_ge(sem_handles[sp], val)
                if o["emit"] is None:
                    continue
                inst = o["emit"](h)
                if o["cc"] is not None:
                    inst.then_inc(sem_handles[o["cc"]], 1)
                elif o["dma"]:
                    inst.then_inc(sem_handles[o["sem"]], 16)
                elif o.get("stamp", True):
                    inst.then_inc(sem_handles[name], 1)
        return body

    blk.tensor(run_engine("PE"))
    blk.scalar(run_engine("ACT"))
    blk.vector(run_engine("DVE"))
    blk.gpsimd(run_engine("POOL"))
    blk.sync(run_engine("SP"))
    stack.close()
    n_ops = sum(len(v) for v in p.eng_ops.values())
    return nc, tap_d, n_ops


def host_inputs(inputs):
    X = np.ascontiguousarray(inputs["L0_attn_x"], dtype=np.float32)
    winT = np.ascontiguousarray(np.asarray(inputs["L0_attn_in_proj_w"]).T, dtype=np.float32)
    convw = np.ascontiguousarray(
        np.asarray(inputs["L0_attn_conv_w"])[:, 0, :].reshape(8, 128, 4)
        .transpose(1, 0, 2).reshape(128, 32), dtype=np.float32)
    convb = np.ascontiguousarray(np.asarray(inputs["L0_attn_conv_b"]).reshape(8, 128).T,
                                 dtype=np.float32)
    wbase = np.ascontiguousarray(np.asarray(inputs["L0_attn_w_base"]).reshape(16, 1),
                                 dtype=np.float32)
    woT = np.ascontiguousarray(np.asarray(inputs["L0_attn_out_proj_w"]).T, dtype=np.float32)
    args = dict(
        wmod=np.ascontiguousarray(inputs["L0_wmod"], dtype=np.float32),
        winT=winT, convw=convw, convb=convb, wbase=wbase, woT=woT,
        wg=np.ascontiguousarray(inputs["L0_ffn_w_gate"], dtype=np.float32),
        wu=np.ascontiguousarray(inputs["L0_ffn_w_up"], dtype=np.float32),
        wd=np.ascontiguousarray(inputs["L0_ffn_w_down"], dtype=np.float32),
        wgTb=np.ascontiguousarray(
            np.asarray(inputs["L0_ffn_w_gate"], dtype=np.float32).T
            .reshape(1024, 22, 128).transpose(1, 0, 2).reshape(22 * 1024, 128)),
        wuTb=np.ascontiguousarray(
            np.asarray(inputs["L0_ffn_w_up"], dtype=np.float32).T
            .reshape(1024, 22, 128).transpose(1, 0, 2).reshape(22 * 1024, 128)),
        wdT=np.ascontiguousarray(np.asarray(inputs["L0_ffn_w_down"],
                                            dtype=np.float32).T),
        clipx=np.ascontiguousarray(np.asarray(inputs["L0_ffn_clip_x"]).reshape(1, 1),
                                   dtype=np.float32),
        clipd=np.ascontiguousarray(np.asarray(inputs["L0_ffn_clip_down"]).reshape(1, 1),
                                   dtype=np.float32),
    )
    Lc = np.ascontiguousarray(inputs["L0_c"], dtype=np.float32)
    zeros3 = np.zeros((3, 1024), dtype=np.float32)
    in_maps = []
    for c in range(8):
        b0, half = c >> 1, c & 1
        in_maps.append(dict(
            x4096=np.ascontiguousarray(X[b0, half * T:(half + 1) * T]),
            xhalo=(np.ascontiguousarray(X[b0, half * T - 3:half * T]) if half else zeros3),
            mask3=np.full((3, 1), 1.0 if half else 0.0, dtype=np.float32),
            crow=np.ascontiguousarray(Lc[b0:b0 + 1]),
            halfsel=np.ascontiguousarray(
                np.stack([np.full(128, 1.0 - half, np.float32),
                          np.full(128, float(half), np.float32)], axis=1)),
            **args))
    return in_maps


def kernel(**inputs):
    nc, _, _ = build_program(taps=DEBUG_TAPS)
    in_maps = host_inputs(inputs)
    res = run_bass_kernel_spmd(nc, in_maps, core_ids=list(range(8)))
    out = np.empty((4, 8192, 1024), dtype=np.float32)
    for c in range(8):
        b0, half = c >> 1, c & 1
        out[b0, half * T:(half + 1) * T] = np.asarray(res.results[c]["out"])
    return out


# version 11 (final)
# speedup vs baseline: 1.1194x; 1.0886x over previous
import numpy as np
import ml_dtypes
import concourse.bass as bass
import concourse.mybir as mybir
from concourse.bass_utils import run_bass_kernel_spmd

F32 = mybir.dt.float32
BF16 = mybir.dt.bfloat16
I32 = mybir.dt.int32
AF = mybir.ActivationFunctionType
OP = mybir.AluOpType
AX = mybir.AxisListType
BF = ml_dtypes.bfloat16

T = 4096
NT = 32
NSB = 8
E = 1040
NH = 16
IF = 2816
NIC = 22

DEBUG_TAPS = []
STAGE = 99
GROUPS8 = [[0, 1, 2, 3, 4, 5, 6, 7]]
GROUPS2 = [[0, 1], [2, 3], [4, 5], [6, 7]]

ENGINES = ("PE", "ACT", "DVE", "POOL", "SP")
QUEUES = {"SP": "dmaSP", "ACT": "dmaACT", "POOL": "dmaPOOL"}


class Plan:
    LAGGED = ()
    RSLOT = 8

    def __init__(self):
        self.eng_ops = {e: [] for e in ENGINES}
        self.seq = {e: 0 for e in ENGINES}
        self.dma_cnt = {q: 0 for q in QUEUES.values()}
        self.slot_use = {}
        self.res = {}
        self.know = {e: {} for e in ENGINES}
        self.snap = {}
        self.echo = {}

    def _resource(self, r):
        return self.res.setdefault(r, {"w": None, "r": []})

    def _need(self, eng, ev):
        sp, val = ev
        return self.know[eng].get(sp, 0) < val

    def _absorb(self, eng, ev):
        k = self.know[eng]
        sp, val = ev
        if k.get(sp, 0) < val:
            k[sp] = val
            for sp2, v2 in self.snap.get(ev, {}).items():
                if k.get(sp2, 0) < v2:
                    k[sp2] = v2

    def op(self, eng, emit, reads=(), writes=(), aka=(), dma=False, cc_sem=None, tag="", stamp=True):
        waits = []
        for r in reads:
            st = self._resource(r)
            if st["w"] is not None and self._need(eng, st["w"]):
                waits.append(st["w"])
        for w in tuple(writes) + tuple(aka):
            st = self._resource(w)
            if st["w"] is not None and self._need(eng, st["w"]):
                waits.append(st["w"])
            for ev in st["r"]:
                if self._need(eng, ev):
                    waits.append(ev)
        wd = {}
        for sp, val in waits:
            wd[sp] = max(wd.get(sp, 0), val)
        sem_name = None
        if dma and cc_sem is None:
            q = QUEUES[eng]
            slot = self.dma_cnt[q] % self.RSLOT
            self.dma_cnt[q] += 1
            sem_name = f"{q}s{slot}"
            prior = self.slot_use.get(sem_name, 0)
            if prior > 0 and self.know[eng].get(sem_name, 0) < 16 * prior:
                wd[sem_name] = max(wd.get(sem_name, 0), 16 * prior)
            self.slot_use[sem_name] = prior + 1
        for sp in list(wd):
            if sp in self.LAGGED and sp != eng:
                val = wd[sp]
                if val == self.seq[sp]:
                    self.op(sp, self.echo[sp], tag="echo")
                wd[sp] = val + 1
        for spval in wd.items():
            self._absorb(eng, spval)
        if cc_sem is not None:
            ev = (cc_sem, 1)
        elif dma:
            ev = (sem_name, 16 * self.slot_use[sem_name])
        elif stamp:
            self.seq[eng] += 1
            ev = (eng, self.seq[eng])
        else:
            ev = (eng, self.seq[eng])
        self.snap.setdefault(ev, dict(self.know[eng]))
        for r in reads:
            self._resource(r)["r"].append(ev)
        for w in writes:
            st = self._resource(w)
            st["w"] = ev
            st["r"] = []
        self.eng_ops[eng].append({"emit": emit, "waits": sorted(wd.items()),
                                  "dma": dma, "cc": cc_sem, "tag": tag,
                                  "stamp": stamp, "sem": sem_name})
        return ev

    def barrier_all_on(self, eng):
        waits = {}
        for e2 in ENGINES:
            if e2 != eng and self.seq[e2] > 0:
                waits[e2] = self.seq[e2]
        for snm, n in self.slot_use.items():
            if n > 0:
                waits[snm] = 16 * n
        final = []
        for sp, val in sorted(waits.items()):
            if self.know[eng].get(sp, 0) < val:
                final.append((sp, val))
                self._absorb(eng, (sp, val))
        self.eng_ops[eng].append({"emit": None, "waits": final, "dma": False,
                                  "cc": None, "tag": "finalwait"})


class Alloc:
    def __init__(self, nc, limit):
        self.nc, self.limit, self.cur = nc, limit, 0
        self.table = {}

    def at(self, name, rows, cols, dtype, offset=None):
        nbytes = cols * {F32: 4, BF16: 2, I32: 4}[dtype]
        if offset is None:
            offset = self.cur
            self.cur += (nbytes + 31) & ~31
            assert self.cur <= self.limit, f"SBUF over budget at {name}: {self.cur}"
        else:
            assert offset + nbytes <= self.limit
        self.table[name] = (offset, nbytes)
        return self.nc.alloc_sbuf_tensor_at(name, [rows, cols], dtype, offset=offset)

    def off(self, name):
        return self.table[name][0]


def build_program(taps=()):
    nc = bass.Bass("TRN2", target_bir_lowering=False)
    p = Plan()
    taps = set(taps)
    base = nc.SBUF_PARTITION_SIZE_BYTES - nc.sbuf_bytes_remaining
    base = (base + 31) & ~31
    al = Alloc(nc, nc.SBUF_PARTITION_SIZE_BYTES - 512)
    al.cur = base

    d_x = nc.dram_tensor("x4096", [T, 1024], F32, kind="ExternalInput").ap()
    d_xh3 = nc.dram_tensor("xhalo", [3, 1024], F32, kind="ExternalInput").ap()
    d_mask3 = nc.dram_tensor("mask3", [3, 1], F32, kind="ExternalInput").ap()
    d_crow = nc.dram_tensor("crow", [1, 1024], F32, kind="ExternalInput").ap()
    d_wmod = nc.dram_tensor("wmod", [1024, 6144], F32, kind="ExternalInput").ap()
    d_winT = nc.dram_tensor("winT", [1024, E], F32, kind="ExternalInput").ap()
    d_convw = nc.dram_tensor("convw", [128, 32], F32, kind="ExternalInput").ap()
    d_convb = nc.dram_tensor("convb", [128, 8], F32, kind="ExternalInput").ap()
    d_wbase = nc.dram_tensor("wbase", [16, 1], F32, kind="ExternalInput").ap()
    d_woT = nc.dram_tensor("woT", [1024, 1024], F32, kind="ExternalInput").ap()
    d_wg = nc.dram_tensor("wg", [IF, 1024], F32, kind="ExternalInput").ap()
    d_wu = nc.dram_tensor("wu", [IF, 1024], F32, kind="ExternalInput").ap()
    d_wd = nc.dram_tensor("wd", [1024, IF], F32, kind="ExternalInput").ap()
    d_clipx = nc.dram_tensor("clipx", [1, 1], F32, kind="ExternalInput").ap()
    d_clipd = nc.dram_tensor("clipd", [1, 1], F32, kind="ExternalInput").ap()
    d_hsel = nc.dram_tensor("halfsel", [128, 2], F32, kind="ExternalInput").ap()
    d_wgT = nc.dram_tensor("wgTb", [NIC * 1024, 128], F32, kind="ExternalInput").ap()
    d_wuT = nc.dram_tensor("wuTb", [NIC * 1024, 128], F32, kind="ExternalInput").ap()
    d_wuT2 = d_wuT
    d_wdT = nc.dram_tensor("wdT", [IF, 1024], F32, kind="ExternalInput").ap()
    d_out = nc.dram_tensor("out", [T, 1024], F32, kind="ExternalOutput").ap()

    d_x2 = nc.dram_tensor("x2scr", [T, 1024], F32, kind="Internal").ap()
    d_h2 = nc.dram_tensor("h2scr", [T, 1024], F32, kind="Internal").ap()
    d_h3 = nc.dram_tensor("h3scr", [NIC * 128, T], F32, kind="Internal").ap()
    d_gin = nc.dram_tensor("ccgin", [64, 1056], F32, kind="Internal").ap()
    d_gout = nc.dram_tensor("ccgout", [128, 1056], F32, kind="Internal").ap()
    d_rin = nc.dram_tensor("ccrin", [1, 2], F32, kind="Internal").ap()
    d_rout = nc.dram_tensor("ccrout", [1, 2], F32, kind="Internal").ap()
    d_rin2 = nc.dram_tensor("ccrin2", [1, 2], F32, kind="Internal").ap()
    d_rout2 = nc.dram_tensor("ccrout2", [1, 2], F32, kind="Internal").ap()

    idn_f = nc.inline_tensor(np.eye(128, dtype=np.float32), name="c_idf").ap()
    idn_b = nc.inline_tensor(np.eye(128, dtype=BF), name="c_idb").ap()
    ut = np.zeros((128, 128), dtype=BF)
    for b_ in range(2):
        for kk in range(64):
            ut[b_ * 64 + kk, b_ * 64 + kk:(b_ + 1) * 64] = 1
    c_utbd = nc.inline_tensor(ut, name="c_utbd").ap()
    bcl = np.zeros((64, T), dtype=BF)
    for z in range(64):
        bcl[z, z * 64:(z + 1) * 64] = 1
    c_bcl = nc.inline_tensor(bcl, name="c_bcl").ap()
    c_lts = nc.inline_tensor(
        np.ascontiguousarray(np.tri(128, 128, -1).T.astype(np.float32)),
        name="c_lts").ap()
    c_onesb = nc.inline_tensor(np.ones((1, 128), dtype=BF), name="c_onesb").ap()
    c_onesf = nc.inline_tensor(np.ones((1, 128), dtype=np.float32), name="c_onesf").ap()

    s_big = al.at("s_big", 128, NT * 1024, BF16)
    s_afm = al.at("s_afm", 128, 4096, F32)
    s_afm2 = al.at("s_afm2", 128, 4096, BF16)
    s_tok16 = al.at("s_tok16", 128, 4 * NT * 16, BF16)
    s_id_f = al.at("s_id_f", 128, 128, F32)
    s_id_b = al.at("s_id_b", 128, 128, BF16)
    s_utbd = al.at("s_utbd", 128, 128, BF16)
    s_bcl = al.at("s_bcl", 64, T, BF16)
    s_lts = al.at("s_lts", 128, 128, F32)
    s_onesb = al.at("s_onesb", 1, 128, BF16)
    s_onesf = al.at("s_onesf", 1, 128, F32)
    s_cw = al.at("s_cw", 128, 32, F32)
    s_cb = al.at("s_cb", 128, 8, F32)
    s_wb = al.at("s_wb", 16, 1, F32)
    s_modc = al.at("s_modc", 128, 48, F32)
    s_mcol = al.at("s_mcol", 128, 10, F32)
    s_stats = al.at("s_stats", 128, 16, F32)
    s_ln2 = al.at("s_ln2", 128, 2 * NT, F32)
    s_carry = al.at("s_carry", 128, 24, F32)
    s_gmax = al.at("s_gmax", 128, 4, F32)
    s_qcol = al.at("s_qcol", 128, 8, F32)
    s_capl = al.at("s_capl", 128, 24, F32)
    s_uvw = al.at("s_uvw", 128, 16, BF16)
    s_uv = al.at("s_uv", 2, E, BF16)
    s_k8 = al.at("s_k8", 2, 1024, BF16)
    s_k8h = al.at("s_k8h", 2, 4, BF16)
    s_xTh = al.at("s_xTh", 128, 24, BF16)
    s_hb3 = al.at("s_hb3", 3, 1024, BF16)
    s_atot = al.at("s_atot", 128, 128, F32)
    s_row = al.at("s_row", 1, 1024, F32)
    s_scr = al.at("s_scr", 128, 1056, F32)
    s_scr2 = al.at("s_scr2", 128, 1056, F32)
    s_agg = al.at("s_agg", 64, 1056, F32)
    s_aggsl = al.at("s_aggsl", 2, 1056, F32, offset=al.off("s_scr2"))
    s_st2l = al.at("s_st2l", 64, 1056, BF16)
    s_sgu = al.at("s_sgu", 128, 160, F32)
    s_i2T = al.at("s_i2T", 2, 128, BF16)
    s_wb1 = al.at("s_wb1", 128, 2048, BF16, offset=al.off("s_scr"))
    s_psc = al.at("s_psc", 128, 16, F32)
    s_den = al.at("s_den", 128, 64, F32)
    s_hsel = al.at("s_hsel", 128, 2, F32)
    s_ind = al.at("s_ind", 128, 2, BF16)
    s_xstage = al.at("s_xstage", 128, 2048, F32)
    s_xT = al.at("s_xT", 128, 2 * 8 * 512, BF16)
    s_hc = al.at("s_hc", 128, 2 * 4096, BF16)
    s_win = al.at("s_win", 128, 8 * E, BF16)
    s_wmodc = al.at("s_wmodc", 128, 2048, F32)
    s_statF = al.at("s_statF", 128, 1056, F32, offset=al.off("s_wmodc") + 0)
    s_xiext = al.at("s_xiext", 128, 2 * 515 + 26, F32)
    s_convscr = al.at("s_convscr", 128, 2 * 512, F32)
    s_st2 = al.at("s_st2", 128, 1056, F32, offset=al.off("s_convscr"))
    s_ytile = al.at("s_ytile", 128, 1024, F32, offset=al.off("s_convscr"))
    s_dump = al.at("s_dump", 128, 1024, F32)
    s_i32 = al.at("s_i32", 128, 1024, I32, offset=al.off("s_dump"))
    s_psfm = al.at("s_psfm", 16, 768, F32, offset=al.off("s_dump"))
    s_dch = al.at("s_dch", 128, 128, F32, offset=al.off("s_dump") + 3072)
    s_rb = al.at("s_rb", 128, 128, F32, offset=al.off("s_dump") + 3584)
    s_x2t = al.at("s_x2t", 128, 2 * 1024, F32, offset=al.off("s_hc"))
    s_h2t = al.at("s_h2t", 128, 2 * 1024, F32, offset=al.off("s_hc") + 8192)
    s_cbexp = al.at("s_cbexp", 128, 2 * 1024, F32, offset=al.off("s_afm"))
    s_gaexp = al.at("s_gaexp", 128, 1024, F32, offset=al.off("s_afm") + 8192)
    s_gdq = al.at("s_gdq", 128, 1024, F32, offset=al.off("s_afm") + 12288)
    s_wqg = al.at("s_wqg", 128, 2048, BF16, offset=al.off("s_xT"))
    s_wqu = al.at("s_wqu", 128, 2048, BF16, offset=al.off("s_xT") + 4096)
    s_fsc = al.at("s_fsc", 128, 2048, F32, offset=al.off("s_xT") + 8192)
    s_wnat = al.at("s_wnat", 128, 2048, F32, offset=al.off("s_wmodc") + 4224)
    s_h3q = al.at("s_h3q", 128, NIC * 256, BF16, offset=al.off("s_hc"))
    s_gbexp = al.at("s_gbexp", 128, 1024, F32, offset=al.off("s_afm") + 8192)
    s_dexp = al.at("s_dexp", 128, 1024, F32, offset=al.off("s_afm"))
    s_x4 = al.at("s_x4", 128, 4096, F32, offset=al.off("s_hc"))

    SB_USED = al.cur
    print(f"[k] sbuf {SB_USED} bytes/partition")
    p.echo["ACT"] = lambda e: e.copy(s_qcol[0:1, 7:8], s_qcol[0:1, 6:7])

    def warmup(func):
        p.op("ACT", lambda e, f=func: e.activation(s_qcol[0:1, 7:8],
                                                   s_qcol[0:1, 6:7], f),
             reads=("fence0",), writes=("fence1",), tag=f"warm{func}")

    psum = [nc.alloc_psum_tensor(f"psA{i}", [128, 512], F32) for i in range(5)]
    ps_trA = nc.alloc_psum_tensor("ps_trA", [128, 512], F32)
    ps_trB = nc.alloc_psum_tensor("ps_trB", [128, 512], F32)
    ps_x = nc.alloc_psum_tensor("ps_x", [128, 512], F32)
    ps_tr = ps_trA

    def dma(eng, out, in_, reads, writes, tag="", transpose=False, aka=()):
        return p.op(eng, (lambda e, o=out, i=in_, tr=transpose:
                          e.dma_start(out=o, in_=i, transpose=tr)),
                    reads=reads, writes=writes, aka=aka, dma=True, tag=tag)

    def trans_f32(dst_ap, src_ap, rd, wr, tag, waka=()):
        r, c = src_ap.shape
        pa = ps_x[0:c, 0:r]
        p.op("PE", lambda e: e.transpose(pa, src_ap, s_id_f[0:r, 0:r]),
             reads=tuple(rd) + ("c:id_f",), writes=("ps_x",), tag="T" + tag)
        p.op("DVE", lambda e: e.tensor_copy(dst_ap, pa),
             reads=("ps_x",), writes=tuple(wr), aka=waka, tag="Tc" + tag)

    def trans_b16(dst_ap, src_ap, rd, wr, tag, waka=()):
        r, c = src_ap.shape
        pa = ps_trA.ap().bitcast(BF16)[0:c, 0:r]
        p.op("PE", lambda e: e.transpose(pa, src_ap, s_id_b[0:r, 0:r]),
             reads=tuple(rd) + ("c:id_b",), writes=("ps_tr0",), tag="T" + tag)
        p.op("DVE", lambda e: e.tensor_copy(dst_ap, pa),
             reads=("ps_tr0",), writes=tuple(wr), aka=waka, tag="Tc" + tag)

    tp_flip = [0]

    TP2 = [0, ((ps_trA, "ps_tr0"), (ps_trB, "ps_tr1"))]

    def pe_t128(dst_ap, src_ap, rd, wr, tag, waka=(), eveng="DVE", rot=None):
        if rot is None:
            rot = TP2
        bank, res = rot[1][rot[0] % len(rot[1])]
        rot[0] += 1
        pa = bank.ap().bitcast(BF16)[0:128, 0:128]
        p.op("PE", lambda e: e.transpose(pa, src_ap, s_id_b[:, :]),
             reads=tuple(rd) + ("c:id_b",), writes=(res,), tag="T" + tag)
        if eveng == "ACT":
            p.op("ACT", lambda e: e.activation(dst_ap, pa, AF.Copy),
                 reads=(res, "fence1"), writes=tuple(wr), aka=waka,
                 tag="Tc" + tag)
        else:
            p.op("DVE", lambda e: e.tensor_copy(dst_ap, pa),
                 reads=(res,), writes=tuple(wr), aka=waka, tag="Tc" + tag)

    tap_d = {}

    def tap(name, src_ap, shape, dtype=F32, reads=()):
        if name not in taps:
            return
        dt_ap = nc.dram_tensor("tap_" + name, list(shape), dtype, kind="ExternalOutput").ap()
        tap_d[name] = True
        dma("SP", dt_ap, src_ap, reads, (f"tap_{name}",), "tap" + name)

    for dst, src, res in [
        (s_id_f[:, :], idn_f, "c:id_f"), (s_id_b[:, :], idn_b, "c:id_b"),
        (s_utbd[:, :], c_utbd, "c:utbd"), (s_bcl[:, :], c_bcl, "c:bcl"),
        (s_lts[:, :], c_lts, "c:lts"), (s_onesb[:, :], c_onesb, "c:onesb"),
        (s_onesf[:, :], c_onesf, "c:onesf"), (s_cw[:, :], d_convw, "c:cw"),
        (s_cb[:, :], d_convb, "c:cb"), (s_wb[:, :], d_wbase, "c:wb"),
        (s_row[0:1, :], d_crow, "crow"),
        (s_mcol[0:1, 0:1], d_clipx, "c:clipx"), (s_mcol[0:1, 1:2], d_clipd, "c:clipd"),
    ]:
        dma("SP", dst, src, (), (res,), "ld")
    dma("POOL", s_win.ap().rearrange("p (c e) -> p c e", c=8),
        d_winT.rearrange("(c p) e -> p c e", p=128), (), ("win",), "ldwin")
    winv = s_win.ap().rearrange("p (c e) -> p c e", c=8)
    i2t = np.zeros((2, 128), dtype=BF)
    i2t[0, :64] = 1; i2t[1, 64:] = 1
    c_i2t = nc.inline_tensor(i2t, name="c_i2t").ap()
    dma("SP", s_i2T[:, :], c_i2t, (), ("c:i2t",), "ldi2t")
    ind2a = np.zeros((128, 2), dtype=BF)
    ind2a[:64, 0] = 1; ind2a[64:, 1] = 1
    c_ind = nc.inline_tensor(ind2a, name="c_ind").ap()
    dma("SP", s_ind[:, :], c_ind, (), ("c:ind",), "ldind")
    dma("SP", s_hsel[:, :], d_hsel, (), ("c:hsel",), "ldhsel")
    p.op("DVE", lambda e: e.memset(s_gmax[:, 0:2], 0.0), writes=("gm0",), tag="zgm")

    p.op("DVE", lambda e: e.memset(s_qcol[0:1, 6:7], 1.0), writes=("fence0",), tag="fz")
    p.op("DVE", lambda e: e.memset(s_afm[:, :], 0.0), writes=("afmA",), tag="zafm")
    warmup(AF.Silu)
    p.op("ACT", lambda e: e.activation(s_row[0:1, :], s_row[0:1, :], AF.Silu),
         reads=("crow",), writes=("crow",), tag="siluc")
    for k in range(8):
        trans_f32(s_capl[:, 16 + k:17 + k], s_row[0:1, k * 128:(k + 1) * 128],
                  ("crow",), ("silc",), f"sc{k}")
    wmv = s_wmodc.ap().rearrange("p (s e) -> p s e", s=4)

    def mod_tr(ob):
        for q4 in range(4):
            col = ob * 4 + q4
            trans_f32(s_modc[:, col:col + 1],
                      s_scr2[0:1, (ob % 2) * 512 + q4 * 128:
                             (ob % 2) * 512 + (q4 + 1) * 128],
                      (f"modev{ob % 2}",), ("modc",), f"mt{ob}_{q4}")

    for ob in range(12):
        bres = f"psum{2 + ob % 2}"
        for k in range(8):
            slot = (ob * 8 + k) % 4
            dma("POOL", wmv[:, slot, :],
                d_wmod[k * 128:(k + 1) * 128, ob * 512:(ob + 1) * 512],
                (), (f"wm{slot}",), f"ldwm{ob}_{k}")
            p.op("PE", lambda e, k=k, slot=slot, ob=ob: e.matmul(
                psum[2 + ob % 2][0:1, 0:512], s_capl[:, 16 + k:17 + k],
                wmv[:, slot, :], start=(k == 0), stop=(k == 7)),
                 reads=(f"wm{slot}", "silc"), writes=(bres,), tag=f"mm{ob}_{k}")
        p.op("DVE", lambda e, ob=ob: e.tensor_copy(
            s_scr2[0:1, (ob % 2) * 512:(ob % 2) * 512 + 512],
            psum[2 + ob % 2][0:1, 0:512]),
             reads=(bres,), writes=(f"modev{ob % 2}",), tag=f"me{ob}")
        if ob >= 1:
            mod_tr(ob - 1)
    mod_tr(11)

    def g_xform(c0, tagp):
        p.op("DVE", lambda e: e.tensor_scalar(s_scr[:, 0:8], s_modc[:, c0:c0 + 8],
                                              0.0, None, OP.is_ge),
             reads=("modc",), writes=("gsgn",), tag="gs" + tagp)
        p.op("DVE", lambda e: e.tensor_scalar(s_scr[:, 0:8], s_scr[:, 0:8], 2.0, -1.0,
                                              OP.mult, OP.add),
             reads=("gsgn",), writes=("gsgn",), tag="g2" + tagp)
        p.op("ACT", lambda e: e.activation(s_scr[:, 8:16], s_modc[:, c0:c0 + 8], AF.Abs),
             reads=("modc",), writes=("gabs",), tag="gb" + tagp)
        p.op("DVE", lambda e: e.tensor_scalar(s_scr[:, 8:16], s_scr[:, 8:16], 0.25,
                                              None, OP.max),
             reads=("gabs",), writes=("gabs",), tag="gm" + tagp)
        p.op("DVE", lambda e: e.tensor_tensor(s_modc[:, c0:c0 + 8], s_scr[:, 0:8],
                                              s_scr[:, 8:16], OP.mult),
             reads=("gsgn", "gabs"), writes=("modc",), tag="gf" + tagp)

    if STAGE >= 2:
        g_xform(16, "a")
        g_xform(40, "b")

    if STAGE < 2:
        p.op("DVE", lambda e: e.memset(s_capl[:, 0:16], 0.01), writes=("cacol", "sacol"), tag="capz")
    p.op("DVE", lambda e: e.tensor_scalar(s_capl[:, 0:8], s_modc[:, 8:16], 1.0, None, OP.add),
         reads=("modc",), writes=("cacol",), tag="cap1") if STAGE >= 2 else None
    p.op("DVE", lambda e: e.tensor_copy(s_capl[:, 8:16], s_modc[:, 0:8]),
         reads=("modc",), writes=("sacol",), tag="sacp") if STAGE >= 2 else None
    p.op("DVE", lambda e: e.tensor_copy(bass.AP(s_uvw, 0, [[16, 128], [2, 8]]),
                                        s_capl[:, 0:8]),
         reads=("cacol",), writes=("uvw",), tag="uvc")
    p.op("DVE", lambda e: e.tensor_copy(bass.AP(s_uvw, 1, [[16, 128], [2, 8]]),
                                        s_capl[:, 8:16]),
         reads=("sacol",), writes=("uvw",), tag="uvs")

    for (n0, n1) in ((0, 512), (512, 1024), (1024, E)):
        def mm_uv(e, n0=n0, n1=n1):
            inst = None
            for dc in range(8):
                inst = e.matmul(ps_x[0:2, 0:n1 - n0], s_uvw[:, dc * 2:(dc + 1) * 2],
                                winv[:, dc, n0:n1], start=(dc == 0), stop=(dc == 7))
            return inst
        p.op("PE", mm_uv, reads=("win", "uvw"), writes=("ps_x",), tag=f"uv{n0}")
        p.op("DVE", lambda e, n0=n0, n1=n1: e.tensor_copy(s_uv[0:2, n0:n1],
                                                          ps_x[0:2, 0:n1 - n0]),
             reads=("ps_x",), writes=("uv",), tag=f"uvc{n0}")

    for dc in range(8):
        p.op("DVE", lambda e, dc=dc: e.tensor_scalar(
            winv[:, dc, :], winv[:, dc, :], s_capl[:, dc:dc + 1], None, OP.mult),
             reads=("win", "cacol"), writes=("win",), tag=f"fold{dc}")

    xstage = s_xstage.ap().rearrange("p (r d) -> p r d", r=2)
    xhat = s_hc.ap().rearrange("p (r d) -> p r d", r=2)
    xcbv = s_hc.ap().rearrange("p (r tt c f) -> p r tt c f", r=2, tt=4, c=8)
    xTv = s_xT.ap().rearrange("p (r tt c f) -> p r tt c f", r=2, tt=4, c=8)
    hcfl = s_hc.ap().rearrange("p (r n) -> p r n", r=2)
    xTfl = s_xT.ap().rearrange("p (r n) -> p r n", r=2)
    xhv = s_big.ap().rearrange("p (t d) -> p t d", t=NT)
    TP4 = [0, ((ps_trA, "ps_tr0"), (ps_trB, "ps_tr1"),
               (psum[2], "psum2"), (psum[3], "psum3"))]

    def ln_tile(src_ap, nrows, rstd_ap, mh_ap, rdres, stat_wr, tag,
                out_ap=None, owr=(), oaka=(), scr_ap=None, scr_wr=None,
                scr_aka=()):
        if scr_ap is None:
            scr_ap, scr_wr, scr_aka = s_dump.ap(), ("dump",), ()
        p.op("ACT", lambda e, scr_ap=scr_ap: e.activation(
            scr_ap[0:nrows, :], src_ap, AF.Copy,
            accum_out=s_mcol[0:nrows, 3:4]),
             reads=rdres + ("fence1",), writes=tuple(scr_wr) + ("mcol3",),
             aka=scr_aka, tag="s1" + tag)
        p.op("ACT", lambda e, scr_ap=scr_ap: e.activation(
            scr_ap[0:nrows, :], src_ap, AF.Square,
            accum_out=s_mcol[0:nrows, 4:5]),
             reads=rdres, writes=tuple(scr_wr) + ("mcol4",), aka=scr_aka,
             tag="s2" + tag)
        p.op("DVE", lambda e: e.tensor_scalar(s_mcol[0:nrows, 3:4], s_mcol[0:nrows, 3:4],
                                              1.0 / 1024, None, OP.mult),
             reads=("mcol3",), writes=("mcol3",), tag="m" + tag)
        p.op("DVE", lambda e: e.scalar_tensor_tensor(
            s_mcol[0:nrows, 5:6], s_mcol[0:nrows, 3:4], 1.0, s_mcol[0:nrows, 3:4],
            OP.bypass, OP.mult),
             reads=("mcol3",), writes=("mcol5",), tag="m2" + tag)
        p.op("DVE", lambda e: e.tensor_scalar(s_mcol[0:nrows, 4:5], s_mcol[0:nrows, 4:5],
                                              1.0 / 1024, None, OP.mult),
             reads=("mcol4",), writes=("mcol4",), tag="e2" + tag)
        p.op("DVE", lambda e: e.scalar_tensor_tensor(
            s_mcol[0:nrows, 4:5], s_mcol[0:nrows, 4:5], 1e-5,
            s_mcol[0:nrows, 5:6], OP.add, OP.subtract),
             reads=("mcol4", "mcol5"), writes=("mcol4",), tag="v" + tag)
        p.op("ACT", lambda e: e.activation(s_mcol[0:nrows, 5:6], s_mcol[0:nrows, 4:5],
                                           AF.Sqrt),
             reads=("mcol4",), writes=("mcol5",), tag="l" + tag)
        p.op("DVE", lambda e: e.reciprocal(rstd_ap, s_mcol[0:nrows, 5:6]),
             reads=("mcol5",), writes=stat_wr, tag="r" + tag)
        p.op("DVE", lambda e: e.scalar_tensor_tensor(mh_ap, s_mcol[0:nrows, 3:4], -1.0,
                                                     rstd_ap, OP.mult, OP.mult),
             reads=("mcol3",) + stat_wr, writes=stat_wr, tag="mh" + tag)
        if out_ap is not None:
            p.op("DVE", lambda e: e.tensor_scalar(out_ap, src_ap, rstd_ap, None,
                                                  OP.mult),
                 reads=rdres + stat_wr, writes=owr, aka=oaka, tag="xo" + tag)

    if STAGE < 3:
        run_main = False
    else:
        run_main = True
    warmup(AF.Sqrt) if run_main else None
    if not run_main:
        pass
    run_main and dma("POOL", s_scr[0:3, 16:1040], d_xh3, (), ("xhalo",), "ldxh")
    dma("SP", s_mcol[0:3, 2:3], d_mask3, (), ("mask3",), "ldm3")
    ln_tile(s_scr[0:3, 16:1040], 3, s_mcol[0:3, 6:7], s_mcol[0:3, 7:8],
            ("xhalo",), ("hstat",), "H", out_ap=s_hb3[0:3, :], owr=("hb3",))
    p.op("DVE", lambda e: e.tensor_scalar(s_hb3[0:3, :], s_hb3[0:3, :],
                                          s_mcol[0:3, 2:3], None, OP.mult),
         reads=("hb3", "mask3"), writes=("hb3",), tag="mskh")
    p.op("DVE", lambda e: e.scalar_tensor_tensor(s_mcol[0:3, 8:9], s_mcol[0:3, 7:8], 1.0,
                                                 s_mcol[0:3, 2:3], OP.bypass, OP.mult),
         reads=("hstat", "mask3"), writes=("mcol8",), tag="mhm")
    for dcx in range(8):
        trans_b16(bass.AP(s_xTh, dcx * 3, [[24, 128], [1, 3]]),
                  s_hb3[0:3, dcx * 128:(dcx + 1) * 128], ("hb3",), ("xTh",), f"th{dcx}")
    p.op("DVE", lambda e: e.tensor_copy(s_mcol[0:3, 9:10], s_mcol[0:3, 2:3]),
         reads=("mask3",), writes=("mcol9",), tag="k8hm")
    p.op("PE", lambda e: e.transpose(ps_x[0:2, 0:3], s_mcol[0:3, 8:10], s_id_f[0:3, 0:3]),
         reads=("mcol8", "mcol9", "c:id_f"), writes=("ps_x",), tag="Tk8h")
    p.op("DVE", lambda e: e.tensor_copy(s_k8h[0:2, 0:3], ps_x[0:2, 0:3]),
         reads=("ps_x",), writes=("k8h",), tag="k8hc")
    for ec in range(8):
        def hmm(e, ec=ec):
            inst = None
            for dc in range(8):
                inst = e.matmul(ps_x[0:128, 0:3], winv[:, dc, ec * 128:(ec + 1) * 128],
                                bass.AP(s_xTh, dc * 3, [[24, 128], [1, 3]]),
                                start=(dc == 0), stop=False)
            return e.matmul(ps_x[0:128, 0:3], s_uv[0:2, ec * 128:(ec + 1) * 128],
                            s_k8h[0:2, 0:3], start=False, stop=True)
        p.op("PE", hmm, reads=("win", "xTh", "uv", "k8h"), writes=("ps_x",), tag=f"hmm{ec}")
        p.op("DVE", lambda e, ec=ec: e.tensor_copy(s_carry[:, ec * 3:(ec + 1) * 3],
                                                   ps_x[0:128, 0:3]),
             reads=("ps_x",), writes=(f"carry{ec}",), tag=f"hc{ec}")

    p.op("DVE", lambda e: e.memset(s_mcol[:, 9:10], 1.0), writes=("mcol9",), tag="onecol")

    def ph_ln(sbk):
        sl = sbk % 2
        for tt in range(4):
            gt = sbk * 4 + tt
            xsl = gt % 2
            dma("POOL", xstage[:, xsl, :],
                d_x[gt * 128:(gt + 1) * 128, :], (), (f"xs{xsl}",), f"ldx{gt}")
            src = xstage[:, xsl, :]
            rs = s_stats[:, 2 * (gt % 8):2 * (gt % 8) + 1]
            mh = s_stats[:, 2 * (gt % 8) + 1:2 * (gt % 8) + 2]
            ln_tile(src, 128, rs, mh, (f"xs{xsl}",), (f"st{gt % 8}",), f"t{gt}",
                    out_ap=xhat[:, sl, tt * 1024:(tt + 1) * 1024],
                    owr=(f"xhat{sl}",), oaka=(f"xcb{sl}",))
            p.op("DVE", lambda e, mh=mh: e.tensor_copy(s_mcol[:, 8:9], mh),
                 reads=(f"st{gt % 8}",), writes=("mcol8",), tag=f"mc{gt}")
            p.op("PE", lambda e: e.transpose(ps_x[0:2, 0:128], s_mcol[:, 8:10],
                                             s_id_f[:, :]),
                 reads=("mcol8", "mcol9", "c:id_f"), writes=("ps_x",), tag=f"Tk8{gt}")
            p.op("DVE", lambda e, sl=sl, tt=tt: e.tensor_copy(
                s_k8[0:2, sl * 512 + tt * 128:sl * 512 + (tt + 1) * 128],
                ps_x[0:2, 0:128]),
                 reads=("ps_x",), writes=("k8",), tag=f"mTc{gt}")

    def ph_xt(sbk):
        sl = sbk % 2
        for tt in range(4):
            for dc in range(8):
                pe_t128(xTv[:, sl, tt, dc, :],
                        xhat[:, sl, tt * 1024 + dc * 128:tt * 1024 + (dc + 1) * 128],
                        (f"xhat{sl}",), (f"xT{sl}",), f"xT{sbk}_{tt}_{dc}",
                        rot=TP4)

    def ph_inp(sbk):
        sl = sbk % 2
        for ec in range(9):
            bank = (0, 1, 4)[ec % 3]
            M = 128 if ec < 8 else 16
            def mm_inp(e, sl=sl, ec=ec, bank=bank, M=M, sbk=sbk):
                for dc in range(8):
                    e.matmul(psum[bank][0:M, 0:512], winv[:, dc, ec * 128:ec * 128 + M],
                             xTv[:, sl, :, dc, :], start=(dc == 0), stop=False)
                return e.matmul(psum[bank][0:M, 0:512],
                                s_uv[0:2, ec * 128:ec * 128 + M],
                                s_k8[0:2, (sbk % 2) * 512:(sbk % 2 + 1) * 512],
                                start=False, stop=True)
            p.op("PE", mm_inp, reads=(f"xT{sl}", "win", "uv", "k8"),
                 writes=(f"psum{bank}",), tag=f"inp{sbk}_{ec}")
            if ec < 8:
                xi = ec % 2
                p.op("ACT", lambda e, xi=xi, bank=bank: e.copy(
                    s_xiext[:, xi * 515 + 3:(xi + 1) * 515], psum[bank][0:128, 0:512]),
                     reads=(f"psum{bank}",), writes=(f"xi{xi}",), tag=f"ev{sbk}_{ec}")
                p.op("DVE", lambda e, xi=xi, ec=ec: e.tensor_copy(
                    s_xiext[:, xi * 515:xi * 515 + 3], s_carry[:, ec * 3:(ec + 1) * 3]),
                     reads=(f"carry{ec}",), writes=(f"xi{xi}",), tag=f"ci{sbk}_{ec}")
                p.op("DVE", lambda e, xi=xi, ec=ec: e.tensor_scalar(
                    s_convscr[:, xi * 512:(xi + 1) * 512],
                    s_xiext[:, xi * 515:xi * 515 + 512],
                    s_cw[:, ec * 4:ec * 4 + 1], s_cb[:, ec:ec + 1],
                    OP.mult, OP.add),
                     reads=(f"xi{xi}", "c:cw", "c:cb"), writes=(f"cs{xi}",),
                     tag=f"c0_{sbk}_{ec}")
                for kk in range(1, 3):
                    p.op("DVE", lambda e, xi=xi, ec=ec, kk=kk: e.scalar_tensor_tensor(
                        s_convscr[:, xi * 512:(xi + 1) * 512],
                        s_xiext[:, xi * 515 + kk:xi * 515 + 512 + kk],
                        s_cw[:, ec * 4 + kk:ec * 4 + kk + 1],
                        s_convscr[:, xi * 512:(xi + 1) * 512], OP.mult, OP.add),
                         reads=(f"xi{xi}", "c:cw", f"cs{xi}"), writes=(f"cs{xi}",),
                         tag=f"c{kk}_{sbk}_{ec}")
                p.op("DVE", lambda e, xi=xi, sl=sl, ec=ec: e.scalar_tensor_tensor(
                    xcbv[:, sl, :, ec, :],
                    s_xiext[:, xi * 515 + 3:xi * 515 + 515],
                    s_cw[:, ec * 4 + 3:ec * 4 + 4],
                    s_convscr[:, xi * 512:(xi + 1) * 512], OP.mult, OP.add),
                     reads=(f"xi{xi}", "c:cw", f"cs{xi}"), writes=(f"xcb{sl}",),
                     aka=(f"xhat{sl}",), tag=f"cb_{sbk}_{ec}")
                p.op("DVE", lambda e, xi=xi, ec=ec: e.tensor_copy(
                    s_carry[:, ec * 3:(ec + 1) * 3],
                    s_xiext[:, xi * 515 + 512:xi * 515 + 515]),
                     reads=(f"xi{xi}",), writes=(f"carry{ec}",), tag=f"cu{sbk}_{ec}")
            else:
                p.op("DVE", lambda e, bank=bank, sbk=sbk: e.tensor_scalar(
                    s_afm[0:16, sbk * 512:(sbk + 1) * 512], psum[bank][0:16, 0:512],
                    s_wb[0:16, 0:1], None, OP.mult),
                     reads=(f"psum{bank}", "c:wb"), writes=("afmA",), tag=f"Aev{sbk}")

    def ph_xh(sbk):
        sl = sbk % 2
        for tt in range(4):
            gt = sbk * 4 + tt
            for dc in range(8):
                pe_t128(xhv[:, gt, dc * 128:(dc + 1) * 128],
                        xcbv[:, sl, tt, dc, :],
                        (f"xcb{sl}",), (f"xh{gt}",), f"xh{gt}_{dc}",
                        eveng="ACT", rot=TP4)

    ph_ln(0)
    for sbk in range(NSB):
        ph_xt(sbk)
        if sbk + 1 < NSB:
            ph_ln(sbk + 1)
        ph_inp(sbk)
        ph_xh(sbk)

    def pe_tr(dst_ap, src_ap, rd, wr, tag, waka=(), f32=False, eveng="DVE"):
        r, c = src_ap.shape
        b0_ = src_ap.base_partition()
        if f32:
            pa = ps_x[0:c, 0:r]
            res = "ps_x"
            ident = s_id_f[b0_:b0_ + r, b0_:b0_ + r]
            idr = "c:id_f"
        else:
            h_ = tp_flip[0]; tp_flip[0] ^= 1
            bank = ps_trA if h_ == 0 else ps_trB
            pa = bank.ap().bitcast(BF16)[0:c, 0:r]
            res = f"ps_tr{h_}"
            ident = s_id_b[b0_:b0_ + r, b0_:b0_ + r]
            idr = "c:id_b"
        p.op("PE", lambda e: e.transpose(pa, src_ap, ident),
             reads=tuple(rd) + (idr,), writes=(res,), tag="T" + tag)
        if eveng == "ACT":
            p.op("ACT", lambda e: e.activation(dst_ap, pa, AF.Copy),
                 reads=(res, "fence1"), writes=tuple(wr), aka=waka,
                 tag="Tc" + tag)
        else:
            p.op("DVE", lambda e: e.tensor_copy(dst_ap, pa),
                 reads=(res,), writes=tuple(wr), aka=waka, tag="Tc" + tag)

    afm3 = s_afm.ap().rearrange("p (c l) -> p c l", c=64)
    A_v = lambda p0: s_afm.ap().rearrange("p (c l) -> p c l", c=64)

    s_rB = [s_dump.ap()[:, 0:512], s_dump.ap()[:, 512:1024],
            s_convscr.ap()[:, 0:512], s_convscr.ap()[:, 512:1024]]
    rB3 = [b.rearrange("p (c l) -> p c l", c=8) for b in s_rB]
    RAKA = {0: ("psfm", "dump"), 1: ("dump", "dch", "rbm"),
            2: ("cs0", "cs1", "ytile"), 3: ("cs0", "cs1", "ytile")}

    def rsh_in(e):
        return e.dma_start(out=s_rB[0], in_=s_afm[0:16, :])
    p.op("SP", rsh_in, reads=("afmA",), writes=("rsh0",), aka=RAKA[0],
         dma=True, tag="rshA")

    def hs_scan2(srcb, dstb, pongb, op, res_in, res_out, tagp):
        cur = srcb
        order = [pongb, dstb, pongb, dstb, pongb, dstb]
        for i, s in enumerate((1, 2, 4, 8, 16, 32)):
            nxt = order[i]
            rin = res_in if i == 0 else f"{tagp}s{i-1}"
            rout = f"{tagp}s{i}" if i < 5 else res_out
            raka = RAKA[nxt] if i < 2 else ()
            p.op("DVE", lambda e, cur=cur, nxt=nxt, s=s: e.tensor_copy(
                rB3[nxt][:, :, 0:s], rB3[cur][:, :, 0:s]),
                 reads=(rin,), writes=(rout,), aka=raka, tag=f"{tagp}c{i}")
            p.op("DVE", lambda e, cur=cur, nxt=nxt, s=s: e.tensor_tensor(
                rB3[nxt][:, :, s:64], rB3[cur][:, :, s:64],
                rB3[cur][:, :, 0:64 - s], op),
                 reads=(rin,), writes=(rout,), tag=f"{tagp}a{i}")
            cur = nxt

    hs_scan2(0, 1, 2, OP.add, "rsh0", "afmCS", "cs")
    hs_scan2(1, 0, 2, OP.min, "afmCS", "afmrm", "rm")

    p.op("DVE", lambda e: e.tensor_copy(s_atot[:, 96:104],
                                        rB3[1][:, :, 63:64].squeeze(2)),
         reads=("afmCS",), writes=("atotR",), tag="exA")
    p.op("DVE", lambda e: e.tensor_copy(s_atot[:, 104:112],
                                        rB3[0][:, :, 63:64].squeeze(2)),
         reads=("afmrm",), writes=("fm",), tag="exF")
    p.op("DVE", lambda e: e.reduce_max(s_atot[:, 112:120], rB3[1][:, :, :],
                                       AX.X),
         reads=("afmCS",), writes=("Mc",), tag="exM")
    def rsh_at(e):
        return e.dma_start(out=s_atot[0:16, 0:64], in_=s_atot[:, 96:104])
    p.op("SP", rsh_at, reads=("atotR",), writes=("atot",), dma=True,
         tag="rshAt")

    warmup(AF.Exp)
    t16 = s_tok16.ap().rearrange("p (f t c) -> p f t c", f=4, t=NT)
    p.op("DVE", lambda e: e.tensor_tensor(
        rB3[2][:, :, :],
        s_atot[:, 104:112].unsqueeze(2).broadcast_to([128, 8, 64]),
        rB3[1][:, :, :], OP.subtract),
         reads=("afmCS", "fm"), writes=("dstmp",), aka=RAKA[2], tag="dsub")
    p.op("DVE", lambda e: e.tensor_tensor(
        rB3[3][:, :, :], rB3[1][:, :, :],
        s_atot[:, 112:120].unsqueeze(2).broadcast_to([128, 8, 64]),
        OP.subtract),
         reads=("afmCS", "Mc"), writes=("ostmp",), aka=RAKA[3], tag="osub")

    def field_to_t16(fi, bsl, emit_exp, rd, waka=()):
        p.op("ACT", emit_exp, reads=rd + ("fence1",), writes=(f"fexp{bsl}",),
             aka=waka, tag=f"fx{fi}")

        def back(e, bsl=bsl):
            return e.dma_start(out=s_afm2[0:16, :], in_=s_rB[bsl])
        p.op("POOL", back, reads=(f"fexp{bsl}",), writes=("fmcur",),
             dma=True, tag=f"fb{fi}")
        for tl in range(NT):
            pe_tr(t16[:, fi, tl, :], s_afm2[0:16, tl * 128:(tl + 1) * 128],
                  ("fmcur",), (f"t16_{fi}",), f"tf{fi}_{tl}", eveng="ACT")

    field_to_t16(2, 2, lambda e: e.activation(s_rB[2], s_rB[2], AF.Exp),
                 ("dstmp",))
    field_to_t16(3, 3, lambda e: e.activation(s_rB[3], s_rB[3], AF.Exp),
                 ("ostmp",))
    field_to_t16(0, 1, lambda e: e.activation(s_rB[1], s_rB[1], AF.Exp,
                                              scale=-1.0), ("afmCS",),
                 waka=("afmCS",))
    field_to_t16(1, 0, lambda e: e.activation(s_rB[0], s_rB[0], AF.Exp),
                 ("afmrm",), waka=("afmrm",))

    wb1 = s_wb1.ap()
    for tl in range(NT):
        xt3 = xhv[:, tl, :].rearrange("p (h n) -> p h n", h=16)
        p.op("DVE", lambda e, tl=tl, xt3=xt3: e.tensor_tensor(
            wb1[:, 0:1024].rearrange("p (h n) -> p h n", h=16),
            t16[:, 2, tl, :].unsqueeze(2).broadcast_to([128, 16, 64]),
            xt3, OP.mult),
             reads=("t16_2", f"xh{tl}"), writes=("wb1",),
             aka=("gsgn", "gabs", "xhalo"), tag=f"w1_{tl}")
        p.op("DVE", lambda e, tl=tl: e.tensor_copy(wb1[:, 1024:1040],
                                                   t16[:, 2, tl, :]),
             reads=("t16_2",), writes=("wb1",), tag=f"w1c_{tl}")
        for (n0, n1, bank) in ((0, 512, 2), (512, 1024, 3), (1024, 1040, 4)):
            p.op("PE", lambda e, n0=n0, n1=n1, bank=bank: e.matmul(
                psum[bank][0:2, 0:n1 - n0], s_ind[:, :], wb1[:, n0:n1],
                start=True, stop=True),
                 reads=("wb1", "c:ind"), writes=(f"psum{bank}",), tag=f"ag{tl}_{n0}")
            p.op("DVE", lambda e, n0=n0, n1=n1, bank=bank: e.tensor_copy(
                s_aggsl[0:2, n0:n1], psum[bank][0:2, 0:n1 - n0]),
                 reads=(f"psum{bank}",), writes=("aggsl",),
                 aka=("modev0", "modev1"), tag=f"agc{tl}_{n0}")
        dma("SP", s_agg[0:0 + 2, 0:1056] if False else
            bass.AP(s_agg, 0, [[1056, 2], [1, 1056]]) if False else s_agg[0:2, :],
            s_aggsl[0:2, :], ("aggsl",), ("agg",), f"agd{tl}") if False else None
        def agd(e, tl=tl):
            return e.dma_start(out=s_agg[2 * tl:2 * tl + 2, 0:1040],
                               in_=s_aggsl[0:2, 0:1040])
        p.op("SP", agd, reads=("aggsl",), writes=("agg",), dma=True, tag=f"agd{tl}")

    pe_tr(s_agg[0:64, 1040:1056], s_atot[0:16, 0:64], ("atot",), ("agg",),
          "atT", f32=True)

    dma("POOL", d_gin, s_agg[0:64, :], ("agg",), ("d_gin",), "gi")
    p.op("POOL", lambda e: e.collective_compute(
        "AllGather", OP.bypass, replica_groups=GROUPS2,
        ins=[d_gin.opt()], outs=[d_gout.opt()]),
         reads=("d_gin",), writes=("d_gout",), cc_sem="cc0", tag="ccag")
    dma("POOL", s_statF[:, :], d_gout, ("d_gout",), ("statF",), "go",
        aka=("wmodc", "wm0", "wm1", "wm2"))

    MAGIC = 12582912.0
    sgu = s_sgu.ap()
    WNAT_AKA = ("wmodc", "wm2", "wm3", "xi0", "xi1")
    WB1_AKA = ("wb1", "rmrow2", "gsgn", "gabs", "xhalo")

    nslot = [0]
    f2slots = [s_wnat.ap()[:, 0:1024], s_wnat.ap()[:, 1024:2048],
               s_afm2.ap().bitcast(F32)[:, 0:1024],
               s_afm2.ap().bitcast(F32)[:, 1024:2048]]
    F2AKA = [WNAT_AKA, WNAT_AKA, ("fmcur",), ("fmcur",)]

    def natload(src_ap, tag):
        slot = nslot[0] % 4
        nslot[0] += 1
        dma("SP", f2slots[slot][:, 0:src_ap.shape[1]], src_ap,
            (), (f"f2s{slot}",), tag, aka=F2AKA[slot])
        return slot

    f2_units = []

    def mkunit(fn):
        f2_units.append(fn)

    for w_i, (dsrc, cb0) in enumerate(((d_wg, 0), (d_wu, 22))):
        for ic in range(NIC):
            def u(w_i=w_i, dsrc=dsrc, cb0=cb0, ic=ic):
                slot = natload(dsrc[ic * 128:(ic + 1) * 128, :], f"nl{w_i}_{ic}")
                p.op("DVE", lambda e, slot=slot, c=cb0 + ic: e.reduce_max(
                    sgu[:, c:c + 1], f2slots[slot][:, 0:1024],
                    AX.X, apply_absolute_value=True),
                     reads=(f"f2s{slot}",), writes=("sgGU",), tag=f"rmw{w_i}_{ic}")
            mkunit(u)
    for hb in range(8):
        for c3, w in ((0, 1024), (1024, 1024), (2048, 768)):
            def u(hb=hb, c3=c3, w=w):
                slot = natload(d_wd[hb * 128:(hb + 1) * 128, c3:c3 + w],
                               f"nd{hb}_{c3}")
                if c3 == 0:
                    p.op("DVE", lambda e, slot=slot, w=w, hb=hb: e.reduce_max(
                        sgu[:, 44 + hb:45 + hb], f2slots[slot][:, 0:w],
                        AX.X, apply_absolute_value=True),
                         reads=(f"f2s{slot}",), writes=("sgD",),
                         tag=f"rmd{hb}_{c3}")
                else:
                    p.op("DVE", lambda e, slot=slot, w=w: e.reduce_max(
                        sgu[:, 156:157], f2slots[slot][:, 0:w],
                        AX.X, apply_absolute_value=True),
                         reads=(f"f2s{slot}",), writes=("sgScr",),
                         tag=f"rmd{hb}_{c3}")
                    p.op("DVE", lambda e, hb=hb: e.tensor_tensor(
                        sgu[:, 44 + hb:45 + hb], sgu[:, 44 + hb:45 + hb],
                        sgu[:, 156:157], OP.max),
                         reads=("sgD", "sgScr"), writes=("sgD",),
                         tag=f"rmx{hb}_{c3}")
            mkunit(u)

    def f2_finalize():
        p.op("DVE", lambda e: e.reciprocal(sgu[:, 60:82], sgu[:, 0:22]),
             reads=("sgGU",), writes=("sgInv",), tag="ivg")
        p.op("DVE", lambda e: e.tensor_scalar(sgu[:, 60:82], sgu[:, 60:82],
                                              127.0, None, OP.mult),
             reads=("sgInv",), writes=("sgInv",), tag="ivg2")
        p.op("DVE", lambda e: e.reciprocal(sgu[:, 82:104], sgu[:, 22:44]),
             reads=("sgGU", "sgInv"), writes=("sgInv",), tag="ivu")
        p.op("DVE", lambda e: e.tensor_scalar(sgu[:, 82:104], sgu[:, 82:104],
                                              127.0, None, OP.mult),
             reads=("sgInv",), writes=("sgInv",), tag="ivu2")
        p.op("DVE", lambda e: e.reciprocal(sgu[:, 52:60], sgu[:, 44:52]),
             reads=("sgD", "sgInv"), writes=("sgInv",), tag="ivd")
        p.op("DVE", lambda e: e.tensor_scalar(sgu[:, 52:60], sgu[:, 52:60],
                                              127.0, None, OP.mult),
             reads=("sgInv",), writes=("sgInv",), tag="ivd2")
        p.op("DVE", lambda e: e.tensor_scalar(sgu[:, 148:156], sgu[:, 44:52],
                                              1.0 / 127.0, None, OP.mult),
             reads=("sgD",), writes=("sgDdq",), tag="ddq")

    pe_tr(s_psfm[0:16, 0:128], s_statF[:, 1040:1056], ("statF",), ("psf0",),
          "atf", f32=True)
    psf = s_psfm.ap().rearrange("p (k c) -> p k c", k=6)

    def hs_line(csrc, cdst, cpong, op, rin, rout, tagp, n=7):
        cur = csrc
        order = []
        slots = [cdst, cpong]
        for i in range(n):
            order.append(slots[(n - 1 - i) % 2])
        for i in range(n):
            s = 1 << i
            nxt = order[i]
            ri = rin if i == 0 else f"{tagp}{i-1}"
            ro = rout if i == n - 1 else f"{tagp}{i}"
            p.op("DVE", lambda e, cur=cur, nxt=nxt, s=s: e.tensor_copy(
                psf[0:16, nxt, 0:s], psf[0:16, cur, 0:s]),
                 reads=(ri,), writes=(ro,), tag=f"{tagp}c{i}")
            p.op("DVE", lambda e, cur=cur, nxt=nxt, s=s: e.tensor_tensor(
                psf[0:16, nxt, s:128], psf[0:16, cur, s:128],
                psf[0:16, cur, 0:128 - s], op),
                 reads=(ri,), writes=(ro,), tag=f"{tagp}a{i}")
            cur = nxt
        assert cur == cdst, (tagp, cur)

    hs_line(0, 1, 4, OP.add, "psf0", "psend", "ps")
    hs_line(1, 2, 4, OP.min, "psend", "rmp", "rm2")
    p.op("DVE", lambda e: e.memset(psf[0:16, 3, 0:1], 0.0), writes=("rmps",), tag="rz")
    p.op("DVE", lambda e: e.tensor_copy(psf[0:16, 3, 1:128], psf[0:16, 2, 0:127]),
         reads=("rmp",), writes=("rmps",), tag="rsh")
    p.op("DVE", lambda e: e.tensor_scalar(psf[0:16, 3, :], psf[0:16, 3, :], 0.0,
                                          None, OP.min),
         reads=("rmps",), writes=("rmps",), tag="rmn")
    pe_tr(s_psc[:, 0:16], psf[0:16, 1, :], ("psend",), ("psendT",), "pst", f32=True)
    def rrow(e):
        return e.dma_start(out=s_scr2[0:1, 0:1024], in_=psf[0:8, 3, :])
    def rrow2(e):
        return e.dma_start(out=s_scr[0:1, 0:1024].bitcast(F32) if False
                           else s_scr[0:1, 0:1024], in_=psf[8:16, 3, :])
    p.op("SP", rrow, reads=("rmps",), writes=("rmrow",), aka=("scr2l", "aggsl"),
         dma=True, tag="rr1")
    p.op("SP", rrow2, reads=("rmps",), writes=("rmrow2",),
         aka=("wb1", "gsgn", "gabs", "xhalo"), dma=True, tag="rr2")

    for h in range(16):
        rsrc = s_scr2[0:1, (h % 8) * 128:(h % 8 + 1) * 128] if h < 8 else                s_scr[0:1, (h - 8) * 128:(h - 8 + 1) * 128]
        rres = "rmrow" if h < 8 else "rmrow2"
        p.op("PE", lambda e, rsrc=rsrc: e.matmul(
            ps_x[0:128, 0:128], s_onesf[0:1, :], rsrc, start=True, stop=True),
             reads=(rres, "c:onesf"), writes=("ps_x",), tag=f"rb{h}")
        p.op("DVE", lambda e, h=h: e.tensor_scalar(
            s_dch[:, :], ps_x[0:128, 0:128], s_psc[:, h:h + 1], None, OP.subtract),
             reads=("ps_x", "psendT"), writes=("dch",), tag=f"de{h}")
        p.op("DVE", lambda e: e.scalar_tensor_tensor(
            s_rb[:, :], s_dch[:, :], 1e4, s_lts[:, :], OP.add, OP.mult),
             reads=("dch", "c:lts"), writes=("rbm",), tag=f"dm{h}")
        p.op("DVE", lambda e: e.tensor_scalar(s_rb[:, :], s_rb[:, :], -1e4,
                                              None, OP.add),
             reads=("rbm",), writes=("rbm",), tag=f"dm2{h}")
        p.op("ACT", lambda e: e.activation(s_dch[:, :], s_rb[:, :], AF.Exp),
             reads=("rbm",), writes=("dch",), tag=f"dx{h}")
        p.op("PE", lambda e, h=h: e.matmul(
            psum[2][0:128, 0:64], s_dch[:, :], s_statF[:, h * 64:(h + 1) * 64],
            start=True, stop=True),
             reads=("dch", "statF"), writes=("psum2",), tag=f"s2m{h}")
        p.op("PE", lambda e, h=h: e.matmul(
            psum[3][0:128, 0:1], s_dch[:, :], s_statF[:, 1024 + h:1025 + h],
            start=True, stop=True),
             reads=("dch", "statF"), writes=("psum3",), tag=f"n2m{h}")
        p.op("DVE", lambda e, h=h: e.tensor_copy(
            s_st2[:, h * 64:(h + 1) * 64], psum[2][0:128, 0:64]),
             reads=("psum2",), writes=("st2",),
             aka=("cs0", "cs1", "dstmp", "ostmp", "fexp2", "fexp3",
                  "afmrm", "fexp0", "rsh0", "ytile"), tag=f"s2c{h}")
        p.op("DVE", lambda e, h=h: e.tensor_copy(
            s_st2[:, 1024 + h:1025 + h], psum[3][0:128, 0:1]),
             reads=("psum3",), writes=("st2",), tag=f"n2c{h}")
    p.op("DVE", lambda e: e.tensor_scalar(
        s_scr2[0:64, 0:1040], s_st2[0:64, 0:1040], s_hsel[0:64, 0:1], None, OP.mult),
         reads=("st2", "c:hsel"), writes=("scr2l",), aka=("rmrow",), tag="hs0")
    p.op("DVE", lambda e: e.tensor_copy(
        s_scr[0:64, 0:1040], s_st2[64:128, 0:1040]),
         reads=("st2",), writes=("wb1",),
         aka=("rmrow2", "gsgn", "gabs", "xhalo"), tag="hscp")
    p.op("DVE", lambda e: e.scalar_tensor_tensor(
        s_st2l[0:64, 0:1040], s_scr[0:64, 0:1040], s_hsel[0:64, 1:2],
        s_scr2[0:64, 0:1040], OP.mult, OP.add),
         reads=("wb1", "c:hsel", "scr2l"), writes=("st2l",), tag="hs1")

    def expand_row(dst_ap, col_ap, rd, wr, tagp, plus1=False, waka=()):
        p.op("PE", lambda e: e.transpose(ps_x[0:1, 0:128], col_ap, s_id_f[:, :]),
             reads=tuple(rd) + ("c:id_f",), writes=("ps_x",), tag="xr" + tagp)
        p.op("DVE", lambda e: e.tensor_scalar(
            s_row[0:1, 0:128], ps_x[0:1, 0:128], 1.0 if plus1 else 0.0, None, OP.add),
             reads=("ps_x",), writes=("mhr",), aka=("rmrow", "crow", "modev"),
             tag="xc" + tagp)
        p.op("PE", lambda e: e.matmul(ps_x[0:128, 0:128], s_onesf[0:1, :],
                                      s_row[0:1, 0:128], start=True, stop=True),
             reads=("mhr", "c:onesf"), writes=("ps_x",), tag="xm" + tagp)
        p.op("DVE", lambda e: e.tensor_copy(dst_ap, ps_x[0:128, 0:128]),
             reads=("ps_x",), writes=tuple(wr), aka=tuple(waka),
             tag="xv" + tagp)

    for j in range(8):
        expand_row(s_gaexp[:, j * 128:(j + 1) * 128], s_modc[:, 16 + j:17 + j],
                   ("modc",), ("gaexp",), f"ga{j}",)
        expand_row(s_cbexp[:, j * 128:(j + 1) * 128], s_modc[:, 32 + j:33 + j],
                   ("modc",), ("cbexp",), f"cb{j}", plus1=True)
        expand_row(s_cbexp[:, 1024 + j * 128:1024 + (j + 1) * 128],
                   s_modc[:, 24 + j:25 + j], ("modc",), ("sbexp",), f"sb{j}")

    dma("POOL", s_win.ap().rearrange("p (c e) -> p c e", c=8)[:, :, 0:1024],
        d_woT.rearrange("(c p) e -> p c e", p=128), ("win",), ("woT",), "ldwoT")
    p.op("DVE", lambda e: e.tensor_tensor(
        s_win.ap().rearrange("p (c e) -> p c e", c=8)[:, :, 0:1024],
        s_win.ap().rearrange("p (c e) -> p c e", c=8)[:, :, 0:1024],
        s_gaexp.ap().unsqueeze(1).broadcast_to([128, 8, 1024]), OP.mult),
         reads=("woT", "gaexp"), writes=("woT",), tag="gafold")

    h2slots = [s_h2t.ap()[:, 0:1024], s_h2t.ap()[:, 1024:2048],
               s_dump.ap()[:, 0:1024], s_statF.ap()[:, 0:1024]]
    h2aka = [("xcb0", "xcb1", "xhat0", "xhat1"),
             ("xcb0", "xcb1", "xhat0", "xhat1"),
             ("dump", "psfm", "rsh0", "afmrm", "fexp0", "fexp1", "dch", "rbm"),
             ("statF",)]
    x2slots = [s_x2t.ap()[:, 0:1024], s_x2t.ap()[:, 1024:2048], s_gdq.ap()]
    x2aka = [("xcb0", "xcb1", "xhat0", "xhat1"),
             ("xcb0", "xcb1", "xhat0", "xhat1"), ()]
    warmup(AF.Sqrt)
    yTv = xTv
    for tl in range(NT):
        sbk2 = tl // 4
        sl = sbk2 % 2
        tt = tl % 4
        xt3 = xhv[:, tl, :].rearrange("p (h n) -> p h n", h=16)
        p.op("DVE", lambda e, tl=tl, xt3=xt3: e.tensor_tensor(
            wb1[:, 0:1024].rearrange("p (h n) -> p h n", h=16),
            t16[:, 0, tl, :].unsqueeze(2).broadcast_to([128, 16, 64]),
            xt3, OP.mult),
             reads=("t16_0", f"xh{tl}"), writes=("wb1",), aka=("rmrow2",),
             tag=f"P{tl}")
        p.op("DVE", lambda e, tl=tl: e.tensor_copy(wb1[:, 1024:1040],
                                                   t16[:, 0, tl, :]),
             reads=("t16_0",), writes=("wb1",), tag=f"Pc{tl}")
        for (n0, n1, bank) in ((0, 512, 2), (512, 1024, 3), (1024, 1040, 4)):
            p.op("PE", lambda e, n0=n0, n1=n1, bank=bank: e.matmul(
                psum[bank][0:128, 0:n1 - n0], s_utbd[:, :], wb1[:, n0:n1],
                start=True, stop=True),
                 reads=("wb1", "c:utbd"), writes=(f"psum{bank}",), tag=f"cp{tl}_{n0}")
        p.op("DVE", lambda e, tl=tl: e.tensor_tensor(
            s_den[:, 0:16], t16[:, 1, tl, :], psum[4][0:128, 0:16], OP.mult),
             reads=("t16_1", "psum4"), writes=("den",), tag=f"d1{tl}")
        for half, bank in ((0, 2), (1, 3)):
            p.op("DVE", lambda e, tl=tl, half=half, bank=bank: e.tensor_tensor(
                s_ytile[:, half * 512:(half + 1) * 512]
                .rearrange("p (h n) -> p h n", h=8),
                t16[:, 1, tl, half * 8:(half + 1) * 8].unsqueeze(2)
                .broadcast_to([128, 8, 64]),
                psum[bank][0:128, 0:512].rearrange("p (h n) -> p h n", h=8),
                OP.mult),
                 reads=("t16_1", f"psum{bank}"), writes=("ytile",),
                 aka=("cs0", "cs1", "st2", "dstmp", "ostmp"),
                 tag=f"y1{tl}_{half}")
        for (n0, n1, bank) in ((0, 512, 2), (512, 1024, 3), (1024, 1040, 4)):
            p.op("PE", lambda e, tl=tl, n0=n0, n1=n1, bank=bank: e.matmul(
                psum[bank][0:128, 0:n1 - n0],
                s_bcl[0:64, tl * 128:(tl + 1) * 128], s_st2l[0:64, n0:n1],
                start=True, stop=True),
                 reads=("st2l", "c:bcl"), writes=(f"psum{bank}",), tag=f"sb{tl}_{n0}")
        p.op("DVE", lambda e, tl=tl: e.tensor_tensor(
            s_den[:, 16:32], t16[:, 3, tl, :], psum[4][0:128, 0:16], OP.mult),
             reads=("t16_3", "psum4"), writes=("den",), tag=f"d2{tl}")
        p.op("DVE", lambda e: e.tensor_tensor(s_den[:, 0:16], s_den[:, 0:16],
                                              s_den[:, 16:32], OP.add),
             reads=("den",), writes=("den",), tag=f"d3{tl}")
        p.op("DVE", lambda e: e.reciprocal(s_den[:, 16:32], s_den[:, 0:16]),
             reads=("den",), writes=("den",), tag=f"dr{tl}")
        for half, bank in ((0, 2), (1, 3)):
            p.op("DVE", lambda e, tl=tl, half=half, bank=bank: e.tensor_tensor(
                s_scr2[:, half * 512:(half + 1) * 512]
                .rearrange("p (h n) -> p h n", h=8),
                t16[:, 3, tl, half * 8:(half + 1) * 8].unsqueeze(2)
                .broadcast_to([128, 8, 64]),
                psum[bank][0:128, 0:512].rearrange("p (h n) -> p h n", h=8),
                OP.mult),
                 reads=("t16_3", f"psum{bank}"), writes=("scr2l",),
                 tag=f"y2{tl}_{half}")
        p.op("DVE", lambda e, tl=tl: e.tensor_tensor(
            s_ytile[:, :], s_ytile[:, :], s_scr2[:, 0:1024], OP.add),
             reads=("ytile", "scr2l"), writes=("ytile",), tag=f"ya{tl}")
        p.op("DVE", lambda e, tl=tl: e.tensor_tensor(
            xhv[:, tl, :].rearrange("p (h n) -> p h n", h=16),
            s_den[:, 16:32].unsqueeze(2).broadcast_to([128, 16, 64]),
            s_ytile[:, :].rearrange("p (h n) -> p h n", h=16), OP.mult),
             reads=("den", "ytile"), writes=(f"y{tl}",), aka=(f"xh{tl}",),
             tag=f"y3{tl}")
        for _ in range(3):
            if f2_units:
                f2_units.pop(0)()
        for dc in range(8):
            pe_t128(yTv[:, sl, tt, dc, :],
                    xhv[:, tl, dc * 128:(dc + 1) * 128],
                    (f"y{tl}",), (f"yT{sl}",), f"yT{tl}_{dc}", waka=(f"xT{sl}",),
                    eveng="ACT")
        if tt == 3:
            for t2 in (0, 1):
                gtl = sbk2 * 4 + t2
                dma("SP", xstage[:, gtl % 2, :],
                    d_x[gtl * 128:(gtl + 1) * 128, :],
                    (), (f"xs{gtl % 2}",), f"ldx2_{gtl}")
            for t2 in range(4):
                gt = sbk2 * 4 + t2
                xsl = gt % 3
                hsl = gt % 4
                x2ap = x2slots[xsl]
                h2ap = h2slots[hsl]
                for half in range(2):
                    bank = half

                    def mmo2(e, sl=sl, t2=t2, half=half, bank=bank):
                        inst = None
                        for dc in range(8):
                            inst = e.matmul(psum[bank][0:128, 0:512],
                                            yTv[:, sl, t2, dc, :],
                                            s_win.ap().rearrange("p (c e) -> p c e", c=8)
                                            [:, dc, half * 512:(half + 1) * 512],
                                            start=(dc == 0), stop=(dc == 7))
                        return inst
                    p.op("PE", mmo2, reads=(f"yT{sl}", "woT"),
                         writes=(f"psum{bank}",), tag=f"op{gt}_{half}")
                    p.op("DVE", lambda e, half=half, bank=bank, x2ap=x2ap,
                         gt=gt: e.tensor_tensor(
                             x2ap[:, half * 512:(half + 1) * 512],
                             psum[bank][0:128, 0:512],
                             xstage[:, gt % 2, half * 512:(half + 1) * 512],
                             OP.add),
                         reads=(f"psum{bank}", f"xs{gt % 2}"),
                         writes=(f"x2t{xsl}",),
                         aka=x2aka[xsl], tag=f"ev2{gt}_{half}")
                dma("ACT", d_x2[gt * 128:(gt + 1) * 128, :], x2ap[:, 0:1024],
                    (f"x2t{xsl}",), (f"d_x2_{gt}",), f"stx2_{gt}")
                if t2 + 2 < 4:
                    gtl = sbk2 * 4 + t2 + 2
                    dma("SP", xstage[:, gtl % 2, :],
                        d_x[gtl * 128:(gtl + 1) * 128, :],
                        (), (f"xs{gtl % 2}",), f"ldx2_{gtl}")
                ln_tile(x2ap[:, 0:1024], 128,
                        s_ln2[:, 2 * gt:2 * gt + 1], s_ln2[:, 2 * gt + 1:2 * gt + 2],
                        (f"x2t{xsl}",), (f"ln2_{gt}",), f"L2{gt}",
                        scr_ap=h2ap, scr_wr=(f"h2t{hsl}",), scr_aka=h2aka[hsl])
                p.op("DVE", lambda e, x2ap=x2ap, h2ap=h2ap, gt=gt: e.tensor_scalar(
                    h2ap[:, 0:1024], x2ap[:, 0:1024],
                    s_ln2[:, 2 * gt:2 * gt + 1], s_ln2[:, 2 * gt + 1:2 * gt + 2],
                    OP.mult, OP.add),
                     reads=(f"x2t{xsl}", f"ln2_{gt}"), writes=(f"h2t{hsl}",),
                     aka=h2aka[hsl], tag=f"h2a{gt}")
                p.op("DVE", lambda e, h2ap=h2ap: e.scalar_tensor_tensor(
                    h2ap[:, 0:1024], h2ap[:, 0:1024], 1.0,
                    s_cbexp[:, 0:1024], OP.bypass, OP.mult),
                     reads=(f"h2t{hsl}", "cbexp"), writes=(f"h2t{hsl}",),
                     tag=f"h2b{gt}")
                p.op("DVE", lambda e, h2ap=h2ap: e.tensor_tensor(
                    h2ap[:, 0:1024], h2ap[:, 0:1024],
                    s_cbexp[:, 1024:2048], OP.add),
                     reads=(f"h2t{hsl}", "sbexp"), writes=(f"h2t{hsl}",),
                     tag=f"h2c{gt}")
                dma("POOL", d_h2[gt * 128:(gt + 1) * 128, :], h2ap[:, 0:1024],
                    (f"h2t{hsl}",), (f"d_h2_{gt}",), f"sth2_{gt}")
                p.op("DVE", lambda e, h2ap=h2ap: e.reduce_max(
                    s_gmax[:, 1:2], h2ap[:, 0:1024], AX.X,
                    apply_absolute_value=True),
                     reads=(f"h2t{hsl}",), writes=("gm1",), tag=f"am{gt}")
                p.op("DVE", lambda e: e.tensor_tensor(
                    s_gmax[:, 0:1], s_gmax[:, 0:1], s_gmax[:, 1:2], OP.max),
                     reads=("gm1", "gm0"), writes=("gm0",), tag=f"amx{gt}")

    tap("y0", xhv[:, 0, :], (128, 1024), BF16, reads=("y0",))
    tap("y17", xhv[:, 17, :], (128, 1024), BF16, reads=("y17",))
    tap("st2", s_st2[:, 0:1040], (128, 1040), reads=("st2",))
    tap("agg", s_agg[0:64, :], (64, 1056), reads=("agg",))
    tap("statF", s_statF[:, :], (128, 1056), reads=("statF",))
    tap("psf", s_psfm[0:16, :], (16, 768), reads=("rmps", "psend"))
    tap("x2a", s_x2t[:, :], (128, 2048), reads=("x2t0", "x2t1"))
    tap("gmax", s_gmax[:, 0:1], (128, 1), reads=("gm0",))

    tap("modc", s_modc[:, :], (128, 48), reads=("modc",))
    tap("capl", s_capl[:, :], (128, 24), reads=("silc", "cacol", "sacol"))
    tap("afm", s_afm[:, :], (128, 4096), reads=("afmA",))
    tap("xh0", xhv[:, 0, :], (128, 1024), BF16, reads=("xh0",))
    tap("xh5", xhv[:, 5, :], (128, 1024), BF16, reads=("xh5",))
    tap("xh31", xhv[:, 31, :], (128, 1024), BF16, reads=("xh31",))
    tap("uv", s_uv[0:2, :], (2, E), BF16, reads=("uv",))
    tap("k8", s_k8[0:2, :], (2, 1024), BF16, reads=("k8",))
    tap("xT", s_xT[:, :], (128, 8192), BF16, reads=("xT0", "xT1"))
    tap("carry", s_carry[:, :], (128, 24),
        reads=tuple(f"carry{ec}" for ec in range(8)))

    while f2_units:
        f2_units.pop(0)()
    f2_finalize()

    p.op("PE", lambda e: e.transpose(ps_x[0:1, 0:128], s_gmax[:, 0:1],
                                     s_id_f[:, :]),
         reads=("gm0", "c:id_f"), writes=("ps_x",), tag="Tgm")
    p.op("DVE", lambda e: e.tensor_copy(s_scr2[0:1, 0:128], ps_x[0:1, 0:128]),
         reads=("ps_x",), writes=("gmrow",), aka=("scr2l", "rmrow"), tag="gmc")
    p.op("DVE", lambda e: e.reduce_max(s_mcol[0:1, 8:9], s_scr2[0:1, 0:128],
                                       AX.X),
         reads=("gmrow",), writes=("gmcell",), aka=("mcol8",), tag="gmr")
    p.op("DVE", lambda e: e.memset(s_mcol[0:1, 9:10], 0.0),
         writes=("gmz",), aka=("mcol9",), tag="gmz")
    dma("POOL", d_rin, s_mcol[0:1, 8:10], ("gmcell", "gmz"), ("d_rin",), "gri")
    p.op("POOL", lambda e: e.collective_compute(
        "AllReduce", OP.max, replica_groups=GROUPS8,
        ins=[d_rin.opt()], outs=[d_rout.opt()]),
         reads=("d_rin",), writes=("d_rout",), cc_sem="cc1", tag="ccar1")
    dma("POOL", s_qcol[0:1, 0:1], d_rout[0:1, 0:1], ("d_rout",), ("gmaxg",), "gro")

    warmup(AF.Sigmoid)
    p.op("ACT", lambda e: e.activation(s_qcol[0:1, 4:6], s_mcol[0:1, 0:2],
                                       AF.Sigmoid),
         reads=("c:clipx", "c:clipd", "fence1"), writes=("sigs",), tag="sig")
    p.op("DVE", lambda e: e.tensor_tensor(s_qcol[0:1, 2:3], s_qcol[0:1, 4:5],
                                          s_qcol[0:1, 0:1], OP.mult),
         reads=("sigs", "gmaxg"), writes=("s1c",), tag="s1a")
    p.op("DVE", lambda e: e.tensor_scalar(s_qcol[0:1, 2:3], s_qcol[0:1, 2:3],
                                          1.0 / 127.0, None, OP.mult),
         reads=("s1c",), writes=("s1c",), tag="s1b")
    p.op("DVE", lambda e: e.reciprocal(s_qcol[0:1, 3:4], s_qcol[0:1, 2:3]),
         reads=("s1c",), writes=("inv1",), tag="s1r")
    p.op("DVE", lambda e: e.tensor_scalar(s_qcol[0:1, 6:7] if False else
                                          s_mcol[0:1, 8:9], s_qcol[0:1, 2:3],
                                          1.0 / 127.0, None, OP.mult),
         reads=("s1c",), writes=("s1d",), aka=("gmcell",), tag="s1d")

    def expand_cell(dst_col, src_cell, rd, wr, tagp):
        p.op("PE", lambda e: e.matmul(ps_x[0:128, 0:1], s_onesf[0:1, :],
                                      src_cell, start=True, stop=True),
             reads=tuple(rd) + ("c:onesf",), writes=("ps_x",), tag="xm" + tagp)
        p.op("DVE", lambda e: e.tensor_copy(dst_col, ps_x[0:128, 0:1]),
             reads=("ps_x",), writes=tuple(wr), tag="xc" + tagp)

    expand_cell(sgu[:, 159:160], s_qcol[0:1, 3:4], ("inv1",), ("inv1col",), "i1")
    expand_cell(sgu[:, 156:157], s_mcol[0:1, 8:9], ("s1d",), ("s1dcol",), "s1")
    p.op("DVE", lambda e: e.tensor_tensor(
        sgu[:, 104:126], sgu[:, 0:22],
        sgu[:, 156:157].broadcast_to([128, 22]), OP.mult),
         reads=("sgGU", "s1dcol"), writes=("cgcu",), tag="cg")
    p.op("DVE", lambda e: e.tensor_tensor(
        sgu[:, 126:148], sgu[:, 22:44],
        sgu[:, 156:157].broadcast_to([128, 22]), OP.mult),
         reads=("sgGU", "s1dcol", "cgcu"), writes=("cgcu",), tag="cu")

    warmup(AF.Copy)
    xqv = s_big.ap().rearrange("p (c t) -> p c t", c=8)
    fscv = s_fsc.ap().rearrange("p (r d) -> p r d", r=2)
    wb1v = s_wb1.ap().rearrange("p (r d) -> p r d", r=2)
    for gt in range(NT):
        xsl = gt % 2
        dma("POOL", fscv[:, xsl, :], d_h2[gt * 128:(gt + 1) * 128, :],
            (f"d_h2_{gt}",), (f"fh{xsl}",), f"ldh2_{gt}",
            aka=("xT1", "yT1"))
        p.op("ACT", lambda e, xsl=xsl: e.activation(
            fscv[:, xsl, :], fscv[:, xsl, :], AF.Copy, scale=sgu[:, 159:160]),
             reads=(f"fh{xsl}", "inv1col", "fence1"), writes=(f"fh{xsl}",),
             tag=f"qs{gt}")
        p.op("DVE", lambda e, xsl=xsl: e.tensor_scalar(
            fscv[:, xsl, :], fscv[:, xsl, :], MAGIC, MAGIC, OP.add, OP.subtract),
             reads=(f"fh{xsl}",), writes=(f"fh{xsl}",), tag=f"qr{gt}")
        p.op("DVE", lambda e, xsl=xsl: e.tensor_scalar(
            wb1v[:, xsl, 0:1024], fscv[:, xsl, :], 127.0, -128.0,
            OP.min, OP.max),
             reads=(f"fh{xsl}",), writes=(f"xq{xsl}",), aka=WB1_AKA,
             tag=f"qc{gt}")
        for dc in range(8):
            pe_t128(xqv[:, dc, gt * 128:(gt + 1) * 128],
                    wb1v[:, xsl, dc * 128:(dc + 1) * 128],
                    (f"xq{xsl}",), (f"xqT{dc}_{gt >> 2}",), f"xqT{gt}_{dc}",
                    waka=(f"y{dc * 4 + (gt >> 3)}", f"xh{dc * 4 + (gt >> 3)}"),
                    rot=TP4)

    warmup(AF.Silu)
    p.op("DVE", lambda e: e.memset(s_gmax[:, 2:4], 0.0), writes=("gm2",),
         tag="zg2")
    wqgv = s_wqg.ap().rearrange("p (r c e) -> p r c e", r=2, c=8)
    wquv = s_wqu.ap().rearrange("p (r c e) -> p r c e", r=2, c=8)
    rawslots = [s_wnat.ap()[:, 0:1024], s_wnat.ap()[:, 1024:2048],
                s_x4.ap()[:, 0:1024], s_x4.ap()[:, 1024:2048]]
    raw3 = [a.rearrange("p (c e) -> p c e", c=8) for a in rawslots]
    RAW_AKA = [WNAT_AKA, WNAT_AKA,
               ("x2t0", "x2t1", "h2t0", "h2t1"),
               ("x2t0", "x2t1", "h2t0", "h2t1")]
    dmpv = s_dump.ap().rearrange("p (c e) -> p c e", c=8)

    def wq_load(ic):
        for w_i, dsrc in enumerate((d_wgT, d_wuT2)):
            slot = 2 * (ic % 2) + w_i
            dma("POOL", raw3[slot][:, :, :],
                dsrc[ic * 1024:(ic + 1) * 1024, :]
                .rearrange("(c p) i -> p c i", p=128),
                (), (f"wnat{slot}",), f"lq{w_i}_{ic}", aka=RAW_AKA[slot])

    def wq_q(ic):
        wsl = ic % 2
        for w_i, (ivc, dstv, wres) in enumerate((
                (sgu[:, 60 + ic:61 + ic],
                 s_wqg.ap()[:, wsl * 1024:(wsl + 1) * 1024], f"wqg{wsl}"),
                (sgu[:, 82 + ic:83 + ic],
                 s_wqu.ap()[:, wsl * 1024:(wsl + 1) * 1024], f"wqu{wsl}"))):
            slot = 2 * (ic % 2) + w_i
            eslot = 2 * (ic % 2) + w_i
            expand_row(s_gdq[:, eslot * 128:(eslot + 1) * 128], ivc,
                       ("sgInv",), (f"ivex{eslot}",), f"ie{w_i}_{ic}",
                       waka=("x2t2",))
            p.op("DVE", lambda e, slot=slot, eslot=eslot: e.tensor_tensor(
                dmpv[:, :, :], raw3[slot][:, :, :],
                s_gdq[:, eslot * 128:(eslot + 1) * 128].unsqueeze(1)
                .broadcast_to([128, 8, 128]), OP.mult),
                 reads=(f"wnat{slot}", f"ivex{eslot}"), writes=("dump",),
                 aka=("h2t2",), tag=f"wm{w_i}_{ic}")
            p.op("DVE", lambda e, dstv=dstv: e.tensor_scalar(
                dstv, s_dump.ap(), MAGIC, MAGIC, OP.add, OP.subtract),
                 reads=("dump",), writes=(wres,), tag=f"wr{w_i}_{ic}")

    def mm_block(ic):
        wsl = ic % 2
        for tb in range(8):
            k = ic * 8 + tb
            bG = 0 if k % 2 == 0 else 2
            bU = 1 if k % 2 == 0 else 3
            h3sl = k % 2

            def mmg(e, wsl=wsl, tb=tb, bG=bG):
                inst = None
                for hc in range(8):
                    inst = e.matmul(psum[bG][0:128, 0:512], wqgv[:, wsl, hc, :],
                                    xqv[:, hc, tb * 512:(tb + 1) * 512],
                                    start=(hc == 0), stop=(hc == 7))
                return inst

            def mmu(e, wsl=wsl, tb=tb, bU=bU):
                inst = None
                for hc in range(8):
                    inst = e.matmul(psum[bU][0:128, 0:512], wquv[:, wsl, hc, :],
                                    xqv[:, hc, tb * 512:(tb + 1) * 512],
                                    start=(hc == 0), stop=(hc == 7))
                return inst

            xqres = tuple(f"xqT{hc}_{tb}" for hc in range(8))
            p.op("PE", mmg, reads=(f"wqg{wsl}",) + xqres, writes=(f"psum{bG}",),
                 tag=f"gm{ic}_{tb}")
            p.op("PE", mmu, reads=(f"wqu{wsl}",) + xqres, writes=(f"psum{bU}",),
                 tag=f"um{ic}_{tb}")
            p.op("ACT", lambda e, ic=ic, bG=bG: e.activation(
                s_fsc[:, 0:512], psum[bG][0:128, 0:512], AF.Silu,
                scale=sgu[:, 104 + ic:105 + ic]),
                 reads=(f"psum{bG}", "cgcu", "fence1"), writes=("fsil",),
                 aka=("fh0",), tag=f"sil{ic}_{tb}")
            p.op("ACT", lambda e, ic=ic, bU=bU: e.activation(
                s_fsc[:, 512:1024], psum[bU][0:128, 0:512], AF.Copy,
                scale=sgu[:, 126 + ic:127 + ic]),
                 reads=(f"psum{bU}", "cgcu", "fence1"), writes=("ftup",),
                 aka=("fh0",), tag=f"tup{ic}_{tb}")
            p.op("DVE", lambda e, h3sl=h3sl: e.tensor_tensor(
                s_fsc[:, 1024 + h3sl * 512:1024 + (h3sl + 1) * 512],
                s_fsc[:, 0:512], s_fsc[:, 512:1024], OP.mult),
                 reads=("fsil", "ftup"), writes=(f"fh3{h3sl}",),
                 aka=("fh1",), tag=f"h3{ic}_{tb}")
            p.op("DVE", lambda e, h3sl=h3sl: e.reduce_max(
                s_gmax[:, 3:4],
                s_fsc[:, 1024 + h3sl * 512:1024 + (h3sl + 1) * 512], AX.X,
                apply_absolute_value=True),
                 reads=(f"fh3{h3sl}",), writes=("gm3",), tag=f"hr{ic}_{tb}")
            p.op("DVE", lambda e: e.tensor_tensor(
                s_gmax[:, 2:3], s_gmax[:, 2:3], s_gmax[:, 3:4], OP.max),
                 reads=("gm3", "gm2"), writes=("gm2",), tag=f"hx{ic}_{tb}")
            dma("POOL", d_h3[ic * 128:(ic + 1) * 128, tb * 512:(tb + 1) * 512],
                s_fsc[:, 1024 + h3sl * 512:1024 + (h3sl + 1) * 512],
                (f"fh3{h3sl}",), (f"d_h3_{ic}_{tb}",), f"sh3_{ic}_{tb}")

    for ic in range(NIC + 2):
        if ic < NIC:
            wq_load(ic)
        if 1 <= ic < NIC + 1:
            wq_q(ic - 1)
        if ic >= 2:
            mm_block(ic - 2)

    p.op("PE", lambda e: e.transpose(ps_x[0:1, 0:128], s_gmax[:, 2:3],
                                     s_id_f[:, :]),
         reads=("gm2", "c:id_f"), writes=("ps_x",), tag="Tg2")
    p.op("DVE", lambda e: e.tensor_copy(s_scr2[0:1, 0:128], ps_x[0:1, 0:128]),
         reads=("ps_x",), writes=("gmrow",), tag="g2c")
    p.op("DVE", lambda e: e.reduce_max(s_mcol[0:1, 6:7], s_scr2[0:1, 0:128],
                                       AX.X),
         reads=("gmrow",), writes=("gm2cell",), tag="g2r")
    p.op("DVE", lambda e: e.memset(s_mcol[0:1, 7:8], 0.0),
         writes=("gm2z",), tag="g2z")
    dma("POOL", d_rin2, s_mcol[0:1, 6:8], ("gm2cell", "gm2z"), ("d_rin2",),
        "g2i")
    p.op("POOL", lambda e: e.collective_compute(
        "AllReduce", OP.max, replica_groups=GROUPS8,
        ins=[d_rin2.opt()], outs=[d_rout2.opt()]),
         reads=("d_rin2",), writes=("d_rout2",), cc_sem="cc2", tag="ccar2")
    dma("POOL", s_qcol[0:1, 1:2], d_rout2[0:1, 0:1], ("d_rout2",), ("gmax2g",),
        "g2o")

    for j in range(8):
        expand_row(s_dexp[:, j * 128:(j + 1) * 128], sgu[:, 52 + j:53 + j],
                   ("sgInv",), ("dexp",), f"dx{j}", waka=("cbexp", "sbexp"))
        expand_row(s_gbexp[:, j * 128:(j + 1) * 128], s_modc[:, 40 + j:41 + j],
                   ("modc",), ("gbexp",), f"gb{j}", waka=("gaexp",))
    wdqv = s_big.ap()[:, 0:NIC * 1024].rearrange("p (c e) -> p c e", c=NIC)
    for ic in range(NIC):
        slot = nslot[0] % 2
        nslot[0] += 1
        dma("POOL", s_wnat[:, slot * 1024:(slot + 1) * 1024],
            d_wdT[ic * 128:(ic + 1) * 128, :], (), (f"wnat{slot}",),
            f"lwd{ic}", aka=WNAT_AKA)
        p.op("DVE", lambda e, slot=slot: e.tensor_tensor(
            s_dump.ap(), s_wnat[:, slot * 1024:(slot + 1) * 1024],
            s_dexp.ap(), OP.mult),
             reads=(f"wnat{slot}", "dexp"), writes=("dump",),
             aka=("h2t2",), tag=f"wdm{ic}")
        p.op("DVE", lambda e, ic=ic: e.tensor_scalar(
            wdqv[:, ic, :], s_dump.ap(), MAGIC, MAGIC, OP.add, OP.subtract),
             reads=("dump",), writes=(f"wdq{ic}",),
             aka=tuple(f"xqT{ic // 4}_{sb}" for sb in range(8)),
             tag=f"wdr{ic}")

    p.op("DVE", lambda e: e.tensor_tensor(s_qcol[0:1, 2:3], s_qcol[0:1, 5:6],
                                          s_qcol[0:1, 1:2], OP.mult),
         reads=("sigs", "gmax2g", "s1c", "inv1"), writes=("s2c",), tag="s2a")
    p.op("DVE", lambda e: e.tensor_scalar(s_qcol[0:1, 2:3], s_qcol[0:1, 2:3],
                                          1.0 / 127.0, None, OP.mult),
         reads=("s2c",), writes=("s2c",), tag="s2b")
    p.op("DVE", lambda e: e.reciprocal(s_qcol[0:1, 3:4], s_qcol[0:1, 2:3]),
         reads=("s2c", "inv1col"), writes=("inv2",), tag="s2r")
    expand_cell(sgu[:, 158:159], s_qcol[0:1, 3:4], ("inv2",), ("inv2col",), "i2")
    expand_cell(sgu[:, 157:158], s_qcol[0:1, 2:3], ("s2c",), ("s2col",), "s2")
    for j in range(8):
        expand_row(s_gdq[:, j * 128:(j + 1) * 128], sgu[:, 148 + j:149 + j],
                   ("sgDdq",), ("gdq",), f"gq{j}",
                   waka=("ivex0", "ivex1", "ivex2", "ivex3"))
    p.op("DVE", lambda e: e.tensor_tensor(s_gdq.ap(), s_gdq.ap(),
                                          s_gbexp.ap(), OP.mult),
         reads=("gdq", "gbexp", "ivex0", "ivex1", "ivex2", "ivex3"),
         writes=("gdq",), tag="gdm")
    p.op("DVE", lambda e: e.tensor_scalar(s_gdq.ap(), s_gdq.ap(),
                                          sgu[:, 157:158], None, OP.mult),
         reads=("gdq", "s2col"), writes=("gdq",), tag="gds")

    fscv2 = s_fsc.ap().rearrange("p (s e) -> p s e", s=4)
    h3q2 = s_wqg.ap().rearrange("p (s e) -> p s e", s=4)
    x4v = s_x4.ap().rearrange("p (s e) -> p s e", s=4)
    banks = [psum[0], psum[1], psum[2], psum[3], psum[4], ps_x, ps_trA, ps_trB]
    bres = ["psum0", "psum1", "psum2", "psum3", "psum4", "ps_x", "ps_tr0",
            "ps_tr1"]
    otbuf = [s_ytile.ap(), s_dump.ap()[:, 0:1024]]
    otaka = [("cs0", "cs1", "ytile", "dstmp", "st2"),
             ("dump", "psfm", "ostmp", "st2")]
    for g in range(8):
        for t4 in range(4):
            t = g * 4 + t4
            dma("POOL", x4v[:, t4, :], d_x2[t * 128:(t + 1) * 128, :],
                (f"d_x2_{t}",), (f"xs{t4}",), f"lx2_{t}",
                aka=("x2t0", "x2t1", "h2t0", "h2t1", "wnat2", "wnat3"))
        for ic in range(NIC):
            fsl = ic % 4
            qsl = ic % 4
            fres = f"f6_{fsl}"
            qres = f"h3w{qsl}"
            dma("SP", fscv2[:, fsl, :],
                d_h3[ic * 128:(ic + 1) * 128, g * 512:(g + 1) * 512],
                (f"d_h3_{ic}_{g}",), (fres,),
                f"lh3_{g}_{ic}",
                aka=("fsil", "ftup", "fh30", "fh31", "fh0", "fh1"))
            p.op("ACT", lambda e, fsl=fsl: e.activation(
                fscv2[:, fsl, :], fscv2[:, fsl, :], AF.Copy,
                scale=sgu[:, 158:159]),
                 reads=(fres, "inv2col", "fence1"), writes=(fres,),
                 tag=f"q2s{g}_{ic}")
            p.op("DVE", lambda e, fsl=fsl: e.tensor_scalar(
                fscv2[:, fsl, :], fscv2[:, fsl, :], 127.0, -128.0,
                OP.min, OP.max),
                 reads=(fres,), writes=(fres,), tag=f"q2c{g}_{ic}")
            p.op("DVE", lambda e, fsl=fsl, qsl=qsl: e.tensor_scalar(
                h3q2[:, qsl, :], fscv2[:, fsl, :], MAGIC, MAGIC,
                OP.add, OP.subtract),
                 reads=(fres,), writes=(qres,),
                 aka=("wqg0", "wqg1"), tag=f"q2r{g}_{ic}")
            for t4 in range(4):
                for half in range(2):
                    bi = t4 * 2 + half
                    p.op("PE", lambda e, qsl=qsl, ic=ic, t4=t4, half=half,
                         bi=bi: e.matmul(
                             banks[bi][0:128, 0:512],
                             h3q2[:, qsl, t4 * 128:(t4 + 1) * 128],
                             wdqv[:, ic, half * 512:(half + 1) * 512],
                             start=(ic == 0), stop=(ic == NIC - 1)),
                         reads=(qres, f"wdq{ic}"), writes=(bres[bi],),
                         tag=f"dm{g}_{ic}_{bi}")
        for t4 in range(4):
            t = g * 4 + t4
            osl = t4 % 2
            for half in range(2):
                bi = t4 * 2 + half
                p.op("DVE", lambda e, bi=bi, half=half, osl=osl:
                     e.scalar_tensor_tensor(
                         otbuf[osl][:, half * 512:(half + 1) * 512],
                         banks[bi][0:128, 0:512], 1.0,
                         s_gdq[:, half * 512:(half + 1) * 512],
                         OP.bypass, OP.mult),
                     reads=(bres[bi], "gdq"), writes=(f"ot{osl}",),
                     aka=otaka[osl], tag=f"oe{t}_{half}")
                p.op("DVE", lambda e, half=half, t4=t4, osl=osl:
                     e.tensor_tensor(
                         otbuf[osl][:, half * 512:(half + 1) * 512],
                         otbuf[osl][:, half * 512:(half + 1) * 512],
                         x4v[:, t4, half * 512:(half + 1) * 512], OP.add),
                     reads=(f"ot{osl}", f"xs{t4}"), writes=(f"ot{osl}",),
                     tag=f"or{t}_{half}")
            dma("POOL", d_out[t * 128:(t + 1) * 128, :], otbuf[osl][:, 0:1024],
                (f"ot{osl}",), (f"ow{t % 4}",), f"sto{t}")

    tap("sgu", sgu[:, :], (128, 160), reads=("sgGU", "sgD", "sgInv", "sgDdq",
                                             "cgcu", "inv1col"))
    tap("qc", s_qcol[0:1, 0:6], (1, 6), reads=("s1c", "inv1", "sigs", "gmaxg",
                                               "s2c", "inv2", "gmax2g"))
    tap("xqT6", xqv[:, 6, 0:1024], (128, 1024), BF16,
        reads=("xqT6_0", "xqT6_1"))
    tap("gm2t", s_gmax[:, 2:3], (128, 1), reads=("gm2",))

    p.op("ACT", p.echo["ACT"], tag="echoF")
    p.barrier_all_on("POOL")
    p.barrier_all_on("SP")

    import contextlib
    stack = contextlib.ExitStack()
    sem_handles = {}
    sems = ["PE", "ACT", "DVE", "POOL", "cc0", "cc1", "cc2"]
    for q in QUEUES.values():
        sems += [f"{q}s{i}" for i in range(Plan.RSLOT)]
    for s in sems:
        sem_handles[s] = stack.enter_context(nc.semaphore("sem_" + s))
    blk = stack.enter_context(nc.Block())

    def run_engine(name):
        def body(h):
            for o in p.eng_ops[name]:
                for sp, val in o["waits"]:
                    h.wait_ge(sem_handles[sp], val)
                if o["emit"] is None:
                    continue
                inst = o["emit"](h)
                if o["cc"] is not None:
                    inst.then_inc(sem_handles[o["cc"]], 1)
                elif o["dma"]:
                    inst.then_inc(sem_handles[o["sem"]], 16)
                elif o.get("stamp", True):
                    inst.then_inc(sem_handles[name], 1)
        return body

    blk.tensor(run_engine("PE"))
    blk.scalar(run_engine("ACT"))
    blk.vector(run_engine("DVE"))
    blk.gpsimd(run_engine("POOL"))
    blk.sync(run_engine("SP"))
    stack.close()
    n_ops = sum(len(v) for v in p.eng_ops.values())
    return nc, tap_d, n_ops


def host_inputs(inputs):
    X = np.ascontiguousarray(inputs["L0_attn_x"], dtype=np.float32)
    winT = np.ascontiguousarray(np.asarray(inputs["L0_attn_in_proj_w"]).T, dtype=np.float32)
    convw = np.ascontiguousarray(
        np.asarray(inputs["L0_attn_conv_w"])[:, 0, :].reshape(8, 128, 4)
        .transpose(1, 0, 2).reshape(128, 32), dtype=np.float32)
    convb = np.ascontiguousarray(np.asarray(inputs["L0_attn_conv_b"]).reshape(8, 128).T,
                                 dtype=np.float32)
    wbase = np.ascontiguousarray(np.asarray(inputs["L0_attn_w_base"]).reshape(16, 1),
                                 dtype=np.float32)
    woT = np.ascontiguousarray(np.asarray(inputs["L0_attn_out_proj_w"]).T, dtype=np.float32)
    args = dict(
        wmod=np.ascontiguousarray(inputs["L0_wmod"], dtype=np.float32),
        winT=winT, convw=convw, convb=convb, wbase=wbase, woT=woT,
        wg=np.ascontiguousarray(inputs["L0_ffn_w_gate"], dtype=np.float32),
        wu=np.ascontiguousarray(inputs["L0_ffn_w_up"], dtype=np.float32),
        wd=np.ascontiguousarray(inputs["L0_ffn_w_down"], dtype=np.float32),
        wgTb=np.ascontiguousarray(
            np.asarray(inputs["L0_ffn_w_gate"], dtype=np.float32).T
            .reshape(1024, 22, 128).transpose(1, 0, 2).reshape(22 * 1024, 128)),
        wuTb=np.ascontiguousarray(
            np.asarray(inputs["L0_ffn_w_up"], dtype=np.float32).T
            .reshape(1024, 22, 128).transpose(1, 0, 2).reshape(22 * 1024, 128)),
        wdT=np.ascontiguousarray(np.asarray(inputs["L0_ffn_w_down"],
                                            dtype=np.float32).T),
        clipx=np.ascontiguousarray(np.asarray(inputs["L0_ffn_clip_x"]).reshape(1, 1),
                                   dtype=np.float32),
        clipd=np.ascontiguousarray(np.asarray(inputs["L0_ffn_clip_down"]).reshape(1, 1),
                                   dtype=np.float32),
    )
    Lc = np.ascontiguousarray(inputs["L0_c"], dtype=np.float32)
    zeros3 = np.zeros((3, 1024), dtype=np.float32)
    in_maps = []
    for c in range(8):
        b0, half = c >> 1, c & 1
        in_maps.append(dict(
            x4096=np.ascontiguousarray(X[b0, half * T:(half + 1) * T]),
            xhalo=(np.ascontiguousarray(X[b0, half * T - 3:half * T]) if half else zeros3),
            mask3=np.full((3, 1), 1.0 if half else 0.0, dtype=np.float32),
            crow=np.ascontiguousarray(Lc[b0:b0 + 1]),
            halfsel=np.ascontiguousarray(
                np.stack([np.full(128, 1.0 - half, np.float32),
                          np.full(128, float(half), np.float32)], axis=1)),
            **args))
    return in_maps


def kernel(**inputs):
    nc, _, _ = build_program(taps=DEBUG_TAPS)
    in_maps = host_inputs(inputs)
    res = run_bass_kernel_spmd(nc, in_maps, core_ids=list(range(8)))
    out = np.empty((4, 8192, 1024), dtype=np.float32)
    for c in range(8):
        b0, half = c >> 1, c & 1
        out[b0, half * T:(half + 1) * T] = np.asarray(res.results[c]["out"])
    return out


# version 13
# speedup vs baseline: 1.1224x; 1.0027x over previous
import numpy as np
import ml_dtypes
import concourse.bass as bass
import concourse.mybir as mybir
from concourse.bass_utils import run_bass_kernel_spmd

F32 = mybir.dt.float32
BF16 = mybir.dt.bfloat16
I32 = mybir.dt.int32
AF = mybir.ActivationFunctionType
OP = mybir.AluOpType
AX = mybir.AxisListType
BF = ml_dtypes.bfloat16

T = 4096
NT = 32
NSB = 8
E = 1040
NH = 16
IF = 2816
NIC = 22

DEBUG_TAPS = []
STAGE = 99
GROUPS8 = [[0, 1, 2, 3, 4, 5, 6, 7]]
GROUPS2 = [[0, 1], [2, 3], [4, 5], [6, 7]]

ENGINES = ("PE", "ACT", "DVE", "POOL", "SP")
QUEUES = {"SP": "dmaSP", "ACT": "dmaACT", "POOL": "dmaPOOL"}


class Plan:
    LAGGED = ()
    RSLOT = 8

    def __init__(self):
        self.eng_ops = {e: [] for e in ENGINES}
        self.seq = {e: 0 for e in ENGINES}
        self.dma_cnt = {q: 0 for q in QUEUES.values()}
        self.slot_use = {}
        self.res = {}
        self.know = {e: {} for e in ENGINES}
        self.snap = {}
        self.echo = {}

    def _resource(self, r):
        return self.res.setdefault(r, {"w": None, "r": []})

    def _need(self, eng, ev):
        sp, val = ev
        return self.know[eng].get(sp, 0) < val

    def _absorb(self, eng, ev):
        k = self.know[eng]
        sp, val = ev
        if k.get(sp, 0) < val:
            k[sp] = val
            for sp2, v2 in self.snap.get(ev, {}).items():
                if k.get(sp2, 0) < v2:
                    k[sp2] = v2

    def op(self, eng, emit, reads=(), writes=(), aka=(), dma=False, cc_sem=None, tag="", stamp=True):
        waits = []
        for r in reads:
            st = self._resource(r)
            if st["w"] is not None and self._need(eng, st["w"]):
                waits.append(st["w"])
        for w in tuple(writes) + tuple(aka):
            st = self._resource(w)
            if st["w"] is not None and self._need(eng, st["w"]):
                waits.append(st["w"])
            for ev in st["r"]:
                if self._need(eng, ev):
                    waits.append(ev)
        wd = {}
        for sp, val in waits:
            wd[sp] = max(wd.get(sp, 0), val)
        sem_name = None
        if dma and cc_sem is None:
            q = QUEUES[eng]
            slot = self.dma_cnt[q] % self.RSLOT
            self.dma_cnt[q] += 1
            sem_name = f"{q}s{slot}"
            prior = self.slot_use.get(sem_name, 0)
            if prior > 0 and self.know[eng].get(sem_name, 0) < 16 * prior:
                wd[sem_name] = max(wd.get(sem_name, 0), 16 * prior)
            self.slot_use[sem_name] = prior + 1
        for sp in list(wd):
            if sp in self.LAGGED and sp != eng:
                val = wd[sp]
                if val == self.seq[sp]:
                    self.op(sp, self.echo[sp], tag="echo")
                wd[sp] = val + 1
        for spval in wd.items():
            self._absorb(eng, spval)
        if cc_sem is not None:
            ev = (cc_sem, 1)
        elif dma:
            ev = (sem_name, 16 * self.slot_use[sem_name])
        elif stamp:
            self.seq[eng] += 1
            ev = (eng, self.seq[eng])
        else:
            ev = (eng, self.seq[eng])
        self.snap.setdefault(ev, dict(self.know[eng]))
        for r in reads:
            self._resource(r)["r"].append(ev)
        for w in writes:
            st = self._resource(w)
            st["w"] = ev
            st["r"] = []
        self.eng_ops[eng].append({"emit": emit, "waits": sorted(wd.items()),
                                  "dma": dma, "cc": cc_sem, "tag": tag,
                                  "stamp": stamp, "sem": sem_name})
        return ev

    def barrier_all_on(self, eng):
        waits = {}
        for e2 in ENGINES:
            if e2 != eng and self.seq[e2] > 0:
                waits[e2] = self.seq[e2]
        for snm, n in self.slot_use.items():
            if n > 0:
                waits[snm] = 16 * n
        final = []
        for sp, val in sorted(waits.items()):
            if self.know[eng].get(sp, 0) < val:
                final.append((sp, val))
                self._absorb(eng, (sp, val))
        self.eng_ops[eng].append({"emit": None, "waits": final, "dma": False,
                                  "cc": None, "tag": "finalwait"})


class Alloc:
    def __init__(self, nc, limit):
        self.nc, self.limit, self.cur = nc, limit, 0
        self.table = {}

    def at(self, name, rows, cols, dtype, offset=None):
        nbytes = cols * {F32: 4, BF16: 2, I32: 4}[dtype]
        if offset is None:
            offset = self.cur
            self.cur += (nbytes + 31) & ~31
            assert self.cur <= self.limit, f"SBUF over budget at {name}: {self.cur}"
        else:
            assert offset + nbytes <= self.limit
        self.table[name] = (offset, nbytes)
        return self.nc.alloc_sbuf_tensor_at(name, [rows, cols], dtype, offset=offset)

    def off(self, name):
        return self.table[name][0]


def build_program(taps=()):
    nc = bass.Bass("TRN2", target_bir_lowering=False)
    p = Plan()
    taps = set(taps)
    base = nc.SBUF_PARTITION_SIZE_BYTES - nc.sbuf_bytes_remaining
    base = (base + 31) & ~31
    al = Alloc(nc, nc.SBUF_PARTITION_SIZE_BYTES - 512)
    al.cur = base

    d_x = nc.dram_tensor("x4096", [T, 1024], F32, kind="ExternalInput").ap()
    d_xh3 = nc.dram_tensor("xhalo", [3, 1024], F32, kind="ExternalInput").ap()
    d_mask3 = nc.dram_tensor("mask3", [3, 1], F32, kind="ExternalInput").ap()
    d_crow = nc.dram_tensor("crow", [1, 1024], F32, kind="ExternalInput").ap()
    d_wmod = nc.dram_tensor("wmod", [1024, 6144], F32, kind="ExternalInput").ap()
    d_wmodb = nc.dram_tensor("wmodb", [96 * 128, 512], F32, kind="ExternalInput").ap()
    d_winT = nc.dram_tensor("winT", [1024, E], F32, kind="ExternalInput").ap()
    d_convw = nc.dram_tensor("convw", [128, 32], F32, kind="ExternalInput").ap()
    d_convb = nc.dram_tensor("convb", [128, 8], F32, kind="ExternalInput").ap()
    d_wbase = nc.dram_tensor("wbase", [16, 1], F32, kind="ExternalInput").ap()
    d_woT = nc.dram_tensor("woT", [1024, 1024], F32, kind="ExternalInput").ap()
    d_wg = nc.dram_tensor("wg", [IF, 1024], F32, kind="ExternalInput").ap()
    d_wu = nc.dram_tensor("wu", [IF, 1024], F32, kind="ExternalInput").ap()
    d_wd = nc.dram_tensor("wd", [1024, IF], F32, kind="ExternalInput").ap()
    d_clipx = nc.dram_tensor("clipx", [1, 1], F32, kind="ExternalInput").ap()
    d_clipd = nc.dram_tensor("clipd", [1, 1], F32, kind="ExternalInput").ap()
    d_hsel = nc.dram_tensor("halfsel", [128, 2], F32, kind="ExternalInput").ap()
    d_wgT = nc.dram_tensor("wgTb", [NIC * 1024, 128], F32, kind="ExternalInput").ap()
    d_wuT = nc.dram_tensor("wuTb", [NIC * 1024, 128], F32, kind="ExternalInput").ap()
    d_wuT2 = d_wuT
    d_wdT = nc.dram_tensor("wdT", [IF, 1024], F32, kind="ExternalInput").ap()
    d_out = nc.dram_tensor("out", [T, 1024], F32, kind="ExternalOutput").ap()

    d_x2 = nc.dram_tensor("x2scr", [T, 1024], F32, kind="Internal").ap()
    d_h2 = nc.dram_tensor("h2scr", [T, 1024], F32, kind="Internal").ap()
    d_h3 = nc.dram_tensor("h3scr", [NIC * 8 * 128, 512], F32, kind="Internal").ap()
    d_gin = nc.dram_tensor("ccgin", [64, 1056], F32, kind="Internal").ap()
    d_gout = nc.dram_tensor("ccgout", [128, 1056], F32, kind="Internal").ap()
    d_rin = nc.dram_tensor("ccrin", [1, 2], F32, kind="Internal").ap()
    d_rout = nc.dram_tensor("ccrout", [1, 2], F32, kind="Internal").ap()
    d_rin2 = nc.dram_tensor("ccrin2", [1, 2], F32, kind="Internal").ap()
    d_rout2 = nc.dram_tensor("ccrout2", [1, 2], F32, kind="Internal").ap()

    idn_f = nc.inline_tensor(np.eye(128, dtype=np.float32), name="c_idf").ap()
    idn_b = nc.inline_tensor(np.eye(128, dtype=BF), name="c_idb").ap()
    ut = np.zeros((128, 128), dtype=BF)
    for b_ in range(2):
        for kk in range(64):
            ut[b_ * 64 + kk, b_ * 64 + kk:(b_ + 1) * 64] = 1
    c_utbd = nc.inline_tensor(ut, name="c_utbd").ap()
    bcl = np.zeros((64, T), dtype=BF)
    for z in range(64):
        bcl[z, z * 64:(z + 1) * 64] = 1
    c_bcl = nc.inline_tensor(bcl, name="c_bcl").ap()
    c_lts = nc.inline_tensor(
        np.ascontiguousarray(np.tri(128, 128, -1).T.astype(np.float32)),
        name="c_lts").ap()
    c_onesb = nc.inline_tensor(np.ones((1, 128), dtype=BF), name="c_onesb").ap()
    c_onesf = nc.inline_tensor(np.ones((1, 128), dtype=np.float32), name="c_onesf").ap()

    s_big = al.at("s_big", 128, NT * 1024, BF16)
    s_afm = al.at("s_afm", 128, 4096, F32)
    s_afm2 = al.at("s_afm2", 128, 4096, BF16)
    s_tok16 = al.at("s_tok16", 128, 4 * NT * 16, BF16)
    s_id_f = al.at("s_id_f", 128, 128, F32)
    s_id_b = al.at("s_id_b", 128, 128, BF16)
    s_utbd = al.at("s_utbd", 128, 128, BF16)
    s_bcl = al.at("s_bcl", 64, T, BF16)
    s_lts = al.at("s_lts", 128, 128, F32)
    s_onesb = al.at("s_onesb", 1, 128, BF16)
    s_onesf = al.at("s_onesf", 1, 128, F32)
    s_cw = al.at("s_cw", 128, 32, F32)
    s_cb = al.at("s_cb", 128, 8, F32)
    s_wb = al.at("s_wb", 16, 1, F32)
    s_modc = al.at("s_modc", 128, 48, F32)
    s_mcol = al.at("s_mcol", 128, 24, F32)
    s_stats = al.at("s_stats", 128, 16, F32)
    s_ln2 = al.at("s_ln2", 128, 2 * NT, F32)
    s_carry = al.at("s_carry", 128, 24, F32)
    s_gmax = al.at("s_gmax", 128, 4, F32)
    s_qcol = al.at("s_qcol", 128, 8, F32)
    s_capl = al.at("s_capl", 128, 24, F32)
    s_uvw = al.at("s_uvw", 128, 16, BF16)
    s_uv = al.at("s_uv", 2, E, BF16)
    s_k8 = al.at("s_k8", 2, 1024, BF16)
    s_k8h = al.at("s_k8h", 2, 4, BF16)
    s_xTh = al.at("s_xTh", 128, 24, BF16)
    s_hb3 = al.at("s_hb3", 3, 1024, BF16)
    s_atot = al.at("s_atot", 128, 128, F32)
    s_row = al.at("s_row", 1, 1024, F32)
    s_scr = al.at("s_scr", 128, 1056, F32)
    s_scr2 = al.at("s_scr2", 128, 1056, F32)
    s_agg = al.at("s_agg", 64, 1056, F32)
    s_aggsl = al.at("s_aggsl", 2, 1056, F32, offset=al.off("s_scr2"))
    s_st2l = al.at("s_st2l", 64, 1056, BF16)
    s_sgu = al.at("s_sgu", 128, 160, F32)
    s_i2T = al.at("s_i2T", 2, 128, BF16)
    s_wb1 = al.at("s_wb1", 128, 2048, BF16, offset=al.off("s_scr"))
    s_psc = al.at("s_psc", 128, 16, F32)
    s_den = al.at("s_den", 128, 64, F32)
    s_hsel = al.at("s_hsel", 128, 2, F32)
    s_ind = al.at("s_ind", 128, 2, BF16)
    s_xstage = al.at("s_xstage", 128, 2048, F32)
    s_xT = al.at("s_xT", 128, 2 * 8 * 512, BF16)
    s_hc = al.at("s_hc", 128, 2 * 4096, BF16)
    s_win = al.at("s_win", 128, 8 * E, BF16)
    s_wmodc = al.at("s_wmodc", 128, 2048, F32)
    s_statF = al.at("s_statF", 128, 1056, F32, offset=al.off("s_wmodc") + 0)
    s_xiext = al.at("s_xiext", 128, 2 * 515 + 26, F32)
    s_convscr = al.at("s_convscr", 128, 2 * 512, F32)
    s_st2 = al.at("s_st2", 128, 1056, F32, offset=al.off("s_convscr"))
    s_ytile = al.at("s_ytile", 128, 1024, F32, offset=al.off("s_convscr"))
    s_dump = al.at("s_dump", 128, 1024, F32)
    s_i32 = al.at("s_i32", 128, 1024, I32, offset=al.off("s_dump"))
    s_psfm = al.at("s_psfm", 16, 768, F32, offset=al.off("s_dump"))
    s_dch = al.at("s_dch", 128, 128, F32, offset=al.off("s_dump") + 3072)
    s_rb = al.at("s_rb", 128, 128, F32, offset=al.off("s_dump") + 3584)
    s_x2t = al.at("s_x2t", 128, 2 * 1024, F32, offset=al.off("s_hc"))
    s_h2t = al.at("s_h2t", 128, 2 * 1024, F32, offset=al.off("s_hc") + 8192)
    s_cbexp = al.at("s_cbexp", 128, 2 * 1024, F32, offset=al.off("s_afm"))
    s_gaexp = al.at("s_gaexp", 128, 1024, F32, offset=al.off("s_afm") + 8192)
    s_gdq = al.at("s_gdq", 128, 1024, F32, offset=al.off("s_afm") + 12288)
    s_wqg = al.at("s_wqg", 128, 2048, BF16, offset=al.off("s_xT"))
    s_wqu = al.at("s_wqu", 128, 2048, BF16, offset=al.off("s_xT") + 4096)
    s_fsc = al.at("s_fsc", 128, 2048, F32, offset=al.off("s_xT") + 8192)
    s_wnat = al.at("s_wnat", 128, 2048, F32, offset=al.off("s_wmodc") + 4224)
    s_h3q = al.at("s_h3q", 128, NIC * 256, BF16, offset=al.off("s_hc"))
    s_gbexp = al.at("s_gbexp", 128, 1024, F32, offset=al.off("s_afm") + 8192)
    s_dexp = al.at("s_dexp", 128, 1024, F32, offset=al.off("s_afm"))
    s_x4 = al.at("s_x4", 128, 4096, F32, offset=al.off("s_hc"))

    SB_USED = al.cur
    print(f"[k] sbuf {SB_USED} bytes/partition")
    p.echo["ACT"] = lambda e: e.copy(s_qcol[0:1, 7:8], s_qcol[0:1, 6:7])

    def warmup(func):
        p.op("ACT", lambda e, f=func: e.activation(s_qcol[0:1, 7:8],
                                                   s_qcol[0:1, 6:7], f),
             reads=("fence0",), writes=("fence1",), tag=f"warm{func}")

    psum = [nc.alloc_psum_tensor(f"psA{i}", [128, 512], F32) for i in range(5)]
    ps_trA = nc.alloc_psum_tensor("ps_trA", [128, 512], F32)
    ps_trB = nc.alloc_psum_tensor("ps_trB", [128, 512], F32)
    ps_x = nc.alloc_psum_tensor("ps_x", [128, 512], F32)
    ps_tr = ps_trA

    def dma(eng, out, in_, reads, writes, tag="", transpose=False, aka=()):
        return p.op(eng, (lambda e, o=out, i=in_, tr=transpose:
                          e.dma_start(out=o, in_=i, transpose=tr)),
                    reads=reads, writes=writes, aka=aka, dma=True, tag=tag)

    def trans_f32(dst_ap, src_ap, rd, wr, tag, waka=()):
        r, c = src_ap.shape
        pa = ps_x[0:c, 0:r]
        p.op("PE", lambda e: e.transpose(pa, src_ap, s_id_f[0:r, 0:r]),
             reads=tuple(rd) + ("c:id_f",), writes=("ps_x",), tag="T" + tag)
        p.op("DVE", lambda e: e.tensor_copy(dst_ap, pa),
             reads=("ps_x",), writes=tuple(wr), aka=waka, tag="Tc" + tag)

    def trans_b16(dst_ap, src_ap, rd, wr, tag, waka=()):
        r, c = src_ap.shape
        pa = ps_trA.ap().bitcast(BF16)[0:c, 0:r]
        p.op("PE", lambda e: e.transpose(pa, src_ap, s_id_b[0:r, 0:r]),
             reads=tuple(rd) + ("c:id_b",), writes=("ps_tr0",), tag="T" + tag)
        p.op("DVE", lambda e: e.tensor_copy(dst_ap, pa),
             reads=("ps_tr0",), writes=tuple(wr), aka=waka, tag="Tc" + tag)

    tp_flip = [0]

    TP2 = [0, ((ps_trA, "ps_tr0"), (ps_trB, "ps_tr1"))]

    def pe_t128(dst_ap, src_ap, rd, wr, tag, waka=(), eveng="DVE", rot=None):
        if rot is None:
            rot = TP2
        bank, res = rot[1][rot[0] % len(rot[1])]
        rot[0] += 1
        pa = bank.ap().bitcast(BF16)[0:128, 0:128]
        p.op("PE", lambda e: e.transpose(pa, src_ap, s_id_b[:, :]),
             reads=tuple(rd) + ("c:id_b",), writes=(res,), tag="T" + tag)
        if eveng == "ACT":
            p.op("ACT", lambda e: e.activation(dst_ap, pa, AF.Copy),
                 reads=(res, "fence1"), writes=tuple(wr), aka=waka,
                 tag="Tc" + tag)
        else:
            p.op("DVE", lambda e: e.tensor_copy(dst_ap, pa),
                 reads=(res,), writes=tuple(wr), aka=waka, tag="Tc" + tag)

    tap_d = {}

    def tap(name, src_ap, shape, dtype=F32, reads=()):
        if name not in taps:
            return
        dt_ap = nc.dram_tensor("tap_" + name, list(shape), dtype, kind="ExternalOutput").ap()
        tap_d[name] = True
        dma("SP", dt_ap, src_ap, reads, (f"tap_{name}",), "tap" + name)

    for dst, src, res in [
        (s_id_f[:, :], idn_f, "c:id_f"), (s_id_b[:, :], idn_b, "c:id_b"),
        (s_utbd[:, :], c_utbd, "c:utbd"), (s_bcl[:, :], c_bcl, "c:bcl"),
        (s_lts[:, :], c_lts, "c:lts"), (s_onesb[:, :], c_onesb, "c:onesb"),
        (s_onesf[:, :], c_onesf, "c:onesf"), (s_cw[:, :], d_convw, "c:cw"),
        (s_cb[:, :], d_convb, "c:cb"), (s_wb[:, :], d_wbase, "c:wb"),
        (s_row[0:1, :], d_crow, "crow"),
        (s_mcol[0:1, 0:1], d_clipx, "c:clipx"), (s_mcol[0:1, 1:2], d_clipd, "c:clipd"),
    ]:
        dma("SP", dst, src, (), (res,), "ld")
    dma("POOL", s_win.ap().rearrange("p (c e) -> p c e", c=8),
        d_winT.rearrange("(c p) e -> p c e", p=128), (), ("win",), "ldwin")
    winv = s_win.ap().rearrange("p (c e) -> p c e", c=8)
    i2t = np.zeros((2, 128), dtype=BF)
    i2t[0, :64] = 1; i2t[1, 64:] = 1
    c_i2t = nc.inline_tensor(i2t, name="c_i2t").ap()
    dma("SP", s_i2T[:, :], c_i2t, (), ("c:i2t",), "ldi2t")
    ind2a = np.zeros((128, 2), dtype=BF)
    ind2a[:64, 0] = 1; ind2a[64:, 1] = 1
    c_ind = nc.inline_tensor(ind2a, name="c_ind").ap()
    dma("SP", s_ind[:, :], c_ind, (), ("c:ind",), "ldind")
    dma("SP", s_hsel[:, :], d_hsel, (), ("c:hsel",), "ldhsel")
    p.op("DVE", lambda e: e.memset(s_gmax[:, 0:2], 0.0), writes=("gm0",), tag="zgm")

    p.op("DVE", lambda e: e.memset(s_qcol[0:1, 6:7], 1.0), writes=("fence0",), tag="fz")
    p.op("DVE", lambda e: e.memset(s_afm[:, :], 0.0), writes=("afmA",), tag="zafm")
    warmup(AF.Silu)
    p.op("ACT", lambda e: e.activation(s_row[0:1, :], s_row[0:1, :], AF.Silu),
         reads=("crow",), writes=("crow",), tag="siluc")
    for k in range(8):
        trans_f32(s_capl[:, 16 + k:17 + k], s_row[0:1, k * 128:(k + 1) * 128],
                  ("crow",), ("silc",), f"sc{k}")
    wmv = s_wmodc.ap().rearrange("p (s e) -> p s e", s=4)

    def mod_tr(ob):
        for q4 in range(4):
            col = ob * 4 + q4
            trans_f32(s_modc[:, col:col + 1],
                      s_scr2[0:1, (ob % 2) * 512 + q4 * 128:
                             (ob % 2) * 512 + (q4 + 1) * 128],
                      (f"modev{ob % 2}",), ("modc",), f"mt{ob}_{q4}")

    for ob in range(12):
        bres = f"psum{2 + ob % 2}"
        for k in range(8):
            slot = (ob * 8 + k) % 4
            dma("POOL", wmv[:, slot, :],
                d_wmodb[(k * 12 + ob) * 128:(k * 12 + ob + 1) * 128, :],
                (), (f"wm{slot}",), f"ldwm{ob}_{k}")
            p.op("PE", lambda e, k=k, slot=slot, ob=ob: e.matmul(
                psum[2 + ob % 2][0:1, 0:512], s_capl[:, 16 + k:17 + k],
                wmv[:, slot, :], start=(k == 0), stop=(k == 7)),
                 reads=(f"wm{slot}", "silc"), writes=(bres,), tag=f"mm{ob}_{k}")
        p.op("DVE", lambda e, ob=ob: e.tensor_copy(
            s_scr2[0:1, (ob % 2) * 512:(ob % 2) * 512 + 512],
            psum[2 + ob % 2][0:1, 0:512]),
             reads=(bres,), writes=(f"modev{ob % 2}",), tag=f"me{ob}")
        if ob >= 1:
            mod_tr(ob - 1)
    mod_tr(11)

    def g_xform(c0, tagp):
        p.op("DVE", lambda e: e.tensor_scalar(s_scr[:, 0:8], s_modc[:, c0:c0 + 8],
                                              0.0, None, OP.is_ge),
             reads=("modc",), writes=("gsgn",), tag="gs" + tagp)
        p.op("DVE", lambda e: e.tensor_scalar(s_scr[:, 0:8], s_scr[:, 0:8], 2.0, -1.0,
                                              OP.mult, OP.add),
             reads=("gsgn",), writes=("gsgn",), tag="g2" + tagp)
        p.op("ACT", lambda e: e.activation(s_scr[:, 8:16], s_modc[:, c0:c0 + 8], AF.Abs),
             reads=("modc",), writes=("gabs",), tag="gb" + tagp)
        p.op("DVE", lambda e: e.tensor_scalar(s_scr[:, 8:16], s_scr[:, 8:16], 0.25,
                                              None, OP.max),
             reads=("gabs",), writes=("gabs",), tag="gm" + tagp)
        p.op("DVE", lambda e: e.tensor_tensor(s_modc[:, c0:c0 + 8], s_scr[:, 0:8],
                                              s_scr[:, 8:16], OP.mult),
             reads=("gsgn", "gabs"), writes=("modc",), tag="gf" + tagp)

    if STAGE >= 2:
        g_xform(16, "a")
        g_xform(40, "b")

    if STAGE < 2:
        p.op("DVE", lambda e: e.memset(s_capl[:, 0:16], 0.01), writes=("cacol", "sacol"), tag="capz")
    p.op("DVE", lambda e: e.tensor_scalar(s_capl[:, 0:8], s_modc[:, 8:16], 1.0, None, OP.add),
         reads=("modc",), writes=("cacol",), tag="cap1") if STAGE >= 2 else None
    p.op("DVE", lambda e: e.tensor_copy(s_capl[:, 8:16], s_modc[:, 0:8]),
         reads=("modc",), writes=("sacol",), tag="sacp") if STAGE >= 2 else None
    p.op("DVE", lambda e: e.tensor_copy(bass.AP(s_uvw, 0, [[16, 128], [2, 8]]),
                                        s_capl[:, 0:8]),
         reads=("cacol",), writes=("uvw",), tag="uvc")
    p.op("DVE", lambda e: e.tensor_copy(bass.AP(s_uvw, 1, [[16, 128], [2, 8]]),
                                        s_capl[:, 8:16]),
         reads=("sacol",), writes=("uvw",), tag="uvs")

    for (n0, n1) in ((0, 512), (512, 1024), (1024, E)):
        def mm_uv(e, n0=n0, n1=n1):
            inst = None
            for dc in range(8):
                inst = e.matmul(ps_x[0:2, 0:n1 - n0], s_uvw[:, dc * 2:(dc + 1) * 2],
                                winv[:, dc, n0:n1], start=(dc == 0), stop=(dc == 7))
            return inst
        p.op("PE", mm_uv, reads=("win", "uvw"), writes=("ps_x",), tag=f"uv{n0}")
        p.op("DVE", lambda e, n0=n0, n1=n1: e.tensor_copy(s_uv[0:2, n0:n1],
                                                          ps_x[0:2, 0:n1 - n0]),
             reads=("ps_x",), writes=("uv",), tag=f"uvc{n0}")

    for dc in range(8):
        p.op("DVE", lambda e, dc=dc: e.tensor_scalar(
            winv[:, dc, :], winv[:, dc, :], s_capl[:, dc:dc + 1], None, OP.mult),
             reads=("win", "cacol"), writes=("win",), tag=f"fold{dc}")

    xstage = s_xstage.ap().rearrange("p (r d) -> p r d", r=2)
    xhat = s_hc.ap().rearrange("p (r d) -> p r d", r=2)
    xcbv = s_hc.ap().rearrange("p (r tt c f) -> p r tt c f", r=2, tt=4, c=8)
    xTv = s_xT.ap().rearrange("p (r tt c f) -> p r tt c f", r=2, tt=4, c=8)
    hcfl = s_hc.ap().rearrange("p (r n) -> p r n", r=2)
    xTfl = s_xT.ap().rearrange("p (r n) -> p r n", r=2)
    xhv = s_big.ap().rearrange("p (t d) -> p t d", t=NT)
    TP4 = [0, ((ps_trA, "ps_tr0"), (ps_trB, "ps_tr1"),
               (psum[2], "psum2"), (psum[3], "psum3"))]

    def ln_tile(src_ap, nrows, rstd_ap, mh_ap, rdres, stat_wr, tag,
                out_ap=None, owr=(), oaka=(), scr_ap=None, scr_wr=None,
                scr_aka=()):
        if scr_ap is None:
            scr_ap, scr_wr, scr_aka = s_dump.ap(), ("dump",), ()
        p.op("ACT", lambda e, scr_ap=scr_ap: e.activation(
            scr_ap[0:nrows, :], src_ap, AF.Copy,
            accum_out=s_mcol[0:nrows, 3:4]),
             reads=rdres + ("fence1",), writes=tuple(scr_wr) + ("mcol3",),
             aka=scr_aka, tag="s1" + tag)
        p.op("ACT", lambda e, scr_ap=scr_ap: e.activation(
            scr_ap[0:nrows, :], src_ap, AF.Square,
            accum_out=s_mcol[0:nrows, 4:5]),
             reads=rdres, writes=tuple(scr_wr) + ("mcol4",), aka=scr_aka,
             tag="s2" + tag)
        p.op("DVE", lambda e: e.tensor_scalar(s_mcol[0:nrows, 3:4], s_mcol[0:nrows, 3:4],
                                              1.0 / 1024, None, OP.mult),
             reads=("mcol3",), writes=("mcol3",), tag="m" + tag)
        p.op("DVE", lambda e: e.scalar_tensor_tensor(
            s_mcol[0:nrows, 5:6], s_mcol[0:nrows, 3:4], 1.0, s_mcol[0:nrows, 3:4],
            OP.bypass, OP.mult),
             reads=("mcol3",), writes=("mcol5",), tag="m2" + tag)
        p.op("DVE", lambda e: e.tensor_scalar(s_mcol[0:nrows, 4:5], s_mcol[0:nrows, 4:5],
                                              1.0 / 1024, None, OP.mult),
             reads=("mcol4",), writes=("mcol4",), tag="e2" + tag)
        p.op("DVE", lambda e: e.scalar_tensor_tensor(
            s_mcol[0:nrows, 4:5], s_mcol[0:nrows, 4:5], 1e-5,
            s_mcol[0:nrows, 5:6], OP.add, OP.subtract),
             reads=("mcol4", "mcol5"), writes=("mcol4",), tag="v" + tag)
        p.op("ACT", lambda e: e.activation(s_mcol[0:nrows, 5:6], s_mcol[0:nrows, 4:5],
                                           AF.Sqrt),
             reads=("mcol4",), writes=("mcol5",), tag="l" + tag)
        p.op("DVE", lambda e: e.reciprocal(rstd_ap, s_mcol[0:nrows, 5:6]),
             reads=("mcol5",), writes=stat_wr, tag="r" + tag)
        p.op("DVE", lambda e: e.scalar_tensor_tensor(mh_ap, s_mcol[0:nrows, 3:4], -1.0,
                                                     rstd_ap, OP.mult, OP.mult),
             reads=("mcol3",) + stat_wr, writes=stat_wr, tag="mh" + tag)
        if out_ap is not None:
            p.op("DVE", lambda e: e.tensor_scalar(out_ap, src_ap, rstd_ap, None,
                                                  OP.mult),
                 reads=rdres + stat_wr, writes=owr, aka=oaka, tag="xo" + tag)

    if STAGE < 3:
        run_main = False
    else:
        run_main = True
    warmup(AF.Sqrt) if run_main else None
    if not run_main:
        pass
    run_main and dma("POOL", s_scr[0:3, 16:1040], d_xh3, (), ("xhalo",), "ldxh")
    dma("SP", s_mcol[0:3, 2:3], d_mask3, (), ("mask3",), "ldm3")
    ln_tile(s_scr[0:3, 16:1040], 3, s_mcol[0:3, 6:7], s_mcol[0:3, 7:8],
            ("xhalo",), ("hstat",), "H", out_ap=s_hb3[0:3, :], owr=("hb3",))
    p.op("DVE", lambda e: e.tensor_scalar(s_hb3[0:3, :], s_hb3[0:3, :],
                                          s_mcol[0:3, 2:3], None, OP.mult),
         reads=("hb3", "mask3"), writes=("hb3",), tag="mskh")
    p.op("DVE", lambda e: e.scalar_tensor_tensor(s_mcol[0:3, 8:9], s_mcol[0:3, 7:8], 1.0,
                                                 s_mcol[0:3, 2:3], OP.bypass, OP.mult),
         reads=("hstat", "mask3"), writes=("mcol8",), tag="mhm")
    for dcx in range(8):
        trans_b16(bass.AP(s_xTh, dcx * 3, [[24, 128], [1, 3]]),
                  s_hb3[0:3, dcx * 128:(dcx + 1) * 128], ("hb3",), ("xTh",), f"th{dcx}")
    p.op("DVE", lambda e: e.tensor_copy(s_mcol[0:3, 9:10], s_mcol[0:3, 2:3]),
         reads=("mask3",), writes=("mcol9",), tag="k8hm")
    p.op("PE", lambda e: e.transpose(ps_x[0:2, 0:3], s_mcol[0:3, 8:10], s_id_f[0:3, 0:3]),
         reads=("mcol8", "mcol9", "c:id_f"), writes=("ps_x",), tag="Tk8h")
    p.op("DVE", lambda e: e.tensor_copy(s_k8h[0:2, 0:3], ps_x[0:2, 0:3]),
         reads=("ps_x",), writes=("k8h",), tag="k8hc")
    for ec in range(8):
        def hmm(e, ec=ec):
            inst = None
            for dc in range(8):
                inst = e.matmul(ps_x[0:128, 0:3], winv[:, dc, ec * 128:(ec + 1) * 128],
                                bass.AP(s_xTh, dc * 3, [[24, 128], [1, 3]]),
                                start=(dc == 0), stop=False)
            return e.matmul(ps_x[0:128, 0:3], s_uv[0:2, ec * 128:(ec + 1) * 128],
                            s_k8h[0:2, 0:3], start=False, stop=True)
        p.op("PE", hmm, reads=("win", "xTh", "uv", "k8h"), writes=("ps_x",), tag=f"hmm{ec}")
        p.op("DVE", lambda e, ec=ec: e.tensor_copy(s_carry[:, ec * 3:(ec + 1) * 3],
                                                   ps_x[0:128, 0:3]),
             reads=("ps_x",), writes=(f"carry{ec}",), tag=f"hc{ec}")

    p.op("DVE", lambda e: e.memset(s_mcol[:, 9:10], 1.0), writes=("mcol9",), tag="onecol")

    def ph_ln(sbk):
        sl = sbk % 2
        sb4 = 8 * (sbk % 2)
        for h2 in range(2):
            tts = (2 * h2, 2 * h2 + 1)
            c0 = 16 + 2 * h2
            q0 = 20 + 2 * h2
            for tt in tts:
                gt = sbk * 4 + tt
                xsl = gt % 2
                dma("POOL", xstage[:, xsl, :],
                    d_x[gt * 128:(gt + 1) * 128, :], (), (f"xs{xsl}",),
                    f"ldx{gt}")
                src_ = xstage[:, xsl, :]
                p.op("ACT", lambda e, src_=src_, tt=tt,
                     c=16 + 2 * h2 + (tt & 1): e.activation(
                         s_dump.ap(), src_, AF.Copy,
                         accum_out=s_mcol[:, c:c + 1]),
                     reads=(f"xs{xsl}", "fence1"), writes=("dump", "mcol3"),
                     tag=f"s1t{gt}")
                p.op("ACT", lambda e, src_=src_, tt=tt,
                     c=20 + 2 * h2 + (tt & 1): e.activation(
                         s_dump.ap(), src_, AF.Square,
                         accum_out=s_mcol[:, c:c + 1]),
                     reads=(f"xs{xsl}",), writes=("dump", "mcol4"),
                     tag=f"s2t{gt}")
            rs2 = bass.AP(s_stats, sb4 + 4 * h2, [[16, 128], [2, 2]])
            mh2 = bass.AP(s_stats, sb4 + 4 * h2 + 1, [[16, 128], [2, 2]])
            p.op("DVE", lambda e, c0=c0: e.tensor_scalar(
                s_mcol[:, c0:c0 + 2], s_mcol[:, c0:c0 + 2],
                1.0 / 1024, None, OP.mult),
                 reads=("mcol3",), writes=("mcol3",), tag=f"bm{sbk}_{h2}")
            p.op("DVE", lambda e, c0=c0: e.scalar_tensor_tensor(
                s_capl[:, 16 + 2 * h2:18 + 2 * h2], s_mcol[:, c0:c0 + 2], 1.0,
                s_mcol[:, c0:c0 + 2], OP.bypass, OP.mult),
                 reads=("mcol3",), writes=("mcol5",), aka=("silc",),
                 tag=f"bm2{sbk}_{h2}")
            p.op("DVE", lambda e, q0=q0: e.tensor_scalar(
                s_mcol[:, q0:q0 + 2], s_mcol[:, q0:q0 + 2],
                1.0 / 1024, None, OP.mult),
                 reads=("mcol4",), writes=("mcol4",), tag=f"be2{sbk}_{h2}")
            p.op("DVE", lambda e, q0=q0, h2=h2: e.scalar_tensor_tensor(
                s_mcol[:, q0:q0 + 2], s_mcol[:, q0:q0 + 2], 1e-5,
                s_capl[:, 16 + 2 * h2:18 + 2 * h2], OP.add, OP.subtract),
                 reads=("mcol4", "mcol5"), writes=("mcol4",), tag=f"bv{sbk}_{h2}")
            p.op("ACT", lambda e, q0=q0, h2=h2: e.activation(
                s_capl[:, 20 + 2 * h2:22 + 2 * h2], s_mcol[:, q0:q0 + 2],
                AF.Sqrt),
                 reads=("mcol4",), writes=("mcol5",), aka=("silc",),
                 tag=f"bl{sbk}_{h2}")
            p.op("DVE", lambda e, rs2=rs2, h2=h2: e.reciprocal(
                rs2, s_capl[:, 20 + 2 * h2:22 + 2 * h2]),
                 reads=("mcol5",), writes=(f"st{sb4}",), tag=f"br{sbk}_{h2}")
            p.op("DVE", lambda e, rs2=rs2, mh2=mh2, c0=c0: e.scalar_tensor_tensor(
                mh2, s_mcol[:, c0:c0 + 2], -1.0, rs2, OP.mult, OP.mult),
                 reads=("mcol3", f"st{sb4}"), writes=(f"st{sb4}",),
                 tag=f"bh{sbk}_{h2}")
            for tt in tts:
                gt = sbk * 4 + tt
                xsl = gt % 2
                p.op("DVE", lambda e, sl=sl, tt=tt, gt=gt, xsl=xsl:
                     e.tensor_scalar(
                         xhat[:, sl, tt * 1024:(tt + 1) * 1024],
                         xstage[:, xsl, :],
                         s_stats[:, 2 * (gt % 8):2 * (gt % 8) + 1],
                         None, OP.mult),
                     reads=(f"xs{xsl}", f"st{sb4}"), writes=(f"xhat{sl}",),
                     aka=(f"xcb{sl}",), tag=f"xh8{gt}")

    def ph_k8(sbk):
        sl = sbk % 2
        sb4 = 8 * (sbk % 2)
        for tt in range(4):
            gt = sbk * 4 + tt
            mh = s_stats[:, 2 * (gt % 8) + 1:2 * (gt % 8) + 2]
            p.op("DVE", lambda e, mh=mh: e.tensor_copy(s_mcol[:, 8:9], mh),
                 reads=(f"st{sb4}",), writes=("mcol8",), tag=f"mc{gt}")
            p.op("PE", lambda e: e.transpose(ps_x[0:2, 0:128], s_mcol[:, 8:10],
                                             s_id_f[:, :]),
                 reads=("mcol8", "mcol9", "c:id_f"), writes=("ps_x",), tag=f"Tk8{gt}")
            p.op("DVE", lambda e, sl=sl, tt=tt: e.tensor_copy(
                s_k8[0:2, sl * 512 + tt * 128:sl * 512 + (tt + 1) * 128],
                ps_x[0:2, 0:128]),
                 reads=("ps_x",), writes=("k8",), tag=f"mTc{gt}")

    def ph_xt(sbk):
        sl = sbk % 2
        for tt in range(4):
            for dc in range(8):
                pe_t128(xTv[:, sl, tt, dc, :],
                        xhat[:, sl, tt * 1024 + dc * 128:tt * 1024 + (dc + 1) * 128],
                        (f"xhat{sl}",), (f"xT{sl}",), f"xT{sbk}_{tt}_{dc}",
                        rot=TP4)

    def ph_inp(sbk):
        sl = sbk % 2
        for ec in range(9):
            bank = (0, 1, 4)[ec % 3]
            M = 128 if ec < 8 else 16
            def mm_inp(e, sl=sl, ec=ec, bank=bank, M=M, sbk=sbk):
                for dc in range(8):
                    e.matmul(psum[bank][0:M, 0:512], winv[:, dc, ec * 128:ec * 128 + M],
                             xTv[:, sl, :, dc, :], start=(dc == 0), stop=False)
                return e.matmul(psum[bank][0:M, 0:512],
                                s_uv[0:2, ec * 128:ec * 128 + M],
                                s_k8[0:2, (sbk % 2) * 512:(sbk % 2 + 1) * 512],
                                start=False, stop=True)
            p.op("PE", mm_inp, reads=(f"xT{sl}", "win", "uv", "k8"),
                 writes=(f"psum{bank}",), tag=f"inp{sbk}_{ec}")
            if ec < 8:
                xi = ec % 2
                p.op("ACT", lambda e, xi=xi, bank=bank: e.copy(
                    s_xiext[:, xi * 515 + 3:(xi + 1) * 515], psum[bank][0:128, 0:512]),
                     reads=(f"psum{bank}",), writes=(f"xi{xi}",), tag=f"ev{sbk}_{ec}")
                p.op("DVE", lambda e, xi=xi, ec=ec: e.tensor_copy(
                    s_xiext[:, xi * 515:xi * 515 + 3], s_carry[:, ec * 3:(ec + 1) * 3]),
                     reads=(f"carry{ec}",), writes=(f"xi{xi}",), tag=f"ci{sbk}_{ec}")
                p.op("DVE", lambda e, xi=xi, ec=ec: e.tensor_scalar(
                    s_convscr[:, xi * 512:(xi + 1) * 512],
                    s_xiext[:, xi * 515:xi * 515 + 512],
                    s_cw[:, ec * 4:ec * 4 + 1], s_cb[:, ec:ec + 1],
                    OP.mult, OP.add),
                     reads=(f"xi{xi}", "c:cw", "c:cb"), writes=(f"cs{xi}",),
                     tag=f"c0_{sbk}_{ec}")
                for kk in range(1, 3):
                    p.op("DVE", lambda e, xi=xi, ec=ec, kk=kk: e.scalar_tensor_tensor(
                        s_convscr[:, xi * 512:(xi + 1) * 512],
                        s_xiext[:, xi * 515 + kk:xi * 515 + 512 + kk],
                        s_cw[:, ec * 4 + kk:ec * 4 + kk + 1],
                        s_convscr[:, xi * 512:(xi + 1) * 512], OP.mult, OP.add),
                         reads=(f"xi{xi}", "c:cw", f"cs{xi}"), writes=(f"cs{xi}",),
                         tag=f"c{kk}_{sbk}_{ec}")
                p.op("DVE", lambda e, xi=xi, sl=sl, ec=ec: e.scalar_tensor_tensor(
                    xcbv[:, sl, :, ec, :],
                    s_xiext[:, xi * 515 + 3:xi * 515 + 515],
                    s_cw[:, ec * 4 + 3:ec * 4 + 4],
                    s_convscr[:, xi * 512:(xi + 1) * 512], OP.mult, OP.add),
                     reads=(f"xi{xi}", "c:cw", f"cs{xi}"), writes=(f"xcb{sl}",),
                     aka=(f"xhat{sl}",), tag=f"cb_{sbk}_{ec}")
                p.op("DVE", lambda e, xi=xi, ec=ec: e.tensor_copy(
                    s_carry[:, ec * 3:(ec + 1) * 3],
                    s_xiext[:, xi * 515 + 512:xi * 515 + 515]),
                     reads=(f"xi{xi}",), writes=(f"carry{ec}",), tag=f"cu{sbk}_{ec}")
            else:
                p.op("DVE", lambda e, bank=bank, sbk=sbk: e.tensor_scalar(
                    s_afm[0:16, sbk * 512:(sbk + 1) * 512], psum[bank][0:16, 0:512],
                    s_wb[0:16, 0:1], None, OP.mult),
                     reads=(f"psum{bank}", "c:wb"), writes=("afmA",), tag=f"Aev{sbk}")

    def ph_xh(sbk):
        sl = sbk % 2
        for tt in range(4):
            gt = sbk * 4 + tt
            for dc in range(8):
                pe_t128(xhv[:, gt, dc * 128:(dc + 1) * 128],
                        xcbv[:, sl, tt, dc, :],
                        (f"xcb{sl}",), (f"xh{gt}",), f"xh{gt}_{dc}",
                        eveng="ACT", rot=TP4)

    ph_ln(0)
    ph_k8(0)
    for sbk in range(NSB):
        ph_xt(sbk)
        if sbk + 1 < NSB:
            ph_ln(sbk + 1)
        ph_inp(sbk)
        if sbk + 1 < NSB:
            ph_k8(sbk + 1)
        ph_xh(sbk)

    def pe_tr(dst_ap, src_ap, rd, wr, tag, waka=(), f32=False, eveng="DVE"):
        r, c = src_ap.shape
        b0_ = src_ap.base_partition()
        if f32:
            pa = ps_x[0:c, 0:r]
            res = "ps_x"
            ident = s_id_f[b0_:b0_ + r, b0_:b0_ + r]
            idr = "c:id_f"
        else:
            h_ = tp_flip[0]; tp_flip[0] ^= 1
            bank = ps_trA if h_ == 0 else ps_trB
            pa = bank.ap().bitcast(BF16)[0:c, 0:r]
            res = f"ps_tr{h_}"
            ident = s_id_b[b0_:b0_ + r, b0_:b0_ + r]
            idr = "c:id_b"
        p.op("PE", lambda e: e.transpose(pa, src_ap, ident),
             reads=tuple(rd) + (idr,), writes=(res,), tag="T" + tag)
        if eveng == "ACT":
            p.op("ACT", lambda e: e.activation(dst_ap, pa, AF.Copy),
                 reads=(res, "fence1"), writes=tuple(wr), aka=waka,
                 tag="Tc" + tag)
        else:
            p.op("DVE", lambda e: e.tensor_copy(dst_ap, pa),
                 reads=(res,), writes=tuple(wr), aka=waka, tag="Tc" + tag)

    afm3 = s_afm.ap().rearrange("p (c l) -> p c l", c=64)
    A_v = lambda p0: s_afm.ap().rearrange("p (c l) -> p c l", c=64)

    s_rB = [s_dump.ap()[:, 0:512], s_dump.ap()[:, 512:1024],
            s_convscr.ap()[:, 0:512], s_convscr.ap()[:, 512:1024]]
    rB3 = [b.rearrange("p (c l) -> p c l", c=8) for b in s_rB]
    RAKA = {0: ("psfm", "dump"), 1: ("dump", "dch", "rbm"),
            2: ("cs0", "cs1", "ytile"), 3: ("cs0", "cs1", "ytile")}

    def rsh_in(e):
        return e.dma_start(out=s_rB[0], in_=s_afm[0:16, :])
    p.op("SP", rsh_in, reads=("afmA",), writes=("rsh0",), aka=RAKA[0],
         dma=True, tag="rshA")

    def hs_scan2(srcb, dstb, pongb, op, res_in, res_out, tagp):
        cur = srcb
        order = [pongb, dstb, pongb, dstb, pongb, dstb]
        for i, s in enumerate((1, 2, 4, 8, 16, 32)):
            nxt = order[i]
            rin = res_in if i == 0 else f"{tagp}s{i-1}"
            rout = f"{tagp}s{i}" if i < 5 else res_out
            raka = RAKA[nxt] if i < 2 else ()
            p.op("DVE", lambda e, cur=cur, nxt=nxt, s=s: e.tensor_copy(
                rB3[nxt][:, :, 0:s], rB3[cur][:, :, 0:s]),
                 reads=(rin,), writes=(rout,), aka=raka, tag=f"{tagp}c{i}")
            p.op("DVE", lambda e, cur=cur, nxt=nxt, s=s: e.tensor_tensor(
                rB3[nxt][:, :, s:64], rB3[cur][:, :, s:64],
                rB3[cur][:, :, 0:64 - s], op),
                 reads=(rin,), writes=(rout,), tag=f"{tagp}a{i}")
            cur = nxt

    hs_scan2(0, 1, 2, OP.add, "rsh0", "afmCS", "cs")
    hs_scan2(1, 0, 2, OP.min, "afmCS", "afmrm", "rm")

    p.op("DVE", lambda e: e.tensor_copy(s_atot[:, 96:104],
                                        rB3[1][:, :, 63:64].squeeze(2)),
         reads=("afmCS",), writes=("atotR",), tag="exA")
    p.op("DVE", lambda e: e.tensor_copy(s_atot[:, 104:112],
                                        rB3[0][:, :, 63:64].squeeze(2)),
         reads=("afmrm",), writes=("fm",), tag="exF")
    p.op("DVE", lambda e: e.reduce_max(s_atot[:, 112:120], rB3[1][:, :, :],
                                       AX.X),
         reads=("afmCS",), writes=("Mc",), tag="exM")
    def rsh_at(e):
        return e.dma_start(out=s_atot[0:16, 0:64], in_=s_atot[:, 96:104])
    p.op("SP", rsh_at, reads=("atotR",), writes=("atot",), dma=True,
         tag="rshAt")

    warmup(AF.Exp)
    t16 = s_tok16.ap().rearrange("p (f t c) -> p f t c", f=4, t=NT)
    p.op("DVE", lambda e: e.tensor_tensor(
        rB3[2][:, :, :],
        s_atot[:, 104:112].unsqueeze(2).broadcast_to([128, 8, 64]),
        rB3[1][:, :, :], OP.subtract),
         reads=("afmCS", "fm"), writes=("dstmp",), aka=RAKA[2], tag="dsub")
    p.op("DVE", lambda e: e.tensor_tensor(
        rB3[3][:, :, :], rB3[1][:, :, :],
        s_atot[:, 112:120].unsqueeze(2).broadcast_to([128, 8, 64]),
        OP.subtract),
         reads=("afmCS", "Mc"), writes=("ostmp",), aka=RAKA[3], tag="osub")

    def field_to_t16(fi, bsl, emit_exp, rd, waka=()):
        p.op("ACT", emit_exp, reads=rd + ("fence1",), writes=(f"fexp{bsl}",),
             aka=waka, tag=f"fx{fi}")

        def back(e, bsl=bsl):
            return e.dma_start(out=s_afm2[0:16, :], in_=s_rB[bsl])
        p.op("POOL", back, reads=(f"fexp{bsl}",), writes=("fmcur",),
             dma=True, tag=f"fb{fi}")
        for tl in range(NT):
            pe_tr(t16[:, fi, tl, :], s_afm2[0:16, tl * 128:(tl + 1) * 128],
                  ("fmcur",), (f"t16_{fi}",), f"tf{fi}_{tl}", eveng="ACT")

    field_to_t16(2, 2, lambda e: e.activation(s_rB[2], s_rB[2], AF.Exp),
                 ("dstmp",))
    field_to_t16(3, 3, lambda e: e.activation(s_rB[3], s_rB[3], AF.Exp),
                 ("ostmp",))
    field_to_t16(0, 1, lambda e: e.activation(s_rB[1], s_rB[1], AF.Exp,
                                              scale=-1.0), ("afmCS",),
                 waka=("afmCS",))
    field_to_t16(1, 0, lambda e: e.activation(s_rB[0], s_rB[0], AF.Exp),
                 ("afmrm",), waka=("afmrm",))

    wb1 = s_wb1.ap()
    for tl in range(NT):
        xt3 = xhv[:, tl, :].rearrange("p (h n) -> p h n", h=16)
        p.op("DVE", lambda e, tl=tl, xt3=xt3: e.tensor_tensor(
            wb1[:, 0:1024].rearrange("p (h n) -> p h n", h=16),
            t16[:, 2, tl, :].unsqueeze(2).broadcast_to([128, 16, 64]),
            xt3, OP.mult),
             reads=("t16_2", f"xh{tl}"), writes=("wb1",),
             aka=("gsgn", "gabs", "xhalo"), tag=f"w1_{tl}")
        p.op("DVE", lambda e, tl=tl: e.tensor_copy(wb1[:, 1024:1040],
                                                   t16[:, 2, tl, :]),
             reads=("t16_2",), writes=("wb1",), tag=f"w1c_{tl}")
        for (n0, n1, bank) in ((0, 512, 2), (512, 1024, 3), (1024, 1040, 4)):
            p.op("PE", lambda e, n0=n0, n1=n1, bank=bank: e.matmul(
                psum[bank][0:2, 0:n1 - n0], s_ind[:, :], wb1[:, n0:n1],
                start=True, stop=True),
                 reads=("wb1", "c:ind"), writes=(f"psum{bank}",), tag=f"ag{tl}_{n0}")
            p.op("DVE", lambda e, n0=n0, n1=n1, bank=bank: e.tensor_copy(
                s_aggsl[0:2, n0:n1], psum[bank][0:2, 0:n1 - n0]),
                 reads=(f"psum{bank}",), writes=("aggsl",),
                 aka=("modev0", "modev1"), tag=f"agc{tl}_{n0}")
        dma("SP", s_agg[0:0 + 2, 0:1056] if False else
            bass.AP(s_agg, 0, [[1056, 2], [1, 1056]]) if False else s_agg[0:2, :],
            s_aggsl[0:2, :], ("aggsl",), ("agg",), f"agd{tl}") if False else None
        def agd(e, tl=tl):
            return e.dma_start(out=s_agg[2 * tl:2 * tl + 2, 0:1040],
                               in_=s_aggsl[0:2, 0:1040])
        p.op("SP", agd, reads=("aggsl",), writes=("agg",), dma=True, tag=f"agd{tl}")

    pe_tr(s_agg[0:64, 1040:1056], s_atot[0:16, 0:64], ("atot",), ("agg",),
          "atT", f32=True)

    dma("POOL", d_gin, s_agg[0:64, :], ("agg",), ("d_gin",), "gi")
    p.op("POOL", lambda e: e.collective_compute(
        "AllGather", OP.bypass, replica_groups=GROUPS2,
        ins=[d_gin.opt()], outs=[d_gout.opt()]),
         reads=("d_gin",), writes=("d_gout",), cc_sem="cc0", tag="ccag")
    dma("POOL", s_statF[:, :], d_gout, ("d_gout",), ("statF",), "go",
        aka=("wmodc", "wm0", "wm1", "wm2"))

    MAGIC = 12582912.0
    sgu = s_sgu.ap()
    WNAT_AKA = ("wmodc", "wm2", "wm3", "xi0", "xi1")
    WB1_AKA = ("wb1", "rmrow2", "gsgn", "gabs", "xhalo")

    nslot = [0]
    f2slots = [s_wnat.ap()[:, 0:1024], s_wnat.ap()[:, 1024:2048],
               s_afm2.ap().bitcast(F32)[:, 0:1024],
               s_afm2.ap().bitcast(F32)[:, 1024:2048]]
    F2AKA = [WNAT_AKA, WNAT_AKA, ("fmcur",), ("fmcur",)]

    def natload(src_ap, tag):
        slot = nslot[0] % 4
        nslot[0] += 1
        dma("SP", f2slots[slot][:, 0:src_ap.shape[1]], src_ap,
            (), (f"f2s{slot}",), tag, aka=F2AKA[slot])
        return slot

    f2_units = []

    def mkunit(fn):
        f2_units.append(fn)

    for w_i, (dsrc, cb0) in enumerate(((d_wg, 0), (d_wu, 22))):
        for ic in range(NIC):
            def u(w_i=w_i, dsrc=dsrc, cb0=cb0, ic=ic):
                slot = natload(dsrc[ic * 128:(ic + 1) * 128, :], f"nl{w_i}_{ic}")
                p.op("DVE", lambda e, slot=slot, c=cb0 + ic: e.reduce_max(
                    sgu[:, c:c + 1], f2slots[slot][:, 0:1024],
                    AX.X, apply_absolute_value=True),
                     reads=(f"f2s{slot}",), writes=("sgGU",), tag=f"rmw{w_i}_{ic}")
            mkunit(u)
    for hb in range(8):
        for c3, w in ((0, 1024), (1024, 1024), (2048, 768)):
            def u(hb=hb, c3=c3, w=w):
                slot = natload(d_wd[hb * 128:(hb + 1) * 128, c3:c3 + w],
                               f"nd{hb}_{c3}")
                if c3 == 0:
                    p.op("DVE", lambda e, slot=slot, w=w, hb=hb: e.reduce_max(
                        sgu[:, 44 + hb:45 + hb], f2slots[slot][:, 0:w],
                        AX.X, apply_absolute_value=True),
                         reads=(f"f2s{slot}",), writes=("sgD",),
                         tag=f"rmd{hb}_{c3}")
                else:
                    p.op("DVE", lambda e, slot=slot, w=w: e.reduce_max(
                        sgu[:, 156:157], f2slots[slot][:, 0:w],
                        AX.X, apply_absolute_value=True),
                         reads=(f"f2s{slot}",), writes=("sgScr",),
                         tag=f"rmd{hb}_{c3}")
                    p.op("DVE", lambda e, hb=hb: e.tensor_tensor(
                        sgu[:, 44 + hb:45 + hb], sgu[:, 44 + hb:45 + hb],
                        sgu[:, 156:157], OP.max),
                         reads=("sgD", "sgScr"), writes=("sgD",),
                         tag=f"rmx{hb}_{c3}")
            mkunit(u)

    def f2_finalize():
        p.op("DVE", lambda e: e.reciprocal(sgu[:, 60:82], sgu[:, 0:22]),
             reads=("sgGU",), writes=("sgInv",), tag="ivg")
        p.op("DVE", lambda e: e.tensor_scalar(sgu[:, 60:82], sgu[:, 60:82],
                                              127.0, None, OP.mult),
             reads=("sgInv",), writes=("sgInv",), tag="ivg2")
        p.op("DVE", lambda e: e.reciprocal(sgu[:, 82:104], sgu[:, 22:44]),
             reads=("sgGU", "sgInv"), writes=("sgInv",), tag="ivu")
        p.op("DVE", lambda e: e.tensor_scalar(sgu[:, 82:104], sgu[:, 82:104],
                                              127.0, None, OP.mult),
             reads=("sgInv",), writes=("sgInv",), tag="ivu2")
        p.op("DVE", lambda e: e.reciprocal(sgu[:, 52:60], sgu[:, 44:52]),
             reads=("sgD", "sgInv"), writes=("sgInv",), tag="ivd")
        p.op("DVE", lambda e: e.tensor_scalar(sgu[:, 52:60], sgu[:, 52:60],
                                              127.0, None, OP.mult),
             reads=("sgInv",), writes=("sgInv",), tag="ivd2")
        p.op("DVE", lambda e: e.tensor_scalar(sgu[:, 148:156], sgu[:, 44:52],
                                              1.0 / 127.0, None, OP.mult),
             reads=("sgD",), writes=("sgDdq",), tag="ddq")

    pe_tr(s_psfm[0:16, 0:128], s_statF[:, 1040:1056], ("statF",), ("psf0",),
          "atf", f32=True)
    psf = s_psfm.ap().rearrange("p (k c) -> p k c", k=6)

    def hs_line(csrc, cdst, cpong, op, rin, rout, tagp, n=7):
        cur = csrc
        order = []
        slots = [cdst, cpong]
        for i in range(n):
            order.append(slots[(n - 1 - i) % 2])
        for i in range(n):
            s = 1 << i
            nxt = order[i]
            ri = rin if i == 0 else f"{tagp}{i-1}"
            ro = rout if i == n - 1 else f"{tagp}{i}"
            p.op("DVE", lambda e, cur=cur, nxt=nxt, s=s: e.tensor_copy(
                psf[0:16, nxt, 0:s], psf[0:16, cur, 0:s]),
                 reads=(ri,), writes=(ro,), tag=f"{tagp}c{i}")
            p.op("DVE", lambda e, cur=cur, nxt=nxt, s=s: e.tensor_tensor(
                psf[0:16, nxt, s:128], psf[0:16, cur, s:128],
                psf[0:16, cur, 0:128 - s], op),
                 reads=(ri,), writes=(ro,), tag=f"{tagp}a{i}")
            cur = nxt
        assert cur == cdst, (tagp, cur)

    hs_line(0, 1, 4, OP.add, "psf0", "psend", "ps")
    hs_line(1, 2, 4, OP.min, "psend", "rmp", "rm2")
    p.op("DVE", lambda e: e.memset(psf[0:16, 3, 0:1], 0.0), writes=("rmps",), tag="rz")
    p.op("DVE", lambda e: e.tensor_copy(psf[0:16, 3, 1:128], psf[0:16, 2, 0:127]),
         reads=("rmp",), writes=("rmps",), tag="rsh")
    p.op("DVE", lambda e: e.tensor_scalar(psf[0:16, 3, :], psf[0:16, 3, :], 0.0,
                                          None, OP.min),
         reads=("rmps",), writes=("rmps",), tag="rmn")
    pe_tr(s_psc[:, 0:16], psf[0:16, 1, :], ("psend",), ("psendT",), "pst", f32=True)
    def rrow(e):
        return e.dma_start(out=s_scr2[0:1, 0:1024], in_=psf[0:8, 3, :])
    def rrow2(e):
        return e.dma_start(out=s_scr[0:1, 0:1024].bitcast(F32) if False
                           else s_scr[0:1, 0:1024], in_=psf[8:16, 3, :])
    p.op("SP", rrow, reads=("rmps",), writes=("rmrow",), aka=("scr2l", "aggsl"),
         dma=True, tag="rr1")
    p.op("SP", rrow2, reads=("rmps",), writes=("rmrow2",),
         aka=("wb1", "gsgn", "gabs", "xhalo"), dma=True, tag="rr2")

    for h in range(16):
        rsrc = s_scr2[0:1, (h % 8) * 128:(h % 8 + 1) * 128] if h < 8 else                s_scr[0:1, (h - 8) * 128:(h - 8 + 1) * 128]
        rres = "rmrow" if h < 8 else "rmrow2"
        p.op("PE", lambda e, rsrc=rsrc: e.matmul(
            ps_x[0:128, 0:128], s_onesf[0:1, :], rsrc, start=True, stop=True),
             reads=(rres, "c:onesf"), writes=("ps_x",), tag=f"rb{h}")
        p.op("DVE", lambda e, h=h: e.tensor_scalar(
            s_dch[:, :], ps_x[0:128, 0:128], s_psc[:, h:h + 1], None, OP.subtract),
             reads=("ps_x", "psendT"), writes=("dch",), tag=f"de{h}")
        p.op("DVE", lambda e: e.scalar_tensor_tensor(
            s_rb[:, :], s_dch[:, :], 1e4, s_lts[:, :], OP.add, OP.mult),
             reads=("dch", "c:lts"), writes=("rbm",), tag=f"dm{h}")
        p.op("DVE", lambda e: e.tensor_scalar(s_rb[:, :], s_rb[:, :], -1e4,
                                              None, OP.add),
             reads=("rbm",), writes=("rbm",), tag=f"dm2{h}")
        p.op("ACT", lambda e: e.activation(s_dch[:, :], s_rb[:, :], AF.Exp),
             reads=("rbm",), writes=("dch",), tag=f"dx{h}")
        p.op("PE", lambda e, h=h: e.matmul(
            psum[2][0:128, 0:64], s_dch[:, :], s_statF[:, h * 64:(h + 1) * 64],
            start=True, stop=True),
             reads=("dch", "statF"), writes=("psum2",), tag=f"s2m{h}")
        p.op("PE", lambda e, h=h: e.matmul(
            psum[3][0:128, 0:1], s_dch[:, :], s_statF[:, 1024 + h:1025 + h],
            start=True, stop=True),
             reads=("dch", "statF"), writes=("psum3",), tag=f"n2m{h}")
        p.op("DVE", lambda e, h=h: e.tensor_copy(
            s_st2[:, h * 64:(h + 1) * 64], psum[2][0:128, 0:64]),
             reads=("psum2",), writes=("st2",),
             aka=("cs0", "cs1", "dstmp", "ostmp", "fexp2", "fexp3",
                  "afmrm", "fexp0", "rsh0", "ytile"), tag=f"s2c{h}")
        p.op("DVE", lambda e, h=h: e.tensor_copy(
            s_st2[:, 1024 + h:1025 + h], psum[3][0:128, 0:1]),
             reads=("psum3",), writes=("st2",), tag=f"n2c{h}")
    p.op("DVE", lambda e: e.tensor_scalar(
        s_scr2[0:64, 0:1040], s_st2[0:64, 0:1040], s_hsel[0:64, 0:1], None, OP.mult),
         reads=("st2", "c:hsel"), writes=("scr2l",), aka=("rmrow",), tag="hs0")
    p.op("DVE", lambda e: e.tensor_copy(
        s_scr[0:64, 0:1040], s_st2[64:128, 0:1040]),
         reads=("st2",), writes=("wb1",),
         aka=("rmrow2", "gsgn", "gabs", "xhalo"), tag="hscp")
    p.op("DVE", lambda e: e.scalar_tensor_tensor(
        s_st2l[0:64, 0:1040], s_scr[0:64, 0:1040], s_hsel[0:64, 1:2],
        s_scr2[0:64, 0:1040], OP.mult, OP.add),
         reads=("wb1", "c:hsel", "scr2l"), writes=("st2l",), tag="hs1")

    def expand_row(dst_ap, col_ap, rd, wr, tagp, plus1=False, waka=()):
        p.op("PE", lambda e: e.transpose(ps_x[0:1, 0:128], col_ap, s_id_f[:, :]),
             reads=tuple(rd) + ("c:id_f",), writes=("ps_x",), tag="xr" + tagp)
        p.op("DVE", lambda e: e.tensor_scalar(
            s_row[0:1, 0:128], ps_x[0:1, 0:128], 1.0 if plus1 else 0.0, None, OP.add),
             reads=("ps_x",), writes=("mhr",), aka=("rmrow", "crow", "modev"),
             tag="xc" + tagp)
        p.op("PE", lambda e: e.matmul(ps_x[0:128, 0:128], s_onesf[0:1, :],
                                      s_row[0:1, 0:128], start=True, stop=True),
             reads=("mhr", "c:onesf"), writes=("ps_x",), tag="xm" + tagp)
        p.op("DVE", lambda e: e.tensor_copy(dst_ap, ps_x[0:128, 0:128]),
             reads=("ps_x",), writes=tuple(wr), aka=tuple(waka),
             tag="xv" + tagp)

    for j in range(8):
        expand_row(s_gaexp[:, j * 128:(j + 1) * 128], s_modc[:, 16 + j:17 + j],
                   ("modc",), ("gaexp",), f"ga{j}",)
        expand_row(s_cbexp[:, j * 128:(j + 1) * 128], s_modc[:, 32 + j:33 + j],
                   ("modc",), ("cbexp",), f"cb{j}", plus1=True)
        expand_row(s_cbexp[:, 1024 + j * 128:1024 + (j + 1) * 128],
                   s_modc[:, 24 + j:25 + j], ("modc",), ("sbexp",), f"sb{j}")

    dma("POOL", s_win.ap().rearrange("p (c e) -> p c e", c=8)[:, :, 0:1024],
        d_woT.rearrange("(c p) e -> p c e", p=128), ("win",), ("woT",), "ldwoT")
    p.op("DVE", lambda e: e.tensor_tensor(
        s_win.ap().rearrange("p (c e) -> p c e", c=8)[:, :, 0:1024],
        s_win.ap().rearrange("p (c e) -> p c e", c=8)[:, :, 0:1024],
        s_gaexp.ap().unsqueeze(1).broadcast_to([128, 8, 1024]), OP.mult),
         reads=("woT", "gaexp"), writes=("woT",), tag="gafold")

    h2slots = [s_h2t.ap()[:, 0:1024], s_h2t.ap()[:, 1024:2048],
               s_dump.ap()[:, 0:1024], s_statF.ap()[:, 0:1024]]
    h2aka = [("xcb0", "xcb1", "xhat0", "xhat1"),
             ("xcb0", "xcb1", "xhat0", "xhat1"),
             ("dump", "psfm", "rsh0", "afmrm", "fexp0", "fexp1", "dch", "rbm"),
             ("statF",)]
    x2slots = [s_x2t.ap()[:, 0:1024], s_x2t.ap()[:, 1024:2048], s_gdq.ap()]
    x2aka = [("xcb0", "xcb1", "xhat0", "xhat1"),
             ("xcb0", "xcb1", "xhat0", "xhat1"), ()]
    warmup(AF.Sqrt)
    yTv = xTv
    for tl in range(NT):
        sbk2 = tl // 4
        sl = sbk2 % 2
        tt = tl % 4
        xt3 = xhv[:, tl, :].rearrange("p (h n) -> p h n", h=16)
        p.op("DVE", lambda e, tl=tl, xt3=xt3: e.tensor_tensor(
            wb1[:, 0:1024].rearrange("p (h n) -> p h n", h=16),
            t16[:, 0, tl, :].unsqueeze(2).broadcast_to([128, 16, 64]),
            xt3, OP.mult),
             reads=("t16_0", f"xh{tl}"), writes=("wb1",), aka=("rmrow2",),
             tag=f"P{tl}")
        p.op("DVE", lambda e, tl=tl: e.tensor_copy(wb1[:, 1024:1040],
                                                   t16[:, 0, tl, :]),
             reads=("t16_0",), writes=("wb1",), tag=f"Pc{tl}")
        for (n0, n1, bank) in ((0, 512, 2), (512, 1024, 3), (1024, 1040, 4)):
            p.op("PE", lambda e, n0=n0, n1=n1, bank=bank: e.matmul(
                psum[bank][0:128, 0:n1 - n0], s_utbd[:, :], wb1[:, n0:n1],
                start=True, stop=True),
                 reads=("wb1", "c:utbd"), writes=(f"psum{bank}",), tag=f"cp{tl}_{n0}")
        p.op("DVE", lambda e, tl=tl: e.tensor_tensor(
            s_den[:, 0:16], t16[:, 1, tl, :], psum[4][0:128, 0:16], OP.mult),
             reads=("t16_1", "psum4"), writes=("den",), tag=f"d1{tl}")
        for half, bank in ((0, 2), (1, 3)):
            p.op("DVE", lambda e, tl=tl, half=half, bank=bank: e.tensor_tensor(
                s_ytile[:, half * 512:(half + 1) * 512]
                .rearrange("p (h n) -> p h n", h=8),
                t16[:, 1, tl, half * 8:(half + 1) * 8].unsqueeze(2)
                .broadcast_to([128, 8, 64]),
                psum[bank][0:128, 0:512].rearrange("p (h n) -> p h n", h=8),
                OP.mult),
                 reads=("t16_1", f"psum{bank}"), writes=("ytile",),
                 aka=("cs0", "cs1", "st2", "dstmp", "ostmp"),
                 tag=f"y1{tl}_{half}")
        for (n0, n1, bank) in ((0, 512, 2), (512, 1024, 3), (1024, 1040, 4)):
            p.op("PE", lambda e, tl=tl, n0=n0, n1=n1, bank=bank: e.matmul(
                psum[bank][0:128, 0:n1 - n0],
                s_bcl[0:64, tl * 128:(tl + 1) * 128], s_st2l[0:64, n0:n1],
                start=True, stop=True),
                 reads=("st2l", "c:bcl"), writes=(f"psum{bank}",), tag=f"sb{tl}_{n0}")
        p.op("DVE", lambda e, tl=tl: e.tensor_tensor(
            s_den[:, 16:32], t16[:, 3, tl, :], psum[4][0:128, 0:16], OP.mult),
             reads=("t16_3", "psum4"), writes=("den",), tag=f"d2{tl}")
        p.op("DVE", lambda e: e.tensor_tensor(s_den[:, 0:16], s_den[:, 0:16],
                                              s_den[:, 16:32], OP.add),
             reads=("den",), writes=("den",), tag=f"d3{tl}")
        p.op("DVE", lambda e: e.reciprocal(s_den[:, 16:32], s_den[:, 0:16]),
             reads=("den",), writes=("den",), tag=f"dr{tl}")
        for half, bank in ((0, 2), (1, 3)):
            p.op("DVE", lambda e, tl=tl, half=half, bank=bank: e.tensor_tensor(
                s_scr2[:, half * 512:(half + 1) * 512]
                .rearrange("p (h n) -> p h n", h=8),
                t16[:, 3, tl, half * 8:(half + 1) * 8].unsqueeze(2)
                .broadcast_to([128, 8, 64]),
                psum[bank][0:128, 0:512].rearrange("p (h n) -> p h n", h=8),
                OP.mult),
                 reads=("t16_3", f"psum{bank}"), writes=("scr2l",),
                 tag=f"y2{tl}_{half}")
        p.op("DVE", lambda e, tl=tl: e.tensor_tensor(
            s_ytile[:, :], s_ytile[:, :], s_scr2[:, 0:1024], OP.add),
             reads=("ytile", "scr2l"), writes=("ytile",), tag=f"ya{tl}")
        p.op("DVE", lambda e, tl=tl: e.tensor_tensor(
            xhv[:, tl, :].rearrange("p (h n) -> p h n", h=16),
            s_den[:, 16:32].unsqueeze(2).broadcast_to([128, 16, 64]),
            s_ytile[:, :].rearrange("p (h n) -> p h n", h=16), OP.mult),
             reads=("den", "ytile"), writes=(f"y{tl}",), aka=(f"xh{tl}",),
             tag=f"y3{tl}")
        for _ in range(3):
            if f2_units:
                f2_units.pop(0)()
        for dc in range(8):
            pe_t128(yTv[:, sl, tt, dc, :],
                    xhv[:, tl, dc * 128:(dc + 1) * 128],
                    (f"y{tl}",), (f"yT{sl}",), f"yT{tl}_{dc}", waka=(f"xT{sl}",),
                    eveng="ACT")
        if tt == 3:
            for t2 in (0, 1):
                gtl = sbk2 * 4 + t2
                dma("SP", xstage[:, gtl % 2, :],
                    d_x[gtl * 128:(gtl + 1) * 128, :],
                    (), (f"xs{gtl % 2}",), f"ldx2_{gtl}")
            for t2 in range(4):
                gt = sbk2 * 4 + t2
                xsl = gt % 3
                hsl = gt % 4
                x2ap = x2slots[xsl]
                h2ap = h2slots[hsl]
                for half in range(2):
                    bank = half

                    def mmo2(e, sl=sl, t2=t2, half=half, bank=bank):
                        inst = None
                        for dc in range(8):
                            inst = e.matmul(psum[bank][0:128, 0:512],
                                            yTv[:, sl, t2, dc, :],
                                            s_win.ap().rearrange("p (c e) -> p c e", c=8)
                                            [:, dc, half * 512:(half + 1) * 512],
                                            start=(dc == 0), stop=(dc == 7))
                        return inst
                    p.op("PE", mmo2, reads=(f"yT{sl}", "woT"),
                         writes=(f"psum{bank}",), tag=f"op{gt}_{half}")
                    p.op("DVE", lambda e, half=half, bank=bank, x2ap=x2ap,
                         gt=gt: e.tensor_tensor(
                             x2ap[:, half * 512:(half + 1) * 512],
                             psum[bank][0:128, 0:512],
                             xstage[:, gt % 2, half * 512:(half + 1) * 512],
                             OP.add),
                         reads=(f"psum{bank}", f"xs{gt % 2}"),
                         writes=(f"x2t{xsl}",),
                         aka=x2aka[xsl], tag=f"ev2{gt}_{half}")
                dma("ACT", d_x2[gt * 128:(gt + 1) * 128, :], x2ap[:, 0:1024],
                    (f"x2t{xsl}",), (f"d_x2_{gt}",), f"stx2_{gt}")
                if t2 + 2 < 4:
                    gtl = sbk2 * 4 + t2 + 2
                    dma("SP", xstage[:, gtl % 2, :],
                        d_x[gtl * 128:(gtl + 1) * 128, :],
                        (), (f"xs{gtl % 2}",), f"ldx2_{gtl}")
                ln_tile(x2ap[:, 0:1024], 128,
                        s_ln2[:, 2 * gt:2 * gt + 1], s_ln2[:, 2 * gt + 1:2 * gt + 2],
                        (f"x2t{xsl}",), (f"ln2_{gt}",), f"L2{gt}",
                        scr_ap=h2ap, scr_wr=(f"h2t{hsl}",), scr_aka=h2aka[hsl])
                p.op("DVE", lambda e, x2ap=x2ap, h2ap=h2ap, gt=gt: e.tensor_scalar(
                    h2ap[:, 0:1024], x2ap[:, 0:1024],
                    s_ln2[:, 2 * gt:2 * gt + 1], s_ln2[:, 2 * gt + 1:2 * gt + 2],
                    OP.mult, OP.add),
                     reads=(f"x2t{xsl}", f"ln2_{gt}"), writes=(f"h2t{hsl}",),
                     aka=h2aka[hsl], tag=f"h2a{gt}")
                p.op("DVE", lambda e, h2ap=h2ap: e.scalar_tensor_tensor(
                    h2ap[:, 0:1024], h2ap[:, 0:1024], 1.0,
                    s_cbexp[:, 0:1024], OP.bypass, OP.mult),
                     reads=(f"h2t{hsl}", "cbexp"), writes=(f"h2t{hsl}",),
                     tag=f"h2b{gt}")
                p.op("DVE", lambda e, h2ap=h2ap: e.tensor_tensor(
                    h2ap[:, 0:1024], h2ap[:, 0:1024],
                    s_cbexp[:, 1024:2048], OP.add),
                     reads=(f"h2t{hsl}", "sbexp"), writes=(f"h2t{hsl}",),
                     tag=f"h2c{gt}")
                dma("POOL", d_h2[gt * 128:(gt + 1) * 128, :], h2ap[:, 0:1024],
                    (f"h2t{hsl}",), (f"d_h2_{gt}",), f"sth2_{gt}")
                p.op("DVE", lambda e, h2ap=h2ap: e.reduce_max(
                    s_gmax[:, 1:2], h2ap[:, 0:1024], AX.X,
                    apply_absolute_value=True),
                     reads=(f"h2t{hsl}",), writes=("gm1",), tag=f"am{gt}")
                p.op("DVE", lambda e: e.tensor_tensor(
                    s_gmax[:, 0:1], s_gmax[:, 0:1], s_gmax[:, 1:2], OP.max),
                     reads=("gm1", "gm0"), writes=("gm0",), tag=f"amx{gt}")

    tap("y0", xhv[:, 0, :], (128, 1024), BF16, reads=("y0",))
    tap("y17", xhv[:, 17, :], (128, 1024), BF16, reads=("y17",))
    tap("st2", s_st2[:, 0:1040], (128, 1040), reads=("st2",))
    tap("agg", s_agg[0:64, :], (64, 1056), reads=("agg",))
    tap("statF", s_statF[:, :], (128, 1056), reads=("statF",))
    tap("psf", s_psfm[0:16, :], (16, 768), reads=("rmps", "psend"))
    tap("x2a", s_x2t[:, :], (128, 2048), reads=("x2t0", "x2t1"))
    tap("gmax", s_gmax[:, 0:1], (128, 1), reads=("gm0",))

    tap("modc", s_modc[:, :], (128, 48), reads=("modc",))
    tap("capl", s_capl[:, :], (128, 24), reads=("silc", "cacol", "sacol"))
    tap("afm", s_afm[:, :], (128, 4096), reads=("afmA",))
    tap("xh0", xhv[:, 0, :], (128, 1024), BF16, reads=("xh0",))
    tap("xh5", xhv[:, 5, :], (128, 1024), BF16, reads=("xh5",))
    tap("xh31", xhv[:, 31, :], (128, 1024), BF16, reads=("xh31",))
    tap("uv", s_uv[0:2, :], (2, E), BF16, reads=("uv",))
    tap("k8", s_k8[0:2, :], (2, 1024), BF16, reads=("k8",))
    tap("xT", s_xT[:, :], (128, 8192), BF16, reads=("xT0", "xT1"))
    tap("carry", s_carry[:, :], (128, 24),
        reads=tuple(f"carry{ec}" for ec in range(8)))

    while f2_units:
        f2_units.pop(0)()
    f2_finalize()

    p.op("PE", lambda e: e.transpose(ps_x[0:1, 0:128], s_gmax[:, 0:1],
                                     s_id_f[:, :]),
         reads=("gm0", "c:id_f"), writes=("ps_x",), tag="Tgm")
    p.op("DVE", lambda e: e.tensor_copy(s_scr2[0:1, 0:128], ps_x[0:1, 0:128]),
         reads=("ps_x",), writes=("gmrow",), aka=("scr2l", "rmrow"), tag="gmc")
    p.op("DVE", lambda e: e.reduce_max(s_mcol[0:1, 8:9], s_scr2[0:1, 0:128],
                                       AX.X),
         reads=("gmrow",), writes=("gmcell",), aka=("mcol8",), tag="gmr")
    p.op("DVE", lambda e: e.memset(s_mcol[0:1, 9:10], 0.0),
         writes=("gmz",), aka=("mcol9",), tag="gmz")
    dma("POOL", d_rin, s_mcol[0:1, 8:10], ("gmcell", "gmz"), ("d_rin",), "gri")
    p.op("POOL", lambda e: e.collective_compute(
        "AllReduce", OP.max, replica_groups=GROUPS8,
        ins=[d_rin.opt()], outs=[d_rout.opt()]),
         reads=("d_rin",), writes=("d_rout",), cc_sem="cc1", tag="ccar1")
    dma("POOL", s_qcol[0:1, 0:1], d_rout[0:1, 0:1], ("d_rout",), ("gmaxg",), "gro")

    warmup(AF.Sigmoid)
    p.op("ACT", lambda e: e.activation(s_qcol[0:1, 4:6], s_mcol[0:1, 0:2],
                                       AF.Sigmoid),
         reads=("c:clipx", "c:clipd", "fence1"), writes=("sigs",), tag="sig")
    p.op("DVE", lambda e: e.tensor_tensor(s_qcol[0:1, 2:3], s_qcol[0:1, 4:5],
                                          s_qcol[0:1, 0:1], OP.mult),
         reads=("sigs", "gmaxg"), writes=("s1c",), tag="s1a")
    p.op("DVE", lambda e: e.tensor_scalar(s_qcol[0:1, 2:3], s_qcol[0:1, 2:3],
                                          1.0 / 127.0, None, OP.mult),
         reads=("s1c",), writes=("s1c",), tag="s1b")
    p.op("DVE", lambda e: e.reciprocal(s_qcol[0:1, 3:4], s_qcol[0:1, 2:3]),
         reads=("s1c",), writes=("inv1",), tag="s1r")
    p.op("DVE", lambda e: e.tensor_scalar(s_qcol[0:1, 6:7] if False else
                                          s_mcol[0:1, 8:9], s_qcol[0:1, 2:3],
                                          1.0 / 127.0, None, OP.mult),
         reads=("s1c",), writes=("s1d",), aka=("gmcell",), tag="s1d")

    def expand_cell(dst_col, src_cell, rd, wr, tagp):
        p.op("PE", lambda e: e.matmul(ps_x[0:128, 0:1], s_onesf[0:1, :],
                                      src_cell, start=True, stop=True),
             reads=tuple(rd) + ("c:onesf",), writes=("ps_x",), tag="xm" + tagp)
        p.op("DVE", lambda e: e.tensor_copy(dst_col, ps_x[0:128, 0:1]),
             reads=("ps_x",), writes=tuple(wr), tag="xc" + tagp)

    expand_cell(sgu[:, 159:160], s_qcol[0:1, 3:4], ("inv1",), ("inv1col",), "i1")
    expand_cell(sgu[:, 156:157], s_mcol[0:1, 8:9], ("s1d",), ("s1dcol",), "s1")
    p.op("DVE", lambda e: e.tensor_tensor(
        sgu[:, 104:126], sgu[:, 0:22],
        sgu[:, 156:157].broadcast_to([128, 22]), OP.mult),
         reads=("sgGU", "s1dcol"), writes=("cgcu",), tag="cg")
    p.op("DVE", lambda e: e.tensor_tensor(
        sgu[:, 126:148], sgu[:, 22:44],
        sgu[:, 156:157].broadcast_to([128, 22]), OP.mult),
         reads=("sgGU", "s1dcol", "cgcu"), writes=("cgcu",), tag="cu")

    warmup(AF.Copy)
    xqv = s_big.ap().rearrange("p (c t) -> p c t", c=8)
    fscv = s_fsc.ap().rearrange("p (r d) -> p r d", r=2)
    wb1v = s_wb1.ap().rearrange("p (r d) -> p r d", r=2)
    for gt in range(NT):
        xsl = gt % 2
        dma("POOL", fscv[:, xsl, :], d_h2[gt * 128:(gt + 1) * 128, :],
            (f"d_h2_{gt}",), (f"fh{xsl}",), f"ldh2_{gt}",
            aka=("xT1", "yT1"))
        p.op("ACT", lambda e, xsl=xsl: e.activation(
            fscv[:, xsl, :], fscv[:, xsl, :], AF.Copy, scale=sgu[:, 159:160]),
             reads=(f"fh{xsl}", "inv1col", "fence1"), writes=(f"fh{xsl}",),
             tag=f"qs{gt}")
        p.op("DVE", lambda e, xsl=xsl: e.tensor_scalar(
            fscv[:, xsl, :], fscv[:, xsl, :], MAGIC, MAGIC, OP.add, OP.subtract),
             reads=(f"fh{xsl}",), writes=(f"fh{xsl}",), tag=f"qr{gt}")
        p.op("DVE", lambda e, xsl=xsl: e.tensor_scalar(
            wb1v[:, xsl, 0:1024], fscv[:, xsl, :], 127.0, -128.0,
            OP.min, OP.max),
             reads=(f"fh{xsl}",), writes=(f"xq{xsl}",), aka=WB1_AKA,
             tag=f"qc{gt}")
        for dc in range(8):
            pe_t128(xqv[:, dc, gt * 128:(gt + 1) * 128],
                    wb1v[:, xsl, dc * 128:(dc + 1) * 128],
                    (f"xq{xsl}",), (f"xqT{dc}_{gt >> 2}",), f"xqT{gt}_{dc}",
                    waka=(f"y{dc * 4 + (gt >> 3)}", f"xh{dc * 4 + (gt >> 3)}"),
                    rot=TP4)

    warmup(AF.Silu)
    p.op("DVE", lambda e: e.memset(s_gmax[:, 2:4], 0.0), writes=("gm2",),
         tag="zg2")
    wqgv = s_wqg.ap().rearrange("p (r c e) -> p r c e", r=2, c=8)
    wquv = s_wqu.ap().rearrange("p (r c e) -> p r c e", r=2, c=8)
    rawslots = [s_wnat.ap()[:, 0:1024], s_wnat.ap()[:, 1024:2048],
                s_x4.ap()[:, 0:1024], s_x4.ap()[:, 1024:2048]]
    raw3 = [a.rearrange("p (c e) -> p c e", c=8) for a in rawslots]
    RAW_AKA = [WNAT_AKA, WNAT_AKA,
               ("x2t0", "x2t1", "h2t0", "h2t1"),
               ("x2t0", "x2t1", "h2t0", "h2t1")]
    dmpv = s_dump.ap().rearrange("p (c e) -> p c e", c=8)

    def wq_load(ic):
        for w_i, dsrc in enumerate((d_wgT, d_wuT2)):
            slot = 2 * (ic % 2) + w_i
            dma("POOL", raw3[slot][:, :, :],
                dsrc[ic * 1024:(ic + 1) * 1024, :]
                .rearrange("(c p) i -> p c i", p=128),
                (), (f"wnat{slot}",), f"lq{w_i}_{ic}", aka=RAW_AKA[slot])

    def wq_q(ic):
        wsl = ic % 2
        for w_i, (ivc, dstv, wres) in enumerate((
                (sgu[:, 60 + ic:61 + ic],
                 s_wqg.ap()[:, wsl * 1024:(wsl + 1) * 1024], f"wqg{wsl}"),
                (sgu[:, 82 + ic:83 + ic],
                 s_wqu.ap()[:, wsl * 1024:(wsl + 1) * 1024], f"wqu{wsl}"))):
            slot = 2 * (ic % 2) + w_i
            eslot = 2 * (ic % 2) + w_i
            expand_row(s_gdq[:, eslot * 128:(eslot + 1) * 128], ivc,
                       ("sgInv",), (f"ivex{eslot}",), f"ie{w_i}_{ic}",
                       waka=("x2t2",))
            p.op("DVE", lambda e, slot=slot, eslot=eslot: e.tensor_tensor(
                dmpv[:, :, :], raw3[slot][:, :, :],
                s_gdq[:, eslot * 128:(eslot + 1) * 128].unsqueeze(1)
                .broadcast_to([128, 8, 128]), OP.mult),
                 reads=(f"wnat{slot}", f"ivex{eslot}"), writes=("dump",),
                 aka=("h2t2",), tag=f"wm{w_i}_{ic}")
            p.op("DVE", lambda e, dstv=dstv: e.tensor_scalar(
                dstv, s_dump.ap(), MAGIC, MAGIC, OP.add, OP.subtract),
                 reads=("dump",), writes=(wres,), tag=f"wr{w_i}_{ic}")

    def mm_block(ic):
        wsl = ic % 2
        for tb in range(8):
            k = ic * 8 + tb
            bG = 0 if k % 2 == 0 else 2
            bU = 1 if k % 2 == 0 else 3
            h3sl = k % 2

            def mmg(e, wsl=wsl, tb=tb, bG=bG):
                inst = None
                for hc in range(8):
                    inst = e.matmul(psum[bG][0:128, 0:512], wqgv[:, wsl, hc, :],
                                    xqv[:, hc, tb * 512:(tb + 1) * 512],
                                    start=(hc == 0), stop=(hc == 7))
                return inst

            def mmu(e, wsl=wsl, tb=tb, bU=bU):
                inst = None
                for hc in range(8):
                    inst = e.matmul(psum[bU][0:128, 0:512], wquv[:, wsl, hc, :],
                                    xqv[:, hc, tb * 512:(tb + 1) * 512],
                                    start=(hc == 0), stop=(hc == 7))
                return inst

            xqres = tuple(f"xqT{hc}_{tb}" for hc in range(8))
            p.op("PE", mmg, reads=(f"wqg{wsl}",) + xqres, writes=(f"psum{bG}",),
                 tag=f"gm{ic}_{tb}")
            p.op("PE", mmu, reads=(f"wqu{wsl}",) + xqres, writes=(f"psum{bU}",),
                 tag=f"um{ic}_{tb}")
            p.op("ACT", lambda e, ic=ic, bG=bG: e.activation(
                s_fsc[:, 0:512], psum[bG][0:128, 0:512], AF.Silu,
                scale=sgu[:, 104 + ic:105 + ic]),
                 reads=(f"psum{bG}", "cgcu", "fence1"), writes=("fsil",),
                 aka=("fh0",), tag=f"sil{ic}_{tb}")
            p.op("ACT", lambda e, ic=ic, bU=bU: e.activation(
                s_fsc[:, 512:1024], psum[bU][0:128, 0:512], AF.Copy,
                scale=sgu[:, 126 + ic:127 + ic]),
                 reads=(f"psum{bU}", "cgcu", "fence1"), writes=("ftup",),
                 aka=("fh0",), tag=f"tup{ic}_{tb}")
            p.op("DVE", lambda e, h3sl=h3sl: e.tensor_tensor(
                s_fsc[:, 1024 + h3sl * 512:1024 + (h3sl + 1) * 512],
                s_fsc[:, 0:512], s_fsc[:, 512:1024], OP.mult),
                 reads=("fsil", "ftup"), writes=(f"fh3{h3sl}",),
                 aka=("fh1",), tag=f"h3{ic}_{tb}")
            p.op("DVE", lambda e, h3sl=h3sl: e.reduce_max(
                s_gmax[:, 3:4],
                s_fsc[:, 1024 + h3sl * 512:1024 + (h3sl + 1) * 512], AX.X,
                apply_absolute_value=True),
                 reads=(f"fh3{h3sl}",), writes=("gm3",), tag=f"hr{ic}_{tb}")
            p.op("DVE", lambda e: e.tensor_tensor(
                s_gmax[:, 2:3], s_gmax[:, 2:3], s_gmax[:, 3:4], OP.max),
                 reads=("gm3", "gm2"), writes=("gm2",), tag=f"hx{ic}_{tb}")
            dma("POOL", d_h3[(ic * 8 + tb) * 128:(ic * 8 + tb + 1) * 128, :],
                s_fsc[:, 1024 + h3sl * 512:1024 + (h3sl + 1) * 512],
                (f"fh3{h3sl}",), (f"d_h3_{ic}_{tb}",), f"sh3_{ic}_{tb}")

    for ic in range(NIC + 2):
        if ic < NIC:
            wq_load(ic)
        if 1 <= ic < NIC + 1:
            wq_q(ic - 1)
        if ic >= 2:
            mm_block(ic - 2)

    p.op("PE", lambda e: e.transpose(ps_x[0:1, 0:128], s_gmax[:, 2:3],
                                     s_id_f[:, :]),
         reads=("gm2", "c:id_f"), writes=("ps_x",), tag="Tg2")
    p.op("DVE", lambda e: e.tensor_copy(s_scr2[0:1, 0:128], ps_x[0:1, 0:128]),
         reads=("ps_x",), writes=("gmrow",), tag="g2c")
    p.op("DVE", lambda e: e.reduce_max(s_mcol[0:1, 6:7], s_scr2[0:1, 0:128],
                                       AX.X),
         reads=("gmrow",), writes=("gm2cell",), tag="g2r")
    p.op("DVE", lambda e: e.memset(s_mcol[0:1, 7:8], 0.0),
         writes=("gm2z",), tag="g2z")
    dma("POOL", d_rin2, s_mcol[0:1, 6:8], ("gm2cell", "gm2z"), ("d_rin2",),
        "g2i")
    p.op("POOL", lambda e: e.collective_compute(
        "AllReduce", OP.max, replica_groups=GROUPS8,
        ins=[d_rin2.opt()], outs=[d_rout2.opt()]),
         reads=("d_rin2",), writes=("d_rout2",), cc_sem="cc2", tag="ccar2")
    dma("POOL", s_qcol[0:1, 1:2], d_rout2[0:1, 0:1], ("d_rout2",), ("gmax2g",),
        "g2o")

    for j in range(8):
        expand_row(s_dexp[:, j * 128:(j + 1) * 128], sgu[:, 52 + j:53 + j],
                   ("sgInv",), ("dexp",), f"dx{j}", waka=("cbexp", "sbexp"))
        expand_row(s_gbexp[:, j * 128:(j + 1) * 128], s_modc[:, 40 + j:41 + j],
                   ("modc",), ("gbexp",), f"gb{j}", waka=("gaexp",))
    wdqv = s_big.ap()[:, 0:NIC * 1024].rearrange("p (c e) -> p c e", c=NIC)
    for ic in range(NIC):
        slot = nslot[0] % 2
        nslot[0] += 1
        dma("POOL", s_wnat[:, slot * 1024:(slot + 1) * 1024],
            d_wdT[ic * 128:(ic + 1) * 128, :], (), (f"wnat{slot}",),
            f"lwd{ic}", aka=WNAT_AKA)
        p.op("DVE", lambda e, slot=slot: e.tensor_tensor(
            s_dump.ap(), s_wnat[:, slot * 1024:(slot + 1) * 1024],
            s_dexp.ap(), OP.mult),
             reads=(f"wnat{slot}", "dexp"), writes=("dump",),
             aka=("h2t2",), tag=f"wdm{ic}")
        p.op("DVE", lambda e, ic=ic: e.tensor_scalar(
            wdqv[:, ic, :], s_dump.ap(), MAGIC, MAGIC, OP.add, OP.subtract),
             reads=("dump",), writes=(f"wdq{ic}",),
             aka=tuple(f"xqT{ic // 4}_{sb}" for sb in range(8)),
             tag=f"wdr{ic}")

    p.op("DVE", lambda e: e.tensor_tensor(s_qcol[0:1, 2:3], s_qcol[0:1, 5:6],
                                          s_qcol[0:1, 1:2], OP.mult),
         reads=("sigs", "gmax2g", "s1c", "inv1"), writes=("s2c",), tag="s2a")
    p.op("DVE", lambda e: e.tensor_scalar(s_qcol[0:1, 2:3], s_qcol[0:1, 2:3],
                                          1.0 / 127.0, None, OP.mult),
         reads=("s2c",), writes=("s2c",), tag="s2b")
    p.op("DVE", lambda e: e.reciprocal(s_qcol[0:1, 3:4], s_qcol[0:1, 2:3]),
         reads=("s2c", "inv1col"), writes=("inv2",), tag="s2r")
    expand_cell(sgu[:, 158:159], s_qcol[0:1, 3:4], ("inv2",), ("inv2col",), "i2")
    expand_cell(sgu[:, 157:158], s_qcol[0:1, 2:3], ("s2c",), ("s2col",), "s2")
    for j in range(8):
        expand_row(s_gdq[:, j * 128:(j + 1) * 128], sgu[:, 148 + j:149 + j],
                   ("sgDdq",), ("gdq",), f"gq{j}",
                   waka=("ivex0", "ivex1", "ivex2", "ivex3"))
    p.op("DVE", lambda e: e.tensor_tensor(s_gdq.ap(), s_gdq.ap(),
                                          s_gbexp.ap(), OP.mult),
         reads=("gdq", "gbexp", "ivex0", "ivex1", "ivex2", "ivex3"),
         writes=("gdq",), tag="gdm")
    p.op("DVE", lambda e: e.tensor_scalar(s_gdq.ap(), s_gdq.ap(),
                                          sgu[:, 157:158], None, OP.mult),
         reads=("gdq", "s2col"), writes=("gdq",), tag="gds")

    fscv2 = s_fsc.ap().rearrange("p (s e) -> p s e", s=4)
    h3q2 = s_wqg.ap().rearrange("p (s e) -> p s e", s=4)
    x4v = s_x4.ap().rearrange("p (s e) -> p s e", s=4)
    banks = [psum[0], psum[1], psum[2], psum[3], psum[4], ps_x, ps_trA, ps_trB]
    bres = ["psum0", "psum1", "psum2", "psum3", "psum4", "ps_x", "ps_tr0",
            "ps_tr1"]
    otbuf = [s_ytile.ap(), s_dump.ap()[:, 0:1024]]
    otaka = [("cs0", "cs1", "ytile", "dstmp", "st2"),
             ("dump", "psfm", "ostmp", "st2")]
    for g in range(8):
        for t4 in range(4):
            t = g * 4 + t4
            dma("POOL", x4v[:, t4, :], d_x2[t * 128:(t + 1) * 128, :],
                (f"d_x2_{t}",), (f"xs{t4}",), f"lx2_{t}",
                aka=("x2t0", "x2t1", "h2t0", "h2t1", "wnat2", "wnat3"))
        for ic in range(NIC):
            fsl = ic % 4
            qsl = ic % 4
            fres = f"f6_{fsl}"
            qres = f"h3w{qsl}"
            dma("SP", fscv2[:, fsl, :],
                d_h3[(ic * 8 + g) * 128:(ic * 8 + g + 1) * 128, :],
                (f"d_h3_{ic}_{g}",), (fres,),
                f"lh3_{g}_{ic}",
                aka=("fsil", "ftup", "fh30", "fh31", "fh0", "fh1"))
            p.op("ACT", lambda e, fsl=fsl: e.activation(
                fscv2[:, fsl, :], fscv2[:, fsl, :], AF.Copy,
                scale=sgu[:, 158:159]),
                 reads=(fres, "inv2col", "fence1"), writes=(fres,),
                 tag=f"q2s{g}_{ic}")
            p.op("DVE", lambda e, fsl=fsl: e.tensor_scalar(
                fscv2[:, fsl, :], fscv2[:, fsl, :], 127.0, -128.0,
                OP.min, OP.max),
                 reads=(fres,), writes=(fres,), tag=f"q2c{g}_{ic}")
            p.op("DVE", lambda e, fsl=fsl, qsl=qsl: e.tensor_scalar(
                h3q2[:, qsl, :], fscv2[:, fsl, :], MAGIC, MAGIC,
                OP.add, OP.subtract),
                 reads=(fres,), writes=(qres,),
                 aka=("wqg0", "wqg1"), tag=f"q2r{g}_{ic}")
            for t4 in range(4):
                for half in range(2):
                    bi = t4 * 2 + half
                    p.op("PE", lambda e, qsl=qsl, ic=ic, t4=t4, half=half,
                         bi=bi: e.matmul(
                             banks[bi][0:128, 0:512],
                             h3q2[:, qsl, t4 * 128:(t4 + 1) * 128],
                             wdqv[:, ic, half * 512:(half + 1) * 512],
                             start=(ic == 0), stop=(ic == NIC - 1)),
                         reads=(qres, f"wdq{ic}"), writes=(bres[bi],),
                         tag=f"dm{g}_{ic}_{bi}")
        for t4 in range(4):
            t = g * 4 + t4
            osl = t4 % 2
            for half in range(2):
                bi = t4 * 2 + half
                p.op("DVE", lambda e, bi=bi, half=half, osl=osl:
                     e.scalar_tensor_tensor(
                         otbuf[osl][:, half * 512:(half + 1) * 512],
                         banks[bi][0:128, 0:512], 1.0,
                         s_gdq[:, half * 512:(half + 1) * 512],
                         OP.bypass, OP.mult),
                     reads=(bres[bi], "gdq"), writes=(f"ot{osl}",),
                     aka=otaka[osl], tag=f"oe{t}_{half}")
                p.op("DVE", lambda e, half=half, t4=t4, osl=osl:
                     e.tensor_tensor(
                         otbuf[osl][:, half * 512:(half + 1) * 512],
                         otbuf[osl][:, half * 512:(half + 1) * 512],
                         x4v[:, t4, half * 512:(half + 1) * 512], OP.add),
                     reads=(f"ot{osl}", f"xs{t4}"), writes=(f"ot{osl}",),
                     tag=f"or{t}_{half}")
            dma("POOL", d_out[t * 128:(t + 1) * 128, :], otbuf[osl][:, 0:1024],
                (f"ot{osl}",), (f"ow{t % 4}",), f"sto{t}")

    tap("sgu", sgu[:, :], (128, 160), reads=("sgGU", "sgD", "sgInv", "sgDdq",
                                             "cgcu", "inv1col"))
    tap("qc", s_qcol[0:1, 0:6], (1, 6), reads=("s1c", "inv1", "sigs", "gmaxg",
                                               "s2c", "inv2", "gmax2g"))
    tap("xqT6", xqv[:, 6, 0:1024], (128, 1024), BF16,
        reads=("xqT6_0", "xqT6_1"))
    tap("gm2t", s_gmax[:, 2:3], (128, 1), reads=("gm2",))

    p.op("ACT", p.echo["ACT"], tag="echoF")
    p.barrier_all_on("POOL")
    p.barrier_all_on("SP")

    import contextlib
    stack = contextlib.ExitStack()
    sem_handles = {}
    sems = ["PE", "ACT", "DVE", "POOL", "cc0", "cc1", "cc2"]
    for q in QUEUES.values():
        sems += [f"{q}s{i}" for i in range(Plan.RSLOT)]
    for s in sems:
        sem_handles[s] = stack.enter_context(nc.semaphore("sem_" + s))
    blk = stack.enter_context(nc.Block())

    def run_engine(name):
        def body(h):
            for o in p.eng_ops[name]:
                for sp, val in o["waits"]:
                    h.wait_ge(sem_handles[sp], val)
                if o["emit"] is None:
                    continue
                inst = o["emit"](h)
                if o["cc"] is not None:
                    inst.then_inc(sem_handles[o["cc"]], 1)
                elif o["dma"]:
                    inst.then_inc(sem_handles[o["sem"]], 16)
                elif o.get("stamp", True):
                    inst.then_inc(sem_handles[name], 1)
        return body

    blk.tensor(run_engine("PE"))
    blk.scalar(run_engine("ACT"))
    blk.vector(run_engine("DVE"))
    blk.gpsimd(run_engine("POOL"))
    blk.sync(run_engine("SP"))
    stack.close()
    n_ops = sum(len(v) for v in p.eng_ops.values())
    return nc, tap_d, n_ops


def host_inputs(inputs):
    X = np.ascontiguousarray(inputs["L0_attn_x"], dtype=np.float32)
    winT = np.ascontiguousarray(np.asarray(inputs["L0_attn_in_proj_w"]).T, dtype=np.float32)
    convw = np.ascontiguousarray(
        np.asarray(inputs["L0_attn_conv_w"])[:, 0, :].reshape(8, 128, 4)
        .transpose(1, 0, 2).reshape(128, 32), dtype=np.float32)
    convb = np.ascontiguousarray(np.asarray(inputs["L0_attn_conv_b"]).reshape(8, 128).T,
                                 dtype=np.float32)
    wbase = np.ascontiguousarray(np.asarray(inputs["L0_attn_w_base"]).reshape(16, 1),
                                 dtype=np.float32)
    woT = np.ascontiguousarray(np.asarray(inputs["L0_attn_out_proj_w"]).T, dtype=np.float32)
    args = dict(
        wmod=np.ascontiguousarray(inputs["L0_wmod"], dtype=np.float32),
        wmodb=np.ascontiguousarray(
            np.asarray(inputs["L0_wmod"], dtype=np.float32)
            .reshape(8, 128, 12, 512).transpose(0, 2, 1, 3)
            .reshape(96 * 128, 512)),
        winT=winT, convw=convw, convb=convb, wbase=wbase, woT=woT,
        wg=np.ascontiguousarray(inputs["L0_ffn_w_gate"], dtype=np.float32),
        wu=np.ascontiguousarray(inputs["L0_ffn_w_up"], dtype=np.float32),
        wd=np.ascontiguousarray(inputs["L0_ffn_w_down"], dtype=np.float32),
        wgTb=np.ascontiguousarray(
            np.asarray(inputs["L0_ffn_w_gate"], dtype=np.float32).T
            .reshape(1024, 22, 128).transpose(1, 0, 2).reshape(22 * 1024, 128)),
        wuTb=np.ascontiguousarray(
            np.asarray(inputs["L0_ffn_w_up"], dtype=np.float32).T
            .reshape(1024, 22, 128).transpose(1, 0, 2).reshape(22 * 1024, 128)),
        wdT=np.ascontiguousarray(np.asarray(inputs["L0_ffn_w_down"],
                                            dtype=np.float32).T),
        clipx=np.ascontiguousarray(np.asarray(inputs["L0_ffn_clip_x"]).reshape(1, 1),
                                   dtype=np.float32),
        clipd=np.ascontiguousarray(np.asarray(inputs["L0_ffn_clip_down"]).reshape(1, 1),
                                   dtype=np.float32),
    )
    Lc = np.ascontiguousarray(inputs["L0_c"], dtype=np.float32)
    zeros3 = np.zeros((3, 1024), dtype=np.float32)
    in_maps = []
    for c in range(8):
        b0, half = c >> 1, c & 1
        in_maps.append(dict(
            x4096=np.ascontiguousarray(X[b0, half * T:(half + 1) * T]),
            xhalo=(np.ascontiguousarray(X[b0, half * T - 3:half * T]) if half else zeros3),
            mask3=np.full((3, 1), 1.0 if half else 0.0, dtype=np.float32),
            crow=np.ascontiguousarray(Lc[b0:b0 + 1]),
            halfsel=np.ascontiguousarray(
                np.stack([np.full(128, 1.0 - half, np.float32),
                          np.full(128, float(half), np.float32)], axis=1)),
            **args))
    return in_maps


def kernel(**inputs):
    nc, _, _ = build_program(taps=DEBUG_TAPS)
    in_maps = host_inputs(inputs)
    res = run_bass_kernel_spmd(nc, in_maps, core_ids=list(range(8)))
    out = np.empty((4, 8192, 1024), dtype=np.float32)
    for c in range(8):
        b0, half = c >> 1, c & 1
        out[b0, half * T:(half + 1) * T] = np.asarray(res.results[c]["out"])
    return out
